# version 02
# baseline (speedup 1.0000x reference)
import numpy as np
import ml_dtypes
from contextlib import ExitStack

import concourse.bass as bass
import concourse.mybir as mybir
from concourse.bass_utils import run_bass_kernel_spmd

F32 = mybir.dt.float32
BF16 = mybir.dt.bfloat16
U8 = mybir.dt.uint8
I32 = mybir.dt.int32
ALU = mybir.AluOpType
AF = mybir.ActivationFunctionType
AX = mybir.AxisListType.X
BF = ml_dtypes.bfloat16

T = 4096
D = 1024
NT = T // 128
DC = D // 128
NH = 16
DH = 64
NEXP = 8
DFF = 4096
IDEPTH = 6
I3 = 3072
DI2 = 5460
DI = 2730

NEG = -1.0e30


def host_constants():
    c = {}
    pos = np.arange(T, dtype=np.float64)

    kk = np.arange(128)[:, None]
    qq = np.arange(512)[None, :]
    m0 = np.stack([(qq >= kk + 128 * o) for o in range(4)]).astype(BF)
    c["masks0"] = m0

    half = 32
    inv = 1.0 / (10000.0 ** (np.arange(half, dtype=np.float64) / half))
    ang = pos[:, None] * inv[None, :]
    cosA = np.cos(ang); sinA = np.sin(ang)
    cosAc = np.concatenate([cosA, cosA], 1)
    sinAs = np.concatenate([-sinA, sinA], 1)
    sc_q = float(DH) ** -0.5
    c["ropeA"] = np.stack([cosAc * sc_q, sinAs * sc_q, cosAc, sinAs]).astype(np.float32)

    qi = np.arange(128)[:, None]
    cc = np.arange(256)[None, :]
    mA = np.where((cc >= qi) & (cc <= qi + 128), 0.0, NEG)
    c["maskA"] = mA.astype(np.float32)

    hidx = np.arange(0, DH, 2, dtype=np.float64)
    invI = 1.0 / (10000.0 ** (hidx / DH))
    fr = pos[:, None] * invI[None, :]
    frc = np.concatenate([fr, fr], 1)
    cosI = np.cos(frc); sinI = np.sin(frc)
    base = (hidx + 0.4 * DH) / (1.4 * DH)
    power = (pos - T // 2) / (256 // 2)
    scale = base[None, :] ** power[:, None]
    scl = np.concatenate([scale, scale], 1)
    sgn = np.concatenate([-np.ones(32), np.ones(32)])[None, :]
    QKS = 8.0
    c["ropeI"] = np.stack([
        cosI * scl * QKS,
        sinI * scl * sgn * QKS,
        cosI / scl,
        sinI / scl * sgn,
    ]).astype(np.float32)

    cc = np.arange(384)[None, :]
    mI = np.where((cc >= qi) & (cc <= qi + 256), 0.0, NEG)
    c["maskI"] = mI.astype(np.float32)
    return c


DEBUG_PRES = []

class SemC:
    __slots__ = ("h", "n", "name", "eng")
    def __init__(self, h, name):
        self.h = h; self.n = 0; self.name = name; self.eng = None

class Tok:
    __slots__ = ("sem", "kind", "val", "frame", "off", "lb", "mul", "add")
    def __init__(self, sem, kind, val=0, frame=None, off=0, lb=0, mul=0, add=0):
        self.sem = sem; self.kind = kind; self.val = val
        self.frame = frame; self.off = off; self.lb = lb
        self.mul = mul; self.add = add

def tok_at(t, g):
    f = t.frame
    return Tok(t.sem, "abs", f.base[t.sem] + f.pre.get(t.sem, 0)
               + g * f.delta[t.sem] + t.off)

class Frame:
    def __init__(self, n, engines, managed=False):
        self.n = n
        self.engines = engines
        self.managed = managed
        self.base = {}
        self.delta = {}
        self.pre = {}
        self.ivar = None

class Prog:
    ENGINES = ("sync", "gpsimd", "scalar", "vector", "tensor")

    def __init__(self, nc, ctx):
        self.nc = nc
        self.ctx = ctx
        self.segments = [("straight", [])]
        self.sems = []
        self.cur_frame = None

    def sem(self, name):
        s = SemC(self.ctx.enter_context(self.nc.semaphore(name)), name)
        self.sems.append(s)
        return s

    def emit(self, eng, fn, waits=(), inc=None, inc_by=1, fence=False):
        ws = list(waits)
        self.segments[-1][-1].append((eng, tuple(ws), fn, inc, inc_by, fence))
        if inc is None:
            return None
        if inc.eng is None:
            inc.eng = eng
        elif inc.eng != eng:
            inc.eng = "MIXED"
        f = self.cur_frame
        if f is None:
            inc.n += inc_by
            return Tok(inc, "abs", inc.n)
        else:
            f.delta[inc] = f.delta.get(inc, 0) + inc_by
            return Tok(inc, "rel", frame=f, off=f.delta[inc])

    def loop(self, n, engines, managed=False):
        prog = self
        class _LoopCM:
            def __enter__(cm):
                f = Frame(n, engines, managed)
                f.base = {s: s.n for s in prog.sems}
                prog.cur_frame = f
                prog.segments.append(("loop", f, []))
                cm.f = f
                return f
            def __exit__(cm, *a):
                f = cm.f
                for (_e, ws, *_r) in prog.segments[-1][-1]:
                    for t in ws:
                        if t.kind == "rel" and t.frame is f and t.lb > 0:
                            need = t.lb * f.delta.get(t.sem, 0) - t.off
                            if need > 0:
                                f.pre[t.sem] = max(f.pre.get(t.sem, 0), need)
                for s, d in f.delta.items():
                    s.n = f.base[s] + n * d + f.pre.get(s, 0)
                DEBUG_PRES.append({s.name: v for s, v in f.pre.items()})
                prog.cur_frame = None
                prog.segments.append(("straight", []))
                return False
        return _LoopCM()


    def _emit_loop_body(self, nc, EMAP, f, ops, i, ivreg):
        from concourse.expressions import RuntimeValue
        seen = set()
        for (e, ws, fn, inc, inc_by, fence) in ops:
            eng = EMAP[e]
            for t in ws:
                if t.kind == "abs":
                    key = (e, t.sem, "abs", t.val)
                    if t.val > 0 and key not in seen:
                        eng.wait_ge(t.sem.h, t.val)
                        seen.add(key)
                elif t.kind == "expr":
                    key = (e, t.sem, "expr", t.mul, t.add)
                    if key not in seen:
                        if ivreg is None:
                            eng.wait_ge(t.sem.h, i * t.mul + t.add)
                        else:
                            eng.wait_ge(t.sem.h, ivreg(eng, t.mul, t.add))
                        seen.add(key)
                elif t.kind == "exprfn":
                    key = (e, t.sem, "exprfn", t.add)
                    if key not in seen:
                        assert ivreg is None, "exprfn unsupported in managed loop"
                        eng.wait_ge(t.sem.h, t.mul(i))
                        seen.add(key)
                elif t.kind == "ivw":
                    key = (e, t.sem, "ivw", tuple(sorted(t.mul.items())))
                    if key not in seen:
                        assert ivreg is not None
                        eng.wait_ge(t.sem.h, ivreg(eng, **t.mul))
                        seen.add(key)
                else:
                    d = f.delta.get(t.sem, 0)
                    k = t.off - t.lb * d + f.pre.get(t.sem, 0)
                    key = (e, t.sem, "rel", d, k)
                    if key not in seen:
                        if ivreg is None:
                            eng.wait_ge(t.sem.h, i * d + (f.base[t.sem] + k))
                        else:
                            eng.wait_ge(t.sem.h, ivreg(eng, d, f.base[t.sem] + k))
                        seen.add(key)
            if ivreg is None:
                inst = fn(eng, i)
            else:
                inst = fn(eng, lambda mul, add, mod=None, vmin=None, shr=None, pmul=None, eng=eng: RuntimeValue(ivreg(eng, mul, add, mod, vmin, shr, pmul)))
            if fence:
                inst = eng.drain()
            if inc is not None:
                assert inst is not None
                inst.then_inc(inc.h, inc_by)

    def barrier_token(self):
        assert self.cur_frame is None
        return [Tok(s, "abs", s.n) for s in self.sems if s.n > 0]

    def finalize(self):
        nc = self.nc
        EMAP = {"sync": nc.sync, "gpsimd": nc.gpsimd, "scalar": nc.scalar,
                "vector": nc.vector, "tensor": nc.tensor}
        ETYPE = {"sync": mybir.EngineType.SP, "gpsimd": mybir.EngineType.Pool,
                 "scalar": mybir.EngineType.Activation, "vector": mybir.EngineType.DVE,
                 "tensor": mybir.EngineType.PE}
        for seg in self.segments:
            if seg[0] == "straight":
                ops = seg[1]
                satisfied = {}
                for (e, ws, fn, inc, inc_by, fence) in ops:
                    eng = EMAP[e]
                    for t in ws:
                        assert t.kind == "abs"
                        key = (e, t.sem)
                        if t.val > 0 and satisfied.get(key, -1) < t.val:
                            eng.wait_ge(t.sem.h, t.val)
                            satisfied[key] = t.val
                    inst = fn(eng, None)
                    if fence:
                        inst = eng.drain()
                    if inc is not None:
                        assert inst is not None
                        inst.then_inc(inc.h, inc_by)
            else:
                _, f, ops = seg
                for s, pch in f.pre.items():
                    if pch > 0:
                        assert s.eng in EMAP, f"pre-charge on mixed-engine sem {s.name}"
                        EMAP[s.eng].drain()
                        EMAP[s.eng].nop(nofuse=True).then_inc(s.h, pch)
                engs = [ETYPE[e] for e in f.engines]
                if not f.managed:
                    with nc.Fori(0, f.n, engines=engs) as i:
                        self._emit_loop_body(nc, EMAP, f, ops, i, None)
                else:
                    def pyval(i, mul, add, mod=None, vmin=None, shr=None, pmul=None):
                        if mul == 0:
                            return add
                        r = i * mul
                        if mod is not None:
                            r %= mod
                        if shr is not None:
                            r >>= shr
                        if vmin is not None:
                            r = min(r, vmin)
                        if pmul is not None:
                            r *= pmul
                        return r + add
                    seen_vals = {}
                    for i in range(f.n):
                        for (e, ws, fn, inc, inc_by, fence) in ops:
                            eng = EMAP[e]
                            for t in ws:
                                if t.kind == "abs":
                                    val = t.val
                                elif t.kind == "expr":
                                    val = i * t.mul + t.add
                                elif t.kind == "exprfn":
                                    val = t.mul(i)
                                elif t.kind == "ivw":
                                    val = pyval(i, **t.mul)
                                else:
                                    d = f.delta.get(t.sem, 0)
                                    k = t.off - t.lb * d + f.pre.get(t.sem, 0)
                                    val = i * d + f.base[t.sem] + k
                                key = (e, t.sem)
                                if val > 0 and seen_vals.get(key, -1) < val:
                                    eng.wait_ge(t.sem.h, val)
                                    seen_vals[key] = val
                            inst = fn(eng, lambda mul, add, mod=None, vmin=None, shr=None, pmul=None, i=i: pyval(i, mul, add, mod, vmin, shr, pmul))
                            if fence:
                                inst = eng.drain()
                            if inc is not None:
                                assert inst is not None
                                inst.then_inc(inc.h, inc_by)
        for s in self.sems:
            assert s.n < 2 ** 30, f"sem {s.name} count {s.n} too high"

def tmax(*tokens):
    absd = {}
    rel = []
    for t in tokens:
        if t is None:
            continue
        ts = t if isinstance(t, list) else [t]
        for x in ts:
            if x is None:
                continue
            if x.kind == "abs":
                if x.sem in absd:
                    absd[x.sem] = max(absd[x.sem], x.val)
                else:
                    absd[x.sem] = x.val
            elif x.kind == "expr":
                rel.append(x)
            else:
                rel.append(x)
    out = [Tok(s, "abs", v) for s, v in absd.items()]
    rd = {}
    for x in rel:
        if x.kind == "expr":
            key = (x.sem, "expr", x.mul, x.add)
        elif x.kind == "exprfn":
            key = (x.sem, "exprfn", x.add)
        else:
            key = (x.sem, x.frame, x.lb)
        if key not in rd or (x.kind == "rel" and rd[key].off < x.off):
            rd[key] = x
    return out + list(rd.values())


def build_program(debug=(), p0_tiles=NT, phases=("p0", "p1"), n_exp=NEXP):
    nc = bass.Bass("TRN2", target_bir_lowering=False)
    dbg = set(debug)

    def di(name, shape, dt=F32):
        return nc.dram_tensor(name, list(shape), dt, kind="ExternalInput").ap()
    x0 = di("x", (T, D))
    wqT = di("wqT", (D, D)); wkT = di("wkT", (D, D)); wvT = di("wvT", (D, D)); woT = di("woT", (D, D))
    bq = di("bq", (D,)); bk = di("bk", (D,)); bv = di("bv", (D,)); bo = di("bo", (D,))
    wrT = di("wrT", (D, NEXP)); br = di("br", (NEXP,))
    w1T = di("w1T", (NEXP, D, DFF)); b1 = di("b1", (NEXP, DFF))
    w2T = di("w2T", (NEXP, DFF, D)); b2 = di("b2", (NEXP, D))
    awqT = di("awqT", (D, D)); awkT = di("awkT", (D, D)); awvT = di("awvT", (D, D)); awoT = di("awoT", (D, D))
    abq = di("abq", (D,)); abk = di("abk", (D,)); abv = di("abv", (D,)); abo = di("abo", (D,))
    iwqkv = di("iwqkv", (IDEPTH, D, I3)); iqs = di("iqs", (IDEPTH, DH)); iks = di("iks", (IDEPTH, DH))
    iwo = di("iwo", (IDEPTH, D, D)); ilng = di("ilng", (IDEPTH, D)); ilnb = di("ilnb", (IDEPTH, D))
    iw1 = di("iw1", (IDEPTH, D, DI2)); ib1 = di("ib1", (IDEPTH, DI2))
    iw2 = di("iw2", (IDEPTH, DI, D)); ib2 = di("ib2", (IDEPTH, D))
    masks0 = di("masks0", (4, 128, 512), BF16)
    ropeA = di("ropeA", (4, T, DH)); maskA = di("maskA", (128, 256))
    ropeI = di("ropeI", (4, T, DH)); maskI = di("maskI", (128, 384))

    def ds_(name, shape, dt=F32):
        return nc.dram_tensor(name, list(shape), dt, kind="Internal").ap()
    attn0_c = ds_("attn0_c", (T, D))
    moe_c = ds_("moe_c", (T, D))
    x1s = nc.dram_tensor("x1s", [T, D], F32, kind="ExternalOutput").ap()
    zs = nc.dram_tensor("zs", [T, D], F32, kind="ExternalOutput").ap()
    attn1_c = nc.dram_tensor("attn1_c", [T, D], F32, kind="ExternalOutput").ap()
    y = nc.dram_tensor("y", [T, D], F32, kind="ExternalOutput").ap()

    dbg_out = {}
    def dout(name, shape, dt=F32):
        ap = nc.dram_tensor(name, list(shape), dt, kind="ExternalOutput").ap()
        dbg_out[name] = ap
        return ap

    with ExitStack() as ctx:
        p = Prog(nc, ctx)

        AR0 = ctx.enter_context(nc.sbuf_tensor("AR0", [128, 73728], U8))
        AR1 = ctx.enter_context(nc.sbuf_tensor("AR1", [128, 65536], U8))
        AR2 = ctx.enter_context(nc.sbuf_tensor("AR2", [128, 66560], U8))

        def v0(off, n, dt=F32, arena=None):
            sz = {F32: 4, BF16: 2, I32: 4, U8: 1}[dt]
            a = arena if arena is not None else AR0
            return a[:, off:off + n * sz].bitcast(dt)
        def v1(off, n, dt=F32):
            return v0(off, n, dt, AR1)
        def v2(off, n, dt=F32):
            return v0(off, n, dt, AR2)

        ident = ctx.enter_context(nc.sbuf_tensor("ident", [128, 128], BF16))
        io32 = ctx.enter_context(nc.sbuf_tensor("io32", [128, 128], I32))
        MASKS = ctx.enter_context(nc.sbuf_tensor("MASKS", [128, NT * NEXP], F32))
        SMALL = ctx.enter_context(nc.sbuf_tensor("SMALL", [128, 2048], U8))
        def vs(off, n, dt=F32):
            sz = {F32: 4, BF16: 2, I32: 4}[dt]
            return SMALL[:, off:off + n * sz].bitcast(dt)

        PSB = [ctx.enter_context(nc.psum_tensor(f"psb{i}", [128, 512], F32))
               for i in range(8)]
        def pbf(i, off, n):
            return PSB[i][:, off:off + n // 2].bitcast(BF16)

        sQ = p.sem("sQ")
        sXT = [p.sem("sXT0"), p.sem("sXT1")]
        sG = p.sem("sG")
        sV = p.sem("sV")
        sS = p.sem("sS")
        sT = p.sem("sT")


        t_clr = []
        for arena in (AR0, AR1, AR2):
            nb = arena.shape[-1]
            for off in range(0, nb, 32768):
                hi = min(off + 32768, nb)
                t_clr.append(p.emit("vector", lambda v, _i, arena=arena, off=off, hi=hi: v.memset(
                    arena[:, off:hi], 0), inc=sV))
        t_clr.append(p.emit("vector", lambda v, _i: v.memset(SMALL[:], 0), inc=sV))
        for b_ in range(8):
            t_clr.append(p.emit("vector", lambda v, _i, b_=b_: v.memset(PSB[b_][:], 0.0), inc=sV))
        clr_all = tmax(*t_clr)
        sGC = p.sem("sGC")
        t_io = p.emit("gpsimd", lambda g, _i: g.iota(io32[:], [[1, 128]], channel_multiplier=-1,
                                                 allow_small_or_imprecise_dtypes=True),
                      inc=sGC)
        t_id = p.emit("vector", lambda v, _i: v.tensor_scalar(out=ident[:], in0=io32[:], scalar1=0,
                                                          scalar2=None, op0=ALU.is_equal),
                      waits=tmax(t_io), inc=sV)

        EPS5 = vs(1024, 1)[:, 0:1]
        t_eps = p.emit("vector", lambda v, _i: v.memset(EPS5, 1.0e-5), waits=clr_all, inc=sV)

        WRB = vs(0, 64, BF16)
        t_wrb = p.emit("gpsimd", lambda g, _i: g.dma_start(
            WRB.rearrange("p (c e) -> p c e", c=DC),
            wrT.rearrange("(c p) e -> p c e", p=128)), waits=clr_all, inc=sG, inc_by=16)
        BRB = vs(128, 8, F32)
        t_brb = p.emit("sync", lambda s, _i: s.dma_start(
            BRB, br[None, :].broadcast_to([128, NEXP])), waits=clr_all, inc=sQ, inc_by=16)

        XNT = [v0(c * 9216 + 1024, T, BF16) for c in range(DC)]
        XT = [v1(0, 1024), v1(4096, 1024)]
        XNB = [v1(8192, 1024, BF16), v1(10240, 1024, BF16)]
        SQD = v1(12288, 1024)
        def stat(i, slot):
            return vs(256 + 64 * (i % 2) + 16 * slot, 4)[:, 0:1]

        tok_xnt = [[None] * NT for _ in range(DC)]
        tok_mask = [None] * NT
        tok_lg = [None] * NT
        prev_xt = [clr_all, clr_all]
        prev_xnb = [None, None]

        ZERO8 = vs(1088, 8)
        p.emit("vector", lambda v, _i: v.memset(ZERO8, 0.0), waits=clr_all, inc=sV)
        for t in range(p0_tiles):
            b = t % 2
            t_ld = p.emit("sync", lambda s, _i, t=t, b=b: s.dma_start(XT[b], x0[t * 128:(t + 1) * 128, :]),
                          waits=tmax(prev_xt[b]), inc=sXT[b], inc_by=16)
            meanraw = stat(b, 0); sqs = stat(b, 1); var0 = stat(b, 2); rstd = stat(b, 3)
            mean = vs(768 + 64 * b, 1)[:, 0:1]
            msq = vs(768 + 64 * b + 16, 1)[:, 0:1]
            t_m0 = p.emit("vector", lambda v, _i, b=b, meanraw=meanraw: v.reduce_sum(meanraw, XT[b], axis=AX),
                          waits=tmax(t_ld), inc=sV, fence=True)
            t_s0 = p.emit("scalar", lambda sc, _i, b=b, sqs=sqs: sc.activation(
                out=SQD, in_=XT[b], func=AF.Square, accum_out=sqs),
                waits=tmax(t_ld), inc=sS, fence=True)
            p.emit("vector", lambda v, _i, mean=mean, meanraw=meanraw: v.tensor_scalar(
                out=mean, in0=meanraw, scalar1=1.0 / D, scalar2=None, op0=ALU.mult), inc=sV)
            t_msq = p.emit("vector", lambda v, _i, msq=msq, meanraw=meanraw: v.tensor_scalar(
                out=msq, in0=meanraw, scalar1=meanraw, scalar2=None, op0=ALU.mult),
                inc=sV, fence=True)
            t_v0 = p.emit("vector", lambda v, _i, var0=var0, sqs=sqs, msq=msq: v.scalar_tensor_tensor(
                out=var0, in0=sqs, scalar=float(D), in1=msq, op0=ALU.mult, op1=ALU.subtract),
                waits=tmax(t_s0), inc=sV, fence=True)
            t_sq = p.emit("scalar", lambda sc, _i, var0=var0, rstd=rstd: sc.activation(
                out=rstd, in_=var0, func=AF.Sqrt, bias=EPS5, scale=1.0 / (D * D)),
                waits=tmax(t_v0, t_eps), inc=sS, fence=True)
            t_rc = p.emit("vector", lambda v, _i, rstd=rstd: v.reciprocal(rstd, rstd),
                          waits=tmax(t_sq), inc=sV, fence=True)
            t_xn = p.emit("vector", lambda v, _i, b=b, mean=mean, rstd=rstd: v.tensor_scalar(
                out=XNB[b], in0=XT[b], scalar1=mean, scalar2=rstd,
                op0=ALU.subtract, op1=ALU.mult), waits=tmax(t_rc, prev_xnb[b]), inc=sV)
            prev_xt[b] = tmax(t_xn, t_s0)
            bank = b
            tr_last = None
            for c in range(DC):
                tr_last = p.emit("tensor", lambda te, _i, b=b, c=c, bank=bank: te.transpose(
                    PSB[bank][:, c * 64:c * 64 + 64].bitcast(BF16),
                    XNB[b][:, c * 128:(c + 1) * 128], ident[:]),
                    waits=tmax(t_xn, t_id,
                               tok_xnt[DC - 1][t - 2] if t >= 2 else None), inc=sT)
            for c in range(DC):
                tok_xnt[c][t] = p.emit("vector", lambda v, _i, c=c, t=t, bank=bank: v.tensor_copy(
                    XNT[c][:, t * 128:(t + 1) * 128],
                    PSB[bank][:, c * 64:c * 64 + 64].bitcast(BF16)),
                    waits=tmax(tr_last), inc=sV)
            prev_xnb[b] = tok_xnt[DC - 1][t]
            lgbank = 2 + b
            t_mm = None
            for c in range(DC):
                t_mm = p.emit("tensor", lambda te, _i, c=c, t=t, lgbank=lgbank: te.matmul(
                    PSB[lgbank][:, 0:NEXP], XNT[c][:, t * 128:(t + 1) * 128],
                    WRB[:, c * 8:(c + 1) * 8],
                    start=(c == 0), stop=(c == DC - 1)),
                    waits=tmax(tok_xnt[c][t], t_wrb if t == 0 else None,
                               tok_lg[t - 2] if t >= 2 else None), inc=sT)
            LG = vs(512 + 64 * b, 8)
            t_lge = p.emit("vector", lambda v, _i, LG=LG, lgbank=lgbank: v.tensor_add(
                LG, PSB[lgbank][:, 0:NEXP], BRB), waits=tmax(t_mm, t_brb), inc=sV,
                fence=True)
            tok_lg[t] = t_lge
            mx = vs(768 + 32, 1)[:, 0:1]
            raw8 = vs(1120, 8)
            P8 = vs(1152, 8)
            p.emit("vector", lambda v, _i, LG=LG, mx=mx: v.reduce_max(mx, LG, axis=AX),
                   inc=sV, fence=True)
            p.emit("vector", lambda v, _i, LG=LG, mx=mx, raw8=raw8: v.tensor_scalar(
                out=raw8, in0=LG, scalar1=mx, scalar2=None, op0=ALU.is_equal),
                inc=sV, fence=True)
            p.emit("vector", lambda v, _i, raw8=raw8, P8=P8: v.tensor_tensor_scan(
                P8, raw8, ZERO8, 0.0, ALU.max, ALU.add), inc=sV, fence=True)
            p.emit("vector", lambda v, _i, t=t, P8=P8: v.tensor_copy(
                MASKS[:, t:t + 1], P8[:, 0:1]), inc=sV)
            tok_mask[t] = p.emit("vector", lambda v, _i, t=t, P8=P8: v.tensor_sub(
                MASKS.rearrange("p (e t2) -> p e t2", e=8)[:, 1:8, t:t + 1],
                P8[:, 1:8, None], P8[:, 0:7, None]),
                inc=sV, fence=True)

        tok_moe_done = None
        if "p1" in phases:
            sW1 = p.sem("sW1")
            sW2 = p.sem("sW2")
            sMCs = [p.sem(f"sMC{i_}") for i_ in range(4)]
            sP1 = p.sem("sP1")
            W1TB = [v2(dc * 8192, DFF, BF16) for dc in range(DC)]
            HB = [v1(fc * 1024, 512, BF16) for fc in range(32)]
            W2B4 = [v1(32768 + i_ * 2048, 1024, BF16) for i_ in range(4)]
            CTB = [v1(40960 + i_ * 4096, 1024) for i_ in range(4)]
            B2BC = v1(57344, 1024)
            B1C = vs(1216, 32)
            SCR = v1(61440, 1024)
            MC4 = vs(1408, 4)
            XG = [v0(c * 9216, 512, BF16) for c in range(DC)]
            KQv = v0(0, 8 * 4608, BF16)
            xnt_all = tmax(*[tok_xnt[c][t] for c in range(DC) for t in range(NT)])
            mask_all = tmax(*[tok_mask[t] for t in range(NT)])

            relu_toks = [None] * 32
            w2fc_tok = [None] * 32
            ct_toks = [None] * 4
            last_w1mm = None
            cp_toks = [None, None]
            with p.loop(64, ["tensor", "scalar", "vector"], managed=True) as fr:
                p.emit("vector", lambda v, iv: v.tensor_copy(
                    MC4[:, 0:4], MASKS[:, bass.ds(iv(4, 0), 4)]),
                    waits=tmax(mask_all, Tok(sV, "rel", frame=fr, off=12, lb=1)),
                    fence=True)
                for ch in range(4):
                    for j in range(8):
                        fc = ch * 8 + j
                        for dc in range(DC):
                            w = []
                            if j == 0 and dc == 0:
                                if ch == 0:
                                    w += [Tok(sS, "rel", frame=fr, off=32, lb=1),
                                          Tok(sV, "rel", frame=fr, off=12, lb=1),
                                          Tok(sW1, "ivw",
                                              mul=dict(mul=1, shr=3, pmul=160, add=160))]
                                    w.append(xnt_all)
                                else:
                                    w.append(Tok(sS, "rel", frame=fr, off=ch * 8))
                            last_w1mm = p.emit("tensor", lambda te, iv, fc=fc, dc=dc: te.matmul(
                                PSB[fc % 8][:, 0:512],
                                W1TB[dc][:, fc * 128:(fc + 1) * 128],
                                XNT[dc][:, bass.ds(iv(512, 0, mod=4096), 512)],
                                start=(dc == 0), stop=(dc == DC - 1)),
                                waits=tmax(*w), inc=sT)
                    for j in range(8):
                        fc = ch * 8 + j
                        w = [Tok(sT, "rel", frame=fr, off=(ch + 1) * 64)] if j == 0 else []
                        relu_toks[fc] = p.emit("scalar", lambda sc, iv, fc=fc: sc.activation(
                            out=HB[fc], in_=PSB[fc % 8][:, 0:512], func=AF.Relu,
                            bias=B1C[:, fc:fc + 1]),
                            waits=tmax(*w), inc=sS)
                for fc in range(32):
                    for tt in range(4):
                        for hf in range(2):
                            w = []
                            if tt == 0 and hf == 0:
                                if fc % 4 == 0:
                                    w.append(Tok(sW2, "expr", mul=32 * 16,
                                                 add=(fc + 4) * 16))
                                if fc == 0:
                                    w.append(Tok(sS, "rel", frame=fr, off=32))
                            tok = p.emit("tensor", lambda te, iv, fc=fc, tt=tt, hf=hf: te.matmul(
                                PSB[2 * tt + hf][:, 0:512],
                                HB[fc][:, tt * 128:(tt + 1) * 128],
                                W2B4[fc % 4][:, hf * 512:(hf + 1) * 512],
                                start=(fc == 0), stop=(fc == 31)),
                                waits=tmax(*w), inc=sT)
                    w2fc_tok[fc] = tok
                for tt in range(4):
                    tok_b2m = p.emit("vector", lambda v, iv, tt=tt: v.tensor_scalar(
                        out=SCR, in0=B2BC, scalar1=MC4[:, tt:tt + 1],
                        scalar2=None, op0=ALU.mult),
                        waits=tmax(Tok(sT, "rel", frame=fr, off=506 + 2 * tt),
                                   Tok(sMCs[tt], "expr", mul=16, add=0),
                                   (ct_toks[tt - 1] if tt > 0 else
                                    Tok(sV, "rel", frame=fr, off=12, lb=1))),
                        inc=sV)
                    last = None
                    for hf in range(2):
                        last = p.emit("vector", lambda v, iv, tt=tt, hf=hf: v.scalar_tensor_tensor(
                            out=CTB[tt][:, hf * 512:(hf + 1) * 512],
                            in0=PSB[2 * tt + hf][:, 0:512],
                            scalar=MC4[:, tt:tt + 1],
                            in1=SCR[:, hf * 512:(hf + 1) * 512],
                            op0=ALU.mult, op1=ALU.add),
                            waits=tmax(tok_b2m), inc=sV)
                    ct_toks[tt] = last

            for e in range(NEXP):
                wprev = clr_all
                if e >= 1:
                    wprev = tmax(tok_at(last_w1mm, e * 8 - 1),
                                 tok_at(relu_toks[31], e * 8 - 1),
                                 tok_at(ct_toks[3], e * 8 - 1))
                for dc in range(DC):
                    p.emit("gpsimd", lambda g, _i, e=e, dc=dc: g.dma_start(
                        W1TB[dc], w1T[e, dc * 128:(dc + 1) * 128, :]),
                        waits=wprev, inc=sW1, inc_by=16)
                p.emit("gpsimd", lambda g, _i, e=e: g.dma_start(
                    B1C.rearrange("p (a o) -> p a o", o=1),
                    b1[e].rearrange("(a q) -> q a", q=128)[:, :, None],
                    allow_slow_non_contiguous=True),
                    waits=wprev, inc=sW1, inc_by=16)
                p.emit("gpsimd", lambda g, _i, e=e: g.dma_start(
                    B2BC, b2[e][None, :].broadcast_to([128, D])),
                    waits=wprev, inc=sW1, inc_by=16)
                for g in range(8):
                    jg = e * 8 + g
                    for fc in range(32):
                        w = []
                        gi = jg * 32 + fc
                        if gi >= 4:
                            pj, pf = divmod(gi - 4, 32)
                            w.append(tok_at(w2fc_tok[pf], pj))
                        else:
                            w = [clr_all]
                        p.emit("gpsimd", lambda s, _i, e=e, fc=fc: s.dma_start(
                            W2B4[fc % 4], w2T[e, fc * 128:(fc + 1) * 128, :]),
                            waits=tmax(*w), inc=sW2, inc_by=16)
                    for tt in range(4):
                        t0 = (g * 4 + tt) * 128
                        p.emit("gpsimd", lambda gp, _i, e=e, t0=t0, tt=tt: gp.dma_start(
                            moe_c[t0:t0 + 128, :], CTB[tt],
                            accum_op=(ALU.add if e > 0 else ALU.bypass)),
                            waits=[tok_at(ct_toks[tt], jg)], inc=sMCs[tt], inc_by=16)
            tok_moe_done = tmax(*[Tok(s_, "abs", s_.n) for s_ in sMCs])

        tok_attn0_done = None
        if "p2" in phases:
            sWS = [p.sem("sWS0"), p.sem("sWS1")]
            sAC = [p.sem("sAC0"), p.sem("sAC1")]
            sP = p.sem("sP")
            KQ = v0(0, 8 * 4608, BF16)
            VA = v2(0, 32 * 1040, BF16)
            QN = v1(0, 1024, BF16)
            SQB = v1(2048, 1024, BF16)
            OUTF = v1(4096, 4096, BF16)
            PRBs = [v1(12288, 512, BF16), v1(13312, 512, BF16)]
            WSL = [v1(22592, 1024, BF16), v1(24640, 1024, BF16)]
            BIAS = v1(26688, 1024, BF16)
            BCS = v1(26688, 512, BF16)
            MBIG = v1(28736, 896, BF16)
            ONESB = v1(30528, 64, BF16)
            ZROB = v1(30656, 512, BF16)
            RECROW = AR1[64:65, 31680:31680 + 2048].bitcast(F32)
            RECBR = AR1[64:65, 33728:33728 + 1024].bitcast(BF16)
            RS16 = v1(34752, 16)
            CT0 = [v1(36864, 1024), v1(40960, 1024)]
            BOB = v1(45056, 1024)
            ESMALL = vs(1344, 1)[:, 0:1]
            SCL0 = 1.0 / (float(np.sqrt(128.0)) * 1.5)

            tok_pe_p1 = Tok(sT, "abs", sT.n)
            tok_sv_p1 = Tok(sV, "abs", sV.n)
            tok_ss_p1 = Tok(sS, "abs", sS.n)
            p1done = tmax(tok_pe_p1, tok_sv_p1, tok_ss_p1)

            t_vaones = p.emit("vector", lambda v, _i: v.memset(VA, 1.0),
                              waits=tmax(tok_moe_done) if tok_moe_done else [], inc=sV)
            t_zrob = p.emit("vector", lambda v, _i: v.memset(ZROB, 0.0), inc=sV)
            t_onesb = p.emit("vector", lambda v, _i: v.memset(ONESB, 1.0), inc=sV)
            t_eps8 = p.emit("vector", lambda v, _i: v.memset(ESMALL, 1.0e-8), inc=sV)
            MBI = AR1[:, 49152:49152 + 3584 * 4].bitcast(I32)
            t_mb1 = p.emit("gpsimd", lambda g, _i: g.iota(
                MBI[:, 0:896], [[1, 896]], base=-384, channel_multiplier=-1,
                allow_small_or_imprecise_dtypes=True),
                waits=tmax(tok_moe_done), inc=sGC)
            t_mb2 = p.emit("vector", lambda v, _i: v.tensor_scalar(
                out=MBIG, in0=MBI[:, 0:896], scalar1=0, scalar2=None,
                op0=ALU.is_ge), waits=tmax(t_mb1), inc=sV)
            t_bo0 = p.emit("sync", lambda s, _i: s.dma_start(
                BOB, bo[None, :].broadcast_to([128, D])),
                waits=tmax(tok_moe_done), inc=sQ, inc_by=16)
            t_bo1 = p.emit("scalar", lambda sc, _i: sc.mul(BOB, BOB, 0.5),
                           waits=tmax(t_bo0), inc=sS)
            setup_done = tmax(t_vaones, t_zrob, t_onesb, t_eps8, t_mb2, t_bo1, p1done)

            WTS = {"v": (wvT, bv), "k": (wkT, bk), "q": (wqT, bq)}
            tok_ws_free = [setup_done, setup_done]
            tok_bank_free = [setup_done] * 8
            tok_kq_write = [None] * NT
            tok_v_write = [None] * NT
            tok_ct_free = [setup_done, setup_done]
            prev_bias_read = setup_done
            ngroups = 1 if "g0" in dbg else 8
            tok_outf_free = setup_done
            prev_phaseB_banks = setup_done

            for g in range(ngroups):
                tiles = [4 * g + t for t in range(4)]
                for wname in ("v", "k", "q"):
                    wT, bvec = WTS[wname]
                    t_b = p.emit("gpsimd", lambda gp, _i, bvec=bvec: gp.dma_start(
                        BIAS, bvec[None, :].broadcast_to([128, D])),
                        waits=tmax(prev_phaseB_banks, prev_bias_read), inc=sG, inc_by=16)
                    mm_last = [None] * 4
                    bias_readers = []
                    wtoks = [None, None]
                    for dc in range(DC):
                        sl = dc % 2
                        wtoks[sl] = p.emit("gpsimd", lambda gp, _i, wT=wT, dc=dc, sl=sl: gp.dma_start(
                            WSL[sl], wT[dc * 128:(dc + 1) * 128, :]),
                            waits=tmax(tok_ws_free[sl]), inc=sWS[sl], inc_by=16)
                        for t in range(4):
                            for hf in range(2):
                                w = [wtoks[sl]]
                                if dc == 0:
                                    w.append(tok_bank_free[2 * t + hf])
                                mm_last[t] = p.emit("tensor", lambda te, _i, g=g, dc=dc, t=t, hf=hf, sl=sl: te.matmul(
                                    PSB[2 * t + hf][:, 0:512],
                                    XNT[dc][:, (4 * g + t) * 128:(4 * g + t + 1) * 128],
                                    WSL[sl][:, hf * 512:(hf + 1) * 512],
                                    start=(dc == 0), stop=(dc == DC - 1)),
                                    waits=tmax(*w), inc=sT)
                        tok_ws_free[sl] = mm_last[3]
                    for t in range(4):
                        tg = tiles[t]
                        if wname == "v":
                            last = None
                            for hf in range(2):
                                last = p.emit("vector", lambda v, _i, tg=tg, t=t, hf=hf: v.tensor_tensor(
                                    out=VA[:, tg * 1040:(tg + 1) * 1040].rearrange(
                                        "p (h c) -> p h c", c=65)[:, hf * 8:(hf + 1) * 8, 0:64],
                                    in0=PSB[2 * t + hf][:, 0:512].rearrange(
                                        "p (h c) -> p h c", c=64),
                                    in1=BIAS[:, hf * 512:(hf + 1) * 512].rearrange(
                                        "p (h c) -> p h c", c=64),
                                    op=ALU.add), waits=tmax(mm_last[t], t_b), inc=sV)
                            tok_v_write[tg] = last
                            tok_bank_free[2 * t] = last
                            tok_bank_free[2 * t + 1] = last
                            bias_readers.append(last)
                        else:
                            tqn = None
                            for hf in range(2):
                                tqn = p.emit("vector", lambda v, _i, t=t, hf=hf: v.tensor_tensor(
                                    out=QN[:, hf * 512:(hf + 1) * 512],
                                    in0=PSB[2 * t + hf][:, 0:512],
                                    in1=BIAS[:, hf * 512:(hf + 1) * 512],
                                    op=ALU.add), waits=tmax(mm_last[t], t_b), inc=sV)
                            bias_readers.append(tqn)
                            tsq = p.emit("scalar", lambda sc, _i: sc.activation(
                                out=SQB, in_=QN, func=AF.Square), waits=tmax(tqn), inc=sS)
                            trd = p.emit("vector", lambda v, _i: v.reduce_sum(
                                RS16, SQB.rearrange("p (h c) -> p h c", c=64), axis=AX),
                                waits=tmax(tsq), inc=sV, fence=True)
                            tsr = p.emit("scalar", lambda sc, _i: sc.activation(
                                out=RS16, in_=RS16, func=AF.Sqrt),
                                waits=tmax(trd), inc=sS, fence=True)
                            tmx_ = p.emit("vector", lambda v, _i: v.tensor_scalar(
                                out=RS16, in0=RS16, scalar1=1.0e-8, scalar2=None,
                                op0=ALU.max), waits=tmax(tsr), inc=sV, fence=True)
                            trc = p.emit("vector", lambda v, _i: v.reciprocal(RS16, RS16),
                                         waits=tmax(tmx_), inc=sV, fence=True)
                            tnm = p.emit("vector", lambda v, _i: v.tensor_tensor(
                                out=SQB.rearrange("p (h c) -> p h c", c=64),
                                in0=QN.rearrange("p (h c) -> p h c", c=64),
                                in1=RS16[:, :, None].broadcast_to([128, 16, 64]),
                                op=ALU.mult), waits=tmax(trc), inc=sV)
                            trt = None
                            for c in range(DC):
                                trt = p.emit("tensor", lambda te, _i, c=c, t=t: te.matmul(
                                    PSB[2 * t + (c % 2)][:, (c // 2) * 128:(c // 2) * 128 + 64].bitcast(BF16),
                                    SQB[:, c * 128:(c + 1) * 128], ident[:],
                                    is_transpose=True, start=True, stop=True),
                                    waits=tmax(tnm), inc=sT)
                            evl = None
                            for c in range(DC):
                                dst_off = c * 4608 + (tg * 128 if wname == "k" else 512 + tg * 128)
                                wv = [trt]
                                if wname == "q":
                                    wv.append(tok_pe_p1)
                                evl = p.emit("vector", lambda v, _i, c=c, t=t, dst_off=dst_off: v.tensor_copy(
                                    KQ[:, dst_off:dst_off + 128],
                                    PSB[2 * t + (c % 2)][:, (c // 2) * 128:(c // 2) * 128 + 64].bitcast(BF16)),
                                    waits=tmax(*wv), inc=sV)
                            tok_kq_write[tg] = evl
                            tok_bank_free[2 * t] = evl
                            tok_bank_free[2 * t + 1] = evl
                    prev_bias_read = tmax(*bias_readers)
                if "nophaseb" in dbg:
                    prev_phaseB_banks = tmax(*[tok_bank_free[i_] for i_ in range(8)])
                    continue
                kv_ready = tmax(*[tok_kq_write[j] for j in tiles],
                                *[tok_v_write[j] for j in tiles],
                                *[tok_kq_write[j] for j in range(4 * g)],
                                *[tok_v_write[j] for j in range(4 * g)])
                hsets = [list(range(0, 6)), list(range(6, 12)), list(range(12, 16))]
                evac_toks = []
                for hs in hsets:
                    pv_last = {}
                    sc_prev = {}
                    nkj = 4 * g + 4
                    for kj in range(nkj):
                        o = kj - 4 * g
                        for idx, h in enumerate(hs):
                            n = kj * len(hs) + idx
                            bk6 = 6 + (n % 2)
                            w = [kv_ready]
                            if n < 2:
                                w.append(prev_phaseB_banks)
                                w.append(setup_done)
                            if n >= 2:
                                w.append(sc_prev.get(n - 2))
                            tsc = p.emit("tensor", lambda te, _i, h=h, kj=kj, g=g, bk6=bk6: te.matmul(
                                PSB[bk6][:, 0:512],
                                KQ[(h % 2) * 64:(h % 2) * 64 + 64,
                                   (h // 2) * 4608 + kj * 128:(h // 2) * 4608 + kj * 128 + 128],
                                KQ[(h % 2) * 64:(h % 2) * 64 + 64,
                                   (h // 2) * 4608 + 512 + 4 * g * 128:
                                   (h // 2) * 4608 + 512 + 4 * g * 128 + 512],
                                start=True, stop=True),
                                waits=tmax(*w), inc=sT)
                            tex = p.emit("scalar", lambda sc, _i, n=n, bk6=bk6: sc.activation(
                                out=PRBs[n % 2], in_=PSB[bk6][:, 0:512],
                                func=AF.Exp, scale=SCL0),
                                waits=tmax(tsc, pv_last.get(n - 2)), inc=sS)
                            sc_prev[n] = tex
                            dep = tex
                            if o >= 0:
                                dep = p.emit("vector", lambda v, _i, n=n, o=o: v.tensor_tensor(
                                    out=PRBs[n % 2], in0=PRBs[n % 2],
                                    in1=MBIG[:, 384 - 128 * o:384 - 128 * o + 512],
                                    op=ALU.mult), waits=tmax(tex), inc=sV)
                            pv_last[n] = p.emit("tensor", lambda te, _i, idx=idx, h=h, kj=kj, n=n, nkj=nkj: te.matmul(
                                PSB[idx][0:65, 0:512],
                                VA[:, kj * 1040 + h * 65:kj * 1040 + h * 65 + 65],
                                PRBs[n % 2],
                                start=(kj == 0), stop=(kj == nkj - 1)),
                                waits=tmax(dep), inc=sT)
                    for idx, h in enumerate(hs):
                        trc0 = p.emit("vector", lambda v, _i, idx=idx: v.reciprocal(
                            RECROW[:, 0:512], PSB[idx][64:65, 0:512]),
                            waits=tmax(pv_last[(nkj - 1) * len(hs) + idx],
                                       tok_outf_free if idx == 0 else None),
                            inc=sV)
                        trcb = p.emit("vector", lambda v, _i: v.tensor_copy(
                            RECBR[:, 0:512], RECROW[:, 0:512]), waits=tmax(trc0), inc=sV)
                        tbc = p.emit("tensor", lambda te, _i: te.matmul(
                            PSB[6][0:64, 0:512], ONESB[64:65, 0:64], RECBR[:, 0:512],
                            start=True, stop=True), waits=tmax(trcb), inc=sT)
                        tbe = p.emit("vector", lambda v, _i: v.tensor_copy(
                            BCS[0:64, 0:512], PSB[6][0:64, 0:512]), waits=tmax(tbc), inc=sV)
                        evac_toks.append(p.emit("vector", lambda v, _i, idx=idx, h=h: v.tensor_tensor(
                            out=OUTF[(h % 2) * 64:(h % 2) * 64 + 64,
                                     (h // 2) * 512:(h // 2) * 512 + 512],
                            in0=PSB[idx][0:64, 0:512], in1=BCS[0:64, 0:512],
                            op=ALU.mult), waits=tmax(tbe), inc=sV))
                    prev_phaseB_banks = tmax(*evac_toks)
                wo_last = [None] * 4
                wtoks = [None, None]
                for dc in range(DC):
                    sl = dc % 2
                    wtoks[sl] = p.emit("gpsimd", lambda gp, _i, dc=dc, sl=sl: gp.dma_start(
                        WSL[sl], woT[dc * 128:(dc + 1) * 128, :]),
                        waits=tmax(tok_ws_free[sl]), inc=sWS[sl], inc_by=16)
                    for t in range(4):
                        for hf in range(2):
                            w = [wtoks[sl], tmax(*evac_toks)]
                            if dc == 0:
                                w.append(prev_phaseB_banks)
                            wo_last[t] = p.emit("tensor", lambda te, _i, dc=dc, t=t, hf=hf, sl=sl: te.matmul(
                                PSB[2 * t + hf][:, 0:512],
                                OUTF[:, dc * 512 + t * 128:dc * 512 + t * 128 + 128],
                                WSL[sl][:, hf * 512:(hf + 1) * 512],
                                start=(dc == 0), stop=(dc == DC - 1)),
                                waits=tmax(*w), inc=sT)
                    tok_ws_free[sl] = wo_last[3]
                ct_sts = []
                for t in range(4):
                    ctsl = t % 2
                    last = None
                    for hf in range(2):
                        last = p.emit("vector", lambda v, _i, t=t, hf=hf, ctsl=ctsl: v.scalar_tensor_tensor(
                            out=CT0[ctsl][:, hf * 512:(hf + 1) * 512],
                            in0=PSB[2 * t + hf][:, 0:512], scalar=0.5,
                            in1=BOB[:, hf * 512:(hf + 1) * 512],
                            op0=ALU.mult, op1=ALU.add),
                            waits=tmax(wo_last[t], t_bo1, tok_ct_free[ctsl]), inc=sV)
                    tok_bank_free[2 * t] = last
                    tok_bank_free[2 * t + 1] = last
                    ct_sts.append(last)
                    tok_ct_free[ctsl] = p.emit("sync", lambda s, _i, g=g, t=t, ctsl=ctsl: s.dma_start(
                        attn0_c[(4 * g + t) * 128:(4 * g + t + 1) * 128, :], CT0[ctsl]),
                        waits=tmax(last), inc=sAC[ctsl], inc_by=16)
                tok_outf_free = tmax(*wo_last)
                prev_phaseB_banks = tmax(*ct_sts)
            tok_attn0_done = tmax(tok_ct_free[0], tok_ct_free[1])

        if "p2d" in dbg:
            d_a0 = dout("d_a0", (T, D))
            wAlla = p.barrier_token()
            p.emit("sync", lambda s, _i, d_a0=d_a0: s.dma_start(d_a0, attn0_c),
                   waits=wAlla, inc=sQ, inc_by=16)
        if "p2q" in dbg:
            d_kq = dout("d_kq", (128, 36864))
            d_va = dout("d_va", (128, 33280))
            wAllq = p.barrier_token()
            p.emit("gpsimd", lambda s, _i, d_kq=d_kq: s.dma_start(
                d_kq, v0(0, 36864, BF16)), waits=wAllq, inc=sG, inc_by=16)
            p.emit("gpsimd", lambda s, _i, d_va=d_va: s.dma_start(
                d_va, v2(0, 33280, BF16)), waits=wAllq, inc=sG, inc_by=16)
        import dataclasses as _dc
        def shuf64(ap):
            return _dc.replace(ap, offset=ap.offset + 32,
                               ap=[[ap.ap[0][0], 128], [64, 16], [-32, 2], [1, 32]])
        tok_p4_done = None
        tok_ynt = [None] * NT
        YNT = [v0(c * 8192, T, BF16) for c in range(DC)]
        if "p4" in phases:
            sL4 = [p.sem("sL4a"), p.sem("sL4b")]
            sO4 = [p.sem("sO4a"), p.sem("sO4b")]
            p4pre = tmax(tok_attn0_done, tok_moe_done,
                         Tok(sT, "abs", sT.n), Tok(sV, "abs", sV.n),
                         Tok(sS, "abs", sS.n))
            X4 = [v1(50176, 1024), v1(54272, 1024)]
            A4s = v1(58368, 1024)
            YB4s = v1(62464, 1024, BF16)
            M4s = v0(65536, 1024)
            Y4s = v0(69632, 1024)
            prev_use = [p4pre, p4pre]
            prev_a4 = p4pre
            prev_m4 = p4pre
            prev_o2 = p4pre
            prev_trl = p4pre
            for t in range(NT):
                b = t % 2
                tl1 = p.emit("sync", lambda s, _i, t=t, b=b: s.dma_start(
                    X4[b], x0[t * 128:(t + 1) * 128, :]), waits=prev_use[b],
                    inc=sL4[b], inc_by=16)
                tl2 = p.emit("sync", lambda s, _i, t=t: s.dma_start(
                    A4s, attn0_c[t * 128:(t + 1) * 128, :]), waits=tmax(prev_a4),
                    inc=sL4[b], inc_by=16)
                tl3 = p.emit("sync", lambda s, _i, t=t: s.dma_start(
                    M4s, moe_c[t * 128:(t + 1) * 128, :]), waits=tmax(prev_m4),
                    inc=sL4[b], inc_by=16)
                tld = tmax(tl1, tl2, tl3)
                t_s1 = p.emit("vector", lambda v, _i, b=b: v.tensor_add(
                    X4[b], X4[b], A4s), waits=tmax(tld), inc=sV)
                t_s2 = p.emit("vector", lambda v, _i, b=b: v.tensor_add(
                    X4[b], X4[b], M4s), waits=tmax(t_s1), inc=sV)
                prev_m4 = t_s2
                t_o1 = p.emit("sync", lambda s, _i, t=t, b=b: s.dma_start(
                    x1s[t * 128:(t + 1) * 128, :], X4[b]), waits=tmax(t_s2),
                    inc=sO4[b], inc_by=16)
                meanraw = stat(b, 0); sqs = stat(b, 1); var0 = stat(b, 2); rstd = stat(b, 3)
                mean = vs(768 + 64 * b, 1)[:, 0:1]
                msq = vs(768 + 64 * b + 16, 1)[:, 0:1]
                t_m0 = p.emit("vector", lambda v, _i, b=b, meanraw=meanraw: v.reduce_sum(
                    meanraw, X4[b], axis=AX), waits=tmax(t_s2), inc=sV, fence=True)
                t_s0 = p.emit("scalar", lambda sc, _i, b=b, sqs=sqs: sc.activation(
                    out=A4s, in_=X4[b], func=AF.Square, accum_out=sqs),
                    waits=tmax(t_s2), inc=sS, fence=True)
                prev_a4 = t_s0
                p.emit("vector", lambda v, _i, mean=mean, meanraw=meanraw: v.tensor_scalar(
                    out=mean, in0=meanraw, scalar1=1.0 / D, scalar2=None, op0=ALU.mult), inc=sV)
                t_msq = p.emit("vector", lambda v, _i, msq=msq, meanraw=meanraw: v.tensor_scalar(
                    out=msq, in0=meanraw, scalar1=meanraw, scalar2=None, op0=ALU.mult),
                    inc=sV, fence=True)
                t_v0 = p.emit("vector", lambda v, _i, var0=var0, sqs=sqs, msq=msq: v.scalar_tensor_tensor(
                    out=var0, in0=sqs, scalar=float(D), in1=msq, op0=ALU.mult, op1=ALU.subtract),
                    waits=tmax(t_s0), inc=sV, fence=True)
                t_sq = p.emit("scalar", lambda sc, _i, var0=var0, rstd=rstd: sc.activation(
                    out=rstd, in_=var0, func=AF.Sqrt, bias=EPS5, scale=1.0 / (D * D)),
                    waits=tmax(t_v0), inc=sS, fence=True)
                t_rc = p.emit("vector", lambda v, _i, rstd=rstd: v.reciprocal(rstd, rstd),
                              waits=tmax(t_sq), inc=sV, fence=True)
                t_yn = p.emit("vector", lambda v, _i, b=b, mean=mean, rstd=rstd: v.tensor_scalar(
                    out=Y4s, in0=X4[b], scalar1=mean, scalar2=rstd,
                    op0=ALU.subtract, op1=ALU.mult),
                    waits=tmax(t_rc, prev_o2), inc=sV)
                t_o2 = p.emit("sync", lambda s, _i, t=t, b=b: s.dma_start(
                    zs[t * 128:(t + 1) * 128, :], Y4s), waits=tmax(t_yn),
                    inc=sO4[b], inc_by=16)
                prev_o2 = t_o2
                t_cb = p.emit("vector", lambda v, _i: v.tensor_copy(
                    YB4s, Y4s), waits=tmax(prev_trl, t_yn), inc=sV)
                trl = None
                for c in range(DC):
                    trl = p.emit("tensor", lambda te, _i, b=b, c=c: te.matmul(
                        PSB[b][:, c * 64:c * 64 + 64].bitcast(BF16),
                        YB4s[:, c * 128:(c + 1) * 128], ident[:],
                        is_transpose=True, start=True, stop=True),
                        waits=tmax(t_cb, p4pre,
                                   tok_ynt[t - 2] if t >= 2 else None), inc=sT)
                prev_trl = trl
                evl = None
                for c in range(DC):
                    evl = p.emit("vector", lambda v, _i, c=c, t=t, b=b: v.tensor_copy(
                        YNT[c][:, t * 128:(t + 1) * 128],
                        PSB[b][:, c * 64:c * 64 + 64].bitcast(BF16)),
                        waits=tmax(trl), inc=sV)
                tok_ynt[t] = evl
                prev_use[b] = tmax(evl, t_o1, t_o2, t_s0, t_yn)
            tok_p4_done = tmax(prev_use[0], prev_use[1])

        tok_attn1_done = None
        if "p5" in phases:
            sR5 = [p.sem("sR5a"), p.sem("sR5b")]
            sA5 = [p.sem("sA5a"), p.sem("sA5b")]
            sWL5 = p.sem("sWL5")
            WQ5 = lambda wi, dc: v2(wi * 16384 + dc * 2048, 1024, BF16)
            w5all = []
            p5pre = tmax(tok_p4_done, Tok(sT, "abs", sT.n), Tok(sV, "abs", sV.n))
            for wi, wt in enumerate([awqT, awkT, awvT, awoT]):
                for dc in range(DC):
                    w5all.append(p.emit("gpsimd", lambda gp, _i, wi=wi, dc=dc, wt=wt: gp.dma_start(
                        WQ5(wi, dc), wt[dc * 128:(dc + 1) * 128, :]),
                        waits=p5pre,
                        inc=sWL5, inc_by=16))
            tok_w5 = Tok(sWL5, "abs", sWL5.n)
            QT5 = v1(0, 1024)
            KT5 = v1(4096, 1024)
            TMP5 = v1(8192, 1024)
            RT5s = [[v1(12288 + (b_ * 4 + j) * 256, 64) for j in range(4)] for b_ in range(2)]
            QB5 = v1(14336, 1024, BF16)
            KB5 = v1(16384, 1024, BF16)
            QF5 = v1(18432, 1024, BF16)
            KR5 = [v1(20480, 1024, BF16), v1(22528, 1024, BF16)]
            VR5 = [v1(24576, 1024, BF16), v1(26624, 1024, BF16)]
            MA5 = v1(28672, 256)
            S5 = v1(29696, 256)
            PR5 = v1(30720, 256, BF16)
            PT5 = [v1(31232, 128, BF16), v1(31488, 128, BF16)]
            OB5 = v1(31744, 1024, BF16)
            OFM5 = v1(33792, 1024, BF16)
            BIA5 = v1(35840, 1024, BF16)
            BOB5 = v1(37888, 1024)
            CT5 = [v1(41984, 1024), v1(46080, 1024)]
            NM5 = vs(1472, 1)[:, 0:1]
            DN5 = vs(1536, 1)[:, 0:1]
            t_ma = p.emit("sync", lambda s, _i: s.dma_start(MA5, maskA),
                          waits=p5pre, inc=sQ, inc_by=16)
            t_bo5 = p.emit("sync", lambda s, _i: s.dma_start(
                BOB5, abo[None, :].broadcast_to([128, D])),
                waits=tmax(t_ma), inc=sQ, inc_by=16)
            B5 = [abq, abk, abv]
            prev_rt = [p5pre, p5pre]
            prev_bias5 = p5pre
            prev_ct5 = [p5pre, p5pre]
            prev_qkv_banks = p5pre
            prev_score_bank = [p5pre, p5pre]
            prev_kv = [p5pre, p5pre]
            prev_qf = p5pre
            prev_mask_h_by_bank = {}
            prev_t4 = {"q": p5pre, "k": p5pre}
            prev_tex = p5pre
            prev_trp = p5pre
            prev_pv_banks = p5pre
            prev_wo_banks = p5pre
            tok_kv = [None] * NT
            for t in range(NT):
                b = t % 2
                rts = RT5s[b]
                rtk = []
                for j in range(4):
                    rtk.append(p.emit("sync", lambda s, _i, j=j, t=t, rts=rts: s.dma_start(
                        rts[j], ropeA[j, t * 128:(t + 1) * 128, :]),
                        waits=tmax(prev_rt[b]), inc=sR5[b], inc_by=16))
                t_rt = tmax(*rtk)
                rope_toks = []
                for wi, wn in enumerate(("q", "k", "v")):
                    mm = None
                    for dc in range(DC):
                        for hf in range(2):
                            w = [tok_w5] if (t == 0 and wi == 0 and dc == 0) else []
                            w.append(tok_ynt[t])
                            if dc == 0:
                                w.append(prev_qkv_banks)
                            mm = p.emit("tensor", lambda te, _i, t=t, wi=wi, dc=dc, hf=hf: te.matmul(
                                PSB[4 + hf][:, 0:512],
                                YNT[dc][:, t * 128:(t + 1) * 128],
                                WQ5(wi, dc)[:, hf * 512:(hf + 1) * 512],
                                start=(dc == 0), stop=(dc == DC - 1)),
                                waits=tmax(*w), inc=sT)
                    tb5 = p.emit("gpsimd", lambda gp, _i, wi=wi: gp.dma_start(
                        BIA5, B5[wi][None, :].broadcast_to([128, D])),
                        waits=tmax(prev_bias5), inc=sG, inc_by=16)
                    if wn == "v":
                        lastv = None
                        for hf in range(2):
                            lastv = p.emit("vector", lambda v, _i, b=b, hf=hf: v.tensor_tensor(
                                out=VR5[b][:, hf * 512:(hf + 1) * 512],
                                in0=PSB[4 + hf][:, 0:512],
                                in1=BIA5[:, hf * 512:(hf + 1) * 512], op=ALU.add),
                                waits=tmax(mm, tb5, prev_kv[b]), inc=sV)
                        prev_bias5 = lastv
                        prev_qkv_banks = lastv
                        tok_v = lastv
                    else:
                        dst = QT5 if wn == "q" else KT5
                        dstb = QB5 if wn == "q" else KB5
                        j0 = 0 if wn == "q" else 2
                        t1 = None
                        for hf in range(2):
                            t1 = p.emit("vector", lambda v, _i, dst=dst, hf=hf: v.tensor_tensor(
                                out=dst[:, hf * 512:(hf + 1) * 512],
                                in0=PSB[4 + hf][:, 0:512],
                                in1=BIA5[:, hf * 512:(hf + 1) * 512], op=ALU.add),
                                waits=tmax(mm, tb5, prev_t4[wn]), inc=sV)
                        prev_bias5 = t1
                        t2 = p.emit("vector", lambda v, _i, dst=dst, rts=rts, j0=j0: v.tensor_tensor(
                            out=TMP5.rearrange("p (h c) -> p h c", c=64),
                            in0=shuf64(dst),
                            in1=rts[j0 + 1][:, None, :].broadcast_to([128, 16, 64]),
                            op=ALU.mult), waits=tmax(t1, t_rt), inc=sV)
                        t3 = p.emit("vector", lambda v, _i, dst=dst, rts=rts, j0=j0: v.tensor_tensor(
                            out=dst.rearrange("p (h c) -> p h c", c=64),
                            in0=dst.rearrange("p (h c) -> p h c", c=64),
                            in1=rts[j0][:, None, :].broadcast_to([128, 16, 64]),
                            op=ALU.mult), waits=tmax(t2), inc=sV)
                        t4 = p.emit("vector", lambda v, _i, dst=dst, dstb=dstb: v.tensor_add(
                            dstb, dst, TMP5), waits=tmax(t3), inc=sV)
                        prev_t4[wn] = t4
                        rope_toks.append(t4)
                        trl = None
                        pb = 6
                        for c in range(DC):
                            trl = p.emit("tensor", lambda te, _i, c=c, dstb=dstb, pb=pb: te.matmul(
                                PSB[pb][:, c * 64:c * 64 + 64].bitcast(BF16),
                                dstb[:, c * 128:(c + 1) * 128], ident[:],
                                is_transpose=True, start=True, stop=True),
                                waits=tmax(t4, prev_score_bank[0] if c == 0 else None),
                                inc=sT)
                        dfm = QF5 if wn == "q" else KR5[b]
                        evl = None
                        for c in range(DC):
                            wv = [trl]
                            if wn == "q" and c == 0:
                                wv.append(prev_qf)
                            if wn == "k" and c == 0:
                                wv.append(prev_kv[b])
                            evl = p.emit("vector", lambda v, _i, c=c, dfm=dfm: v.tensor_copy(
                                dfm[:, c * 128:(c + 1) * 128],
                                PSB[pb][:, c * 64:c * 64 + 64].bitcast(BF16)),
                                waits=tmax(*wv), inc=sV)
                        if wn == "q":
                            tok_q = evl
                        else:
                            tok_k = evl
                        prev_score_bank[0] = evl
                        prev_qkv_banks = evl
                tok_kv[t] = tmax(tok_k, tok_v)
                nk = 2 if t >= 1 else 1
                W_ = 128 * nk
                koff = 256 - W_
                pv_done = []
                prev_mask_h = None
                prev_pv_h = {}
                for h in range(NH):
                    sb = 6 + (h % 2)
                    smms = []
                    for ki in range(nk):
                        kt = t - (nk - 1) + ki
                        w = [tok_q, tok_kv[kt]] if ki == 0 else []
                        if ki == 0:
                            if h < 2:
                                w.append(prev_score_bank[1])
                            else:
                                w.append(prev_mask_h_by_bank.get(sb))
                        smms.append(p.emit("tensor", lambda te, _i, h=h, kt=kt, ki=ki, sb=sb: te.matmul(
                            PSB[sb][:, ki * 128:(ki + 1) * 128],
                            KR5[kt % 2][(h % 2) * 64:(h % 2) * 64 + 64,
                                        (h // 2) * 128:(h // 2) * 128 + 128],
                            QF5[(h % 2) * 64:(h % 2) * 64 + 64,
                                (h // 2) * 128:(h // 2) * 128 + 128],
                            start=True, stop=True, is_transpose=True) if False else te.matmul(
                            PSB[sb][:, ki * 128:(ki + 1) * 128],
                            QF5[(h % 2) * 64:(h % 2) * 64 + 64,
                                (h // 2) * 128:(h // 2) * 128 + 128],
                            KR5[kt % 2][(h % 2) * 64:(h % 2) * 64 + 64,
                                        (h // 2) * 128:(h // 2) * 128 + 128],
                            start=True, stop=True),
                            waits=tmax(*w), inc=sT))
                    tmsk = p.emit("vector", lambda v, _i, W_=W_, koff=koff, sb=sb: v.tensor_tensor(
                        out=S5[:, 0:W_], in0=PSB[sb][:, 0:W_],
                        in1=MA5[:, koff:koff + W_], op=ALU.add),
                        waits=tmax(*smms, t_ma, prev_tex), inc=sV)
                    prev_mask_h_by_bank[sb] = tmsk
                    tmx5 = p.emit("vector", lambda v, _i, W_=W_: v.reduce_max(
                        NM5, S5[:, 0:W_], axis=AX, negate=True),
                        waits=tmax(tmsk), inc=sV, fence=True)
                    tex5 = p.emit("scalar", lambda sc, _i, W_=W_: sc.activation(
                        out=PR5[:, 0:W_], in_=S5[:, 0:W_], func=AF.Exp,
                        bias=NM5, accum_out=DN5),
                        waits=tmax(tmx5, prev_trp), inc=sS, fence=True)
                    prev_tex = tex5
                    trc5 = p.emit("vector", lambda v, _i: v.reciprocal(DN5, DN5),
                                  waits=tmax(tex5), inc=sV, fence=True)
                    tnr5 = p.emit("vector", lambda v, _i, W_=W_: v.tensor_scalar(
                        out=PR5[:, 0:W_], in0=PR5[:, 0:W_], scalar1=DN5,
                        scalar2=None, op0=ALU.mult), waits=tmax(trc5), inc=sV)
                    trp = []
                    for ki in range(nk):
                        trp.append(p.emit("tensor", lambda te, _i, ki=ki: te.matmul(
                            PSB[5][:, 64 * ki:64 * ki + 64].bitcast(BF16),
                            PR5[:, ki * 128:(ki + 1) * 128], ident[:],
                            is_transpose=True, start=True, stop=True),
                            waits=tmax(tnr5,
                                       prev_pv_h.get(h - 2) if ki == 0 else None),
                            inc=sT))
                    prev_trp = trp[-1]
                    tpt = []
                    for ki in range(nk):
                        tpt.append(p.emit("vector", lambda v, _i, ki=ki: v.tensor_copy(
                            PT5[ki], PSB[5][:, 64 * ki:64 * ki + 64].bitcast(BF16)),
                            waits=tmax(trp[-1]), inc=sV))
                    pvb = h % 2
                    last = None
                    for ki in range(nk):
                        kt = t - (nk - 1) + ki
                        w = [tpt[ki]]
                        if h < 2 and ki == 0:
                            w.append(prev_pv_banks)
                        last = p.emit("tensor", lambda te, _i, h=h, ki=ki, kt=kt, nk=nk: te.matmul(
                            PSB[2 if h < 8 else 3][:, (h % 8) * 64:(h % 8) * 64 + 64],
                            PT5[ki],
                            VR5[kt % 2][:, h * 64:h * 64 + 64],
                            start=(ki == 0), stop=(ki == nk - 1)),
                            waits=tmax(*w), inc=sT)
                    prev_pv_h[h] = last
                    pv_done.append(last)
                tob = []
                for hb in range(2):
                    tob.append(p.emit("vector", lambda v, _i, hb=hb: v.tensor_copy(
                        OB5[:, hb * 512:(hb + 1) * 512], PSB[2 + hb][:, 0:512]),
                        waits=tmax(*pv_done), inc=sV))
                trw = None
                for c in range(DC):
                    trw = p.emit("tensor", lambda te, _i, c=c: te.matmul(
                        PSB[4 + (c % 2)][:, (c // 2) * 128:(c // 2) * 128 + 64].bitcast(BF16),
                        OB5[:, c * 128:(c + 1) * 128], ident[:],
                        is_transpose=True, start=True, stop=True),
                        waits=tmax(*tob), inc=sT)
                tofm = None
                for c in range(DC):
                    tofm = p.emit("vector", lambda v, _i, c=c: v.tensor_copy(
                        OFM5[:, c * 128:(c + 1) * 128],
                        PSB[4 + (c % 2)][:, (c // 2) * 128:(c // 2) * 128 + 64].bitcast(BF16)),
                        waits=tmax(trw), inc=sV)
                wo_last = None
                for dc in range(DC):
                    for hf in range(2):
                        w = [tofm]
                        if dc == 0 and hf == 0:
                            w.append(prev_wo_banks)
                        wo_last = p.emit("tensor", lambda te, _i, dc=dc, hf=hf: te.matmul(
                            PSB[2 + hf][:, 0:512],
                            OFM5[:, dc * 128:(dc + 1) * 128],
                            WQ5(3, dc)[:, hf * 512:(hf + 1) * 512],
                            start=(dc == 0), stop=(dc == DC - 1)),
                            waits=tmax(*w), inc=sT)
                tct5 = None
                for hf in range(2):
                    tct5 = p.emit("vector", lambda v, _i, b=b, hf=hf: v.tensor_tensor(
                        out=CT5[b][:, hf * 512:(hf + 1) * 512],
                        in0=PSB[2 + hf][:, 0:512],
                        in1=BOB5[:, hf * 512:(hf + 1) * 512], op=ALU.add),
                        waits=tmax(wo_last, t_bo5, prev_ct5[b]), inc=sV)
                prev_wo_banks = tct5
                prev_ct5[b] = p.emit("sync", lambda s, _i, t=t, b=b: s.dma_start(
                    attn1_c[t * 128:(t + 1) * 128, :], CT5[b]),
                    waits=tmax(tct5), inc=sA5[b], inc_by=16)
                prev_pv_banks = tct5
                prev_score_bank[1] = tmax(*[prev_mask_h_by_bank[s_] for s_ in (6, 7)
                                            if s_ in prev_mask_h_by_bank])
                prev_kv[0] = prev_kv[1] = tmax(*pv_done)
                prev_score_bank[0] = tmax(prev_score_bank[1])
                prev_qkv_banks = tofm
                prev_qf = tmax(*pv_done, prev_score_bank[1])
                prev_rt[b] = tmax(*rope_toks)
            tok_attn1_done = tmax(prev_ct5[0], prev_ct5[1])

        if "p7" in phases:
            sL7 = [p.sem("sL7a"), p.sem("sL7b")]
            sY7 = [p.sem("sY7a"), p.sem("sY7b")]
            p7pre = tmax(tok_attn1_done, Tok(sV, "abs", sV.n),
                         Tok(sT, "abs", sT.n), Tok(sS, "abs", sS.n))
            X7 = [v1(0, 1024), v1(4096, 1024)]
            A7 = [v1(8192, 1024), v1(12288, 1024)]
            Z7 = [v1(16384, 1024), v1(20480, 1024)]
            prev7 = [p7pre, p7pre]
            for t in range(NT):
                b = t % 2
                l1 = p.emit("sync", lambda s, _i, t=t, b=b: s.dma_start(
                    X7[b], x1s[t * 128:(t + 1) * 128, :]), waits=prev7[b],
                    inc=sL7[b], inc_by=16)
                l2 = p.emit("sync", lambda s, _i, t=t, b=b: s.dma_start(
                    A7[b], attn1_c[t * 128:(t + 1) * 128, :]), waits=prev7[b],
                    inc=sL7[b], inc_by=16)
                l3 = p.emit("sync", lambda s, _i, t=t, b=b: s.dma_start(
                    Z7[b], zs[t * 128:(t + 1) * 128, :]), waits=prev7[b],
                    inc=sL7[b], inc_by=16)
                a1 = p.emit("vector", lambda v, _i, b=b: v.tensor_add(
                    X7[b], X7[b], A7[b]), waits=tmax(l1, l2, l3), inc=sV)
                a2 = p.emit("vector", lambda v, _i, b=b: v.tensor_add(
                    X7[b], X7[b], Z7[b]), waits=tmax(a1), inc=sV)
                prev7[b] = p.emit("sync", lambda s, _i, t=t, b=b: s.dma_start(
                    y[t * 128:(t + 1) * 128, :], X7[b]), waits=tmax(a2),
                    inc=sY7[b], inc_by=16)
            p.emit("sync", lambda s, _i: s.nop(),
                   waits=tmax(prev7[0], prev7[1]))
            p.emit("gpsimd", lambda g, _i: g.nop(),
                   waits=tmax(prev7[0], prev7[1]))

        if "p5d" in dbg:
            d_a1 = dout("d_a1", (T, D))
            wAll5 = p.barrier_token()
            p.emit("sync", lambda s, _i, d_a1=d_a1: s.dma_start(d_a1, attn1_c),
                   waits=wAll5, inc=sQ, inc_by=16)
        if "p0x" in dbg:
            d_xt = dout("d_xt", (2, 128, 1024))
            d_xnb = dout("d_xnb", (2, 128, 1024))
            d_small = dout("d_small", (128, 512))
            wAllx = p.barrier_token()
            for i in range(2):
                p.emit("sync", lambda s, _i, i=i, d_xt=d_xt: s.dma_start(d_xt[i], XT[i]),
                       waits=wAllx, inc=sQ, inc_by=16)
                p.emit("gpsimd", lambda g, _i, i=i, d_xnb=d_xnb: g.dma_start(d_xnb[i], XNB[i]),
                       waits=wAllx, inc=sG, inc_by=16)
            p.emit("sync", lambda s, _i, d_small=d_small: s.dma_start(
                d_small, SMALL[:, 0:2048].bitcast(F32)), waits=wAllx, inc=sQ, inc_by=16)

        if "p0" in dbg:
            d_xnt = dout("d_xnt", (DC, 128, T))
            d_masks = dout("d_masks", (128, NT * NEXP))
            wAll = p.barrier_token()
            for c in range(DC):
                p.emit("gpsimd", lambda g, _i, c=c, d_xnt=d_xnt: g.dma_start(
                    d_xnt[c], XNT[c]), waits=wAll, inc=sG, inc_by=16)
            p.emit("sync", lambda s, _i, d_masks=d_masks: s.dma_start(d_masks, MASKS[:]),
                   waits=wAll, inc=sQ, inc_by=16)

        wEnd = p.barrier_token()
        p.emit("sync", lambda s, _i: s.nop(), waits=wEnd)
        p.emit("gpsimd", lambda g, _i: g.nop(), waits=wEnd)

        p.finalize()
    return nc, dbg_out



def build_p6a():
    nc = bass.Bass("TRN2", target_bir_lowering=False)
    zin = nc.dram_tensor("zin", [T, D], F32, kind="ExternalInput").ap()
    wqkv = nc.dram_tensor("wqkv", [D, I3], F32, kind="ExternalInput").ap()
    wo = nc.dram_tensor("wo", [D, D], F32, kind="ExternalInput").ap()
    qs = nc.dram_tensor("qs", [DH], F32, kind="ExternalInput").ap()
    ks = nc.dram_tensor("ks", [DH], F32, kind="ExternalInput").ap()
    ropeI = nc.dram_tensor("ropeI", [4, T, DH], F32, kind="ExternalInput").ap()
    maskI = nc.dram_tensor("maskI", [128, 384], F32, kind="ExternalInput").ap()
    zout = nc.dram_tensor("zout", [T, D], F32, kind="ExternalOutput").ap()

    with ExitStack() as ctx:
        p = Prog(nc, ctx)
        sb = lambda n, sh, dt: ctx.enter_context(nc.sbuf_tensor(n, sh, dt))
        WQK = sb("WQK", [128, 8 * I3], BF16)
        WOB = sb("WOB", [128, 8 * 1024], BF16)
        QSB = sb("QSB", [128, 64], F32)
        KSB = sb("KSB", [128, 64], F32)
        MIB = sb("MIB", [128, 384], F32)
        RTB = sb("RTB", [128, 2 * 4 * 64], F32)
        SH = sb("SH", [128, 2 * 1024], F32)
        SHB = sb("SHB", [128, 2 * 1024], BF16)
        XR = sb("XR", [128, 2 * 1024], F32)
        XST = sb("XST", [128, 1024], BF16)
        QN = sb("QN", [128, 1024], F32)
        SQ6 = sb("SQ6", [128, 1024], F32)
        TMP6 = sb("TMP6", [128, 1024], F32)
        SM6 = sb("SM6", [128, 64], F32)
        QB = sb("QB", [128, 1024], BF16)
        KB = sb("KB", [128, 1024], BF16)
        QT = sb("QT", [128, 1024], BF16)
        KT = sb("KT", [128, 3 * 1024], BF16)
        VR = sb("VR", [128, 3 * 1024], BF16)
        S5 = sb("S5", [128, 384], F32)
        PR = sb("PR", [128, 384], BF16)
        PT = sb("PT", [128, 384], BF16)
        OB = sb("OB", [128, 1024], BF16)
        OFM = sb("OFM", [128, 1024], BF16)
        CT = sb("CT", [128, 2 * 1024], F32)
        ident = sb("ident", [128, 128], BF16)
        io6 = sb("io6", [128, 128], I32)
        PSB = [ctx.enter_context(nc.psum_tensor(f"psb{i}", [128, 512], F32))
               for i in range(8)]
        RS16a = SM6[:, 0:16]
        RS16b = SM6[:, 16:32]
        NM6 = SM6[:, 32:33]
        DN6 = SM6[:, 33:34]

        import dataclasses as _dc

        def shuf64(ap):
            return _dc.replace(ap, offset=ap.offset + 32,
                               ap=[[ap.ap[0][0], 128], [64, 16], [-32, 2], [1, 32]])

        sW = p.sem("sW")
        sM = p.sem("sM")
        sLD = [p.sem("sLDa"), p.sem("sLDb")]
        sR6 = [p.sem("sR6a"), p.sem("sR6b")]
        sZO = [p.sem("sZOa"), p.sem("sZOb")]
        sV = p.sem("sV"); sS = p.sem("sS"); sT = p.sem("sT"); sGC = p.sem("sGC")

        t_clr = [p.emit("vector", lambda v, _i, b_=b_: v.memset(PSB[b_][:], 0.0), inc=sV)
                 for b_ in range(8)]
        t_clr.append(p.emit("vector", lambda v, _i: v.memset(SH[:], 0.0), inc=sV))
        clr = tmax(*t_clr)
        t_io = p.emit("gpsimd", lambda g, _i: g.iota(io6[:], [[1, 128]], channel_multiplier=-1,
                                                     allow_small_or_imprecise_dtypes=True), inc=sGC)
        t_id = p.emit("vector", lambda v, _i: v.tensor_scalar(
            out=ident[:], in0=io6[:], scalar1=0, scalar2=None, op0=ALU.is_equal),
            waits=tmax(t_io), inc=sV)
        wl = []
        for dc in range(DC):
            wl.append(p.emit("gpsimd", lambda g, _i, dc=dc: g.dma_start(
                WQK[:, dc * I3:(dc + 1) * I3], wqkv[dc * 128:(dc + 1) * 128, :]),
                inc=sW, inc_by=16))
        for dc in range(DC):
            wl.append(p.emit("gpsimd", lambda g, _i, dc=dc: g.dma_start(
                WOB[:, dc * 1024:(dc + 1) * 1024], wo[dc * 128:(dc + 1) * 128, :]),
                inc=sW, inc_by=16))
        tok_w = wl[-1]
        m1_ = p.emit("sync", lambda s, _i: s.dma_start(
            QSB[:], qs[None, :].broadcast_to([128, DH])), inc=sM, inc_by=16)
        m2_ = p.emit("sync", lambda s, _i: s.dma_start(
            KSB[:], ks[None, :].broadcast_to([128, DH])), inc=sM, inc_by=16)
        m3_ = p.emit("sync", lambda s, _i: s.dma_start(MIB[:], maskI), inc=sM, inc_by=16)
        tok_m = tmax(m3_)

        wSH = [clr, clr]
        wSHB = [tmax(t_id), tmax(t_id)]
        wXR = [clr, clr]
        wQNprev = tmax(t_id)
        wTMPprev = tmax(t_id)
        evac_prev = tmax(t_id)
        tok_b45_read = tmax(t_id)
        tok_b6_read = {6: tmax(t_id), 7: tmax(t_id)}
        prev_tex = tmax(t_id)
        prev_trp = tmax(t_id)
        prev_ct = tmax(t_id)
        prev_rt = [clr, clr]
        prev_store = [clr, clr]
        tok_evq = None
        tok_evk = [None] * NT
        tok_vcp = [None] * NT

        for t in range(NT):
            b = t % 2
            k3 = t % 3
            shb = SH[:, b * 1024:(b + 1) * 1024]
            shbb = SHB[:, b * 1024:(b + 1) * 1024]
            xrb = XR[:, b * 1024:(b + 1) * 1024]
            l1 = p.emit("sync", lambda s, _i, t=t, shb=shb: s.dma_start(
                shb[:, 0:512], zin[t * 128:(t + 1) * 128, 0:512]),
                waits=tmax(wSH[b]), inc=sLD[b], inc_by=16)
            if t == 0:
                l2 = p.emit("sync", lambda s, _i, shb=shb: s.dma_start(
                    shb[1:128, 512:1024], zin[0:127, 512:1024]), inc=sLD[b], inc_by=16)
            else:
                l2 = p.emit("sync", lambda s, _i, t=t, shb=shb: s.dma_start(
                    shb[:, 512:1024], zin[t * 128 - 1:(t + 1) * 128 - 1, 512:1024]),
                    inc=sLD[b], inc_by=16)
            l3 = p.emit("sync", lambda s, _i, t=t, xrb=xrb: s.dma_start(
                xrb, zin[t * 128:(t + 1) * 128, :]),
                waits=tmax(wXR[b]), inc=sLD[b], inc_by=16)
            ld = tmax(l1, l2, l3)
            rts = [RTB[:, (b * 4 + j) * 64:(b * 4 + j + 1) * 64] for j in range(4)]
            rl = []
            for j in range(4):
                rl.append(p.emit("sync", lambda s, _i, t=t, j=j, rts=rts: s.dma_start(
                    rts[j], ropeI[j, t * 128:(t + 1) * 128, :]),
                    waits=tmax(prev_rt[b]), inc=sR6[b], inc_by=16))
            t_rt = tmax(*rl)
            c1 = p.emit("vector", lambda v, _i, shbb=shbb, shb=shb: v.tensor_copy(shbb, shb),
                        waits=tmax(ld, wSHB[b], clr), inc=sV)
            wSH[b] = c1
            trl = None
            for c in range(DC):
                trl = p.emit("tensor", lambda te, _i, c=c, shbb=shbb: te.matmul(
                    PSB[0][:, c * 64:c * 64 + 64].bitcast(BF16),
                    shbb[:, c * 128:(c + 1) * 128], ident[:],
                    is_transpose=True, start=True, stop=True),
                    waits=tmax(c1, t_id, evac_prev if c == 0 else None), inc=sT)
            wSHB[b] = trl
            evl = None
            for c in range(DC):
                evl = p.emit("vector", lambda v, _i, c=c: v.tensor_copy(
                    XST[:, c * 128:(c + 1) * 128],
                    PSB[0][:, c * 64:c * 64 + 64].bitcast(BF16)),
                    waits=tmax(trl), inc=sV)
            evac_prev = evl
            for wi, wn in enumerate(("q", "k", "v")):
                mm = None
                for dc in range(DC):
                    for hf in range(2):
                        w = [evl, tok_b45_read] if dc == 0 else []
                        if t == 0 and wi == 0 and dc == 0:
                            w.append(tok_w)
                        mm = p.emit("tensor", lambda te, _i, wi=wi, dc=dc, hf=hf: te.matmul(
                            PSB[4 + hf][:, 0:512],
                            XST[:, dc * 128:(dc + 1) * 128],
                            WQK[:, dc * I3 + wi * 1024 + hf * 512:
                                dc * I3 + wi * 1024 + (hf + 1) * 512],
                            start=(dc == 0), stop=(dc == DC - 1)),
                            waits=tmax(*w), inc=sT)
                if wn == "v":
                    lastv = None
                    for hf in range(2):
                        lastv = p.emit("vector", lambda v, _i, k3=k3, hf=hf: v.tensor_copy(
                            VR[:, k3 * 1024 + hf * 512:k3 * 1024 + (hf + 1) * 512],
                            PSB[4 + hf][:, 0:512]),
                            waits=tmax(mm), inc=sV)
                    tok_vcp[t] = lastv
                    tok_b45_read = lastv
                else:
                    qn = None
                    for hf in range(2):
                        qn = p.emit("vector", lambda v, _i, hf=hf: v.tensor_copy(
                            QN[:, hf * 512:(hf + 1) * 512], PSB[4 + hf][:, 0:512]),
                            waits=tmax(mm, wQNprev), inc=sV)
                    tok_b45_read = qn
                    t_sq = p.emit("scalar", lambda sc, _i: sc.activation(
                        out=SQ6[:], in_=QN[:], func=AF.Square), waits=tmax(qn), inc=sS)
                    t_rs = p.emit("vector", lambda v, _i: v.tensor_reduce(
                        RS16a, SQ6.rearrange("p (h c) -> p h c", c=64), axis=AX,
                        op=ALU.add), waits=tmax(t_sq), inc=sV, fence=True)
                    t_sr = p.emit("scalar", lambda sc, _i: sc.activation(
                        out=RS16b, in_=RS16a, func=AF.Sqrt), waits=tmax(t_rs), inc=sS,
                        fence=True)
                    t_mx = p.emit("vector", lambda v, _i: v.tensor_scalar(
                        out=RS16b, in0=RS16b, scalar1=1.0e-12, scalar2=None,
                        op0=ALU.max), waits=tmax(t_sr), inc=sV, fence=True)
                    t_rc = p.emit("vector", lambda v, _i: v.reciprocal(RS16b, RS16b),
                                  waits=tmax(t_mx), inc=sV, fence=True)
                    scb = QSB if wn == "q" else KSB
                    m1 = p.emit("vector", lambda v, _i: v.tensor_tensor(
                        out=QN.rearrange("p (h c) -> p h c", c=64),
                        in0=QN.rearrange("p (h c) -> p h c", c=64),
                        in1=RS16b[:, :, None].broadcast_to([128, 16, 64]),
                        op=ALU.mult), waits=tmax(t_rc), inc=sV)
                    m2 = p.emit("vector", lambda v, _i, scb=scb: v.tensor_tensor(
                        out=QN.rearrange("p (h c) -> p h c", c=64),
                        in0=QN.rearrange("p (h c) -> p h c", c=64),
                        in1=scb[:, None, :].broadcast_to([128, 16, 64]),
                        op=ALU.mult), waits=tmax(m1, tok_m), inc=sV)
                    j0 = 0 if wn == "q" else 2
                    dstb = QB if wn == "q" else KB
                    t2 = p.emit("vector", lambda v, _i, rts=rts, j0=j0: v.tensor_tensor(
                        out=TMP6.rearrange("p (h c) -> p h c", c=64),
                        in0=shuf64(QN[:]),
                        in1=rts[j0 + 1][:, None, :].broadcast_to([128, 16, 64]),
                        op=ALU.mult), waits=tmax(m2, t_rt, wTMPprev), inc=sV)
                    t3 = p.emit("vector", lambda v, _i, rts=rts, j0=j0: v.tensor_tensor(
                        out=QN.rearrange("p (h c) -> p h c", c=64),
                        in0=QN.rearrange("p (h c) -> p h c", c=64),
                        in1=rts[j0][:, None, :].broadcast_to([128, 16, 64]),
                        op=ALU.mult), waits=tmax(t2), inc=sV)
                    t4 = p.emit("vector", lambda v, _i, dstb=dstb: v.tensor_add(
                        dstb[:], QN[:], TMP6[:]), waits=tmax(t3), inc=sV)
                    wQNprev = t4
                    wTMPprev = t4
                    dfm = QT[:] if wn == "q" else KT[:, k3 * 1024:(k3 + 1) * 1024]
                    trq = None
                    for c in range(DC):
                        trq = p.emit("tensor", lambda te, _i, c=c, dstb=dstb: te.matmul(
                            PSB[6][:, c * 64:c * 64 + 64].bitcast(BF16),
                            dstb[:, c * 128:(c + 1) * 128], ident[:],
                            is_transpose=True, start=True, stop=True),
                            waits=tmax(t4, tok_b6_read[6] if c == 0 else None), inc=sT)
                    evq = None
                    for c in range(DC):
                        evq = p.emit("vector", lambda v, _i, c=c, dfm=dfm: v.tensor_copy(
                            dfm[:, c * 128:(c + 1) * 128],
                            PSB[6][:, c * 64:c * 64 + 64].bitcast(BF16)),
                            waits=tmax(trq), inc=sV)
                    tok_b6_read[6] = evq
                    if wn == "q":
                        tok_evq = evq
                    else:
                        tok_evk[t] = evq
            prev_rt[b] = wTMPprev
            nk = min(t + 1, 3)
            W_ = 128 * nk
            pv_done = []
            for h in range(NH):
                sbk = 6 + (h % 2)
                smms = []
                for ki in range(nk):
                    kt = t - (nk - 1) + ki
                    ksl = kt % 3
                    w = []
                    if ki == 0:
                        w = [tok_evq, tok_evk[kt], tok_vcp[kt], tok_b6_read[sbk]]
                    smms.append(p.emit("tensor", lambda te, _i, h=h, ki=ki, ksl=ksl, sbk=sbk: te.matmul(
                        PSB[sbk][:, ki * 128:(ki + 1) * 128],
                        QT[(h % 2) * 64:(h % 2) * 64 + 64,
                           (h // 2) * 128:(h // 2) * 128 + 128],
                        KT[(h % 2) * 64:(h % 2) * 64 + 64,
                           ksl * 1024 + (h // 2) * 128:ksl * 1024 + (h // 2) * 128 + 128],
                        start=True, stop=True),
                        waits=tmax(*w), inc=sT))
                tmsk = p.emit("vector", lambda v, _i, W_=W_, nk=nk, sbk=sbk: v.tensor_tensor(
                    out=S5[:, 0:W_], in0=PSB[sbk][:, 0:W_],
                    in1=MIB[:, (3 - nk) * 128:384], op=ALU.add),
                    waits=tmax(*smms, tok_m, prev_tex), inc=sV)
                tok_b6_read[sbk] = tmsk
                tmx = p.emit("vector", lambda v, _i, W_=W_: v.reduce_max(
                    NM6, S5[:, 0:W_], axis=AX, negate=True),
                    waits=tmax(tmsk), inc=sV, fence=True)
                tex = p.emit("scalar", lambda sc, _i, W_=W_: sc.activation(
                    out=PR[:, 0:W_], in_=S5[:, 0:W_], func=AF.Exp,
                    bias=NM6, accum_out=DN6),
                    waits=tmax(tmx, prev_trp), inc=sS, fence=True)
                prev_tex = tex
                trc = p.emit("vector", lambda v, _i: v.reciprocal(DN6, DN6),
                             waits=tmax(tex), inc=sV, fence=True)
                tnr = p.emit("vector", lambda v, _i, W_=W_: v.tensor_scalar(
                    out=PR[:, 0:W_], in0=PR[:, 0:W_], scalar1=DN6,
                    scalar2=None, op0=ALU.mult), waits=tmax(trc), inc=sV)
                trp = []
                for ki in range(nk):
                    trp.append(p.emit("tensor", lambda te, _i, ki=ki: te.matmul(
                        PSB[5][:, ki * 64:ki * 64 + 64].bitcast(BF16),
                        PR[:, ki * 128:(ki + 1) * 128], ident[:],
                        is_transpose=True, start=True, stop=True),
                        waits=tmax(tnr), inc=sT))
                prev_trp = trp[-1]
                tpt = []
                for ki in range(nk):
                    tpt.append(p.emit("vector", lambda v, _i, ki=ki: v.tensor_copy(
                        PT[:, ki * 128:(ki + 1) * 128],
                        PSB[5][:, ki * 64:ki * 64 + 64].bitcast(BF16)),
                        waits=tmax(trp[ki]), inc=sV))
                last = None
                for ki in range(nk):
                    kt = t - (nk - 1) + ki
                    ksl = kt % 3
                    w = [tpt[ki]]
                    if h < 2 and ki == 0:
                        w.append(prev_ct)
                    last = p.emit("tensor", lambda te, _i, h=h, ki=ki, ksl=ksl, nk=nk: te.matmul(
                        PSB[2 if h < 8 else 3][:, (h % 8) * 64:(h % 8) * 64 + 64],
                        PT[:, ki * 128:(ki + 1) * 128],
                        VR[:, ksl * 1024 + h * 64:ksl * 1024 + h * 64 + 64],
                        start=(ki == 0), stop=(ki == nk - 1)),
                        waits=tmax(*w), inc=sT)
                pv_done.append(last)
            tob = []
            for hb in range(2):
                tob.append(p.emit("vector", lambda v, _i, hb=hb: v.tensor_copy(
                    OB[:, hb * 512:(hb + 1) * 512], PSB[2 + hb][:, 0:512]),
                    waits=tmax(*pv_done), inc=sV))
            trw = None
            for c in range(DC):
                trw = p.emit("tensor", lambda te, _i, c=c: te.matmul(
                    PSB[4 + (c % 2)][:, (c // 2) * 128:(c // 2) * 128 + 64].bitcast(BF16),
                    OB[:, c * 128:(c + 1) * 128], ident[:],
                    is_transpose=True, start=True, stop=True),
                    waits=tmax(*tob), inc=sT)
            tofm = None
            for c in range(DC):
                tofm = p.emit("vector", lambda v, _i, c=c: v.tensor_copy(
                    OFM[:, c * 128:(c + 1) * 128],
                    PSB[4 + (c % 2)][:, (c // 2) * 128:(c // 2) * 128 + 64].bitcast(BF16)),
                    waits=tmax(trw), inc=sV)
            tok_b45_read = tofm
            wo_last = None
            for dc in range(DC):
                for hf in range(2):
                    w = [tofm]
                    if dc == 0 and hf == 0:
                        w.append(prev_ct)
                    wo_last = p.emit("tensor", lambda te, _i, dc=dc, hf=hf: te.matmul(
                        PSB[2 + hf][:, 0:512],
                        OFM[:, dc * 128:(dc + 1) * 128],
                        WOB[:, dc * 1024 + hf * 512:dc * 1024 + (hf + 1) * 512],
                        start=(dc == 0), stop=(dc == DC - 1)),
                        waits=tmax(*w), inc=sT)
            ctb = CT[:, b * 1024:(b + 1) * 1024]
            tct = None
            for hf in range(2):
                tct = p.emit("vector", lambda v, _i, ctb=ctb, hf=hf, xrb=xrb: v.tensor_tensor(
                    out=ctb[:, hf * 512:(hf + 1) * 512],
                    in0=PSB[2 + hf][:, 0:512],
                    in1=xrb[:, hf * 512:(hf + 1) * 512], op=ALU.add),
                    waits=tmax(wo_last, ld, prev_store[b]), inc=sV)
            prev_ct = tct
            wXR[b] = tct
            prev_store[b] = p.emit("sync", lambda s, _i, t=t, ctb=ctb: s.dma_start(
                zout[t * 128:(t + 1) * 128, :], ctb),
                waits=tmax(tct), inc=sZO[b], inc_by=16)
        p.emit("sync", lambda s, _i: s.nop(), waits=tmax(prev_store[0], prev_store[1]))
        p.emit("gpsimd", lambda g, _i: g.nop(), waits=tmax(prev_store[0], prev_store[1]))
        p.finalize()
    return nc



def build_p6f():
    nc = bass.Bass("TRN2", target_bir_lowering=False)
    zin = nc.dram_tensor("zin", [T, D], F32, kind="ExternalInput").ap()
    w1 = nc.dram_tensor("w1", [D, DI2], F32, kind="ExternalInput").ap()
    b1 = nc.dram_tensor("b1", [DI2], F32, kind="ExternalInput").ap()
    w2 = nc.dram_tensor("w2", [DI, D], F32, kind="ExternalInput").ap()
    b2 = nc.dram_tensor("b2", [D], F32, kind="ExternalInput").ap()
    lng = nc.dram_tensor("lng", [D], F32, kind="ExternalInput").ap()
    lnb = nc.dram_tensor("lnb", [D], F32, kind="ExternalInput").ap()
    zout = nc.dram_tensor("zout", [T, D], F32, kind="ExternalOutput").ap()

    NFC = 22
    with ExitStack() as ctx:
        p = Prog(nc, ctx)
        sb = lambda n, sh, dt: ctx.enter_context(nc.sbuf_tensor(n, sh, dt))
        W1B = sb("W1B", [128, 8 * DI2], BF16)
        W2B = sb("W2B", [128, NFC * 1024], BF16)
        B1BC = sb("B1BC", [128, DI2], F32)
        GBC = sb("GBC", [128, D], F32)
        BBC = sb("BBC", [128, D], F32)
        B2BC = sb("B2BC", [128, D], F32)
        F2 = sb("F2", [128, 2 * 1024], F32)
        FR = sb("FR", [128, 2 * 1024], F32)
        XNS = sb("XNS", [128, 1024], F32)
        XNB = sb("XNB", [128, 1024], BF16)
        XNT6 = sb("XNT6", [128, 1024], BF16)
        ASB = sb("ASB", [128, 512], F32)
        GSB = sb("GSB", [128, 512], F32)
        GEB = sb("GEB", [128, 512], BF16)
        HB = sb("HB", [128, DI], BF16)
        HTB = sb("HTB", [128, NFC * 128], BF16)
        OUTB = sb("OUTB", [128, 1024], F32)
        SM6 = sb("SM6", [128, 64], F32)
        ident = sb("ident", [128, 128], BF16)
        io6 = GEB[:, 0:256].bitcast(I32)
        PSB = [ctx.enter_context(nc.psum_tensor(f"psb{i}", [128, 512], F32))
               for i in range(8)]

        sW = p.sem("sW")
        sM = p.sem("sM")
        sFD = [p.sem("sFDa"), p.sem("sFDb")]
        sZO = [p.sem("sZOa"), p.sem("sZOb")]
        sV = p.sem("sV"); sS = p.sem("sS"); sT = p.sem("sT"); sGC = p.sem("sGC")

        t_clr = [p.emit("vector", lambda v, _i, b_=b_: v.memset(PSB[b_][:], 0.0), inc=sV)
                 for b_ in range(8)]
        t_clr.append(p.emit("vector", lambda v, _i: v.memset(F2[:], 0.0), inc=sV))
        clr = tmax(*t_clr)
        t_io = p.emit("gpsimd", lambda g, _i: g.iota(io6[:], [[1, 128]], channel_multiplier=-1,
                                                     allow_small_or_imprecise_dtypes=True), inc=sGC)
        t_id = p.emit("vector", lambda v, _i: v.tensor_scalar(
            out=ident[:], in0=io6[:], scalar1=0, scalar2=None, op0=ALU.is_equal),
            waits=tmax(t_io), inc=sV)
        wl = []
        for dc in range(DC):
            wl.append(p.emit("gpsimd", lambda g, _i, dc=dc: g.dma_start(
                W1B[:, dc * DI2:(dc + 1) * DI2], w1[dc * 128:(dc + 1) * 128, :]),
                inc=sW, inc_by=16))
        for fc in range(NFC):
            cw = min(128, DI - fc * 128)
            wl.append(p.emit("gpsimd", lambda g, _i, fc=fc, cw=cw: g.dma_start(
                W2B[0:cw, fc * 1024:(fc + 1) * 1024], w2[fc * 128:fc * 128 + cw, :]),
                inc=sW, inc_by=16))
        tok_w = wl[-1]
        ml = []
        ml.append(p.emit("sync", lambda s, _i: s.dma_start(
            B1BC[:], b1[None, :].broadcast_to([128, DI2])), inc=sM, inc_by=16))
        ml.append(p.emit("sync", lambda s, _i: s.dma_start(
            GBC[:], lng[None, :].broadcast_to([128, D])), inc=sM, inc_by=16))
        ml.append(p.emit("sync", lambda s, _i: s.dma_start(
            BBC[:], lnb[None, :].broadcast_to([128, D])), inc=sM, inc_by=16))
        ml.append(p.emit("sync", lambda s, _i: s.dma_start(
            B2BC[:], b2[None, :].broadcast_to([128, D])), inc=sM, inc_by=16))
        tok_m = ml[-1]

        EPSB = SM6[:, 40:41]
        t_eps = p.emit("vector", lambda v, _i: v.memset(EPSB, 1.0e-5), inc=sV)

        wF2 = [clr, clr]
        wFR = [clr, clr]
        wXNS = tmax(t_id)
        wXNB = tmax(t_id)
        evac_prev = tmax(t_id)
        tok_bank_read = [tmax(t_id)] * 3
        prev_th = tmax(t_id)
        prev_ht_evac = [tmax(t_id), tmax(t_id)]
        prev_w2_read = tmax(t_id)
        prev_store = [clr, clr]
        prev_out_read = clr

        for t in range(NT):
            b = t % 2
            f2b = F2[:, b * 1024:(b + 1) * 1024]
            frb = FR[:, b * 1024:(b + 1) * 1024]
            outb = OUTB[:]
            f1 = p.emit("sync", lambda s, _i, t=t, f2b=f2b: s.dma_start(
                f2b[:, 0:512], zin[t * 128:(t + 1) * 128, 0:512]),
                waits=tmax(wF2[b]), inc=sFD[b], inc_by=16)
            if t == 0:
                f2_ = p.emit("sync", lambda s, _i, f2b=f2b: s.dma_start(
                    f2b[1:128, 512:1024], zin[0:127, 512:1024]), inc=sFD[b], inc_by=16)
            else:
                f2_ = p.emit("sync", lambda s, _i, t=t, f2b=f2b: s.dma_start(
                    f2b[:, 512:1024], zin[t * 128 - 1:(t + 1) * 128 - 1, 512:1024]),
                    inc=sFD[b], inc_by=16)
            f3 = p.emit("sync", lambda s, _i, t=t, frb=frb: s.dma_start(
                frb, zin[t * 128:(t + 1) * 128, :]),
                waits=tmax(wFR[b]), inc=sFD[b], inc_by=16)
            ld = tmax(f1, f2_, f3)
            meanraw = SM6[:, 0:1]; sqs = SM6[:, 1:2]; var0 = SM6[:, 2:3]
            rstd = SM6[:, 3:4]; mean = SM6[:, 4:5]; msq = SM6[:, 5:6]
            t_m0 = p.emit("vector", lambda v, _i, f2b=f2b, meanraw=meanraw: v.reduce_sum(
                meanraw, f2b, axis=AX), waits=tmax(ld), inc=sV, fence=True)
            t_s0 = p.emit("scalar", lambda sc, _i, f2b=f2b, sqs=sqs: sc.activation(
                out=XNS[:], in_=f2b, func=AF.Square, accum_out=sqs),
                waits=tmax(ld, wXNS), inc=sS, fence=True)
            p.emit("vector", lambda v, _i, mean=mean, meanraw=meanraw: v.tensor_scalar(
                out=mean, in0=meanraw, scalar1=1.0 / D, scalar2=None, op0=ALU.mult),
                waits=tmax(t_m0), inc=sV)
            t_msq = p.emit("vector", lambda v, _i, msq=msq, meanraw=meanraw: v.tensor_scalar(
                out=msq, in0=meanraw, scalar1=meanraw, scalar2=None, op0=ALU.mult),
                inc=sV, fence=True)
            t_v0 = p.emit("vector", lambda v, _i, var0=var0, sqs=sqs, msq=msq: v.scalar_tensor_tensor(
                out=var0, in0=sqs, scalar=float(D), in1=msq, op0=ALU.mult, op1=ALU.subtract),
                waits=tmax(t_s0, t_msq), inc=sV, fence=True)
            t_sq = p.emit("scalar", lambda sc, _i, var0=var0, rstd=rstd: sc.activation(
                out=rstd, in_=var0, func=AF.Sqrt, bias=EPSB, scale=1.0 / (D * D)),
                waits=tmax(t_v0, t_eps), inc=sS, fence=True)
            t_rc = p.emit("vector", lambda v, _i, rstd=rstd: v.reciprocal(rstd, rstd),
                          waits=tmax(t_sq), inc=sV, fence=True)
            t_xn = p.emit("vector", lambda v, _i, f2b=f2b, mean=mean, rstd=rstd: v.tensor_scalar(
                out=XNS[:], in0=f2b, scalar1=mean, scalar2=rstd,
                op0=ALU.subtract, op1=ALU.mult), waits=tmax(t_rc, t_s0), inc=sV)
            wF2[b] = tmax(t_xn, t_s0)
            t_xg = p.emit("vector", lambda v, _i: v.tensor_tensor(
                out=XNS[:], in0=XNS[:], in1=GBC[:], op=ALU.mult),
                waits=tmax(t_xn, tok_m), inc=sV)
            t_xb = p.emit("vector", lambda v, _i: v.tensor_add(
                XNB[:], XNS[:], BBC[:]), waits=tmax(t_xg, wXNB), inc=sV)
            wXNS = t_xb
            trl = None
            for c in range(DC):
                trl = p.emit("tensor", lambda te, _i, c=c: te.matmul(
                    PSB[0][:, c * 64:c * 64 + 64].bitcast(BF16),
                    XNB[:, c * 128:(c + 1) * 128], ident[:],
                    is_transpose=True, start=True, stop=True),
                    waits=tmax(t_xb, t_id, evac_prev if c == 0 else None), inc=sT)
            wXNB = trl
            evl = None
            for c in range(DC):
                evl = p.emit("vector", lambda v, _i, c=c: v.tensor_copy(
                    XNT6[:, c * 128:(c + 1) * 128],
                    PSB[0][:, c * 64:c * 64 + 64].bitcast(BF16)),
                    waits=tmax(trl), inc=sV)
            evac_prev = evl
            th_tok = [None] * 6
            for j in range(6):
                aw = 512 if j < 5 else 170
                bkA = 2 + (j % 3) * 2
                bkB = 3 + (j % 3) * 2
                mmA = None; mmB = None
                for dc in range(DC):
                    w = []
                    if dc == 0:
                        w = [evl, tok_bank_read[j % 3]]
                        if t == 0 and j == 0:
                            w.append(tok_w)
                    mmA = p.emit("tensor", lambda te, _i, j=j, dc=dc, aw=aw, bkA=bkA: te.matmul(
                        PSB[bkA][:, 0:aw],
                        XNT6[:, dc * 128:(dc + 1) * 128],
                        W1B[:, dc * DI2 + j * 512:dc * DI2 + j * 512 + aw],
                        start=(dc == 0), stop=(dc == DC - 1)),
                        waits=tmax(*w), inc=sT)
                for dc in range(DC):
                    mmB = p.emit("tensor", lambda te, _i, j=j, dc=dc, aw=aw, bkB=bkB: te.matmul(
                        PSB[bkB][:, 0:aw],
                        XNT6[:, dc * 128:(dc + 1) * 128],
                        W1B[:, dc * DI2 + DI + j * 512:dc * DI2 + DI + j * 512 + aw],
                        start=(dc == 0), stop=(dc == DC - 1)),
                        inc=sT)
                ta = p.emit("vector", lambda v, _i, j=j, aw=aw, bkA=bkA: v.tensor_tensor(
                    out=ASB[:, 0:aw], in0=PSB[bkA][:, 0:aw],
                    in1=B1BC[:, j * 512:j * 512 + aw], op=ALU.add),
                    waits=tmax(mmA, mmB, tok_m, prev_th), inc=sV)
                tg = p.emit("vector", lambda v, _i, j=j, aw=aw, bkB=bkB: v.tensor_tensor(
                    out=GSB[:, 0:aw], in0=PSB[bkB][:, 0:aw],
                    in1=B1BC[:, DI + j * 512:DI + j * 512 + aw], op=ALU.add),
                    waits=tmax(ta), inc=sV)
                tok_bank_read[j % 3] = tg
                ge = p.emit("scalar", lambda sc, _i, aw=aw: sc.activation(
                    out=GEB[:, 0:aw], in_=GSB[:, 0:aw], func=AF.Gelu),
                    waits=tmax(tg, t_id if (t == 0 and j == 0) else None), inc=sS)
                th = p.emit("vector", lambda v, _i, j=j, aw=aw: v.tensor_tensor(
                    out=HB[:, j * 512:j * 512 + aw], in0=ASB[:, 0:aw],
                    in1=GEB[:, 0:aw], op=ALU.mult),
                    waits=tmax(ge), inc=sV)
                th_tok[j] = th
                prev_th = th
            ht_ev = None
            for fc in range(NFC):
                cw = min(128, DI - fc * 128)
                bk = fc % 2
                trh = p.emit("tensor", lambda te, _i, fc=fc, cw=cw, bk=bk: te.matmul(
                    PSB[bk][:, 0:64].bitcast(BF16)[0:cw, :],
                    HB[:, fc * 128:fc * 128 + cw], ident[:],
                    is_transpose=True, start=True, stop=True),
                    waits=tmax(th_tok[min(fc // 4, 5)], prev_ht_evac[bk],
                               th_tok[5] if fc == 0 else None), inc=sT)
                ht_ev = p.emit("vector", lambda v, _i, fc=fc, cw=cw, bk=bk: v.tensor_copy(
                    HTB[0:cw, fc * 128:(fc + 1) * 128],
                    PSB[bk][:, 0:64].bitcast(BF16)[0:cw, :]),
                    waits=tmax(trh), inc=sV)
                prev_ht_evac[bk] = ht_ev
            w2_last = None
            for fc in range(NFC):
                cw = min(128, DI - fc * 128)
                for hf in range(2):
                    w = []
                    if fc == 0:
                        w = [ht_ev, prev_w2_read]
                    w2_last = p.emit("tensor", lambda te, _i, fc=fc, cw=cw, hf=hf: te.matmul(
                        PSB[6 + hf][:, 0:512],
                        HTB[0:cw, fc * 128:(fc + 1) * 128],
                        W2B[0:cw, fc * 1024 + hf * 512:fc * 1024 + (hf + 1) * 512],
                        start=(fc == 0), stop=(fc == NFC - 1)),
                        waits=tmax(*w), inc=sT)
            to2 = None
            for hf in range(2):
                to1 = p.emit("vector", lambda v, _i, outb=outb, hf=hf: v.tensor_tensor(
                    out=outb[:, hf * 512:(hf + 1) * 512],
                    in0=PSB[6 + hf][:, 0:512],
                    in1=B2BC[:, hf * 512:(hf + 1) * 512], op=ALU.add),
                    waits=tmax(w2_last, tok_m, prev_out_read), inc=sV)
                to2 = p.emit("vector", lambda v, _i, outb=outb, frb=frb, hf=hf: v.tensor_add(
                    outb[:, hf * 512:(hf + 1) * 512],
                    outb[:, hf * 512:(hf + 1) * 512],
                    frb[:, hf * 512:(hf + 1) * 512]),
                    waits=tmax(to1, ld), inc=sV)
            prev_w2_read = to2
            tok_bank_read[2] = to2
            wFR[b] = to2
            prev_out_read = p.emit("sync", lambda s, _i, t=t, outb=outb: s.dma_start(
                zout[t * 128:(t + 1) * 128, :], outb),
                waits=tmax(to2), inc=sZO[b], inc_by=16)
            prev_store[b] = prev_out_read
        p.emit("sync", lambda s, _i: s.nop(), waits=tmax(prev_store[0], prev_store[1]))
        p.emit("gpsimd", lambda g, _i: g.nop(), waits=tmax(prev_store[0], prev_store[1]))
        p.finalize()
    return nc



def build_p7s():
    nc = bass.Bass("TRN2", target_bir_lowering=False)
    x1 = nc.dram_tensor("x1", [T, D], F32, kind="ExternalInput").ap()
    a1 = nc.dram_tensor("a1", [T, D], F32, kind="ExternalInput").ap()
    z6 = nc.dram_tensor("z6", [T, D], F32, kind="ExternalInput").ap()
    y = nc.dram_tensor("y", [T, D], F32, kind="ExternalOutput").ap()
    with ExitStack() as ctx:
        p = Prog(nc, ctx)
        X7 = ctx.enter_context(nc.sbuf_tensor("X7", [128, 2 * 3 * 1024], F32))
        sL7 = [p.sem("sL7a"), p.sem("sL7b")]
        sY7 = [p.sem("sY7a"), p.sem("sY7b")]
        sV = p.sem("sV")
        prev7 = [None, None]
        for t in range(NT):
            b = t % 2
            xb = X7[:, b * 3072:b * 3072 + 1024]
            ab = X7[:, b * 3072 + 1024:b * 3072 + 2048]
            zb = X7[:, b * 3072 + 2048:b * 3072 + 3072]
            l1 = p.emit("sync", lambda s, _i, t=t, xb=xb: s.dma_start(
                xb, x1[t * 128:(t + 1) * 128, :]), waits=tmax(prev7[b]),
                inc=sL7[b], inc_by=16)
            l2 = p.emit("sync", lambda s, _i, t=t, ab=ab: s.dma_start(
                ab, a1[t * 128:(t + 1) * 128, :]), waits=tmax(prev7[b]),
                inc=sL7[b], inc_by=16)
            l3 = p.emit("sync", lambda s, _i, t=t, zb=zb: s.dma_start(
                zb, z6[t * 128:(t + 1) * 128, :]), waits=tmax(prev7[b]),
                inc=sL7[b], inc_by=16)
            a_1 = p.emit("vector", lambda v, _i, xb=xb, ab=ab: v.tensor_add(
                xb, xb, ab), waits=tmax(l1, l2, l3), inc=sV)
            a_2 = p.emit("vector", lambda v, _i, xb=xb, zb=zb: v.tensor_add(
                xb, xb, zb), waits=tmax(a_1), inc=sV)
            prev7[b] = p.emit("sync", lambda s, _i, t=t, xb=xb: s.dma_start(
                y[t * 128:(t + 1) * 128, :], xb), waits=tmax(a_2),
                inc=sY7[b], inc_by=16)
        p.emit("sync", lambda s, _i: s.nop(), waits=tmax(prev7[0], prev7[1]))
        p.emit("gpsimd", lambda g, _i: g.nop(), waits=tmax(prev7[0], prev7[1]))
        p.finalize()
    return nc


def prep_core_inputs(inputs, core):
    g = inputs
    cm = {
        "x": np.ascontiguousarray(g["L0_attn_x"][core]),
        "wqT": np.ascontiguousarray(g["L0_attn_Wq"].T),
        "wkT": np.ascontiguousarray(g["L0_attn_Wk"].T),
        "wvT": np.ascontiguousarray(g["L0_attn_Wv"].T),
        "woT": np.ascontiguousarray(g["L0_attn_Wo"].T),
        "bq": g["L0_attn_bq"], "bk": g["L0_attn_bk"],
        "bv": g["L0_attn_bv"], "bo": g["L0_attn_bo"],
        "wrT": np.ascontiguousarray(g["L0_ffn_Wr"].T),
        "br": g["L0_ffn_br"],
        "w1T": np.ascontiguousarray(np.swapaxes(g["L0_ffn_W1"], 1, 2)),
        "b1": g["L0_ffn_b1"],
        "w2T": np.ascontiguousarray(np.swapaxes(g["L0_ffn_W2"], 1, 2)),
        "b2": g["L0_ffn_b2"],
        "awqT": np.ascontiguousarray(g["L1_attn_Wq"].T),
        "awkT": np.ascontiguousarray(g["L1_attn_Wk"].T),
        "awvT": np.ascontiguousarray(g["L1_attn_Wv"].T),
        "awoT": np.ascontiguousarray(g["L1_attn_Wo"].T),
        "abq": g["L1_attn_bq"], "abk": g["L1_attn_bk"],
        "abv": g["L1_attn_bv"], "abo": g["L1_attn_bo"],
        "iwqkv": g["L1_ffn_Wqkv"], "iqs": g["L1_ffn_q_scale"], "iks": g["L1_ffn_k_scale"],
        "iwo": g["L1_ffn_Wo"], "ilng": g["L1_ffn_ln_g"], "ilnb": g["L1_ffn_ln_b"],
        "iw1": g["L1_ffn_W1"], "ib1": g["L1_ffn_b1"],
        "iw2": g["L1_ffn_W2"], "ib2": g["L1_ffn_b2"],
    }
    cm.update(host_constants())
    return {k: np.ascontiguousarray(v) for k, v in cm.items()}


def kernel(**inputs):
    cores = list(range(8))
    g = inputs
    nc1, _ = build_program(phases=("p0", "p1", "p2", "p4", "p5"))
    in_maps = [prep_core_inputs(inputs, core) for core in cores]
    r1 = run_bass_kernel_spmd(nc1, in_maps, core_ids=cores).results
    x1s = [r["x1s"] for r in r1]
    a1s = [r["attn1_c"] for r in r1]
    z = [r["zs"] for r in r1]
    hc = host_constants()
    nca = build_p6a()
    ncf = build_p6f()
    for i in range(IDEPTH):
        ma = [{"zin": z[c],
               "wqkv": np.ascontiguousarray(g["L1_ffn_Wqkv"][i]),
               "wo": np.ascontiguousarray(g["L1_ffn_Wo"][i]),
               "qs": np.ascontiguousarray(g["L1_ffn_q_scale"][i]),
               "ks": np.ascontiguousarray(g["L1_ffn_k_scale"][i]),
               "ropeI": hc["ropeI"], "maskI": hc["maskI"]} for c in cores]
        ra = run_bass_kernel_spmd(nca, ma, core_ids=cores).results
        z = [r["zout"] for r in ra]
        mf = [{"zin": z[c],
               "w1": np.ascontiguousarray(g["L1_ffn_W1"][i]),
               "b1": np.ascontiguousarray(g["L1_ffn_b1"][i]),
               "w2": np.ascontiguousarray(g["L1_ffn_W2"][i]),
               "b2": np.ascontiguousarray(g["L1_ffn_b2"][i]),
               "lng": np.ascontiguousarray(g["L1_ffn_ln_g"][i]),
               "lnb": np.ascontiguousarray(g["L1_ffn_ln_b"][i])} for c in cores]
        rf = run_bass_kernel_spmd(ncf, mf, core_ids=cores).results
        z = [r["zout"] for r in rf]
    nc7 = build_p7s()
    m7 = [{"x1": x1s[c], "a1": a1s[c], "z6": z[c]} for c in cores]
    r7 = run_bass_kernel_spmd(nc7, m7, core_ids=cores).results
    return np.stack([r["y"] for r in r7], axis=0)


# version 03
# speedup vs baseline: 1.0079x; 1.0079x over previous
import numpy as np
import ml_dtypes
from contextlib import ExitStack

import concourse.bass as bass
import concourse.mybir as mybir
from concourse.bass_utils import run_bass_kernel_spmd

F32 = mybir.dt.float32
BF16 = mybir.dt.bfloat16
U8 = mybir.dt.uint8
I32 = mybir.dt.int32
ALU = mybir.AluOpType
AF = mybir.ActivationFunctionType
AX = mybir.AxisListType.X
BF = ml_dtypes.bfloat16

T = 4096
D = 1024
NT = T // 128
DC = D // 128
NH = 16
DH = 64
NEXP = 8
DFF = 4096
IDEPTH = 6
I3 = 3072
DI2 = 5460
DI = 2730

NEG = -1.0e30


def host_constants():
    c = {}
    pos = np.arange(T, dtype=np.float64)

    kk = np.arange(128)[:, None]
    qq = np.arange(512)[None, :]
    m0 = np.stack([(qq >= kk + 128 * o) for o in range(4)]).astype(BF)
    c["masks0"] = m0

    half = 32
    inv = 1.0 / (10000.0 ** (np.arange(half, dtype=np.float64) / half))
    ang = pos[:, None] * inv[None, :]
    cosA = np.cos(ang); sinA = np.sin(ang)
    cosAc = np.concatenate([cosA, cosA], 1)
    sinAs = np.concatenate([-sinA, sinA], 1)
    sc_q = float(DH) ** -0.5
    c["ropeA"] = np.stack([cosAc * sc_q, sinAs * sc_q, cosAc, sinAs]).astype(np.float32)

    qi = np.arange(128)[:, None]
    cc = np.arange(256)[None, :]
    mA = np.where((cc >= qi) & (cc <= qi + 128), 0.0, NEG)
    c["maskA"] = mA.astype(np.float32)

    hidx = np.arange(0, DH, 2, dtype=np.float64)
    invI = 1.0 / (10000.0 ** (hidx / DH))
    fr = pos[:, None] * invI[None, :]
    frc = np.concatenate([fr, fr], 1)
    cosI = np.cos(frc); sinI = np.sin(frc)
    base = (hidx + 0.4 * DH) / (1.4 * DH)
    power = (pos - T // 2) / (256 // 2)
    scale = base[None, :] ** power[:, None]
    scl = np.concatenate([scale, scale], 1)
    sgn = np.concatenate([-np.ones(32), np.ones(32)])[None, :]
    QKS = 8.0
    c["ropeI"] = np.stack([
        cosI * scl * QKS,
        sinI * scl * sgn * QKS,
        cosI / scl,
        sinI / scl * sgn,
    ]).astype(np.float32)

    cc = np.arange(384)[None, :]
    mI = np.where((cc >= qi) & (cc <= qi + 256), 0.0, NEG)
    c["maskI"] = mI.astype(np.float32)
    return c


DEBUG_PRES = []

class SemC:
    __slots__ = ("h", "n", "name", "eng")
    def __init__(self, h, name):
        self.h = h; self.n = 0; self.name = name; self.eng = None

class Tok:
    __slots__ = ("sem", "kind", "val", "frame", "off", "lb", "mul", "add")
    def __init__(self, sem, kind, val=0, frame=None, off=0, lb=0, mul=0, add=0):
        self.sem = sem; self.kind = kind; self.val = val
        self.frame = frame; self.off = off; self.lb = lb
        self.mul = mul; self.add = add

def tok_at(t, g):
    f = t.frame
    return Tok(t.sem, "abs", f.base[t.sem] + f.pre.get(t.sem, 0)
               + g * f.delta[t.sem] + t.off)

class Frame:
    def __init__(self, n, engines, managed=False):
        self.n = n
        self.engines = engines
        self.managed = managed
        self.base = {}
        self.delta = {}
        self.pre = {}
        self.ivar = None

class Prog:
    ENGINES = ("sync", "gpsimd", "scalar", "vector", "tensor")

    def __init__(self, nc, ctx):
        self.nc = nc
        self.ctx = ctx
        self.segments = [("straight", [])]
        self.sems = []
        self.cur_frame = None

    def sem(self, name):
        s = SemC(self.ctx.enter_context(self.nc.semaphore(name)), name)
        self.sems.append(s)
        return s

    def emit(self, eng, fn, waits=(), inc=None, inc_by=1, fence=False):
        ws = list(waits)
        self.segments[-1][-1].append((eng, tuple(ws), fn, inc, inc_by, fence))
        if inc is None:
            return None
        if inc.eng is None:
            inc.eng = eng
        elif inc.eng != eng:
            inc.eng = "MIXED"
        f = self.cur_frame
        if f is None:
            inc.n += inc_by
            return Tok(inc, "abs", inc.n)
        else:
            f.delta[inc] = f.delta.get(inc, 0) + inc_by
            return Tok(inc, "rel", frame=f, off=f.delta[inc])

    def loop(self, n, engines, managed=False):
        prog = self
        class _LoopCM:
            def __enter__(cm):
                f = Frame(n, engines, managed)
                f.base = {s: s.n for s in prog.sems}
                prog.cur_frame = f
                prog.segments.append(("loop", f, []))
                cm.f = f
                return f
            def __exit__(cm, *a):
                f = cm.f
                for (_e, ws, *_r) in prog.segments[-1][-1]:
                    for t in ws:
                        if t.kind == "rel" and t.frame is f and t.lb > 0:
                            need = t.lb * f.delta.get(t.sem, 0) - t.off
                            if need > 0:
                                f.pre[t.sem] = max(f.pre.get(t.sem, 0), need)
                for s, d in f.delta.items():
                    s.n = f.base[s] + n * d + f.pre.get(s, 0)
                DEBUG_PRES.append({s.name: v for s, v in f.pre.items()})
                prog.cur_frame = None
                prog.segments.append(("straight", []))
                return False
        return _LoopCM()


    def _emit_loop_body(self, nc, EMAP, f, ops, i, ivreg):
        from concourse.expressions import RuntimeValue
        seen = set()
        for (e, ws, fn, inc, inc_by, fence) in ops:
            eng = EMAP[e]
            for t in ws:
                if t.kind == "abs":
                    key = (e, t.sem, "abs", t.val)
                    if t.val > 0 and key not in seen:
                        eng.wait_ge(t.sem.h, t.val)
                        seen.add(key)
                elif t.kind == "expr":
                    key = (e, t.sem, "expr", t.mul, t.add)
                    if key not in seen:
                        if ivreg is None:
                            eng.wait_ge(t.sem.h, i * t.mul + t.add)
                        else:
                            eng.wait_ge(t.sem.h, ivreg(eng, t.mul, t.add))
                        seen.add(key)
                elif t.kind == "exprfn":
                    key = (e, t.sem, "exprfn", t.add)
                    if key not in seen:
                        assert ivreg is None, "exprfn unsupported in managed loop"
                        eng.wait_ge(t.sem.h, t.mul(i))
                        seen.add(key)
                elif t.kind == "ivw":
                    key = (e, t.sem, "ivw", tuple(sorted(t.mul.items())))
                    if key not in seen:
                        assert ivreg is not None
                        eng.wait_ge(t.sem.h, ivreg(eng, **t.mul))
                        seen.add(key)
                else:
                    d = f.delta.get(t.sem, 0)
                    k = t.off - t.lb * d + f.pre.get(t.sem, 0)
                    key = (e, t.sem, "rel", d, k)
                    if key not in seen:
                        if ivreg is None:
                            eng.wait_ge(t.sem.h, i * d + (f.base[t.sem] + k))
                        else:
                            eng.wait_ge(t.sem.h, ivreg(eng, d, f.base[t.sem] + k))
                        seen.add(key)
            if ivreg is None:
                inst = fn(eng, i)
            else:
                inst = fn(eng, lambda mul, add, mod=None, vmin=None, shr=None, pmul=None, eng=eng: RuntimeValue(ivreg(eng, mul, add, mod, vmin, shr, pmul)))
            if fence:
                inst = eng.drain()
            if inc is not None:
                assert inst is not None
                inst.then_inc(inc.h, inc_by)

    def barrier_token(self):
        assert self.cur_frame is None
        return [Tok(s, "abs", s.n) for s in self.sems if s.n > 0]

    def finalize(self):
        nc = self.nc
        EMAP = {"sync": nc.sync, "gpsimd": nc.gpsimd, "scalar": nc.scalar,
                "vector": nc.vector, "tensor": nc.tensor}
        ETYPE = {"sync": mybir.EngineType.SP, "gpsimd": mybir.EngineType.Pool,
                 "scalar": mybir.EngineType.Activation, "vector": mybir.EngineType.DVE,
                 "tensor": mybir.EngineType.PE}
        for seg in self.segments:
            if seg[0] == "straight":
                ops = seg[1]
                satisfied = {}
                for (e, ws, fn, inc, inc_by, fence) in ops:
                    eng = EMAP[e]
                    for t in ws:
                        assert t.kind == "abs"
                        key = (e, t.sem)
                        if t.val > 0 and satisfied.get(key, -1) < t.val:
                            eng.wait_ge(t.sem.h, t.val)
                            satisfied[key] = t.val
                    inst = fn(eng, None)
                    if fence:
                        inst = eng.drain()
                    if inc is not None:
                        assert inst is not None
                        inst.then_inc(inc.h, inc_by)
            else:
                _, f, ops = seg
                for s, pch in f.pre.items():
                    if pch > 0:
                        assert s.eng in EMAP, f"pre-charge on mixed-engine sem {s.name}"
                        EMAP[s.eng].drain()
                        EMAP[s.eng].nop(nofuse=True).then_inc(s.h, pch)
                engs = [ETYPE[e] for e in f.engines]
                if not f.managed:
                    with nc.Fori(0, f.n, engines=engs) as i:
                        self._emit_loop_body(nc, EMAP, f, ops, i, None)
                else:
                    def pyval(i, mul, add, mod=None, vmin=None, shr=None, pmul=None):
                        if mul == 0:
                            return add
                        r = i * mul
                        if mod is not None:
                            r %= mod
                        if shr is not None:
                            r >>= shr
                        if vmin is not None:
                            r = min(r, vmin)
                        if pmul is not None:
                            r *= pmul
                        return r + add
                    seen_vals = {}
                    for i in range(f.n):
                        for (e, ws, fn, inc, inc_by, fence) in ops:
                            eng = EMAP[e]
                            for t in ws:
                                if t.kind == "abs":
                                    val = t.val
                                elif t.kind == "expr":
                                    val = i * t.mul + t.add
                                elif t.kind == "exprfn":
                                    val = t.mul(i)
                                elif t.kind == "ivw":
                                    val = pyval(i, **t.mul)
                                else:
                                    d = f.delta.get(t.sem, 0)
                                    k = t.off - t.lb * d + f.pre.get(t.sem, 0)
                                    val = i * d + f.base[t.sem] + k
                                key = (e, t.sem)
                                if val > 0 and seen_vals.get(key, -1) < val:
                                    eng.wait_ge(t.sem.h, val)
                                    seen_vals[key] = val
                            inst = fn(eng, lambda mul, add, mod=None, vmin=None, shr=None, pmul=None, i=i: pyval(i, mul, add, mod, vmin, shr, pmul))
                            if fence:
                                inst = eng.drain()
                            if inc is not None:
                                assert inst is not None
                                inst.then_inc(inc.h, inc_by)
        for s in self.sems:
            assert s.n < 2 ** 30, f"sem {s.name} count {s.n} too high"

def tmax(*tokens):
    absd = {}
    rel = []
    for t in tokens:
        if t is None:
            continue
        ts = t if isinstance(t, list) else [t]
        for x in ts:
            if x is None:
                continue
            if x.kind == "abs":
                if x.sem in absd:
                    absd[x.sem] = max(absd[x.sem], x.val)
                else:
                    absd[x.sem] = x.val
            elif x.kind == "expr":
                rel.append(x)
            else:
                rel.append(x)
    out = [Tok(s, "abs", v) for s, v in absd.items()]
    rd = {}
    for x in rel:
        if x.kind == "expr":
            key = (x.sem, "expr", x.mul, x.add)
        elif x.kind == "exprfn":
            key = (x.sem, "exprfn", x.add)
        else:
            key = (x.sem, x.frame, x.lb)
        if key not in rd or (x.kind == "rel" and rd[key].off < x.off):
            rd[key] = x
    return out + list(rd.values())


def build_program(debug=(), p0_tiles=NT, phases=("p0", "p1"), n_exp=NEXP):
    nc = bass.Bass("TRN2", target_bir_lowering=False)
    dbg = set(debug)

    def di(name, shape, dt=F32):
        return nc.dram_tensor(name, list(shape), dt, kind="ExternalInput").ap()
    x0 = di("x", (T, D))
    wqT = di("wqT", (D, D)); wkT = di("wkT", (D, D)); wvT = di("wvT", (D, D)); woT = di("woT", (D, D))
    bq = di("bq", (D,)); bk = di("bk", (D,)); bv = di("bv", (D,)); bo = di("bo", (D,))
    wrT = di("wrT", (D, NEXP)); br = di("br", (NEXP,))
    w1T = di("w1T", (NEXP, D, DFF)); b1 = di("b1", (NEXP, DFF))
    w2T = di("w2T", (NEXP, DFF, D)); b2 = di("b2", (NEXP, D))
    awqT = di("awqT", (D, D)); awkT = di("awkT", (D, D)); awvT = di("awvT", (D, D)); awoT = di("awoT", (D, D))
    abq = di("abq", (D,)); abk = di("abk", (D,)); abv = di("abv", (D,)); abo = di("abo", (D,))
    iwqkv = di("iwqkv", (IDEPTH, D, I3)); iqs = di("iqs", (IDEPTH, DH)); iks = di("iks", (IDEPTH, DH))
    iwo = di("iwo", (IDEPTH, D, D)); ilng = di("ilng", (IDEPTH, D)); ilnb = di("ilnb", (IDEPTH, D))
    iw1 = di("iw1", (IDEPTH, D, DI2)); ib1 = di("ib1", (IDEPTH, DI2))
    iw2 = di("iw2", (IDEPTH, DI, D)); ib2 = di("ib2", (IDEPTH, D))
    masks0 = di("masks0", (4, 128, 512), BF16)
    ropeA = di("ropeA", (4, T, DH)); maskA = di("maskA", (128, 256))
    ropeI = di("ropeI", (4, T, DH)); maskI = di("maskI", (128, 384))

    def ds_(name, shape, dt=F32):
        return nc.dram_tensor(name, list(shape), dt, kind="Internal").ap()
    attn0_c = ds_("attn0_c", (T, D))
    moe_c = ds_("moe_c", (T, D))
    x1s = nc.dram_tensor("x1s", [T, D], F32, kind="ExternalOutput").ap()
    zs = nc.dram_tensor("zs", [T, D], F32, kind="ExternalOutput").ap()
    attn1_c = nc.dram_tensor("attn1_c", [T, D], F32, kind="ExternalOutput").ap()
    y = nc.dram_tensor("y", [T, D], F32, kind="ExternalOutput").ap()

    dbg_out = {}
    def dout(name, shape, dt=F32):
        ap = nc.dram_tensor(name, list(shape), dt, kind="ExternalOutput").ap()
        dbg_out[name] = ap
        return ap

    with ExitStack() as ctx:
        p = Prog(nc, ctx)

        AR0 = ctx.enter_context(nc.sbuf_tensor("AR0", [128, 73728], U8))
        AR1 = ctx.enter_context(nc.sbuf_tensor("AR1", [128, 65536], U8))
        AR2 = ctx.enter_context(nc.sbuf_tensor("AR2", [128, 66560], U8))

        def v0(off, n, dt=F32, arena=None):
            sz = {F32: 4, BF16: 2, I32: 4, U8: 1}[dt]
            a = arena if arena is not None else AR0
            return a[:, off:off + n * sz].bitcast(dt)
        def v1(off, n, dt=F32):
            return v0(off, n, dt, AR1)
        def v2(off, n, dt=F32):
            return v0(off, n, dt, AR2)

        ident = ctx.enter_context(nc.sbuf_tensor("ident", [128, 128], BF16))
        io32 = ctx.enter_context(nc.sbuf_tensor("io32", [128, 128], I32))
        MASKS = ctx.enter_context(nc.sbuf_tensor("MASKS", [128, NT * NEXP], F32))
        SMALL = ctx.enter_context(nc.sbuf_tensor("SMALL", [128, 2048], U8))
        def vs(off, n, dt=F32):
            sz = {F32: 4, BF16: 2, I32: 4}[dt]
            return SMALL[:, off:off + n * sz].bitcast(dt)

        PSB = [ctx.enter_context(nc.psum_tensor(f"psb{i}", [128, 512], F32))
               for i in range(8)]
        def pbf(i, off, n):
            return PSB[i][:, off:off + n // 2].bitcast(BF16)

        sQ = p.sem("sQ")
        sXT = [p.sem("sXT0"), p.sem("sXT1")]
        sG = p.sem("sG")
        sV = p.sem("sV")
        sS = p.sem("sS")
        sT = p.sem("sT")


        t_clr = []
        for arena in (AR0, AR1, AR2):
            nb = arena.shape[-1]
            for off in range(0, nb, 32768):
                hi = min(off + 32768, nb)
                t_clr.append(p.emit("vector", lambda v, _i, arena=arena, off=off, hi=hi: v.memset(
                    arena[:, off:hi], 0), inc=sV))
        t_clr.append(p.emit("vector", lambda v, _i: v.memset(SMALL[:], 0), inc=sV))
        for b_ in range(8):
            t_clr.append(p.emit("vector", lambda v, _i, b_=b_: v.memset(PSB[b_][:], 0.0), inc=sV))
        clr_all = tmax(*t_clr)
        sGC = p.sem("sGC")
        t_io = p.emit("gpsimd", lambda g, _i: g.iota(io32[:], [[1, 128]], channel_multiplier=-1,
                                                 allow_small_or_imprecise_dtypes=True),
                      inc=sGC)
        t_id = p.emit("vector", lambda v, _i: v.tensor_scalar(out=ident[:], in0=io32[:], scalar1=0,
                                                          scalar2=None, op0=ALU.is_equal),
                      waits=tmax(t_io), inc=sV)

        EPS5 = vs(1024, 1)[:, 0:1]
        t_eps = p.emit("vector", lambda v, _i: v.memset(EPS5, 1.0e-5), waits=clr_all, inc=sV)

        WRB = vs(0, 64, BF16)
        t_wrb = p.emit("gpsimd", lambda g, _i: g.dma_start(
            WRB.rearrange("p (c e) -> p c e", c=DC),
            wrT.rearrange("(c p) e -> p c e", p=128)), waits=clr_all, inc=sG, inc_by=16)
        BRB = vs(128, 8, F32)
        t_brb = p.emit("sync", lambda s, _i: s.dma_start(
            BRB, br[None, :].broadcast_to([128, NEXP])), waits=clr_all, inc=sQ, inc_by=16)

        XNT = [v0(c * 9216 + 1024, T, BF16) for c in range(DC)]
        XT = [v1(0, 1024), v1(4096, 1024)]
        XNB = [v1(8192, 1024, BF16), v1(10240, 1024, BF16)]
        SQD = v1(12288, 1024)
        def stat(i, slot):
            return vs(256 + 64 * (i % 2) + 16 * slot, 4)[:, 0:1]

        tok_xnt = [[None] * NT for _ in range(DC)]
        tok_mask = [None] * NT
        tok_lg = [None] * NT
        prev_xt = [clr_all, clr_all]
        prev_xnb = [None, None]

        ZERO8 = vs(1088, 8)
        p.emit("vector", lambda v, _i: v.memset(ZERO8, 0.0), waits=clr_all, inc=sV)
        for t in range(p0_tiles):
            b = t % 2
            t_ld = p.emit("sync", lambda s, _i, t=t, b=b: s.dma_start(XT[b], x0[t * 128:(t + 1) * 128, :]),
                          waits=tmax(prev_xt[b]), inc=sXT[b], inc_by=16)
            meanraw = stat(b, 0); sqs = stat(b, 1); var0 = stat(b, 2); rstd = stat(b, 3)
            mean = vs(768 + 64 * b, 1)[:, 0:1]
            msq = vs(768 + 64 * b + 16, 1)[:, 0:1]
            t_m0 = p.emit("vector", lambda v, _i, b=b, meanraw=meanraw: v.reduce_sum(meanraw, XT[b], axis=AX),
                          waits=tmax(t_ld), inc=sV, fence=True)
            t_s0 = p.emit("scalar", lambda sc, _i, b=b, sqs=sqs: sc.activation(
                out=SQD, in_=XT[b], func=AF.Square, accum_out=sqs),
                waits=tmax(t_ld), inc=sS, fence=True)
            p.emit("vector", lambda v, _i, mean=mean, meanraw=meanraw: v.tensor_scalar(
                out=mean, in0=meanraw, scalar1=1.0 / D, scalar2=None, op0=ALU.mult), inc=sV)
            t_msq = p.emit("vector", lambda v, _i, msq=msq, meanraw=meanraw: v.tensor_scalar(
                out=msq, in0=meanraw, scalar1=meanraw, scalar2=None, op0=ALU.mult),
                inc=sV, fence=True)
            t_v0 = p.emit("vector", lambda v, _i, var0=var0, sqs=sqs, msq=msq: v.scalar_tensor_tensor(
                out=var0, in0=sqs, scalar=float(D), in1=msq, op0=ALU.mult, op1=ALU.subtract),
                waits=tmax(t_s0), inc=sV, fence=True)
            t_sq = p.emit("scalar", lambda sc, _i, var0=var0, rstd=rstd: sc.activation(
                out=rstd, in_=var0, func=AF.Sqrt, bias=EPS5, scale=1.0 / (D * D)),
                waits=tmax(t_v0, t_eps), inc=sS, fence=True)
            t_rc = p.emit("vector", lambda v, _i, rstd=rstd: v.reciprocal(rstd, rstd),
                          waits=tmax(t_sq), inc=sV, fence=True)
            t_xn = p.emit("vector", lambda v, _i, b=b, mean=mean, rstd=rstd: v.tensor_scalar(
                out=XNB[b], in0=XT[b], scalar1=mean, scalar2=rstd,
                op0=ALU.subtract, op1=ALU.mult), waits=tmax(t_rc, prev_xnb[b]), inc=sV)
            prev_xt[b] = tmax(t_xn, t_s0)
            bank = b
            tr_last = None
            for c in range(DC):
                tr_last = p.emit("tensor", lambda te, _i, b=b, c=c, bank=bank: te.transpose(
                    PSB[bank][:, c * 64:c * 64 + 64].bitcast(BF16),
                    XNB[b][:, c * 128:(c + 1) * 128], ident[:]),
                    waits=tmax(t_xn, t_id,
                               tok_xnt[DC - 1][t - 2] if t >= 2 else None), inc=sT)
            for c in range(DC):
                tok_xnt[c][t] = p.emit("vector", lambda v, _i, c=c, t=t, bank=bank: v.tensor_copy(
                    XNT[c][:, t * 128:(t + 1) * 128],
                    PSB[bank][:, c * 64:c * 64 + 64].bitcast(BF16)),
                    waits=tmax(tr_last), inc=sV)
            prev_xnb[b] = tok_xnt[DC - 1][t]
            lgbank = 2 + b
            t_mm = None
            for c in range(DC):
                t_mm = p.emit("tensor", lambda te, _i, c=c, t=t, lgbank=lgbank: te.matmul(
                    PSB[lgbank][:, 0:NEXP], XNT[c][:, t * 128:(t + 1) * 128],
                    WRB[:, c * 8:(c + 1) * 8],
                    start=(c == 0), stop=(c == DC - 1)),
                    waits=tmax(tok_xnt[c][t], t_wrb if t == 0 else None,
                               tok_lg[t - 2] if t >= 2 else None), inc=sT)
            LG = vs(512 + 64 * b, 8)
            t_lge = p.emit("vector", lambda v, _i, LG=LG, lgbank=lgbank: v.tensor_add(
                LG, PSB[lgbank][:, 0:NEXP], BRB), waits=tmax(t_mm, t_brb), inc=sV,
                fence=True)
            tok_lg[t] = t_lge
            mx = vs(768 + 32, 1)[:, 0:1]
            raw8 = vs(1120, 8)
            P8 = vs(1152, 8)
            p.emit("vector", lambda v, _i, LG=LG, mx=mx: v.reduce_max(mx, LG, axis=AX),
                   inc=sV, fence=True)
            p.emit("vector", lambda v, _i, LG=LG, mx=mx, raw8=raw8: v.tensor_scalar(
                out=raw8, in0=LG, scalar1=mx, scalar2=None, op0=ALU.is_equal),
                inc=sV, fence=True)
            p.emit("vector", lambda v, _i, raw8=raw8, P8=P8: v.tensor_tensor_scan(
                P8, raw8, ZERO8, 0.0, ALU.max, ALU.add), inc=sV, fence=True)
            p.emit("vector", lambda v, _i, t=t, P8=P8: v.tensor_copy(
                MASKS[:, t:t + 1], P8[:, 0:1]), inc=sV)
            tok_mask[t] = p.emit("vector", lambda v, _i, t=t, P8=P8: v.tensor_sub(
                MASKS.rearrange("p (e t2) -> p e t2", e=8)[:, 1:8, t:t + 1],
                P8[:, 1:8, None], P8[:, 0:7, None]),
                inc=sV, fence=True)

        tok_moe_done = None
        if "p1" in phases:
            sW1 = p.sem("sW1")
            sW2 = p.sem("sW2")
            sMCs = [p.sem(f"sMC{i_}") for i_ in range(4)]
            sP1 = p.sem("sP1")
            W1TB = [v2(dc * 8192, DFF, BF16) for dc in range(DC)]
            HB = [v1(fc * 1024, 512, BF16) for fc in range(32)]
            W2B4 = [v1(32768 + i_ * 2048, 1024, BF16) for i_ in range(4)]
            CTB = [v1(40960 + i_ * 4096, 1024) for i_ in range(4)]
            B2BC = v1(57344, 1024)
            B1C = vs(1216, 32)
            SCR = v1(61440, 1024)
            MC4 = vs(1408, 4)
            XG = [v0(c * 9216, 512, BF16) for c in range(DC)]
            KQv = v0(0, 8 * 4608, BF16)
            xnt_all = tmax(*[tok_xnt[c][t] for c in range(DC) for t in range(NT)])
            mask_all = tmax(*[tok_mask[t] for t in range(NT)])

            relu_toks = [None] * 32
            w2fc_tok = [None] * 32
            ct_toks = [None] * 4
            last_w1mm = None
            cp_toks = [None, None]
            with p.loop(64, ["tensor", "scalar", "vector"], managed=True) as fr:
                p.emit("vector", lambda v, iv: v.tensor_copy(
                    MC4[:, 0:4], MASKS[:, bass.ds(iv(4, 0), 4)]),
                    waits=tmax(mask_all, Tok(sV, "rel", frame=fr, off=12, lb=1)),
                    fence=True)
                for ch in range(4):
                    for j in range(8):
                        fc = ch * 8 + j
                        for dc in range(DC):
                            w = []
                            if j == 0 and dc == 0:
                                if ch == 0:
                                    w += [Tok(sS, "rel", frame=fr, off=32, lb=1),
                                          Tok(sV, "rel", frame=fr, off=12, lb=1),
                                          Tok(sW1, "ivw",
                                              mul=dict(mul=1, shr=3, pmul=160, add=160))]
                                    w.append(xnt_all)
                                else:
                                    w.append(Tok(sS, "rel", frame=fr, off=ch * 8))
                            last_w1mm = p.emit("tensor", lambda te, iv, fc=fc, dc=dc: te.matmul(
                                PSB[fc % 8][:, 0:512],
                                W1TB[dc][:, fc * 128:(fc + 1) * 128],
                                XNT[dc][:, bass.ds(iv(512, 0, mod=4096), 512)],
                                start=(dc == 0), stop=(dc == DC - 1)),
                                waits=tmax(*w), inc=sT)
                    for j in range(8):
                        fc = ch * 8 + j
                        w = [Tok(sT, "rel", frame=fr, off=(ch + 1) * 64)] if j == 0 else []
                        relu_toks[fc] = p.emit("scalar", lambda sc, iv, fc=fc: sc.activation(
                            out=HB[fc], in_=PSB[fc % 8][:, 0:512], func=AF.Relu,
                            bias=B1C[:, fc:fc + 1]),
                            waits=tmax(*w), inc=sS)
                for fc in range(32):
                    for tt in range(4):
                        for hf in range(2):
                            w = []
                            if tt == 0 and hf == 0:
                                if fc % 4 == 0:
                                    w.append(Tok(sW2, "expr", mul=32 * 16,
                                                 add=(fc + 4) * 16))
                                if fc == 0:
                                    w.append(Tok(sS, "rel", frame=fr, off=32))
                            tok = p.emit("tensor", lambda te, iv, fc=fc, tt=tt, hf=hf: te.matmul(
                                PSB[2 * tt + hf][:, 0:512],
                                HB[fc][:, tt * 128:(tt + 1) * 128],
                                W2B4[fc % 4][:, hf * 512:(hf + 1) * 512],
                                start=(fc == 0), stop=(fc == 31)),
                                waits=tmax(*w), inc=sT)
                    w2fc_tok[fc] = tok
                for tt in range(4):
                    tok_b2m = p.emit("vector", lambda v, iv, tt=tt: v.tensor_scalar(
                        out=SCR, in0=B2BC, scalar1=MC4[:, tt:tt + 1],
                        scalar2=None, op0=ALU.mult),
                        waits=tmax(Tok(sT, "rel", frame=fr, off=506 + 2 * tt),
                                   Tok(sMCs[tt], "expr", mul=16, add=0),
                                   (ct_toks[tt - 1] if tt > 0 else
                                    Tok(sV, "rel", frame=fr, off=12, lb=1))),
                        inc=sV)
                    last = None
                    for hf in range(2):
                        last = p.emit("vector", lambda v, iv, tt=tt, hf=hf: v.scalar_tensor_tensor(
                            out=CTB[tt][:, hf * 512:(hf + 1) * 512],
                            in0=PSB[2 * tt + hf][:, 0:512],
                            scalar=MC4[:, tt:tt + 1],
                            in1=SCR[:, hf * 512:(hf + 1) * 512],
                            op0=ALU.mult, op1=ALU.add),
                            waits=tmax(tok_b2m), inc=sV)
                    ct_toks[tt] = last

            for e in range(NEXP):
                wprev = clr_all
                if e >= 1:
                    wprev = tmax(tok_at(last_w1mm, e * 8 - 1),
                                 tok_at(relu_toks[31], e * 8 - 1),
                                 tok_at(ct_toks[3], e * 8 - 1))
                for dc in range(DC):
                    p.emit("gpsimd", lambda g, _i, e=e, dc=dc: g.dma_start(
                        W1TB[dc], w1T[e, dc * 128:(dc + 1) * 128, :]),
                        waits=wprev, inc=sW1, inc_by=16)
                p.emit("gpsimd", lambda g, _i, e=e: g.dma_start(
                    B1C.rearrange("p (a o) -> p a o", o=1),
                    b1[e].rearrange("(a q) -> q a", q=128)[:, :, None],
                    allow_slow_non_contiguous=True),
                    waits=wprev, inc=sW1, inc_by=16)
                p.emit("gpsimd", lambda g, _i, e=e: g.dma_start(
                    B2BC, b2[e][None, :].broadcast_to([128, D])),
                    waits=wprev, inc=sW1, inc_by=16)
                for g in range(8):
                    jg = e * 8 + g
                    for fc in range(32):
                        w = []
                        gi = jg * 32 + fc
                        if gi >= 4:
                            pj, pf = divmod(gi - 4, 32)
                            w.append(tok_at(w2fc_tok[pf], pj))
                        else:
                            w = [clr_all]
                        p.emit("gpsimd", lambda s, _i, e=e, fc=fc: s.dma_start(
                            W2B4[fc % 4], w2T[e, fc * 128:(fc + 1) * 128, :]),
                            waits=tmax(*w), inc=sW2, inc_by=16)
                    for tt in range(4):
                        t0 = (g * 4 + tt) * 128
                        p.emit("gpsimd", lambda gp, _i, e=e, t0=t0, tt=tt: gp.dma_start(
                            moe_c[t0:t0 + 128, :], CTB[tt],
                            accum_op=(ALU.add if e > 0 else ALU.bypass)),
                            waits=[tok_at(ct_toks[tt], jg)], inc=sMCs[tt], inc_by=16)
            tok_moe_done = tmax(*[Tok(s_, "abs", s_.n) for s_ in sMCs])

        tok_attn0_done = None
        if "p2" in phases:
            sWS = [p.sem("sWS0"), p.sem("sWS1")]
            sAC = [p.sem("sAC0"), p.sem("sAC1")]
            sP = p.sem("sP")
            KQ = v0(0, 8 * 4608, BF16)
            VA = v2(0, 32 * 1040, BF16)
            QN = v1(0, 1024, BF16)
            SQB = v1(2048, 1024, BF16)
            OUTF = v1(4096, 4096, BF16)
            PRBs = [v1(12288, 512, BF16), v1(13312, 512, BF16)]
            WSL = [v1(22592, 1024, BF16), v1(24640, 1024, BF16)]
            BIAS = v1(26688, 1024, BF16)
            BCS = v1(26688, 512, BF16)
            MBIG = v1(28736, 896, BF16)
            ONESB = v1(30528, 64, BF16)
            ZROB = v1(30656, 512, BF16)
            RECROW = AR1[64:65, 31680:31680 + 2048].bitcast(F32)
            RECBR = AR1[64:65, 33728:33728 + 1024].bitcast(BF16)
            RS16 = v1(34752, 16)
            CT0 = [v1(36864, 1024), v1(40960, 1024)]
            BOB = v1(45056, 1024)
            ESMALL = vs(1344, 1)[:, 0:1]
            SCL0 = 1.0 / (float(np.sqrt(128.0)) * 1.5)

            tok_pe_p1 = Tok(sT, "abs", sT.n)
            tok_sv_p1 = Tok(sV, "abs", sV.n)
            tok_ss_p1 = Tok(sS, "abs", sS.n)
            p1done = tmax(tok_pe_p1, tok_sv_p1, tok_ss_p1)

            t_vaones = p.emit("vector", lambda v, _i: v.memset(VA, 1.0),
                              waits=tmax(tok_moe_done) if tok_moe_done else [], inc=sV)
            t_zrob = p.emit("vector", lambda v, _i: v.memset(ZROB, 0.0), inc=sV)
            t_onesb = p.emit("vector", lambda v, _i: v.memset(ONESB, 1.0), inc=sV)
            t_eps8 = p.emit("vector", lambda v, _i: v.memset(ESMALL, 1.0e-8), inc=sV)
            MBI = AR1[:, 49152:49152 + 3584 * 4].bitcast(I32)
            t_mb1 = p.emit("gpsimd", lambda g, _i: g.iota(
                MBI[:, 0:896], [[1, 896]], base=-384, channel_multiplier=-1,
                allow_small_or_imprecise_dtypes=True),
                waits=tmax(tok_moe_done), inc=sGC)
            t_mb2 = p.emit("vector", lambda v, _i: v.tensor_scalar(
                out=MBIG, in0=MBI[:, 0:896], scalar1=0, scalar2=None,
                op0=ALU.is_ge), waits=tmax(t_mb1), inc=sV)
            t_bo0 = p.emit("sync", lambda s, _i: s.dma_start(
                BOB, bo[None, :].broadcast_to([128, D])),
                waits=tmax(tok_moe_done), inc=sQ, inc_by=16)
            t_bo1 = p.emit("scalar", lambda sc, _i: sc.mul(BOB, BOB, 0.5),
                           waits=tmax(t_bo0), inc=sS)
            setup_done = tmax(t_vaones, t_zrob, t_onesb, t_eps8, t_mb2, t_bo1, p1done)

            WTS = {"v": (wvT, bv), "k": (wkT, bk), "q": (wqT, bq)}
            tok_ws_free = [setup_done, setup_done]
            tok_bank_free = [setup_done] * 8
            tok_kq_write = [None] * NT
            tok_v_write = [None] * NT
            tok_ct_free = [setup_done, setup_done]
            prev_bias_read = setup_done
            ngroups = 1 if "g0" in dbg else 8
            tok_outf_free = setup_done
            prev_phaseB_banks = setup_done

            for g in range(ngroups):
                tiles = [4 * g + t for t in range(4)]
                for wname in ("v", "k", "q"):
                    wT, bvec = WTS[wname]
                    t_b = p.emit("gpsimd", lambda gp, _i, bvec=bvec: gp.dma_start(
                        BIAS, bvec[None, :].broadcast_to([128, D])),
                        waits=tmax(prev_phaseB_banks, prev_bias_read), inc=sG, inc_by=16)
                    mm_last = [None] * 4
                    bias_readers = []
                    wtoks = [None, None]
                    for dc in range(DC):
                        sl = dc % 2
                        wtoks[sl] = p.emit("gpsimd", lambda gp, _i, wT=wT, dc=dc, sl=sl: gp.dma_start(
                            WSL[sl], wT[dc * 128:(dc + 1) * 128, :]),
                            waits=tmax(tok_ws_free[sl]), inc=sWS[sl], inc_by=16)
                        for t in range(4):
                            for hf in range(2):
                                w = [wtoks[sl]]
                                if dc == 0:
                                    w.append(tok_bank_free[2 * t + hf])
                                mm_last[t] = p.emit("tensor", lambda te, _i, g=g, dc=dc, t=t, hf=hf, sl=sl: te.matmul(
                                    PSB[2 * t + hf][:, 0:512],
                                    XNT[dc][:, (4 * g + t) * 128:(4 * g + t + 1) * 128],
                                    WSL[sl][:, hf * 512:(hf + 1) * 512],
                                    start=(dc == 0), stop=(dc == DC - 1)),
                                    waits=tmax(*w), inc=sT)
                        tok_ws_free[sl] = mm_last[3]
                    for t in range(4):
                        tg = tiles[t]
                        if wname == "v":
                            last = None
                            for hf in range(2):
                                last = p.emit("vector", lambda v, _i, tg=tg, t=t, hf=hf: v.tensor_tensor(
                                    out=VA[:, tg * 1040:(tg + 1) * 1040].rearrange(
                                        "p (h c) -> p h c", c=65)[:, hf * 8:(hf + 1) * 8, 0:64],
                                    in0=PSB[2 * t + hf][:, 0:512].rearrange(
                                        "p (h c) -> p h c", c=64),
                                    in1=BIAS[:, hf * 512:(hf + 1) * 512].rearrange(
                                        "p (h c) -> p h c", c=64),
                                    op=ALU.add), waits=tmax(mm_last[t], t_b), inc=sV)
                            tok_v_write[tg] = last
                            tok_bank_free[2 * t] = last
                            tok_bank_free[2 * t + 1] = last
                            bias_readers.append(last)
                        else:
                            tqn = None
                            for hf in range(2):
                                tqn = p.emit("vector", lambda v, _i, t=t, hf=hf: v.tensor_tensor(
                                    out=QN[:, hf * 512:(hf + 1) * 512],
                                    in0=PSB[2 * t + hf][:, 0:512],
                                    in1=BIAS[:, hf * 512:(hf + 1) * 512],
                                    op=ALU.add), waits=tmax(mm_last[t], t_b), inc=sV)
                            bias_readers.append(tqn)
                            tsq = p.emit("scalar", lambda sc, _i: sc.activation(
                                out=SQB, in_=QN, func=AF.Square), waits=tmax(tqn), inc=sS)
                            trd = p.emit("vector", lambda v, _i: v.reduce_sum(
                                RS16, SQB.rearrange("p (h c) -> p h c", c=64), axis=AX),
                                waits=tmax(tsq), inc=sV, fence=True)
                            tsr = p.emit("scalar", lambda sc, _i: sc.activation(
                                out=RS16, in_=RS16, func=AF.Sqrt),
                                waits=tmax(trd), inc=sS, fence=True)
                            tmx_ = p.emit("vector", lambda v, _i: v.tensor_scalar(
                                out=RS16, in0=RS16, scalar1=1.0e-8, scalar2=None,
                                op0=ALU.max), waits=tmax(tsr), inc=sV, fence=True)
                            trc = p.emit("vector", lambda v, _i: v.reciprocal(RS16, RS16),
                                         waits=tmax(tmx_), inc=sV, fence=True)
                            tnm = p.emit("vector", lambda v, _i: v.tensor_tensor(
                                out=SQB.rearrange("p (h c) -> p h c", c=64),
                                in0=QN.rearrange("p (h c) -> p h c", c=64),
                                in1=RS16[:, :, None].broadcast_to([128, 16, 64]),
                                op=ALU.mult), waits=tmax(trc), inc=sV)
                            trt = None
                            for c in range(DC):
                                trt = p.emit("tensor", lambda te, _i, c=c, t=t: te.matmul(
                                    PSB[2 * t + (c % 2)][:, (c // 2) * 128:(c // 2) * 128 + 64].bitcast(BF16),
                                    SQB[:, c * 128:(c + 1) * 128], ident[:],
                                    is_transpose=True, start=True, stop=True),
                                    waits=tmax(tnm), inc=sT)
                            evl = None
                            for c in range(DC):
                                dst_off = c * 4608 + (tg * 128 if wname == "k" else 512 + tg * 128)
                                wv = [trt]
                                if wname == "q":
                                    wv.append(tok_pe_p1)
                                evl = p.emit("vector", lambda v, _i, c=c, t=t, dst_off=dst_off: v.tensor_copy(
                                    KQ[:, dst_off:dst_off + 128],
                                    PSB[2 * t + (c % 2)][:, (c // 2) * 128:(c // 2) * 128 + 64].bitcast(BF16)),
                                    waits=tmax(*wv), inc=sV)
                            tok_kq_write[tg] = evl
                            tok_bank_free[2 * t] = evl
                            tok_bank_free[2 * t + 1] = evl
                    prev_bias_read = tmax(*bias_readers)
                if "nophaseb" in dbg:
                    prev_phaseB_banks = tmax(*[tok_bank_free[i_] for i_ in range(8)])
                    continue
                kv_ready = tmax(*[tok_kq_write[j] for j in tiles],
                                *[tok_v_write[j] for j in tiles],
                                *[tok_kq_write[j] for j in range(4 * g)],
                                *[tok_v_write[j] for j in range(4 * g)])
                hsets = [list(range(0, 6)), list(range(6, 12)), list(range(12, 16))]
                evac_toks = []
                for hs in hsets:
                    pv_last = {}
                    sc_prev = {}
                    nkj = 4 * g + 4
                    for kj in range(nkj):
                        o = kj - 4 * g
                        for idx, h in enumerate(hs):
                            n = kj * len(hs) + idx
                            bk6 = 6 + (n % 2)
                            w = [kv_ready]
                            if n < 2:
                                w.append(prev_phaseB_banks)
                                w.append(setup_done)
                            if n >= 2:
                                w.append(sc_prev.get(n - 2))
                            tsc = p.emit("tensor", lambda te, _i, h=h, kj=kj, g=g, bk6=bk6: te.matmul(
                                PSB[bk6][:, 0:512],
                                KQ[(h % 2) * 64:(h % 2) * 64 + 64,
                                   (h // 2) * 4608 + kj * 128:(h // 2) * 4608 + kj * 128 + 128],
                                KQ[(h % 2) * 64:(h % 2) * 64 + 64,
                                   (h // 2) * 4608 + 512 + 4 * g * 128:
                                   (h // 2) * 4608 + 512 + 4 * g * 128 + 512],
                                start=True, stop=True),
                                waits=tmax(*w), inc=sT)
                            tex = p.emit("scalar", lambda sc, _i, n=n, bk6=bk6: sc.activation(
                                out=PRBs[n % 2], in_=PSB[bk6][:, 0:512],
                                func=AF.Exp, scale=SCL0),
                                waits=tmax(tsc, pv_last.get(n - 2)), inc=sS)
                            sc_prev[n] = tex
                            dep = tex
                            if o >= 0:
                                dep = p.emit("vector", lambda v, _i, n=n, o=o: v.tensor_tensor(
                                    out=PRBs[n % 2], in0=PRBs[n % 2],
                                    in1=MBIG[:, 384 - 128 * o:384 - 128 * o + 512],
                                    op=ALU.mult), waits=tmax(tex), inc=sV)
                            pv_last[n] = p.emit("tensor", lambda te, _i, idx=idx, h=h, kj=kj, n=n, nkj=nkj: te.matmul(
                                PSB[idx][0:65, 0:512],
                                VA[:, kj * 1040 + h * 65:kj * 1040 + h * 65 + 65],
                                PRBs[n % 2],
                                start=(kj == 0), stop=(kj == nkj - 1)),
                                waits=tmax(dep), inc=sT)
                    for idx, h in enumerate(hs):
                        trc0 = p.emit("vector", lambda v, _i, idx=idx: v.reciprocal(
                            RECROW[:, 0:512], PSB[idx][64:65, 0:512]),
                            waits=tmax(pv_last[(nkj - 1) * len(hs) + idx],
                                       tok_outf_free if idx == 0 else None),
                            inc=sV)
                        trcb = p.emit("vector", lambda v, _i: v.tensor_copy(
                            RECBR[:, 0:512], RECROW[:, 0:512]), waits=tmax(trc0), inc=sV)
                        tbc = p.emit("tensor", lambda te, _i: te.matmul(
                            PSB[6][0:64, 0:512], ONESB[64:65, 0:64], RECBR[:, 0:512],
                            start=True, stop=True), waits=tmax(trcb), inc=sT)
                        tbe = p.emit("vector", lambda v, _i: v.tensor_copy(
                            BCS[0:64, 0:512], PSB[6][0:64, 0:512]), waits=tmax(tbc), inc=sV)
                        evac_toks.append(p.emit("vector", lambda v, _i, idx=idx, h=h: v.tensor_tensor(
                            out=OUTF[(h % 2) * 64:(h % 2) * 64 + 64,
                                     (h // 2) * 512:(h // 2) * 512 + 512],
                            in0=PSB[idx][0:64, 0:512], in1=BCS[0:64, 0:512],
                            op=ALU.mult), waits=tmax(tbe), inc=sV))
                    prev_phaseB_banks = tmax(*evac_toks)
                wo_last = [None] * 4
                wtoks = [None, None]
                for dc in range(DC):
                    sl = dc % 2
                    wtoks[sl] = p.emit("gpsimd", lambda gp, _i, dc=dc, sl=sl: gp.dma_start(
                        WSL[sl], woT[dc * 128:(dc + 1) * 128, :]),
                        waits=tmax(tok_ws_free[sl]), inc=sWS[sl], inc_by=16)
                    for t in range(4):
                        for hf in range(2):
                            w = [wtoks[sl], tmax(*evac_toks)]
                            if dc == 0:
                                w.append(prev_phaseB_banks)
                            wo_last[t] = p.emit("tensor", lambda te, _i, dc=dc, t=t, hf=hf, sl=sl: te.matmul(
                                PSB[2 * t + hf][:, 0:512],
                                OUTF[:, dc * 512 + t * 128:dc * 512 + t * 128 + 128],
                                WSL[sl][:, hf * 512:(hf + 1) * 512],
                                start=(dc == 0), stop=(dc == DC - 1)),
                                waits=tmax(*w), inc=sT)
                    tok_ws_free[sl] = wo_last[3]
                ct_sts = []
                for t in range(4):
                    ctsl = t % 2
                    last = None
                    for hf in range(2):
                        last = p.emit("vector", lambda v, _i, t=t, hf=hf, ctsl=ctsl: v.scalar_tensor_tensor(
                            out=CT0[ctsl][:, hf * 512:(hf + 1) * 512],
                            in0=PSB[2 * t + hf][:, 0:512], scalar=0.5,
                            in1=BOB[:, hf * 512:(hf + 1) * 512],
                            op0=ALU.mult, op1=ALU.add),
                            waits=tmax(wo_last[t], t_bo1, tok_ct_free[ctsl]), inc=sV)
                    tok_bank_free[2 * t] = last
                    tok_bank_free[2 * t + 1] = last
                    ct_sts.append(last)
                    tok_ct_free[ctsl] = p.emit("sync", lambda s, _i, g=g, t=t, ctsl=ctsl: s.dma_start(
                        attn0_c[(4 * g + t) * 128:(4 * g + t + 1) * 128, :], CT0[ctsl]),
                        waits=tmax(last), inc=sAC[ctsl], inc_by=16)
                tok_outf_free = tmax(*wo_last)
                prev_phaseB_banks = tmax(*ct_sts)
            tok_attn0_done = tmax(tok_ct_free[0], tok_ct_free[1])

        if "p2d" in dbg:
            d_a0 = dout("d_a0", (T, D))
            wAlla = p.barrier_token()
            p.emit("sync", lambda s, _i, d_a0=d_a0: s.dma_start(d_a0, attn0_c),
                   waits=wAlla, inc=sQ, inc_by=16)
        if "p2q" in dbg:
            d_kq = dout("d_kq", (128, 36864))
            d_va = dout("d_va", (128, 33280))
            wAllq = p.barrier_token()
            p.emit("gpsimd", lambda s, _i, d_kq=d_kq: s.dma_start(
                d_kq, v0(0, 36864, BF16)), waits=wAllq, inc=sG, inc_by=16)
            p.emit("gpsimd", lambda s, _i, d_va=d_va: s.dma_start(
                d_va, v2(0, 33280, BF16)), waits=wAllq, inc=sG, inc_by=16)
        import dataclasses as _dc
        def shuf64(ap):
            return _dc.replace(ap, offset=ap.offset + 32,
                               ap=[[ap.ap[0][0], 128], [64, 16], [-32, 2], [1, 32]])
        tok_p4_done = None
        tok_ynt = [None] * NT
        YNT = [v0(c * 8192, T, BF16) for c in range(DC)]
        if "p4" in phases:
            sL4 = [p.sem("sL4a"), p.sem("sL4b")]
            sO4 = [p.sem("sO4a"), p.sem("sO4b")]
            p4pre = tmax(tok_attn0_done, tok_moe_done,
                         Tok(sT, "abs", sT.n), Tok(sV, "abs", sV.n),
                         Tok(sS, "abs", sS.n))
            X4 = [v1(50176, 1024), v1(54272, 1024)]
            A4s = v1(58368, 1024)
            YB4s = v1(62464, 1024, BF16)
            M4s = v0(65536, 1024)
            Y4s = v0(69632, 1024)
            prev_use = [p4pre, p4pre]
            prev_a4 = p4pre
            prev_m4 = p4pre
            prev_o2 = p4pre
            prev_trl = p4pre
            for t in range(NT):
                b = t % 2
                tl1 = p.emit("sync", lambda s, _i, t=t, b=b: s.dma_start(
                    X4[b], x0[t * 128:(t + 1) * 128, :]), waits=prev_use[b],
                    inc=sL4[b], inc_by=16)
                tl2 = p.emit("sync", lambda s, _i, t=t: s.dma_start(
                    A4s, attn0_c[t * 128:(t + 1) * 128, :]), waits=tmax(prev_a4),
                    inc=sL4[b], inc_by=16)
                tl3 = p.emit("sync", lambda s, _i, t=t: s.dma_start(
                    M4s, moe_c[t * 128:(t + 1) * 128, :]), waits=tmax(prev_m4),
                    inc=sL4[b], inc_by=16)
                tld = tmax(tl1, tl2, tl3)
                t_s1 = p.emit("vector", lambda v, _i, b=b: v.tensor_add(
                    X4[b], X4[b], A4s), waits=tmax(tld), inc=sV)
                t_s2 = p.emit("vector", lambda v, _i, b=b: v.tensor_add(
                    X4[b], X4[b], M4s), waits=tmax(t_s1), inc=sV)
                prev_m4 = t_s2
                t_o1 = p.emit("sync", lambda s, _i, t=t, b=b: s.dma_start(
                    x1s[t * 128:(t + 1) * 128, :], X4[b]), waits=tmax(t_s2),
                    inc=sO4[b], inc_by=16)
                meanraw = stat(b, 0); sqs = stat(b, 1); var0 = stat(b, 2); rstd = stat(b, 3)
                mean = vs(768 + 64 * b, 1)[:, 0:1]
                msq = vs(768 + 64 * b + 16, 1)[:, 0:1]
                t_m0 = p.emit("vector", lambda v, _i, b=b, meanraw=meanraw: v.reduce_sum(
                    meanraw, X4[b], axis=AX), waits=tmax(t_s2), inc=sV, fence=True)
                t_s0 = p.emit("scalar", lambda sc, _i, b=b, sqs=sqs: sc.activation(
                    out=A4s, in_=X4[b], func=AF.Square, accum_out=sqs),
                    waits=tmax(t_s2), inc=sS, fence=True)
                prev_a4 = t_s0
                p.emit("vector", lambda v, _i, mean=mean, meanraw=meanraw: v.tensor_scalar(
                    out=mean, in0=meanraw, scalar1=1.0 / D, scalar2=None, op0=ALU.mult), inc=sV)
                t_msq = p.emit("vector", lambda v, _i, msq=msq, meanraw=meanraw: v.tensor_scalar(
                    out=msq, in0=meanraw, scalar1=meanraw, scalar2=None, op0=ALU.mult),
                    inc=sV, fence=True)
                t_v0 = p.emit("vector", lambda v, _i, var0=var0, sqs=sqs, msq=msq: v.scalar_tensor_tensor(
                    out=var0, in0=sqs, scalar=float(D), in1=msq, op0=ALU.mult, op1=ALU.subtract),
                    waits=tmax(t_s0), inc=sV, fence=True)
                t_sq = p.emit("scalar", lambda sc, _i, var0=var0, rstd=rstd: sc.activation(
                    out=rstd, in_=var0, func=AF.Sqrt, bias=EPS5, scale=1.0 / (D * D)),
                    waits=tmax(t_v0), inc=sS, fence=True)
                t_rc = p.emit("vector", lambda v, _i, rstd=rstd: v.reciprocal(rstd, rstd),
                              waits=tmax(t_sq), inc=sV, fence=True)
                t_yn = p.emit("vector", lambda v, _i, b=b, mean=mean, rstd=rstd: v.tensor_scalar(
                    out=Y4s, in0=X4[b], scalar1=mean, scalar2=rstd,
                    op0=ALU.subtract, op1=ALU.mult),
                    waits=tmax(t_rc, prev_o2), inc=sV)
                t_o2 = p.emit("sync", lambda s, _i, t=t, b=b: s.dma_start(
                    zs[t * 128:(t + 1) * 128, :], Y4s), waits=tmax(t_yn),
                    inc=sO4[b], inc_by=16)
                prev_o2 = t_o2
                t_cb = p.emit("vector", lambda v, _i: v.tensor_copy(
                    YB4s, Y4s), waits=tmax(prev_trl, t_yn), inc=sV)
                trl = None
                for c in range(DC):
                    trl = p.emit("tensor", lambda te, _i, b=b, c=c: te.matmul(
                        PSB[b][:, c * 64:c * 64 + 64].bitcast(BF16),
                        YB4s[:, c * 128:(c + 1) * 128], ident[:],
                        is_transpose=True, start=True, stop=True),
                        waits=tmax(t_cb, p4pre,
                                   tok_ynt[t - 2] if t >= 2 else None), inc=sT)
                prev_trl = trl
                evl = None
                for c in range(DC):
                    evl = p.emit("vector", lambda v, _i, c=c, t=t, b=b: v.tensor_copy(
                        YNT[c][:, t * 128:(t + 1) * 128],
                        PSB[b][:, c * 64:c * 64 + 64].bitcast(BF16)),
                        waits=tmax(trl), inc=sV)
                tok_ynt[t] = evl
                prev_use[b] = tmax(evl, t_o1, t_o2, t_s0, t_yn)
            tok_p4_done = tmax(prev_use[0], prev_use[1])

        tok_attn1_done = None
        if "p5" in phases:
            sR5 = [p.sem("sR5a"), p.sem("sR5b")]
            sA5 = [p.sem("sA5a"), p.sem("sA5b")]
            sWL5 = p.sem("sWL5")
            WQ5 = lambda wi, dc: v2(wi * 16384 + dc * 2048, 1024, BF16)
            w5all = []
            p5pre = tmax(tok_p4_done, Tok(sT, "abs", sT.n), Tok(sV, "abs", sV.n))
            for wi, wt in enumerate([awqT, awkT, awvT, awoT]):
                for dc in range(DC):
                    w5all.append(p.emit("gpsimd", lambda gp, _i, wi=wi, dc=dc, wt=wt: gp.dma_start(
                        WQ5(wi, dc), wt[dc * 128:(dc + 1) * 128, :]),
                        waits=p5pre,
                        inc=sWL5, inc_by=16))
            tok_w5 = Tok(sWL5, "abs", sWL5.n)
            QT5 = v1(0, 1024)
            KT5 = v1(4096, 1024)
            TMP5 = v1(8192, 1024)
            RT5s = [[v1(12288 + (b_ * 4 + j) * 256, 64) for j in range(4)] for b_ in range(2)]
            QB5 = v1(14336, 1024, BF16)
            KB5 = v1(16384, 1024, BF16)
            QF5 = v1(18432, 1024, BF16)
            KR5 = [v1(20480, 1024, BF16), v1(22528, 1024, BF16)]
            VR5 = [v1(24576, 1024, BF16), v1(26624, 1024, BF16)]
            MA5 = v1(28672, 256)
            S5 = v1(29696, 256)
            PR5 = v1(30720, 256, BF16)
            PT5 = [v1(31232, 128, BF16), v1(31488, 128, BF16)]
            OB5 = v1(31744, 1024, BF16)
            OFM5 = v1(33792, 1024, BF16)
            BIA5 = v1(35840, 1024, BF16)
            BOB5 = v1(37888, 1024)
            CT5 = [v1(41984, 1024), v1(46080, 1024)]
            NM5 = vs(1472, 1)[:, 0:1]
            DN5 = vs(1536, 1)[:, 0:1]
            t_ma = p.emit("sync", lambda s, _i: s.dma_start(MA5, maskA),
                          waits=p5pre, inc=sQ, inc_by=16)
            t_bo5 = p.emit("sync", lambda s, _i: s.dma_start(
                BOB5, abo[None, :].broadcast_to([128, D])),
                waits=tmax(t_ma), inc=sQ, inc_by=16)
            B5 = [abq, abk, abv]
            prev_rt = [p5pre, p5pre]
            prev_bias5 = p5pre
            prev_ct5 = [p5pre, p5pre]
            prev_qkv_banks = p5pre
            prev_score_bank = [p5pre, p5pre]
            prev_kv = [p5pre, p5pre]
            prev_qf = p5pre
            prev_mask_h_by_bank = {}
            prev_t4 = {"q": p5pre, "k": p5pre}
            prev_tex = p5pre
            prev_trp = p5pre
            prev_pv_banks = p5pre
            prev_wo_banks = p5pre
            tok_kv = [None] * NT
            for t in range(NT):
                b = t % 2
                rts = RT5s[b]
                rtk = []
                for j in range(4):
                    rtk.append(p.emit("sync", lambda s, _i, j=j, t=t, rts=rts: s.dma_start(
                        rts[j], ropeA[j, t * 128:(t + 1) * 128, :]),
                        waits=tmax(prev_rt[b]), inc=sR5[b], inc_by=16))
                t_rt = tmax(*rtk)
                rope_toks = []
                for wi, wn in enumerate(("q", "k", "v")):
                    mm = None
                    for dc in range(DC):
                        for hf in range(2):
                            w = [tok_w5] if (t == 0 and wi == 0 and dc == 0) else []
                            w.append(tok_ynt[t])
                            if dc == 0:
                                w.append(prev_qkv_banks)
                            mm = p.emit("tensor", lambda te, _i, t=t, wi=wi, dc=dc, hf=hf: te.matmul(
                                PSB[4 + hf][:, 0:512],
                                YNT[dc][:, t * 128:(t + 1) * 128],
                                WQ5(wi, dc)[:, hf * 512:(hf + 1) * 512],
                                start=(dc == 0), stop=(dc == DC - 1)),
                                waits=tmax(*w), inc=sT)
                    tb5 = p.emit("gpsimd", lambda gp, _i, wi=wi: gp.dma_start(
                        BIA5, B5[wi][None, :].broadcast_to([128, D])),
                        waits=tmax(prev_bias5), inc=sG, inc_by=16)
                    if wn == "v":
                        lastv = None
                        for hf in range(2):
                            lastv = p.emit("vector", lambda v, _i, b=b, hf=hf: v.tensor_tensor(
                                out=VR5[b][:, hf * 512:(hf + 1) * 512],
                                in0=PSB[4 + hf][:, 0:512],
                                in1=BIA5[:, hf * 512:(hf + 1) * 512], op=ALU.add),
                                waits=tmax(mm, tb5, prev_kv[b]), inc=sV)
                        prev_bias5 = lastv
                        prev_qkv_banks = lastv
                        tok_v = lastv
                    else:
                        dst = QT5 if wn == "q" else KT5
                        dstb = QB5 if wn == "q" else KB5
                        j0 = 0 if wn == "q" else 2
                        t1 = None
                        for hf in range(2):
                            t1 = p.emit("vector", lambda v, _i, dst=dst, hf=hf: v.tensor_tensor(
                                out=dst[:, hf * 512:(hf + 1) * 512],
                                in0=PSB[4 + hf][:, 0:512],
                                in1=BIA5[:, hf * 512:(hf + 1) * 512], op=ALU.add),
                                waits=tmax(mm, tb5, prev_t4[wn]), inc=sV)
                        prev_bias5 = t1
                        t2 = p.emit("vector", lambda v, _i, dst=dst, rts=rts, j0=j0: v.tensor_tensor(
                            out=TMP5.rearrange("p (h c) -> p h c", c=64),
                            in0=shuf64(dst),
                            in1=rts[j0 + 1][:, None, :].broadcast_to([128, 16, 64]),
                            op=ALU.mult), waits=tmax(t1, t_rt), inc=sV)
                        t3 = p.emit("vector", lambda v, _i, dst=dst, rts=rts, j0=j0: v.tensor_tensor(
                            out=dst.rearrange("p (h c) -> p h c", c=64),
                            in0=dst.rearrange("p (h c) -> p h c", c=64),
                            in1=rts[j0][:, None, :].broadcast_to([128, 16, 64]),
                            op=ALU.mult), waits=tmax(t2), inc=sV)
                        t4 = p.emit("vector", lambda v, _i, dst=dst, dstb=dstb: v.tensor_add(
                            dstb, dst, TMP5), waits=tmax(t3), inc=sV)
                        prev_t4[wn] = t4
                        rope_toks.append(t4)
                        trl = None
                        pb = 6
                        for c in range(DC):
                            trl = p.emit("tensor", lambda te, _i, c=c, dstb=dstb, pb=pb: te.matmul(
                                PSB[pb][:, c * 64:c * 64 + 64].bitcast(BF16),
                                dstb[:, c * 128:(c + 1) * 128], ident[:],
                                is_transpose=True, start=True, stop=True),
                                waits=tmax(t4, prev_score_bank[0] if c == 0 else None),
                                inc=sT)
                        dfm = QF5 if wn == "q" else KR5[b]
                        evl = None
                        for c in range(DC):
                            wv = [trl]
                            if wn == "q" and c == 0:
                                wv.append(prev_qf)
                            if wn == "k" and c == 0:
                                wv.append(prev_kv[b])
                            evl = p.emit("vector", lambda v, _i, c=c, dfm=dfm: v.tensor_copy(
                                dfm[:, c * 128:(c + 1) * 128],
                                PSB[pb][:, c * 64:c * 64 + 64].bitcast(BF16)),
                                waits=tmax(*wv), inc=sV)
                        if wn == "q":
                            tok_q = evl
                        else:
                            tok_k = evl
                        prev_score_bank[0] = evl
                        prev_qkv_banks = evl
                tok_kv[t] = tmax(tok_k, tok_v)
                nk = 2 if t >= 1 else 1
                W_ = 128 * nk
                koff = 256 - W_
                pv_done = []
                prev_mask_h = None
                prev_pv_h = {}
                for h in range(NH):
                    sb = 6 + (h % 2)
                    smms = []
                    for ki in range(nk):
                        kt = t - (nk - 1) + ki
                        w = [tok_q, tok_kv[kt]] if ki == 0 else []
                        if ki == 0:
                            if h < 2:
                                w.append(prev_score_bank[1])
                            else:
                                w.append(prev_mask_h_by_bank.get(sb))
                        smms.append(p.emit("tensor", lambda te, _i, h=h, kt=kt, ki=ki, sb=sb: te.matmul(
                            PSB[sb][:, ki * 128:(ki + 1) * 128],
                            KR5[kt % 2][(h % 2) * 64:(h % 2) * 64 + 64,
                                        (h // 2) * 128:(h // 2) * 128 + 128],
                            QF5[(h % 2) * 64:(h % 2) * 64 + 64,
                                (h // 2) * 128:(h // 2) * 128 + 128],
                            start=True, stop=True, is_transpose=True) if False else te.matmul(
                            PSB[sb][:, ki * 128:(ki + 1) * 128],
                            QF5[(h % 2) * 64:(h % 2) * 64 + 64,
                                (h // 2) * 128:(h // 2) * 128 + 128],
                            KR5[kt % 2][(h % 2) * 64:(h % 2) * 64 + 64,
                                        (h // 2) * 128:(h // 2) * 128 + 128],
                            start=True, stop=True),
                            waits=tmax(*w), inc=sT))
                    tmsk = p.emit("vector", lambda v, _i, W_=W_, koff=koff, sb=sb: v.tensor_tensor(
                        out=S5[:, 0:W_], in0=PSB[sb][:, 0:W_],
                        in1=MA5[:, koff:koff + W_], op=ALU.add),
                        waits=tmax(*smms, t_ma, prev_tex), inc=sV)
                    prev_mask_h_by_bank[sb] = tmsk
                    tmx5 = p.emit("vector", lambda v, _i, W_=W_: v.reduce_max(
                        NM5, S5[:, 0:W_], axis=AX, negate=True),
                        waits=tmax(tmsk), inc=sV, fence=True)
                    tex5 = p.emit("scalar", lambda sc, _i, W_=W_: sc.activation(
                        out=PR5[:, 0:W_], in_=S5[:, 0:W_], func=AF.Exp,
                        bias=NM5, accum_out=DN5),
                        waits=tmax(tmx5, prev_trp), inc=sS, fence=True)
                    prev_tex = tex5
                    trc5 = p.emit("vector", lambda v, _i: v.reciprocal(DN5, DN5),
                                  waits=tmax(tex5), inc=sV, fence=True)
                    tnr5 = p.emit("vector", lambda v, _i, W_=W_: v.tensor_scalar(
                        out=PR5[:, 0:W_], in0=PR5[:, 0:W_], scalar1=DN5,
                        scalar2=None, op0=ALU.mult), waits=tmax(trc5), inc=sV)
                    trp = []
                    for ki in range(nk):
                        trp.append(p.emit("tensor", lambda te, _i, ki=ki: te.matmul(
                            PSB[5][:, 64 * ki:64 * ki + 64].bitcast(BF16),
                            PR5[:, ki * 128:(ki + 1) * 128], ident[:],
                            is_transpose=True, start=True, stop=True),
                            waits=tmax(tnr5,
                                       prev_pv_h.get(h - 2) if ki == 0 else None),
                            inc=sT))
                    prev_trp = trp[-1]
                    tpt = []
                    for ki in range(nk):
                        tpt.append(p.emit("vector", lambda v, _i, ki=ki: v.tensor_copy(
                            PT5[ki], PSB[5][:, 64 * ki:64 * ki + 64].bitcast(BF16)),
                            waits=tmax(trp[-1]), inc=sV))
                    pvb = h % 2
                    last = None
                    for ki in range(nk):
                        kt = t - (nk - 1) + ki
                        w = [tpt[ki]]
                        if h < 2 and ki == 0:
                            w.append(prev_pv_banks)
                        last = p.emit("tensor", lambda te, _i, h=h, ki=ki, kt=kt, nk=nk: te.matmul(
                            PSB[2 if h < 8 else 3][:, (h % 8) * 64:(h % 8) * 64 + 64],
                            PT5[ki],
                            VR5[kt % 2][:, h * 64:h * 64 + 64],
                            start=(ki == 0), stop=(ki == nk - 1)),
                            waits=tmax(*w), inc=sT)
                    prev_pv_h[h] = last
                    pv_done.append(last)
                tob = []
                for hb in range(2):
                    tob.append(p.emit("vector", lambda v, _i, hb=hb: v.tensor_copy(
                        OB5[:, hb * 512:(hb + 1) * 512], PSB[2 + hb][:, 0:512]),
                        waits=tmax(*pv_done), inc=sV))
                trw = None
                for c in range(DC):
                    trw = p.emit("tensor", lambda te, _i, c=c: te.matmul(
                        PSB[4 + (c % 2)][:, (c // 2) * 128:(c // 2) * 128 + 64].bitcast(BF16),
                        OB5[:, c * 128:(c + 1) * 128], ident[:],
                        is_transpose=True, start=True, stop=True),
                        waits=tmax(*tob), inc=sT)
                tofm = None
                for c in range(DC):
                    tofm = p.emit("vector", lambda v, _i, c=c: v.tensor_copy(
                        OFM5[:, c * 128:(c + 1) * 128],
                        PSB[4 + (c % 2)][:, (c // 2) * 128:(c // 2) * 128 + 64].bitcast(BF16)),
                        waits=tmax(trw), inc=sV)
                wo_last = None
                for dc in range(DC):
                    for hf in range(2):
                        w = [tofm]
                        if dc == 0 and hf == 0:
                            w.append(prev_wo_banks)
                        wo_last = p.emit("tensor", lambda te, _i, dc=dc, hf=hf: te.matmul(
                            PSB[2 + hf][:, 0:512],
                            OFM5[:, dc * 128:(dc + 1) * 128],
                            WQ5(3, dc)[:, hf * 512:(hf + 1) * 512],
                            start=(dc == 0), stop=(dc == DC - 1)),
                            waits=tmax(*w), inc=sT)
                tct5 = None
                for hf in range(2):
                    tct5 = p.emit("vector", lambda v, _i, b=b, hf=hf: v.tensor_tensor(
                        out=CT5[b][:, hf * 512:(hf + 1) * 512],
                        in0=PSB[2 + hf][:, 0:512],
                        in1=BOB5[:, hf * 512:(hf + 1) * 512], op=ALU.add),
                        waits=tmax(wo_last, t_bo5, prev_ct5[b]), inc=sV)
                prev_wo_banks = tct5
                prev_ct5[b] = p.emit("sync", lambda s, _i, t=t, b=b: s.dma_start(
                    attn1_c[t * 128:(t + 1) * 128, :], CT5[b]),
                    waits=tmax(tct5), inc=sA5[b], inc_by=16)
                prev_pv_banks = tct5
                prev_score_bank[1] = tmax(*[prev_mask_h_by_bank[s_] for s_ in (6, 7)
                                            if s_ in prev_mask_h_by_bank])
                prev_kv[0] = prev_kv[1] = tmax(*pv_done)
                prev_score_bank[0] = tmax(prev_score_bank[1])
                prev_qkv_banks = tofm
                prev_qf = tmax(*pv_done, prev_score_bank[1])
                prev_rt[b] = tmax(*rope_toks)
            tok_attn1_done = tmax(prev_ct5[0], prev_ct5[1])

        if "p7" in phases:
            sL7 = [p.sem("sL7a"), p.sem("sL7b")]
            sY7 = [p.sem("sY7a"), p.sem("sY7b")]
            p7pre = tmax(tok_attn1_done, Tok(sV, "abs", sV.n),
                         Tok(sT, "abs", sT.n), Tok(sS, "abs", sS.n))
            X7 = [v1(0, 1024), v1(4096, 1024)]
            A7 = [v1(8192, 1024), v1(12288, 1024)]
            Z7 = [v1(16384, 1024), v1(20480, 1024)]
            prev7 = [p7pre, p7pre]
            for t in range(NT):
                b = t % 2
                l1 = p.emit("sync", lambda s, _i, t=t, b=b: s.dma_start(
                    X7[b], x1s[t * 128:(t + 1) * 128, :]), waits=prev7[b],
                    inc=sL7[b], inc_by=16)
                l2 = p.emit("sync", lambda s, _i, t=t, b=b: s.dma_start(
                    A7[b], attn1_c[t * 128:(t + 1) * 128, :]), waits=prev7[b],
                    inc=sL7[b], inc_by=16)
                l3 = p.emit("sync", lambda s, _i, t=t, b=b: s.dma_start(
                    Z7[b], zs[t * 128:(t + 1) * 128, :]), waits=prev7[b],
                    inc=sL7[b], inc_by=16)
                a1 = p.emit("vector", lambda v, _i, b=b: v.tensor_add(
                    X7[b], X7[b], A7[b]), waits=tmax(l1, l2, l3), inc=sV)
                a2 = p.emit("vector", lambda v, _i, b=b: v.tensor_add(
                    X7[b], X7[b], Z7[b]), waits=tmax(a1), inc=sV)
                prev7[b] = p.emit("sync", lambda s, _i, t=t, b=b: s.dma_start(
                    y[t * 128:(t + 1) * 128, :], X7[b]), waits=tmax(a2),
                    inc=sY7[b], inc_by=16)
            p.emit("sync", lambda s, _i: s.nop(),
                   waits=tmax(prev7[0], prev7[1]))
            p.emit("gpsimd", lambda g, _i: g.nop(),
                   waits=tmax(prev7[0], prev7[1]))

        if "p5d" in dbg:
            d_a1 = dout("d_a1", (T, D))
            wAll5 = p.barrier_token()
            p.emit("sync", lambda s, _i, d_a1=d_a1: s.dma_start(d_a1, attn1_c),
                   waits=wAll5, inc=sQ, inc_by=16)
        if "p0x" in dbg:
            d_xt = dout("d_xt", (2, 128, 1024))
            d_xnb = dout("d_xnb", (2, 128, 1024))
            d_small = dout("d_small", (128, 512))
            wAllx = p.barrier_token()
            for i in range(2):
                p.emit("sync", lambda s, _i, i=i, d_xt=d_xt: s.dma_start(d_xt[i], XT[i]),
                       waits=wAllx, inc=sQ, inc_by=16)
                p.emit("gpsimd", lambda g, _i, i=i, d_xnb=d_xnb: g.dma_start(d_xnb[i], XNB[i]),
                       waits=wAllx, inc=sG, inc_by=16)
            p.emit("sync", lambda s, _i, d_small=d_small: s.dma_start(
                d_small, SMALL[:, 0:2048].bitcast(F32)), waits=wAllx, inc=sQ, inc_by=16)

        if "p0" in dbg:
            d_xnt = dout("d_xnt", (DC, 128, T))
            d_masks = dout("d_masks", (128, NT * NEXP))
            wAll = p.barrier_token()
            for c in range(DC):
                p.emit("gpsimd", lambda g, _i, c=c, d_xnt=d_xnt: g.dma_start(
                    d_xnt[c], XNT[c]), waits=wAll, inc=sG, inc_by=16)
            p.emit("sync", lambda s, _i, d_masks=d_masks: s.dma_start(d_masks, MASKS[:]),
                   waits=wAll, inc=sQ, inc_by=16)

        wEnd = p.barrier_token()
        p.emit("sync", lambda s, _i: s.nop(), waits=wEnd)
        p.emit("gpsimd", lambda g, _i: g.nop(), waits=wEnd)

        p.finalize()
    return nc, dbg_out



import dataclasses as _dc

def shuf64(ap):
    return _dc.replace(ap, offset=ap.offset + 32,
                       ap=[[ap.ap[0][0], 128], [64, 16], [-32, 2], [1, 32]])


def emit_p6a_sec(p, E, zin, zout, wqkv, wo, qs, ks, ropeI):
    entry = p.barrier_token()
    sW = E.sW; sM = E.sM; sLD = E.sLD; sR6 = E.sR6; sZO = E.sZO
    sV = E.sV; sS = E.sS; sT = E.sT
    WQK = E.WQK; WOB = E.WOB; QSB = E.QSB; KSB = E.KSB; MIB = E.MIB
    RTB = E.RTB; SH = E.SH; SHB = E.SHB; XR = E.XR; XST = E.XST
    QN = E.QN; SQ6 = E.SQ6; TMP6 = E.TMP6; SM6 = E.SM6
    QB = E.QB; KB = E.KB; QT = E.QT; KT = E.KT; VR = E.VR
    S5 = E.S5; PR = E.PR; PT = E.PT; OB = E.OB; OFM = E.OFM; CT = E.CT
    ident = E.ident; PSB = E.PSB
    t_id = E.t_id
    RS16a = SM6[:, 0:16]
    RS16b = SM6[:, 16:32]
    NM6 = SM6[:, 32:33]
    DN6 = SM6[:, 33:34]

    z0r = p.emit("vector", lambda v, _i: v.memset(SH[0:1, 512:1024], 0.0),
                 waits=entry, inc=sV)
    wl = []
    for dc in range(DC):
        wl.append(p.emit("gpsimd", lambda g, _i, dc=dc: g.dma_start(
            WQK[:, dc * I3:(dc + 1) * I3], wqkv[dc * 128:(dc + 1) * 128, :]),
            waits=entry, inc=sW, inc_by=16))
    for dc in range(DC):
        wl.append(p.emit("gpsimd", lambda g, _i, dc=dc: g.dma_start(
            WOB[:, dc * 1024:(dc + 1) * 1024], wo[dc * 128:(dc + 1) * 128, :]),
            waits=entry, inc=sW, inc_by=16))
    tok_w = wl[-1]
    m1_ = p.emit("sync", lambda s, _i: s.dma_start(
        QSB[:], qs[None, :].broadcast_to([128, DH])), waits=entry, inc=sM, inc_by=16)
    m2_ = p.emit("sync", lambda s, _i: s.dma_start(
        KSB[:], ks[None, :].broadcast_to([128, DH])), waits=entry, inc=sM, inc_by=16)
    tok_qk = tmax(m1_, m2_)
    tok_m = p.emit("sync", lambda s, _i: s.dma_start(MIB[:], E.maskI),
                   waits=entry, inc=sM, inc_by=16)

    ent = tmax(entry, z0r)
    wSH = [ent, ent]
    wSHB = [ent, ent]
    wXR = [ent, ent]
    wQNprev = ent
    wTMPprev = ent
    evac_prev = ent
    tok_b45_read = ent
    tok_b6_read = {6: ent, 7: ent}
    prev_tex = ent
    prev_trp = ent
    prev_ct = ent
    prev_rt = [ent, ent]
    prev_store = [ent, ent]
    tok_evq = None
    tok_evk = [None] * NT
    tok_vcp = [None] * NT

    for t in range(NT):
        b = t % 2
        k3 = t % 3
        shb = SH[:, b * 1024:(b + 1) * 1024]
        shbb = SHB[:, b * 1024:(b + 1) * 1024]
        xrb = XR[:, b * 1024:(b + 1) * 1024]
        l1 = p.emit("sync", lambda s, _i, t=t, shb=shb: s.dma_start(
            shb[:, 0:512], zin[t * 128:(t + 1) * 128, 0:512]),
            waits=tmax(wSH[b]), inc=sLD[b], inc_by=16)
        if t == 0:
            l2 = p.emit("sync", lambda s, _i, shb=shb: s.dma_start(
                shb[1:128, 512:1024], zin[0:127, 512:1024]), inc=sLD[b], inc_by=16)
        else:
            l2 = p.emit("sync", lambda s, _i, t=t, shb=shb: s.dma_start(
                shb[:, 512:1024], zin[t * 128 - 1:(t + 1) * 128 - 1, 512:1024]),
                inc=sLD[b], inc_by=16)
        l3 = p.emit("sync", lambda s, _i, t=t, xrb=xrb: s.dma_start(
            xrb, zin[t * 128:(t + 1) * 128, :]),
            waits=tmax(wXR[b]), inc=sLD[b], inc_by=16)
        ld = tmax(l1, l2, l3)
        rts = [RTB[:, (b * 4 + j) * 64:(b * 4 + j + 1) * 64] for j in range(4)]
        rl = []
        for j in range(4):
            rl.append(p.emit("sync", lambda s, _i, t=t, j=j, rts=rts: s.dma_start(
                rts[j], ropeI[j, t * 128:(t + 1) * 128, :]),
                waits=tmax(prev_rt[b]), inc=sR6[b], inc_by=16))
        t_rt = tmax(*rl)
        c1 = p.emit("vector", lambda v, _i, shbb=shbb, shb=shb: v.tensor_copy(shbb, shb),
                    waits=tmax(ld, wSHB[b]), inc=sV)
        wSH[b] = c1
        trl = None
        for c in range(DC):
            trl = p.emit("tensor", lambda te, _i, c=c, shbb=shbb: te.matmul(
                PSB[0][:, c * 64:c * 64 + 64].bitcast(BF16),
                shbb[:, c * 128:(c + 1) * 128], ident[:],
                is_transpose=True, start=True, stop=True),
                waits=tmax(c1, t_id, evac_prev if c == 0 else None), inc=sT)
        wSHB[b] = trl
        evl = None
        for c in range(DC):
            evl = p.emit("vector", lambda v, _i, c=c: v.tensor_copy(
                XST[:, c * 128:(c + 1) * 128],
                PSB[0][:, c * 64:c * 64 + 64].bitcast(BF16)),
                waits=tmax(trl), inc=sV)
        evac_prev = evl
        for wi, wn in enumerate(("q", "k", "v")):
            mm = None
            for dc in range(DC):
                for hf in range(2):
                    w = [evl, tok_b45_read] if dc == 0 else []
                    if t == 0 and wi == 0 and dc == 0:
                        w.append(tok_w)
                    mm = p.emit("tensor", lambda te, _i, wi=wi, dc=dc, hf=hf: te.matmul(
                        PSB[4 + hf][:, 0:512],
                        XST[:, dc * 128:(dc + 1) * 128],
                        WQK[:, dc * I3 + wi * 1024 + hf * 512:
                            dc * I3 + wi * 1024 + (hf + 1) * 512],
                        start=(dc == 0), stop=(dc == DC - 1)),
                        waits=tmax(*w), inc=sT)
            if wn == "v":
                lastv = None
                for hf in range(2):
                    lastv = p.emit("vector", lambda v, _i, k3=k3, hf=hf: v.tensor_copy(
                        VR[:, k3 * 1024 + hf * 512:k3 * 1024 + (hf + 1) * 512],
                        PSB[4 + hf][:, 0:512]),
                        waits=tmax(mm), inc=sV)
                tok_vcp[t] = lastv
                tok_b45_read = lastv
            else:
                qn = None
                for hf in range(2):
                    qn = p.emit("vector", lambda v, _i, hf=hf: v.tensor_copy(
                        QN[:, hf * 512:(hf + 1) * 512], PSB[4 + hf][:, 0:512]),
                        waits=tmax(mm, wQNprev), inc=sV)
                tok_b45_read = qn
                t_sq = p.emit("scalar", lambda sc, _i: sc.activation(
                    out=SQ6[:], in_=QN[:], func=AF.Square), waits=tmax(qn), inc=sS)
                t_rs = p.emit("vector", lambda v, _i: v.tensor_reduce(
                    RS16a, SQ6.rearrange("p (h c) -> p h c", c=64), axis=AX,
                    op=ALU.add), waits=tmax(t_sq), inc=sV, fence=True)
                t_sr = p.emit("scalar", lambda sc, _i: sc.activation(
                    out=RS16b, in_=RS16a, func=AF.Sqrt), waits=tmax(t_rs), inc=sS,
                    fence=True)
                t_mx = p.emit("vector", lambda v, _i: v.tensor_scalar(
                    out=RS16b, in0=RS16b, scalar1=1.0e-12, scalar2=None,
                    op0=ALU.max), waits=tmax(t_sr), inc=sV, fence=True)
                t_rc = p.emit("vector", lambda v, _i: v.reciprocal(RS16b, RS16b),
                              waits=tmax(t_mx), inc=sV, fence=True)
                scb = QSB if wn == "q" else KSB
                m1 = p.emit("vector", lambda v, _i: v.tensor_tensor(
                    out=QN.rearrange("p (h c) -> p h c", c=64),
                    in0=QN.rearrange("p (h c) -> p h c", c=64),
                    in1=RS16b[:, :, None].broadcast_to([128, 16, 64]),
                    op=ALU.mult), waits=tmax(t_rc), inc=sV)
                m2 = p.emit("vector", lambda v, _i, scb=scb: v.tensor_tensor(
                    out=QN.rearrange("p (h c) -> p h c", c=64),
                    in0=QN.rearrange("p (h c) -> p h c", c=64),
                    in1=scb[:, None, :].broadcast_to([128, 16, 64]),
                    op=ALU.mult), waits=tmax(m1, tok_qk), inc=sV)
                j0 = 0 if wn == "q" else 2
                dstb = QB if wn == "q" else KB
                t2 = p.emit("vector", lambda v, _i, rts=rts, j0=j0: v.tensor_tensor(
                    out=TMP6.rearrange("p (h c) -> p h c", c=64),
                    in0=shuf64(QN[:]),
                    in1=rts[j0 + 1][:, None, :].broadcast_to([128, 16, 64]),
                    op=ALU.mult), waits=tmax(m2, t_rt, wTMPprev), inc=sV)
                t3 = p.emit("vector", lambda v, _i, rts=rts, j0=j0: v.tensor_tensor(
                    out=QN.rearrange("p (h c) -> p h c", c=64),
                    in0=QN.rearrange("p (h c) -> p h c", c=64),
                    in1=rts[j0][:, None, :].broadcast_to([128, 16, 64]),
                    op=ALU.mult), waits=tmax(t2), inc=sV)
                t4 = p.emit("vector", lambda v, _i, dstb=dstb: v.tensor_add(
                    dstb[:], QN[:], TMP6[:]), waits=tmax(t3), inc=sV)
                wQNprev = t4
                wTMPprev = t4
                dfm = QT[:] if wn == "q" else KT[:, k3 * 1024:(k3 + 1) * 1024]
                trq = None
                for c in range(DC):
                    trq = p.emit("tensor", lambda te, _i, c=c, dstb=dstb: te.matmul(
                        PSB[6][:, c * 64:c * 64 + 64].bitcast(BF16),
                        dstb[:, c * 128:(c + 1) * 128], ident[:],
                        is_transpose=True, start=True, stop=True),
                        waits=tmax(t4, tok_b6_read[6] if c == 0 else None), inc=sT)
                evq = None
                for c in range(DC):
                    evq = p.emit("vector", lambda v, _i, c=c, dfm=dfm: v.tensor_copy(
                        dfm[:, c * 128:(c + 1) * 128],
                        PSB[6][:, c * 64:c * 64 + 64].bitcast(BF16)),
                        waits=tmax(trq), inc=sV)
                tok_b6_read[6] = evq
                if wn == "q":
                    tok_evq = evq
                else:
                    tok_evk[t] = evq
        prev_rt[b] = wTMPprev
        nk = min(t + 1, 3)
        W_ = 128 * nk
        pv_done = []
        for h in range(NH):
            sbk = 6 + (h % 2)
            smms = []
            for ki in range(nk):
                kt = t - (nk - 1) + ki
                ksl = kt % 3
                w = []
                if ki == 0:
                    w = [tok_evq, tok_evk[kt], tok_vcp[kt], tok_b6_read[sbk]]
                smms.append(p.emit("tensor", lambda te, _i, h=h, ki=ki, ksl=ksl, sbk=sbk: te.matmul(
                    PSB[sbk][:, ki * 128:(ki + 1) * 128],
                    QT[(h % 2) * 64:(h % 2) * 64 + 64,
                       (h // 2) * 128:(h // 2) * 128 + 128],
                    KT[(h % 2) * 64:(h % 2) * 64 + 64,
                       ksl * 1024 + (h // 2) * 128:ksl * 1024 + (h // 2) * 128 + 128],
                    start=True, stop=True),
                    waits=tmax(*w), inc=sT))
            tmsk = p.emit("vector", lambda v, _i, W_=W_, nk=nk, sbk=sbk: v.tensor_tensor(
                out=S5[:, 0:W_], in0=PSB[sbk][:, 0:W_],
                in1=MIB[:, (3 - nk) * 128:384], op=ALU.add),
                waits=tmax(*smms, tok_m, prev_tex), inc=sV)
            tok_b6_read[sbk] = tmsk
            tmx = p.emit("vector", lambda v, _i, W_=W_: v.reduce_max(
                NM6, S5[:, 0:W_], axis=AX, negate=True),
                waits=tmax(tmsk), inc=sV, fence=True)
            tex = p.emit("scalar", lambda sc, _i, W_=W_: sc.activation(
                out=PR[:, 0:W_], in_=S5[:, 0:W_], func=AF.Exp,
                bias=NM6, accum_out=DN6),
                waits=tmax(tmx, prev_trp), inc=sS, fence=True)
            prev_tex = tex
            trc = p.emit("vector", lambda v, _i: v.reciprocal(DN6, DN6),
                         waits=tmax(tex), inc=sV, fence=True)
            tnr = p.emit("vector", lambda v, _i, W_=W_: v.tensor_scalar(
                out=PR[:, 0:W_], in0=PR[:, 0:W_], scalar1=DN6,
                scalar2=None, op0=ALU.mult), waits=tmax(trc), inc=sV)
            trp = []
            for ki in range(nk):
                trp.append(p.emit("tensor", lambda te, _i, ki=ki: te.matmul(
                    PSB[5][:, ki * 64:ki * 64 + 64].bitcast(BF16),
                    PR[:, ki * 128:(ki + 1) * 128], ident[:],
                    is_transpose=True, start=True, stop=True),
                    waits=tmax(tnr), inc=sT))
            prev_trp = trp[-1]
            tpt = []
            for ki in range(nk):
                tpt.append(p.emit("vector", lambda v, _i, ki=ki: v.tensor_copy(
                    PT[:, ki * 128:(ki + 1) * 128],
                    PSB[5][:, ki * 64:ki * 64 + 64].bitcast(BF16)),
                    waits=tmax(trp[ki]), inc=sV))
            last = None
            for ki in range(nk):
                kt = t - (nk - 1) + ki
                ksl = kt % 3
                w = [tpt[ki]]
                if h < 2 and ki == 0:
                    w.append(prev_ct)
                last = p.emit("tensor", lambda te, _i, h=h, ki=ki, ksl=ksl, nk=nk: te.matmul(
                    PSB[2 if h < 8 else 3][:, (h % 8) * 64:(h % 8) * 64 + 64],
                    PT[:, ki * 128:(ki + 1) * 128],
                    VR[:, ksl * 1024 + h * 64:ksl * 1024 + h * 64 + 64],
                    start=(ki == 0), stop=(ki == nk - 1)),
                    waits=tmax(*w), inc=sT)
            pv_done.append(last)
        tob = []
        for hb in range(2):
            tob.append(p.emit("vector", lambda v, _i, hb=hb: v.tensor_copy(
                OB[:, hb * 512:(hb + 1) * 512], PSB[2 + hb][:, 0:512]),
                waits=tmax(*pv_done), inc=sV))
        trw = None
        for c in range(DC):
            trw = p.emit("tensor", lambda te, _i, c=c: te.matmul(
                PSB[4 + (c % 2)][:, (c // 2) * 128:(c // 2) * 128 + 64].bitcast(BF16),
                OB[:, c * 128:(c + 1) * 128], ident[:],
                is_transpose=True, start=True, stop=True),
                waits=tmax(*tob), inc=sT)
        tofm = None
        for c in range(DC):
            tofm = p.emit("vector", lambda v, _i, c=c: v.tensor_copy(
                OFM[:, c * 128:(c + 1) * 128],
                PSB[4 + (c % 2)][:, (c // 2) * 128:(c // 2) * 128 + 64].bitcast(BF16)),
                waits=tmax(trw), inc=sV)
        tok_b45_read = tofm
        wo_last = None
        for dc in range(DC):
            for hf in range(2):
                w = [tofm]
                if dc == 0 and hf == 0:
                    w.append(prev_ct)
                wo_last = p.emit("tensor", lambda te, _i, dc=dc, hf=hf: te.matmul(
                    PSB[2 + hf][:, 0:512],
                    OFM[:, dc * 128:(dc + 1) * 128],
                    WOB[:, dc * 1024 + hf * 512:dc * 1024 + (hf + 1) * 512],
                    start=(dc == 0), stop=(dc == DC - 1)),
                    waits=tmax(*w), inc=sT)
        ctb = CT[:, b * 1024:(b + 1) * 1024]
        tct = None
        for hf in range(2):
            tct = p.emit("vector", lambda v, _i, ctb=ctb, hf=hf, xrb=xrb: v.tensor_tensor(
                out=ctb[:, hf * 512:(hf + 1) * 512],
                in0=PSB[2 + hf][:, 0:512],
                in1=xrb[:, hf * 512:(hf + 1) * 512], op=ALU.add),
                waits=tmax(wo_last, ld, prev_store[b]), inc=sV)
        prev_ct = tct
        wXR[b] = tct
        prev_store[b] = p.emit("sync", lambda s, _i, t=t, ctb=ctb: s.dma_start(
            zout[t * 128:(t + 1) * 128, :], ctb),
            waits=tmax(tct), inc=sZO[b], inc_by=16)
    return tmax(prev_store[0], prev_store[1])


def emit_p6f_sec(p, E, zin, zout, w1, b1, w2, b2, lng, lnb):
    NFC = 22
    entry = p.barrier_token()
    sW = E.sW; sM = E.sM; sFD = E.sFD; sZO = E.sZO
    sV = E.sV; sS = E.sS; sT = E.sT
    W1B = E.W1B; W2B = E.W2B; B1BC = E.B1BC; GBC = E.GBC; BBC = E.BBC
    B2BC = E.B2BC; F2 = E.F2; FR = E.FR; XNS = E.XNS; XNB = E.XNB
    XNT6 = E.XNT6; ASB = E.ASB; GSB = E.GSB; GEB = E.GEB; HB = E.HB
    HTB = E.HTB; OUTB = E.OUTB; SM6 = E.SM6
    ident = E.ident; PSB = E.PSB
    t_id = E.t_id
    EPSB = SM6[:, 40:41]

    z0r = p.emit("vector", lambda v, _i: v.memset(F2[0:1, 512:1024], 0.0),
                 waits=entry, inc=sV)
    t_eps = p.emit("vector", lambda v, _i: v.memset(EPSB, 1.0e-5),
                   waits=entry, inc=sV)
    wl = []
    for dc in range(DC):
        wl.append(p.emit("gpsimd", lambda g, _i, dc=dc: g.dma_start(
            W1B[:, dc * DI2:(dc + 1) * DI2], w1[dc * 128:(dc + 1) * 128, :]),
            waits=entry, inc=sW, inc_by=16))
    for fc in range(NFC):
        cw = min(128, DI - fc * 128)
        wl.append(p.emit("gpsimd", lambda g, _i, fc=fc, cw=cw: g.dma_start(
            W2B[0:cw, fc * 1024:(fc + 1) * 1024], w2[fc * 128:fc * 128 + cw, :]),
            waits=entry, inc=sW, inc_by=16))
    tok_w = wl[-1]
    ml = []
    ml.append(p.emit("sync", lambda s, _i: s.dma_start(
        B1BC[:], b1[None, :].broadcast_to([128, DI2])), waits=entry, inc=sM, inc_by=16))
    ml.append(p.emit("sync", lambda s, _i: s.dma_start(
        GBC[:], lng[None, :].broadcast_to([128, D])), waits=entry, inc=sM, inc_by=16))
    ml.append(p.emit("sync", lambda s, _i: s.dma_start(
        BBC[:], lnb[None, :].broadcast_to([128, D])), waits=entry, inc=sM, inc_by=16))
    ml.append(p.emit("sync", lambda s, _i: s.dma_start(
        B2BC[:], b2[None, :].broadcast_to([128, D])), waits=entry, inc=sM, inc_by=16))
    tok_m = ml[-1]

    ent = tmax(entry, z0r, t_eps)
    wF2 = [ent, ent]
    wFR = [ent, ent]
    wXNS = ent
    wXNB = ent
    evac_prev = ent
    tok_bank_read = [ent] * 3
    prev_th = ent
    prev_ht_evac = [ent, ent]
    prev_w2_read = ent
    prev_store = [ent, ent]
    prev_out_read = ent

    for t in range(NT):
        b = t % 2
        f2b = F2[:, b * 1024:(b + 1) * 1024]
        frb = FR[:, b * 1024:(b + 1) * 1024]
        outb = OUTB[:]
        f1 = p.emit("sync", lambda s, _i, t=t, f2b=f2b: s.dma_start(
            f2b[:, 0:512], zin[t * 128:(t + 1) * 128, 0:512]),
            waits=tmax(wF2[b]), inc=sFD[b], inc_by=16)
        if t == 0:
            f2_ = p.emit("sync", lambda s, _i, f2b=f2b: s.dma_start(
                f2b[1:128, 512:1024], zin[0:127, 512:1024]), inc=sFD[b], inc_by=16)
        else:
            f2_ = p.emit("sync", lambda s, _i, t=t, f2b=f2b: s.dma_start(
                f2b[:, 512:1024], zin[t * 128 - 1:(t + 1) * 128 - 1, 512:1024]),
                inc=sFD[b], inc_by=16)
        f3 = p.emit("sync", lambda s, _i, t=t, frb=frb: s.dma_start(
            frb, zin[t * 128:(t + 1) * 128, :]),
            waits=tmax(wFR[b]), inc=sFD[b], inc_by=16)
        ld = tmax(f1, f2_, f3)
        meanraw = SM6[:, 0:1]; sqs = SM6[:, 1:2]; var0 = SM6[:, 2:3]
        rstd = SM6[:, 3:4]; mean = SM6[:, 4:5]; msq = SM6[:, 5:6]
        t_m0 = p.emit("vector", lambda v, _i, f2b=f2b, meanraw=meanraw: v.reduce_sum(
            meanraw, f2b, axis=AX), waits=tmax(ld), inc=sV, fence=True)
        t_s0 = p.emit("scalar", lambda sc, _i, f2b=f2b, sqs=sqs: sc.activation(
            out=XNS[:], in_=f2b, func=AF.Square, accum_out=sqs),
            waits=tmax(ld, wXNS), inc=sS, fence=True)
        p.emit("vector", lambda v, _i, mean=mean, meanraw=meanraw: v.tensor_scalar(
            out=mean, in0=meanraw, scalar1=1.0 / D, scalar2=None, op0=ALU.mult),
            waits=tmax(t_m0), inc=sV)
        t_msq = p.emit("vector", lambda v, _i, msq=msq, meanraw=meanraw: v.tensor_scalar(
            out=msq, in0=meanraw, scalar1=meanraw, scalar2=None, op0=ALU.mult),
            inc=sV, fence=True)
        t_v0 = p.emit("vector", lambda v, _i, var0=var0, sqs=sqs, msq=msq: v.scalar_tensor_tensor(
            out=var0, in0=sqs, scalar=float(D), in1=msq, op0=ALU.mult, op1=ALU.subtract),
            waits=tmax(t_s0, t_msq), inc=sV, fence=True)
        t_sq = p.emit("scalar", lambda sc, _i, var0=var0, rstd=rstd: sc.activation(
            out=rstd, in_=var0, func=AF.Sqrt, bias=EPSB, scale=1.0 / (D * D)),
            waits=tmax(t_v0), inc=sS, fence=True)
        t_rc = p.emit("vector", lambda v, _i, rstd=rstd: v.reciprocal(rstd, rstd),
                      waits=tmax(t_sq), inc=sV, fence=True)
        t_xn = p.emit("vector", lambda v, _i, f2b=f2b, mean=mean, rstd=rstd: v.tensor_scalar(
            out=XNS[:], in0=f2b, scalar1=mean, scalar2=rstd,
            op0=ALU.subtract, op1=ALU.mult), waits=tmax(t_rc, t_s0), inc=sV)
        wF2[b] = tmax(t_xn, t_s0)
        t_xg = p.emit("vector", lambda v, _i: v.tensor_tensor(
            out=XNS[:], in0=XNS[:], in1=GBC[:], op=ALU.mult),
            waits=tmax(t_xn, tok_m), inc=sV)
        t_xb = p.emit("vector", lambda v, _i: v.tensor_add(
            XNB[:], XNS[:], BBC[:]), waits=tmax(t_xg, wXNB), inc=sV)
        wXNS = t_xb
        trl = None
        for c in range(DC):
            trl = p.emit("tensor", lambda te, _i, c=c: te.matmul(
                PSB[0][:, c * 64:c * 64 + 64].bitcast(BF16),
                XNB[:, c * 128:(c + 1) * 128], ident[:],
                is_transpose=True, start=True, stop=True),
                waits=tmax(t_xb, t_id, evac_prev if c == 0 else None), inc=sT)
        wXNB = trl
        evl = None
        for c in range(DC):
            evl = p.emit("vector", lambda v, _i, c=c: v.tensor_copy(
                XNT6[:, c * 128:(c + 1) * 128],
                PSB[0][:, c * 64:c * 64 + 64].bitcast(BF16)),
                waits=tmax(trl), inc=sV)
        evac_prev = evl
        th_tok = [None] * 6
        for j in range(6):
            aw = 512 if j < 5 else 170
            bkA = 2 + (j % 3) * 2
            bkB = 3 + (j % 3) * 2
            mmA = None; mmB = None
            for dc in range(DC):
                w = []
                if dc == 0:
                    w = [evl, tok_bank_read[j % 3]]
                    if t == 0 and j == 0:
                        w.append(tok_w)
                mmA = p.emit("tensor", lambda te, _i, j=j, dc=dc, aw=aw, bkA=bkA: te.matmul(
                    PSB[bkA][:, 0:aw],
                    XNT6[:, dc * 128:(dc + 1) * 128],
                    W1B[:, dc * DI2 + j * 512:dc * DI2 + j * 512 + aw],
                    start=(dc == 0), stop=(dc == DC - 1)),
                    waits=tmax(*w), inc=sT)
            for dc in range(DC):
                mmB = p.emit("tensor", lambda te, _i, j=j, dc=dc, aw=aw, bkB=bkB: te.matmul(
                    PSB[bkB][:, 0:aw],
                    XNT6[:, dc * 128:(dc + 1) * 128],
                    W1B[:, dc * DI2 + DI + j * 512:dc * DI2 + DI + j * 512 + aw],
                    start=(dc == 0), stop=(dc == DC - 1)),
                    inc=sT)
            ta = p.emit("vector", lambda v, _i, j=j, aw=aw, bkA=bkA: v.tensor_tensor(
                out=ASB[:, 0:aw], in0=PSB[bkA][:, 0:aw],
                in1=B1BC[:, j * 512:j * 512 + aw], op=ALU.add),
                waits=tmax(mmA, mmB, tok_m, prev_th), inc=sV)
            tg = p.emit("vector", lambda v, _i, j=j, aw=aw, bkB=bkB: v.tensor_tensor(
                out=GSB[:, 0:aw], in0=PSB[bkB][:, 0:aw],
                in1=B1BC[:, DI + j * 512:DI + j * 512 + aw], op=ALU.add),
                waits=tmax(ta), inc=sV)
            tok_bank_read[j % 3] = tg
            ge = p.emit("scalar", lambda sc, _i, aw=aw: sc.activation(
                out=GEB[:, 0:aw], in_=GSB[:, 0:aw], func=AF.Gelu),
                waits=tmax(tg, t_id if (t == 0 and j == 0) else None), inc=sS)
            th = p.emit("vector", lambda v, _i, j=j, aw=aw: v.tensor_tensor(
                out=HB[:, j * 512:j * 512 + aw], in0=ASB[:, 0:aw],
                in1=GEB[:, 0:aw], op=ALU.mult),
                waits=tmax(ge), inc=sV)
            th_tok[j] = th
            prev_th = th
        ht_ev = None
        for fc in range(NFC):
            cw = min(128, DI - fc * 128)
            bk = fc % 2
            trh = p.emit("tensor", lambda te, _i, fc=fc, cw=cw, bk=bk: te.matmul(
                PSB[bk][:, 0:64].bitcast(BF16)[0:cw, :],
                HB[:, fc * 128:fc * 128 + cw], ident[:],
                is_transpose=True, start=True, stop=True),
                waits=tmax(th_tok[min(fc // 4, 5)], prev_ht_evac[bk],
                           th_tok[5] if fc == 0 else None), inc=sT)
            ht_ev = p.emit("vector", lambda v, _i, fc=fc, cw=cw, bk=bk: v.tensor_copy(
                HTB[0:cw, fc * 128:(fc + 1) * 128],
                PSB[bk][:, 0:64].bitcast(BF16)[0:cw, :]),
                waits=tmax(trh), inc=sV)
            prev_ht_evac[bk] = ht_ev
        w2_last = None
        for fc in range(NFC):
            cw = min(128, DI - fc * 128)
            for hf in range(2):
                w = []
                if fc == 0:
                    w = [ht_ev, prev_w2_read]
                w2_last = p.emit("tensor", lambda te, _i, fc=fc, cw=cw, hf=hf: te.matmul(
                    PSB[6 + hf][:, 0:512],
                    HTB[0:cw, fc * 128:(fc + 1) * 128],
                    W2B[0:cw, fc * 1024 + hf * 512:fc * 1024 + (hf + 1) * 512],
                    start=(fc == 0), stop=(fc == NFC - 1)),
                    waits=tmax(*w), inc=sT)
        to2 = None
        for hf in range(2):
            to1 = p.emit("vector", lambda v, _i, outb=outb, hf=hf: v.tensor_tensor(
                out=outb[:, hf * 512:(hf + 1) * 512],
                in0=PSB[6 + hf][:, 0:512],
                in1=B2BC[:, hf * 512:(hf + 1) * 512], op=ALU.add),
                waits=tmax(w2_last, tok_m, prev_out_read), inc=sV)
            to2 = p.emit("vector", lambda v, _i, outb=outb, frb=frb, hf=hf: v.tensor_add(
                outb[:, hf * 512:(hf + 1) * 512],
                outb[:, hf * 512:(hf + 1) * 512],
                frb[:, hf * 512:(hf + 1) * 512]),
                waits=tmax(to1, ld), inc=sV)
        prev_w2_read = to2
        tok_bank_read[2] = to2
        wFR[b] = to2
        prev_out_read = p.emit("sync", lambda s, _i, t=t, outb=outb: s.dma_start(
            zout[t * 128:(t + 1) * 128, :], outb),
            waits=tmax(to2), inc=sZO[b], inc_by=16)
        prev_store[b] = prev_out_read
    return tmax(prev_store[0], prev_store[1])


def build_p6chain():
    from types import SimpleNamespace
    nc = bass.Bass("TRN2", target_bir_lowering=False)
    zin = nc.dram_tensor("zin", [T, D], F32, kind="ExternalInput").ap()
    iwqkv = nc.dram_tensor("iwqkv", [IDEPTH, D, I3], F32, kind="ExternalInput").ap()
    iqs = nc.dram_tensor("iqs", [IDEPTH, DH], F32, kind="ExternalInput").ap()
    iks = nc.dram_tensor("iks", [IDEPTH, DH], F32, kind="ExternalInput").ap()
    iwo = nc.dram_tensor("iwo", [IDEPTH, D, D], F32, kind="ExternalInput").ap()
    ilng = nc.dram_tensor("ilng", [IDEPTH, D], F32, kind="ExternalInput").ap()
    ilnb = nc.dram_tensor("ilnb", [IDEPTH, D], F32, kind="ExternalInput").ap()
    iw1 = nc.dram_tensor("iw1", [IDEPTH, D, DI2], F32, kind="ExternalInput").ap()
    ib1 = nc.dram_tensor("ib1", [IDEPTH, DI2], F32, kind="ExternalInput").ap()
    iw2 = nc.dram_tensor("iw2", [IDEPTH, DI, D], F32, kind="ExternalInput").ap()
    ib2 = nc.dram_tensor("ib2", [IDEPTH, D], F32, kind="ExternalInput").ap()
    ropeI = nc.dram_tensor("ropeI", [4, T, DH], F32, kind="ExternalInput").ap()
    maskI = nc.dram_tensor("maskI", [128, 384], F32, kind="ExternalInput").ap()
    zA = nc.dram_tensor("zA", [T, D], F32, kind="Internal").ap()
    zB = nc.dram_tensor("zB", [T, D], F32, kind="Internal").ap()
    zout = nc.dram_tensor("zout", [T, D], F32, kind="ExternalOutput").ap()

    with ExitStack() as ctx:
        p = Prog(nc, ctx)
        E = SimpleNamespace()
        E.ident = ctx.enter_context(nc.sbuf_tensor("ident", [128, 128], BF16))
        io6 = ctx.enter_context(nc.sbuf_tensor("io6", [128, 128], I32))
        E.SM6 = ctx.enter_context(nc.sbuf_tensor("SM6", [128, 64], F32))
        AR = ctx.enter_context(nc.sbuf_tensor("AR6", [128, 211712], U8))
        cur = [0]
        def al(nelem, dt):
            sz = {F32: 4, BF16: 2, I32: 4, U8: 1}[dt]
            off = (cur[0] + 63) // 64 * 64
            cur[0] = off + nelem * sz
            return AR[:, off:off + nelem * sz].bitcast(dt)
        E.MIB = al(384, F32)
        E.WQK = al(8 * I3, BF16); E.WOB = al(8 * 1024, BF16)
        E.QSB = al(64, F32); E.KSB = al(64, F32)
        E.RTB = al(2 * 4 * 64, F32)
        E.SH = al(2 * 1024, F32); E.SHB = al(2 * 1024, BF16)
        E.XR = al(2 * 1024, F32); E.XST = al(1024, BF16)
        E.QN = al(1024, F32); E.SQ6 = al(1024, F32); E.TMP6 = al(1024, F32)
        E.QB = al(1024, BF16); E.KB = al(1024, BF16)
        E.QT = al(1024, BF16); E.KT = al(3 * 1024, BF16); E.VR = al(3 * 1024, BF16)
        E.S5 = al(384, F32); E.PR = al(384, BF16); E.PT = al(384, BF16)
        E.OB = al(1024, BF16); E.OFM = al(1024, BF16); E.CT = al(2 * 1024, F32)
        attn_end = cur[0]
        cur[0] = 0
        E.W1B = al(8 * DI2, BF16); E.W2B = al(22 * 1024, BF16)
        E.B1BC = al(DI2, F32); E.GBC = al(D, F32); E.BBC = al(D, F32)
        E.B2BC = al(D, F32); E.F2 = al(2 * 1024, F32); E.FR = al(2 * 1024, F32)
        E.XNS = al(1024, F32); E.XNB = al(1024, BF16); E.XNT6 = al(1024, BF16)
        E.ASB = al(512, F32); E.GSB = al(512, F32); E.GEB = al(512, BF16)
        E.HB = al(DI, BF16); E.HTB = al(22 * 128, BF16); E.OUTB = al(1024, F32)
        assert max(attn_end, cur[0]) <= 211712, (attn_end, cur[0])
        E.PSB = [ctx.enter_context(nc.psum_tensor(f"psb{i}", [128, 512], F32))
                 for i in range(8)]

        E.sW = p.sem("sW"); E.sM = p.sem("sM")
        E.sLD = [p.sem("sLDa"), p.sem("sLDb")]
        E.sR6 = [p.sem("sR6a"), p.sem("sR6b")]
        E.sFD = [p.sem("sFDa"), p.sem("sFDb")]
        E.sZO = [p.sem("sZOa"), p.sem("sZOb")]
        E.sV = p.sem("sV"); E.sS = p.sem("sS"); E.sT = p.sem("sT")
        sGC = p.sem("sGC")

        t_clr = [p.emit("vector", lambda v, _i, b_=b_: v.memset(E.PSB[b_][:], 0.0),
                        inc=E.sV) for b_ in range(8)]
        clr = tmax(*t_clr)
        t_io = p.emit("gpsimd", lambda g, _i: g.iota(
            io6[:], [[1, 128]], channel_multiplier=-1,
            allow_small_or_imprecise_dtypes=True), inc=sGC)
        E.t_id = p.emit("vector", lambda v, _i: v.tensor_scalar(
            out=E.ident[:], in0=io6[:], scalar1=0, scalar2=None, op0=ALU.is_equal),
            waits=tmax(t_io, clr), inc=E.sV)
        E.maskI = maskI

        cur_z = zin
        other = {id(zA): zB, id(zB): zA}
        last = None
        for i in range(IDEPTH):
            nxt = zA if cur_z is not zA else zB
            emit_p6a_sec(p, E, cur_z, nxt, iwqkv[i], iwo[i], iqs[i], iks[i], ropeI)
            cur_z = nxt
            nxt = zout if i == IDEPTH - 1 else (zA if cur_z is not zA else zB)
            last = emit_p6f_sec(p, E, cur_z, nxt, iw1[i], ib1[i], iw2[i], ib2[i],
                                ilng[i], ilnb[i])
            cur_z = nxt
        p.emit("sync", lambda s, _i: s.nop(), waits=last)
        p.emit("gpsimd", lambda g, _i: g.nop(), waits=last)
        p.finalize()
    return nc


def build_p6a():
    nc = bass.Bass("TRN2", target_bir_lowering=False)
    zin = nc.dram_tensor("zin", [T, D], F32, kind="ExternalInput").ap()
    wqkv = nc.dram_tensor("wqkv", [D, I3], F32, kind="ExternalInput").ap()
    wo = nc.dram_tensor("wo", [D, D], F32, kind="ExternalInput").ap()
    qs = nc.dram_tensor("qs", [DH], F32, kind="ExternalInput").ap()
    ks = nc.dram_tensor("ks", [DH], F32, kind="ExternalInput").ap()
    ropeI = nc.dram_tensor("ropeI", [4, T, DH], F32, kind="ExternalInput").ap()
    maskI = nc.dram_tensor("maskI", [128, 384], F32, kind="ExternalInput").ap()
    zout = nc.dram_tensor("zout", [T, D], F32, kind="ExternalOutput").ap()

    with ExitStack() as ctx:
        p = Prog(nc, ctx)
        sb = lambda n, sh, dt: ctx.enter_context(nc.sbuf_tensor(n, sh, dt))
        WQK = sb("WQK", [128, 8 * I3], BF16)
        WOB = sb("WOB", [128, 8 * 1024], BF16)
        QSB = sb("QSB", [128, 64], F32)
        KSB = sb("KSB", [128, 64], F32)
        MIB = sb("MIB", [128, 384], F32)
        RTB = sb("RTB", [128, 2 * 4 * 64], F32)
        SH = sb("SH", [128, 2 * 1024], F32)
        SHB = sb("SHB", [128, 2 * 1024], BF16)
        XR = sb("XR", [128, 2 * 1024], F32)
        XST = sb("XST", [128, 1024], BF16)
        QN = sb("QN", [128, 1024], F32)
        SQ6 = sb("SQ6", [128, 1024], F32)
        TMP6 = sb("TMP6", [128, 1024], F32)
        SM6 = sb("SM6", [128, 64], F32)
        QB = sb("QB", [128, 1024], BF16)
        KB = sb("KB", [128, 1024], BF16)
        QT = sb("QT", [128, 1024], BF16)
        KT = sb("KT", [128, 3 * 1024], BF16)
        VR = sb("VR", [128, 3 * 1024], BF16)
        S5 = sb("S5", [128, 384], F32)
        PR = sb("PR", [128, 384], BF16)
        PT = sb("PT", [128, 384], BF16)
        OB = sb("OB", [128, 1024], BF16)
        OFM = sb("OFM", [128, 1024], BF16)
        CT = sb("CT", [128, 2 * 1024], F32)
        ident = sb("ident", [128, 128], BF16)
        io6 = sb("io6", [128, 128], I32)
        PSB = [ctx.enter_context(nc.psum_tensor(f"psb{i}", [128, 512], F32))
               for i in range(8)]
        RS16a = SM6[:, 0:16]
        RS16b = SM6[:, 16:32]
        NM6 = SM6[:, 32:33]
        DN6 = SM6[:, 33:34]

        import dataclasses as _dc

        def shuf64(ap):
            return _dc.replace(ap, offset=ap.offset + 32,
                               ap=[[ap.ap[0][0], 128], [64, 16], [-32, 2], [1, 32]])

        sW = p.sem("sW")
        sM = p.sem("sM")
        sLD = [p.sem("sLDa"), p.sem("sLDb")]
        sR6 = [p.sem("sR6a"), p.sem("sR6b")]
        sZO = [p.sem("sZOa"), p.sem("sZOb")]
        sV = p.sem("sV"); sS = p.sem("sS"); sT = p.sem("sT"); sGC = p.sem("sGC")

        t_clr = [p.emit("vector", lambda v, _i, b_=b_: v.memset(PSB[b_][:], 0.0), inc=sV)
                 for b_ in range(8)]
        t_clr.append(p.emit("vector", lambda v, _i: v.memset(SH[:], 0.0), inc=sV))
        clr = tmax(*t_clr)
        t_io = p.emit("gpsimd", lambda g, _i: g.iota(io6[:], [[1, 128]], channel_multiplier=-1,
                                                     allow_small_or_imprecise_dtypes=True), inc=sGC)
        t_id = p.emit("vector", lambda v, _i: v.tensor_scalar(
            out=ident[:], in0=io6[:], scalar1=0, scalar2=None, op0=ALU.is_equal),
            waits=tmax(t_io), inc=sV)
        wl = []
        for dc in range(DC):
            wl.append(p.emit("gpsimd", lambda g, _i, dc=dc: g.dma_start(
                WQK[:, dc * I3:(dc + 1) * I3], wqkv[dc * 128:(dc + 1) * 128, :]),
                inc=sW, inc_by=16))
        for dc in range(DC):
            wl.append(p.emit("gpsimd", lambda g, _i, dc=dc: g.dma_start(
                WOB[:, dc * 1024:(dc + 1) * 1024], wo[dc * 128:(dc + 1) * 128, :]),
                inc=sW, inc_by=16))
        tok_w = wl[-1]
        m1_ = p.emit("sync", lambda s, _i: s.dma_start(
            QSB[:], qs[None, :].broadcast_to([128, DH])), inc=sM, inc_by=16)
        m2_ = p.emit("sync", lambda s, _i: s.dma_start(
            KSB[:], ks[None, :].broadcast_to([128, DH])), inc=sM, inc_by=16)
        m3_ = p.emit("sync", lambda s, _i: s.dma_start(MIB[:], maskI), inc=sM, inc_by=16)
        tok_m = tmax(m3_)

        wSH = [clr, clr]
        wSHB = [tmax(t_id), tmax(t_id)]
        wXR = [clr, clr]
        wQNprev = tmax(t_id)
        wTMPprev = tmax(t_id)
        evac_prev = tmax(t_id)
        tok_b45_read = tmax(t_id)
        tok_b6_read = {6: tmax(t_id), 7: tmax(t_id)}
        prev_tex = tmax(t_id)
        prev_trp = tmax(t_id)
        prev_ct = tmax(t_id)
        prev_rt = [clr, clr]
        prev_store = [clr, clr]
        tok_evq = None
        tok_evk = [None] * NT
        tok_vcp = [None] * NT

        for t in range(NT):
            b = t % 2
            k3 = t % 3
            shb = SH[:, b * 1024:(b + 1) * 1024]
            shbb = SHB[:, b * 1024:(b + 1) * 1024]
            xrb = XR[:, b * 1024:(b + 1) * 1024]
            l1 = p.emit("sync", lambda s, _i, t=t, shb=shb: s.dma_start(
                shb[:, 0:512], zin[t * 128:(t + 1) * 128, 0:512]),
                waits=tmax(wSH[b]), inc=sLD[b], inc_by=16)
            if t == 0:
                l2 = p.emit("sync", lambda s, _i, shb=shb: s.dma_start(
                    shb[1:128, 512:1024], zin[0:127, 512:1024]), inc=sLD[b], inc_by=16)
            else:
                l2 = p.emit("sync", lambda s, _i, t=t, shb=shb: s.dma_start(
                    shb[:, 512:1024], zin[t * 128 - 1:(t + 1) * 128 - 1, 512:1024]),
                    inc=sLD[b], inc_by=16)
            l3 = p.emit("sync", lambda s, _i, t=t, xrb=xrb: s.dma_start(
                xrb, zin[t * 128:(t + 1) * 128, :]),
                waits=tmax(wXR[b]), inc=sLD[b], inc_by=16)
            ld = tmax(l1, l2, l3)
            rts = [RTB[:, (b * 4 + j) * 64:(b * 4 + j + 1) * 64] for j in range(4)]
            rl = []
            for j in range(4):
                rl.append(p.emit("sync", lambda s, _i, t=t, j=j, rts=rts: s.dma_start(
                    rts[j], ropeI[j, t * 128:(t + 1) * 128, :]),
                    waits=tmax(prev_rt[b]), inc=sR6[b], inc_by=16))
            t_rt = tmax(*rl)
            c1 = p.emit("vector", lambda v, _i, shbb=shbb, shb=shb: v.tensor_copy(shbb, shb),
                        waits=tmax(ld, wSHB[b], clr), inc=sV)
            wSH[b] = c1
            trl = None
            for c in range(DC):
                trl = p.emit("tensor", lambda te, _i, c=c, shbb=shbb: te.matmul(
                    PSB[0][:, c * 64:c * 64 + 64].bitcast(BF16),
                    shbb[:, c * 128:(c + 1) * 128], ident[:],
                    is_transpose=True, start=True, stop=True),
                    waits=tmax(c1, t_id, evac_prev if c == 0 else None), inc=sT)
            wSHB[b] = trl
            evl = None
            for c in range(DC):
                evl = p.emit("vector", lambda v, _i, c=c: v.tensor_copy(
                    XST[:, c * 128:(c + 1) * 128],
                    PSB[0][:, c * 64:c * 64 + 64].bitcast(BF16)),
                    waits=tmax(trl), inc=sV)
            evac_prev = evl
            for wi, wn in enumerate(("q", "k", "v")):
                mm = None
                for dc in range(DC):
                    for hf in range(2):
                        w = [evl, tok_b45_read] if dc == 0 else []
                        if t == 0 and wi == 0 and dc == 0:
                            w.append(tok_w)
                        mm = p.emit("tensor", lambda te, _i, wi=wi, dc=dc, hf=hf: te.matmul(
                            PSB[4 + hf][:, 0:512],
                            XST[:, dc * 128:(dc + 1) * 128],
                            WQK[:, dc * I3 + wi * 1024 + hf * 512:
                                dc * I3 + wi * 1024 + (hf + 1) * 512],
                            start=(dc == 0), stop=(dc == DC - 1)),
                            waits=tmax(*w), inc=sT)
                if wn == "v":
                    lastv = None
                    for hf in range(2):
                        lastv = p.emit("vector", lambda v, _i, k3=k3, hf=hf: v.tensor_copy(
                            VR[:, k3 * 1024 + hf * 512:k3 * 1024 + (hf + 1) * 512],
                            PSB[4 + hf][:, 0:512]),
                            waits=tmax(mm), inc=sV)
                    tok_vcp[t] = lastv
                    tok_b45_read = lastv
                else:
                    qn = None
                    for hf in range(2):
                        qn = p.emit("vector", lambda v, _i, hf=hf: v.tensor_copy(
                            QN[:, hf * 512:(hf + 1) * 512], PSB[4 + hf][:, 0:512]),
                            waits=tmax(mm, wQNprev), inc=sV)
                    tok_b45_read = qn
                    t_sq = p.emit("scalar", lambda sc, _i: sc.activation(
                        out=SQ6[:], in_=QN[:], func=AF.Square), waits=tmax(qn), inc=sS)
                    t_rs = p.emit("vector", lambda v, _i: v.tensor_reduce(
                        RS16a, SQ6.rearrange("p (h c) -> p h c", c=64), axis=AX,
                        op=ALU.add), waits=tmax(t_sq), inc=sV, fence=True)
                    t_sr = p.emit("scalar", lambda sc, _i: sc.activation(
                        out=RS16b, in_=RS16a, func=AF.Sqrt), waits=tmax(t_rs), inc=sS,
                        fence=True)
                    t_mx = p.emit("vector", lambda v, _i: v.tensor_scalar(
                        out=RS16b, in0=RS16b, scalar1=1.0e-12, scalar2=None,
                        op0=ALU.max), waits=tmax(t_sr), inc=sV, fence=True)
                    t_rc = p.emit("vector", lambda v, _i: v.reciprocal(RS16b, RS16b),
                                  waits=tmax(t_mx), inc=sV, fence=True)
                    scb = QSB if wn == "q" else KSB
                    m1 = p.emit("vector", lambda v, _i: v.tensor_tensor(
                        out=QN.rearrange("p (h c) -> p h c", c=64),
                        in0=QN.rearrange("p (h c) -> p h c", c=64),
                        in1=RS16b[:, :, None].broadcast_to([128, 16, 64]),
                        op=ALU.mult), waits=tmax(t_rc), inc=sV)
                    m2 = p.emit("vector", lambda v, _i, scb=scb: v.tensor_tensor(
                        out=QN.rearrange("p (h c) -> p h c", c=64),
                        in0=QN.rearrange("p (h c) -> p h c", c=64),
                        in1=scb[:, None, :].broadcast_to([128, 16, 64]),
                        op=ALU.mult), waits=tmax(m1, tok_m), inc=sV)
                    j0 = 0 if wn == "q" else 2
                    dstb = QB if wn == "q" else KB
                    t2 = p.emit("vector", lambda v, _i, rts=rts, j0=j0: v.tensor_tensor(
                        out=TMP6.rearrange("p (h c) -> p h c", c=64),
                        in0=shuf64(QN[:]),
                        in1=rts[j0 + 1][:, None, :].broadcast_to([128, 16, 64]),
                        op=ALU.mult), waits=tmax(m2, t_rt, wTMPprev), inc=sV)
                    t3 = p.emit("vector", lambda v, _i, rts=rts, j0=j0: v.tensor_tensor(
                        out=QN.rearrange("p (h c) -> p h c", c=64),
                        in0=QN.rearrange("p (h c) -> p h c", c=64),
                        in1=rts[j0][:, None, :].broadcast_to([128, 16, 64]),
                        op=ALU.mult), waits=tmax(t2), inc=sV)
                    t4 = p.emit("vector", lambda v, _i, dstb=dstb: v.tensor_add(
                        dstb[:], QN[:], TMP6[:]), waits=tmax(t3), inc=sV)
                    wQNprev = t4
                    wTMPprev = t4
                    dfm = QT[:] if wn == "q" else KT[:, k3 * 1024:(k3 + 1) * 1024]
                    trq = None
                    for c in range(DC):
                        trq = p.emit("tensor", lambda te, _i, c=c, dstb=dstb: te.matmul(
                            PSB[6][:, c * 64:c * 64 + 64].bitcast(BF16),
                            dstb[:, c * 128:(c + 1) * 128], ident[:],
                            is_transpose=True, start=True, stop=True),
                            waits=tmax(t4, tok_b6_read[6] if c == 0 else None), inc=sT)
                    evq = None
                    for c in range(DC):
                        evq = p.emit("vector", lambda v, _i, c=c, dfm=dfm: v.tensor_copy(
                            dfm[:, c * 128:(c + 1) * 128],
                            PSB[6][:, c * 64:c * 64 + 64].bitcast(BF16)),
                            waits=tmax(trq), inc=sV)
                    tok_b6_read[6] = evq
                    if wn == "q":
                        tok_evq = evq
                    else:
                        tok_evk[t] = evq
            prev_rt[b] = wTMPprev
            nk = min(t + 1, 3)
            W_ = 128 * nk
            pv_done = []
            for h in range(NH):
                sbk = 6 + (h % 2)
                smms = []
                for ki in range(nk):
                    kt = t - (nk - 1) + ki
                    ksl = kt % 3
                    w = []
                    if ki == 0:
                        w = [tok_evq, tok_evk[kt], tok_vcp[kt], tok_b6_read[sbk]]
                    smms.append(p.emit("tensor", lambda te, _i, h=h, ki=ki, ksl=ksl, sbk=sbk: te.matmul(
                        PSB[sbk][:, ki * 128:(ki + 1) * 128],
                        QT[(h % 2) * 64:(h % 2) * 64 + 64,
                           (h // 2) * 128:(h // 2) * 128 + 128],
                        KT[(h % 2) * 64:(h % 2) * 64 + 64,
                           ksl * 1024 + (h // 2) * 128:ksl * 1024 + (h // 2) * 128 + 128],
                        start=True, stop=True),
                        waits=tmax(*w), inc=sT))
                tmsk = p.emit("vector", lambda v, _i, W_=W_, nk=nk, sbk=sbk: v.tensor_tensor(
                    out=S5[:, 0:W_], in0=PSB[sbk][:, 0:W_],
                    in1=MIB[:, (3 - nk) * 128:384], op=ALU.add),
                    waits=tmax(*smms, tok_m, prev_tex), inc=sV)
                tok_b6_read[sbk] = tmsk
                tmx = p.emit("vector", lambda v, _i, W_=W_: v.reduce_max(
                    NM6, S5[:, 0:W_], axis=AX, negate=True),
                    waits=tmax(tmsk), inc=sV, fence=True)
                tex = p.emit("scalar", lambda sc, _i, W_=W_: sc.activation(
                    out=PR[:, 0:W_], in_=S5[:, 0:W_], func=AF.Exp,
                    bias=NM6, accum_out=DN6),
                    waits=tmax(tmx, prev_trp), inc=sS, fence=True)
                prev_tex = tex
                trc = p.emit("vector", lambda v, _i: v.reciprocal(DN6, DN6),
                             waits=tmax(tex), inc=sV, fence=True)
                tnr = p.emit("vector", lambda v, _i, W_=W_: v.tensor_scalar(
                    out=PR[:, 0:W_], in0=PR[:, 0:W_], scalar1=DN6,
                    scalar2=None, op0=ALU.mult), waits=tmax(trc), inc=sV)
                trp = []
                for ki in range(nk):
                    trp.append(p.emit("tensor", lambda te, _i, ki=ki: te.matmul(
                        PSB[5][:, ki * 64:ki * 64 + 64].bitcast(BF16),
                        PR[:, ki * 128:(ki + 1) * 128], ident[:],
                        is_transpose=True, start=True, stop=True),
                        waits=tmax(tnr), inc=sT))
                prev_trp = trp[-1]
                tpt = []
                for ki in range(nk):
                    tpt.append(p.emit("vector", lambda v, _i, ki=ki: v.tensor_copy(
                        PT[:, ki * 128:(ki + 1) * 128],
                        PSB[5][:, ki * 64:ki * 64 + 64].bitcast(BF16)),
                        waits=tmax(trp[ki]), inc=sV))
                last = None
                for ki in range(nk):
                    kt = t - (nk - 1) + ki
                    ksl = kt % 3
                    w = [tpt[ki]]
                    if h < 2 and ki == 0:
                        w.append(prev_ct)
                    last = p.emit("tensor", lambda te, _i, h=h, ki=ki, ksl=ksl, nk=nk: te.matmul(
                        PSB[2 if h < 8 else 3][:, (h % 8) * 64:(h % 8) * 64 + 64],
                        PT[:, ki * 128:(ki + 1) * 128],
                        VR[:, ksl * 1024 + h * 64:ksl * 1024 + h * 64 + 64],
                        start=(ki == 0), stop=(ki == nk - 1)),
                        waits=tmax(*w), inc=sT)
                pv_done.append(last)
            tob = []
            for hb in range(2):
                tob.append(p.emit("vector", lambda v, _i, hb=hb: v.tensor_copy(
                    OB[:, hb * 512:(hb + 1) * 512], PSB[2 + hb][:, 0:512]),
                    waits=tmax(*pv_done), inc=sV))
            trw = None
            for c in range(DC):
                trw = p.emit("tensor", lambda te, _i, c=c: te.matmul(
                    PSB[4 + (c % 2)][:, (c // 2) * 128:(c // 2) * 128 + 64].bitcast(BF16),
                    OB[:, c * 128:(c + 1) * 128], ident[:],
                    is_transpose=True, start=True, stop=True),
                    waits=tmax(*tob), inc=sT)
            tofm = None
            for c in range(DC):
                tofm = p.emit("vector", lambda v, _i, c=c: v.tensor_copy(
                    OFM[:, c * 128:(c + 1) * 128],
                    PSB[4 + (c % 2)][:, (c // 2) * 128:(c // 2) * 128 + 64].bitcast(BF16)),
                    waits=tmax(trw), inc=sV)
            tok_b45_read = tofm
            wo_last = None
            for dc in range(DC):
                for hf in range(2):
                    w = [tofm]
                    if dc == 0 and hf == 0:
                        w.append(prev_ct)
                    wo_last = p.emit("tensor", lambda te, _i, dc=dc, hf=hf: te.matmul(
                        PSB[2 + hf][:, 0:512],
                        OFM[:, dc * 128:(dc + 1) * 128],
                        WOB[:, dc * 1024 + hf * 512:dc * 1024 + (hf + 1) * 512],
                        start=(dc == 0), stop=(dc == DC - 1)),
                        waits=tmax(*w), inc=sT)
            ctb = CT[:, b * 1024:(b + 1) * 1024]
            tct = None
            for hf in range(2):
                tct = p.emit("vector", lambda v, _i, ctb=ctb, hf=hf, xrb=xrb: v.tensor_tensor(
                    out=ctb[:, hf * 512:(hf + 1) * 512],
                    in0=PSB[2 + hf][:, 0:512],
                    in1=xrb[:, hf * 512:(hf + 1) * 512], op=ALU.add),
                    waits=tmax(wo_last, ld, prev_store[b]), inc=sV)
            prev_ct = tct
            wXR[b] = tct
            prev_store[b] = p.emit("sync", lambda s, _i, t=t, ctb=ctb: s.dma_start(
                zout[t * 128:(t + 1) * 128, :], ctb),
                waits=tmax(tct), inc=sZO[b], inc_by=16)
        p.emit("sync", lambda s, _i: s.nop(), waits=tmax(prev_store[0], prev_store[1]))
        p.emit("gpsimd", lambda g, _i: g.nop(), waits=tmax(prev_store[0], prev_store[1]))
        p.finalize()
    return nc



def build_p6f():
    nc = bass.Bass("TRN2", target_bir_lowering=False)
    zin = nc.dram_tensor("zin", [T, D], F32, kind="ExternalInput").ap()
    w1 = nc.dram_tensor("w1", [D, DI2], F32, kind="ExternalInput").ap()
    b1 = nc.dram_tensor("b1", [DI2], F32, kind="ExternalInput").ap()
    w2 = nc.dram_tensor("w2", [DI, D], F32, kind="ExternalInput").ap()
    b2 = nc.dram_tensor("b2", [D], F32, kind="ExternalInput").ap()
    lng = nc.dram_tensor("lng", [D], F32, kind="ExternalInput").ap()
    lnb = nc.dram_tensor("lnb", [D], F32, kind="ExternalInput").ap()
    zout = nc.dram_tensor("zout", [T, D], F32, kind="ExternalOutput").ap()

    NFC = 22
    with ExitStack() as ctx:
        p = Prog(nc, ctx)
        sb = lambda n, sh, dt: ctx.enter_context(nc.sbuf_tensor(n, sh, dt))
        W1B = sb("W1B", [128, 8 * DI2], BF16)
        W2B = sb("W2B", [128, NFC * 1024], BF16)
        B1BC = sb("B1BC", [128, DI2], F32)
        GBC = sb("GBC", [128, D], F32)
        BBC = sb("BBC", [128, D], F32)
        B2BC = sb("B2BC", [128, D], F32)
        F2 = sb("F2", [128, 2 * 1024], F32)
        FR = sb("FR", [128, 2 * 1024], F32)
        XNS = sb("XNS", [128, 1024], F32)
        XNB = sb("XNB", [128, 1024], BF16)
        XNT6 = sb("XNT6", [128, 1024], BF16)
        ASB = sb("ASB", [128, 512], F32)
        GSB = sb("GSB", [128, 512], F32)
        GEB = sb("GEB", [128, 512], BF16)
        HB = sb("HB", [128, DI], BF16)
        HTB = sb("HTB", [128, NFC * 128], BF16)
        OUTB = sb("OUTB", [128, 1024], F32)
        SM6 = sb("SM6", [128, 64], F32)
        ident = sb("ident", [128, 128], BF16)
        io6 = GEB[:, 0:256].bitcast(I32)
        PSB = [ctx.enter_context(nc.psum_tensor(f"psb{i}", [128, 512], F32))
               for i in range(8)]

        sW = p.sem("sW")
        sM = p.sem("sM")
        sFD = [p.sem("sFDa"), p.sem("sFDb")]
        sZO = [p.sem("sZOa"), p.sem("sZOb")]
        sV = p.sem("sV"); sS = p.sem("sS"); sT = p.sem("sT"); sGC = p.sem("sGC")

        t_clr = [p.emit("vector", lambda v, _i, b_=b_: v.memset(PSB[b_][:], 0.0), inc=sV)
                 for b_ in range(8)]
        t_clr.append(p.emit("vector", lambda v, _i: v.memset(F2[:], 0.0), inc=sV))
        clr = tmax(*t_clr)
        t_io = p.emit("gpsimd", lambda g, _i: g.iota(io6[:], [[1, 128]], channel_multiplier=-1,
                                                     allow_small_or_imprecise_dtypes=True), inc=sGC)
        t_id = p.emit("vector", lambda v, _i: v.tensor_scalar(
            out=ident[:], in0=io6[:], scalar1=0, scalar2=None, op0=ALU.is_equal),
            waits=tmax(t_io), inc=sV)
        wl = []
        for dc in range(DC):
            wl.append(p.emit("gpsimd", lambda g, _i, dc=dc: g.dma_start(
                W1B[:, dc * DI2:(dc + 1) * DI2], w1[dc * 128:(dc + 1) * 128, :]),
                inc=sW, inc_by=16))
        for fc in range(NFC):
            cw = min(128, DI - fc * 128)
            wl.append(p.emit("gpsimd", lambda g, _i, fc=fc, cw=cw: g.dma_start(
                W2B[0:cw, fc * 1024:(fc + 1) * 1024], w2[fc * 128:fc * 128 + cw, :]),
                inc=sW, inc_by=16))
        tok_w = wl[-1]
        ml = []
        ml.append(p.emit("sync", lambda s, _i: s.dma_start(
            B1BC[:], b1[None, :].broadcast_to([128, DI2])), inc=sM, inc_by=16))
        ml.append(p.emit("sync", lambda s, _i: s.dma_start(
            GBC[:], lng[None, :].broadcast_to([128, D])), inc=sM, inc_by=16))
        ml.append(p.emit("sync", lambda s, _i: s.dma_start(
            BBC[:], lnb[None, :].broadcast_to([128, D])), inc=sM, inc_by=16))
        ml.append(p.emit("sync", lambda s, _i: s.dma_start(
            B2BC[:], b2[None, :].broadcast_to([128, D])), inc=sM, inc_by=16))
        tok_m = ml[-1]

        EPSB = SM6[:, 40:41]
        t_eps = p.emit("vector", lambda v, _i: v.memset(EPSB, 1.0e-5), inc=sV)

        wF2 = [clr, clr]
        wFR = [clr, clr]
        wXNS = tmax(t_id)
        wXNB = tmax(t_id)
        evac_prev = tmax(t_id)
        tok_bank_read = [tmax(t_id)] * 3
        prev_th = tmax(t_id)
        prev_ht_evac = [tmax(t_id), tmax(t_id)]
        prev_w2_read = tmax(t_id)
        prev_store = [clr, clr]
        prev_out_read = clr

        for t in range(NT):
            b = t % 2
            f2b = F2[:, b * 1024:(b + 1) * 1024]
            frb = FR[:, b * 1024:(b + 1) * 1024]
            outb = OUTB[:]
            f1 = p.emit("sync", lambda s, _i, t=t, f2b=f2b: s.dma_start(
                f2b[:, 0:512], zin[t * 128:(t + 1) * 128, 0:512]),
                waits=tmax(wF2[b]), inc=sFD[b], inc_by=16)
            if t == 0:
                f2_ = p.emit("sync", lambda s, _i, f2b=f2b: s.dma_start(
                    f2b[1:128, 512:1024], zin[0:127, 512:1024]), inc=sFD[b], inc_by=16)
            else:
                f2_ = p.emit("sync", lambda s, _i, t=t, f2b=f2b: s.dma_start(
                    f2b[:, 512:1024], zin[t * 128 - 1:(t + 1) * 128 - 1, 512:1024]),
                    inc=sFD[b], inc_by=16)
            f3 = p.emit("sync", lambda s, _i, t=t, frb=frb: s.dma_start(
                frb, zin[t * 128:(t + 1) * 128, :]),
                waits=tmax(wFR[b]), inc=sFD[b], inc_by=16)
            ld = tmax(f1, f2_, f3)
            meanraw = SM6[:, 0:1]; sqs = SM6[:, 1:2]; var0 = SM6[:, 2:3]
            rstd = SM6[:, 3:4]; mean = SM6[:, 4:5]; msq = SM6[:, 5:6]
            t_m0 = p.emit("vector", lambda v, _i, f2b=f2b, meanraw=meanraw: v.reduce_sum(
                meanraw, f2b, axis=AX), waits=tmax(ld), inc=sV, fence=True)
            t_s0 = p.emit("scalar", lambda sc, _i, f2b=f2b, sqs=sqs: sc.activation(
                out=XNS[:], in_=f2b, func=AF.Square, accum_out=sqs),
                waits=tmax(ld, wXNS), inc=sS, fence=True)
            p.emit("vector", lambda v, _i, mean=mean, meanraw=meanraw: v.tensor_scalar(
                out=mean, in0=meanraw, scalar1=1.0 / D, scalar2=None, op0=ALU.mult),
                waits=tmax(t_m0), inc=sV)
            t_msq = p.emit("vector", lambda v, _i, msq=msq, meanraw=meanraw: v.tensor_scalar(
                out=msq, in0=meanraw, scalar1=meanraw, scalar2=None, op0=ALU.mult),
                inc=sV, fence=True)
            t_v0 = p.emit("vector", lambda v, _i, var0=var0, sqs=sqs, msq=msq: v.scalar_tensor_tensor(
                out=var0, in0=sqs, scalar=float(D), in1=msq, op0=ALU.mult, op1=ALU.subtract),
                waits=tmax(t_s0, t_msq), inc=sV, fence=True)
            t_sq = p.emit("scalar", lambda sc, _i, var0=var0, rstd=rstd: sc.activation(
                out=rstd, in_=var0, func=AF.Sqrt, bias=EPSB, scale=1.0 / (D * D)),
                waits=tmax(t_v0, t_eps), inc=sS, fence=True)
            t_rc = p.emit("vector", lambda v, _i, rstd=rstd: v.reciprocal(rstd, rstd),
                          waits=tmax(t_sq), inc=sV, fence=True)
            t_xn = p.emit("vector", lambda v, _i, f2b=f2b, mean=mean, rstd=rstd: v.tensor_scalar(
                out=XNS[:], in0=f2b, scalar1=mean, scalar2=rstd,
                op0=ALU.subtract, op1=ALU.mult), waits=tmax(t_rc, t_s0), inc=sV)
            wF2[b] = tmax(t_xn, t_s0)
            t_xg = p.emit("vector", lambda v, _i: v.tensor_tensor(
                out=XNS[:], in0=XNS[:], in1=GBC[:], op=ALU.mult),
                waits=tmax(t_xn, tok_m), inc=sV)
            t_xb = p.emit("vector", lambda v, _i: v.tensor_add(
                XNB[:], XNS[:], BBC[:]), waits=tmax(t_xg, wXNB), inc=sV)
            wXNS = t_xb
            trl = None
            for c in range(DC):
                trl = p.emit("tensor", lambda te, _i, c=c: te.matmul(
                    PSB[0][:, c * 64:c * 64 + 64].bitcast(BF16),
                    XNB[:, c * 128:(c + 1) * 128], ident[:],
                    is_transpose=True, start=True, stop=True),
                    waits=tmax(t_xb, t_id, evac_prev if c == 0 else None), inc=sT)
            wXNB = trl
            evl = None
            for c in range(DC):
                evl = p.emit("vector", lambda v, _i, c=c: v.tensor_copy(
                    XNT6[:, c * 128:(c + 1) * 128],
                    PSB[0][:, c * 64:c * 64 + 64].bitcast(BF16)),
                    waits=tmax(trl), inc=sV)
            evac_prev = evl
            th_tok = [None] * 6
            for j in range(6):
                aw = 512 if j < 5 else 170
                bkA = 2 + (j % 3) * 2
                bkB = 3 + (j % 3) * 2
                mmA = None; mmB = None
                for dc in range(DC):
                    w = []
                    if dc == 0:
                        w = [evl, tok_bank_read[j % 3]]
                        if t == 0 and j == 0:
                            w.append(tok_w)
                    mmA = p.emit("tensor", lambda te, _i, j=j, dc=dc, aw=aw, bkA=bkA: te.matmul(
                        PSB[bkA][:, 0:aw],
                        XNT6[:, dc * 128:(dc + 1) * 128],
                        W1B[:, dc * DI2 + j * 512:dc * DI2 + j * 512 + aw],
                        start=(dc == 0), stop=(dc == DC - 1)),
                        waits=tmax(*w), inc=sT)
                for dc in range(DC):
                    mmB = p.emit("tensor", lambda te, _i, j=j, dc=dc, aw=aw, bkB=bkB: te.matmul(
                        PSB[bkB][:, 0:aw],
                        XNT6[:, dc * 128:(dc + 1) * 128],
                        W1B[:, dc * DI2 + DI + j * 512:dc * DI2 + DI + j * 512 + aw],
                        start=(dc == 0), stop=(dc == DC - 1)),
                        inc=sT)
                ta = p.emit("vector", lambda v, _i, j=j, aw=aw, bkA=bkA: v.tensor_tensor(
                    out=ASB[:, 0:aw], in0=PSB[bkA][:, 0:aw],
                    in1=B1BC[:, j * 512:j * 512 + aw], op=ALU.add),
                    waits=tmax(mmA, mmB, tok_m, prev_th), inc=sV)
                tg = p.emit("vector", lambda v, _i, j=j, aw=aw, bkB=bkB: v.tensor_tensor(
                    out=GSB[:, 0:aw], in0=PSB[bkB][:, 0:aw],
                    in1=B1BC[:, DI + j * 512:DI + j * 512 + aw], op=ALU.add),
                    waits=tmax(ta), inc=sV)
                tok_bank_read[j % 3] = tg
                ge = p.emit("scalar", lambda sc, _i, aw=aw: sc.activation(
                    out=GEB[:, 0:aw], in_=GSB[:, 0:aw], func=AF.Gelu),
                    waits=tmax(tg, t_id if (t == 0 and j == 0) else None), inc=sS)
                th = p.emit("vector", lambda v, _i, j=j, aw=aw: v.tensor_tensor(
                    out=HB[:, j * 512:j * 512 + aw], in0=ASB[:, 0:aw],
                    in1=GEB[:, 0:aw], op=ALU.mult),
                    waits=tmax(ge), inc=sV)
                th_tok[j] = th
                prev_th = th
            ht_ev = None
            for fc in range(NFC):
                cw = min(128, DI - fc * 128)
                bk = fc % 2
                trh = p.emit("tensor", lambda te, _i, fc=fc, cw=cw, bk=bk: te.matmul(
                    PSB[bk][:, 0:64].bitcast(BF16)[0:cw, :],
                    HB[:, fc * 128:fc * 128 + cw], ident[:],
                    is_transpose=True, start=True, stop=True),
                    waits=tmax(th_tok[min(fc // 4, 5)], prev_ht_evac[bk],
                               th_tok[5] if fc == 0 else None), inc=sT)
                ht_ev = p.emit("vector", lambda v, _i, fc=fc, cw=cw, bk=bk: v.tensor_copy(
                    HTB[0:cw, fc * 128:(fc + 1) * 128],
                    PSB[bk][:, 0:64].bitcast(BF16)[0:cw, :]),
                    waits=tmax(trh), inc=sV)
                prev_ht_evac[bk] = ht_ev
            w2_last = None
            for fc in range(NFC):
                cw = min(128, DI - fc * 128)
                for hf in range(2):
                    w = []
                    if fc == 0:
                        w = [ht_ev, prev_w2_read]
                    w2_last = p.emit("tensor", lambda te, _i, fc=fc, cw=cw, hf=hf: te.matmul(
                        PSB[6 + hf][:, 0:512],
                        HTB[0:cw, fc * 128:(fc + 1) * 128],
                        W2B[0:cw, fc * 1024 + hf * 512:fc * 1024 + (hf + 1) * 512],
                        start=(fc == 0), stop=(fc == NFC - 1)),
                        waits=tmax(*w), inc=sT)
            to2 = None
            for hf in range(2):
                to1 = p.emit("vector", lambda v, _i, outb=outb, hf=hf: v.tensor_tensor(
                    out=outb[:, hf * 512:(hf + 1) * 512],
                    in0=PSB[6 + hf][:, 0:512],
                    in1=B2BC[:, hf * 512:(hf + 1) * 512], op=ALU.add),
                    waits=tmax(w2_last, tok_m, prev_out_read), inc=sV)
                to2 = p.emit("vector", lambda v, _i, outb=outb, frb=frb, hf=hf: v.tensor_add(
                    outb[:, hf * 512:(hf + 1) * 512],
                    outb[:, hf * 512:(hf + 1) * 512],
                    frb[:, hf * 512:(hf + 1) * 512]),
                    waits=tmax(to1, ld), inc=sV)
            prev_w2_read = to2
            tok_bank_read[2] = to2
            wFR[b] = to2
            prev_out_read = p.emit("sync", lambda s, _i, t=t, outb=outb: s.dma_start(
                zout[t * 128:(t + 1) * 128, :], outb),
                waits=tmax(to2), inc=sZO[b], inc_by=16)
            prev_store[b] = prev_out_read
        p.emit("sync", lambda s, _i: s.nop(), waits=tmax(prev_store[0], prev_store[1]))
        p.emit("gpsimd", lambda g, _i: g.nop(), waits=tmax(prev_store[0], prev_store[1]))
        p.finalize()
    return nc



def build_p7s():
    nc = bass.Bass("TRN2", target_bir_lowering=False)
    x1 = nc.dram_tensor("x1", [T, D], F32, kind="ExternalInput").ap()
    a1 = nc.dram_tensor("a1", [T, D], F32, kind="ExternalInput").ap()
    z6 = nc.dram_tensor("z6", [T, D], F32, kind="ExternalInput").ap()
    y = nc.dram_tensor("y", [T, D], F32, kind="ExternalOutput").ap()
    with ExitStack() as ctx:
        p = Prog(nc, ctx)
        X7 = ctx.enter_context(nc.sbuf_tensor("X7", [128, 2 * 3 * 1024], F32))
        sL7 = [p.sem("sL7a"), p.sem("sL7b")]
        sY7 = [p.sem("sY7a"), p.sem("sY7b")]
        sV = p.sem("sV")
        prev7 = [None, None]
        for t in range(NT):
            b = t % 2
            xb = X7[:, b * 3072:b * 3072 + 1024]
            ab = X7[:, b * 3072 + 1024:b * 3072 + 2048]
            zb = X7[:, b * 3072 + 2048:b * 3072 + 3072]
            l1 = p.emit("sync", lambda s, _i, t=t, xb=xb: s.dma_start(
                xb, x1[t * 128:(t + 1) * 128, :]), waits=tmax(prev7[b]),
                inc=sL7[b], inc_by=16)
            l2 = p.emit("sync", lambda s, _i, t=t, ab=ab: s.dma_start(
                ab, a1[t * 128:(t + 1) * 128, :]), waits=tmax(prev7[b]),
                inc=sL7[b], inc_by=16)
            l3 = p.emit("sync", lambda s, _i, t=t, zb=zb: s.dma_start(
                zb, z6[t * 128:(t + 1) * 128, :]), waits=tmax(prev7[b]),
                inc=sL7[b], inc_by=16)
            a_1 = p.emit("vector", lambda v, _i, xb=xb, ab=ab: v.tensor_add(
                xb, xb, ab), waits=tmax(l1, l2, l3), inc=sV)
            a_2 = p.emit("vector", lambda v, _i, xb=xb, zb=zb: v.tensor_add(
                xb, xb, zb), waits=tmax(a_1), inc=sV)
            prev7[b] = p.emit("sync", lambda s, _i, t=t, xb=xb: s.dma_start(
                y[t * 128:(t + 1) * 128, :], xb), waits=tmax(a_2),
                inc=sY7[b], inc_by=16)
        p.emit("sync", lambda s, _i: s.nop(), waits=tmax(prev7[0], prev7[1]))
        p.emit("gpsimd", lambda g, _i: g.nop(), waits=tmax(prev7[0], prev7[1]))
        p.finalize()
    return nc


def prep_core_inputs(inputs, core):
    g = inputs
    cm = {
        "x": np.ascontiguousarray(g["L0_attn_x"][core]),
        "wqT": np.ascontiguousarray(g["L0_attn_Wq"].T),
        "wkT": np.ascontiguousarray(g["L0_attn_Wk"].T),
        "wvT": np.ascontiguousarray(g["L0_attn_Wv"].T),
        "woT": np.ascontiguousarray(g["L0_attn_Wo"].T),
        "bq": g["L0_attn_bq"], "bk": g["L0_attn_bk"],
        "bv": g["L0_attn_bv"], "bo": g["L0_attn_bo"],
        "wrT": np.ascontiguousarray(g["L0_ffn_Wr"].T),
        "br": g["L0_ffn_br"],
        "w1T": np.ascontiguousarray(np.swapaxes(g["L0_ffn_W1"], 1, 2)),
        "b1": g["L0_ffn_b1"],
        "w2T": np.ascontiguousarray(np.swapaxes(g["L0_ffn_W2"], 1, 2)),
        "b2": g["L0_ffn_b2"],
        "awqT": np.ascontiguousarray(g["L1_attn_Wq"].T),
        "awkT": np.ascontiguousarray(g["L1_attn_Wk"].T),
        "awvT": np.ascontiguousarray(g["L1_attn_Wv"].T),
        "awoT": np.ascontiguousarray(g["L1_attn_Wo"].T),
        "abq": g["L1_attn_bq"], "abk": g["L1_attn_bk"],
        "abv": g["L1_attn_bv"], "abo": g["L1_attn_bo"],
        "iwqkv": g["L1_ffn_Wqkv"], "iqs": g["L1_ffn_q_scale"], "iks": g["L1_ffn_k_scale"],
        "iwo": g["L1_ffn_Wo"], "ilng": g["L1_ffn_ln_g"], "ilnb": g["L1_ffn_ln_b"],
        "iw1": g["L1_ffn_W1"], "ib1": g["L1_ffn_b1"],
        "iw2": g["L1_ffn_W2"], "ib2": g["L1_ffn_b2"],
    }
    cm.update(host_constants())
    return {k: np.ascontiguousarray(v) for k, v in cm.items()}


def kernel(**inputs):
    cores = list(range(8))
    g = inputs
    nc1, _ = build_program(phases=("p0", "p1", "p2", "p4", "p5"))
    in_maps = [prep_core_inputs(inputs, core) for core in cores]
    r1 = run_bass_kernel_spmd(nc1, in_maps, core_ids=cores).results
    x1s = [r["x1s"] for r in r1]
    a1s = [r["attn1_c"] for r in r1]
    z = [r["zs"] for r in r1]
    hc = host_constants()
    nc6 = build_p6chain()
    wk = {"iwqkv": np.ascontiguousarray(g["L1_ffn_Wqkv"]),
          "iqs": np.ascontiguousarray(g["L1_ffn_q_scale"]),
          "iks": np.ascontiguousarray(g["L1_ffn_k_scale"]),
          "iwo": np.ascontiguousarray(g["L1_ffn_Wo"]),
          "ilng": np.ascontiguousarray(g["L1_ffn_ln_g"]),
          "ilnb": np.ascontiguousarray(g["L1_ffn_ln_b"]),
          "iw1": np.ascontiguousarray(g["L1_ffn_W1"]),
          "ib1": np.ascontiguousarray(g["L1_ffn_b1"]),
          "iw2": np.ascontiguousarray(g["L1_ffn_W2"]),
          "ib2": np.ascontiguousarray(g["L1_ffn_b2"]),
          "ropeI": hc["ropeI"], "maskI": hc["maskI"]}
    m6 = [dict(zin=z[c], **wk) for c in cores]
    r6 = run_bass_kernel_spmd(nc6, m6, core_ids=cores).results
    z = [r["zout"] for r in r6]
    nc7 = build_p7s()
    m7 = [{"x1": x1s[c], "a1": a1s[c], "z6": z[c]} for c in cores]
    r7 = run_bass_kernel_spmd(nc7, m7, core_ids=cores).results
    return np.stack([r["y"] for r in r7], axis=0)


# version 05 (final)
# speedup vs baseline: 1.0462x; 1.0379x over previous
import numpy as np
import ml_dtypes
from contextlib import ExitStack

import concourse.bass as bass
import concourse.mybir as mybir
from concourse.bass_utils import run_bass_kernel_spmd

F32 = mybir.dt.float32
BF16 = mybir.dt.bfloat16
U8 = mybir.dt.uint8
I32 = mybir.dt.int32
ALU = mybir.AluOpType
AF = mybir.ActivationFunctionType
AX = mybir.AxisListType.X
BF = ml_dtypes.bfloat16

T = 4096
D = 1024
NT = T // 128
DC = D // 128
NH = 16
DH = 64
NEXP = 8
DFF = 4096
IDEPTH = 6
I3 = 3072
DI2 = 5460
DI = 2730

NEG = -1.0e30


def host_constants():
    c = {}
    pos = np.arange(T, dtype=np.float64)

    kk = np.arange(128)[:, None]
    qq = np.arange(512)[None, :]
    m0 = np.stack([(qq >= kk + 128 * o) for o in range(4)]).astype(BF)
    c["masks0"] = m0

    half = 32
    inv = 1.0 / (10000.0 ** (np.arange(half, dtype=np.float64) / half))
    ang = pos[:, None] * inv[None, :]
    cosA = np.cos(ang); sinA = np.sin(ang)
    cosAc = np.concatenate([cosA, cosA], 1)
    sinAs = np.concatenate([-sinA, sinA], 1)
    sc_q = float(DH) ** -0.5
    c["ropeA"] = np.stack([cosAc * sc_q, sinAs * sc_q, cosAc, sinAs]).astype(np.float32)

    qi = np.arange(128)[:, None]
    cc = np.arange(256)[None, :]
    mA = np.where((cc >= qi) & (cc <= qi + 128), 0.0, NEG)
    c["maskA"] = mA.astype(np.float32)

    hidx = np.arange(0, DH, 2, dtype=np.float64)
    invI = 1.0 / (10000.0 ** (hidx / DH))
    fr = pos[:, None] * invI[None, :]
    frc = np.concatenate([fr, fr], 1)
    cosI = np.cos(frc); sinI = np.sin(frc)
    base = (hidx + 0.4 * DH) / (1.4 * DH)
    power = (pos - T // 2) / (256 // 2)
    scale = base[None, :] ** power[:, None]
    scl = np.concatenate([scale, scale], 1)
    sgn = np.concatenate([-np.ones(32), np.ones(32)])[None, :]
    QKS = 8.0
    c["ropeI"] = np.stack([
        cosI * scl * QKS,
        sinI * scl * sgn * QKS,
        cosI / scl,
        sinI / scl * sgn,
    ]).astype(np.float32)

    cc = np.arange(384)[None, :]
    mI = np.where((cc >= qi) & (cc <= qi + 256), 0.0, NEG)
    c["maskI"] = mI.astype(np.float32)
    return c


DEBUG_PRES = []

class SemC:
    __slots__ = ("h", "n", "name", "eng")
    def __init__(self, h, name):
        self.h = h; self.n = 0; self.name = name; self.eng = None

class Tok:
    __slots__ = ("sem", "kind", "val", "frame", "off", "lb", "mul", "add")
    def __init__(self, sem, kind, val=0, frame=None, off=0, lb=0, mul=0, add=0):
        self.sem = sem; self.kind = kind; self.val = val
        self.frame = frame; self.off = off; self.lb = lb
        self.mul = mul; self.add = add

def tok_at(t, g):
    f = t.frame
    return Tok(t.sem, "abs", f.base[t.sem] + f.pre.get(t.sem, 0)
               + g * f.delta[t.sem] + t.off)

class Frame:
    def __init__(self, n, engines, managed=False):
        self.n = n
        self.engines = engines
        self.managed = managed
        self.base = {}
        self.delta = {}
        self.pre = {}
        self.ivar = None

class Prog:
    ENGINES = ("sync", "gpsimd", "scalar", "vector", "tensor")

    def __init__(self, nc, ctx):
        self.nc = nc
        self.ctx = ctx
        self.segments = [("straight", [])]
        self.sems = []
        self.cur_frame = None

    def sem(self, name):
        s = SemC(self.ctx.enter_context(self.nc.semaphore(name)), name)
        self.sems.append(s)
        return s

    def emit(self, eng, fn, waits=(), inc=None, inc_by=1, fence=False):
        ws = list(waits)
        self.segments[-1][-1].append((eng, tuple(ws), fn, inc, inc_by, fence))
        if inc is None:
            return None
        if inc.eng is None:
            inc.eng = eng
        elif inc.eng != eng:
            inc.eng = "MIXED"
        f = self.cur_frame
        if f is None:
            inc.n += inc_by
            return Tok(inc, "abs", inc.n)
        else:
            f.delta[inc] = f.delta.get(inc, 0) + inc_by
            return Tok(inc, "rel", frame=f, off=f.delta[inc])

    def loop(self, n, engines, managed=False):
        prog = self
        class _LoopCM:
            def __enter__(cm):
                f = Frame(n, engines, managed)
                f.base = {s: s.n for s in prog.sems}
                prog.cur_frame = f
                prog.segments.append(("loop", f, []))
                cm.f = f
                return f
            def __exit__(cm, *a):
                f = cm.f
                for (_e, ws, *_r) in prog.segments[-1][-1]:
                    for t in ws:
                        if t.kind == "rel" and t.frame is f and t.lb > 0:
                            need = t.lb * f.delta.get(t.sem, 0) - t.off
                            if need > 0:
                                f.pre[t.sem] = max(f.pre.get(t.sem, 0), need)
                for s, d in f.delta.items():
                    s.n = f.base[s] + n * d + f.pre.get(s, 0)
                DEBUG_PRES.append({s.name: v for s, v in f.pre.items()})
                prog.cur_frame = None
                prog.segments.append(("straight", []))
                return False
        return _LoopCM()


    def _emit_loop_body(self, nc, EMAP, f, ops, i, ivreg):
        from concourse.expressions import RuntimeValue
        seen = set()
        for (e, ws, fn, inc, inc_by, fence) in ops:
            eng = EMAP[e]
            for t in ws:
                if t.kind == "abs":
                    key = (e, t.sem, "abs", t.val)
                    if t.val > 0 and key not in seen:
                        eng.wait_ge(t.sem.h, t.val)
                        seen.add(key)
                elif t.kind == "expr":
                    key = (e, t.sem, "expr", t.mul, t.add)
                    if key not in seen:
                        if ivreg is None:
                            eng.wait_ge(t.sem.h, i * t.mul + t.add)
                        else:
                            eng.wait_ge(t.sem.h, ivreg(eng, t.mul, t.add))
                        seen.add(key)
                elif t.kind == "exprfn":
                    key = (e, t.sem, "exprfn", t.add)
                    if key not in seen:
                        assert ivreg is None, "exprfn unsupported in managed loop"
                        eng.wait_ge(t.sem.h, t.mul(i))
                        seen.add(key)
                elif t.kind == "ivw":
                    key = (e, t.sem, "ivw", tuple(sorted(t.mul.items())))
                    if key not in seen:
                        assert ivreg is not None
                        eng.wait_ge(t.sem.h, ivreg(eng, **t.mul))
                        seen.add(key)
                else:
                    d = f.delta.get(t.sem, 0)
                    k = t.off - t.lb * d + f.pre.get(t.sem, 0)
                    key = (e, t.sem, "rel", d, k)
                    if key not in seen:
                        if ivreg is None:
                            eng.wait_ge(t.sem.h, i * d + (f.base[t.sem] + k))
                        else:
                            eng.wait_ge(t.sem.h, ivreg(eng, d, f.base[t.sem] + k))
                        seen.add(key)
            if ivreg is None:
                inst = fn(eng, i)
            else:
                inst = fn(eng, lambda mul, add, mod=None, vmin=None, shr=None, pmul=None, eng=eng: RuntimeValue(ivreg(eng, mul, add, mod, vmin, shr, pmul)))
            if fence:
                inst = eng.drain()
            if inc is not None:
                assert inst is not None
                inst.then_inc(inc.h, inc_by)

    def barrier_token(self):
        assert self.cur_frame is None
        return [Tok(s, "abs", s.n) for s in self.sems if s.n > 0]

    def finalize(self):
        nc = self.nc
        EMAP = {"sync": nc.sync, "gpsimd": nc.gpsimd, "scalar": nc.scalar,
                "vector": nc.vector, "tensor": nc.tensor}
        ETYPE = {"sync": mybir.EngineType.SP, "gpsimd": mybir.EngineType.Pool,
                 "scalar": mybir.EngineType.Activation, "vector": mybir.EngineType.DVE,
                 "tensor": mybir.EngineType.PE}
        for seg in self.segments:
            if seg[0] == "straight":
                ops = seg[1]
                satisfied = {}
                for (e, ws, fn, inc, inc_by, fence) in ops:
                    eng = EMAP[e]
                    for t in ws:
                        assert t.kind == "abs"
                        key = (e, t.sem)
                        if t.val > 0 and satisfied.get(key, -1) < t.val:
                            eng.wait_ge(t.sem.h, t.val)
                            satisfied[key] = t.val
                    inst = fn(eng, None)
                    if fence:
                        inst = eng.drain()
                    if inc is not None:
                        assert inst is not None
                        inst.then_inc(inc.h, inc_by)
            else:
                _, f, ops = seg
                for s, pch in f.pre.items():
                    if pch > 0:
                        assert s.eng in EMAP, f"pre-charge on mixed-engine sem {s.name}"
                        EMAP[s.eng].drain()
                        EMAP[s.eng].nop(nofuse=True).then_inc(s.h, pch)
                engs = [ETYPE[e] for e in f.engines]
                if not f.managed:
                    with nc.Fori(0, f.n, engines=engs) as i:
                        self._emit_loop_body(nc, EMAP, f, ops, i, None)
                else:
                    def pyval(i, mul, add, mod=None, vmin=None, shr=None, pmul=None):
                        if mul == 0:
                            return add
                        r = i * mul
                        if mod is not None:
                            r %= mod
                        if shr is not None:
                            r >>= shr
                        if vmin is not None:
                            r = min(r, vmin)
                        if pmul is not None:
                            r *= pmul
                        return r + add
                    seen_vals = {}
                    for i in range(f.n):
                        for (e, ws, fn, inc, inc_by, fence) in ops:
                            eng = EMAP[e]
                            for t in ws:
                                if t.kind == "abs":
                                    val = t.val
                                elif t.kind == "expr":
                                    val = i * t.mul + t.add
                                elif t.kind == "exprfn":
                                    val = t.mul(i)
                                elif t.kind == "ivw":
                                    val = pyval(i, **t.mul)
                                else:
                                    d = f.delta.get(t.sem, 0)
                                    k = t.off - t.lb * d + f.pre.get(t.sem, 0)
                                    val = i * d + f.base[t.sem] + k
                                key = (e, t.sem)
                                if val > 0 and seen_vals.get(key, -1) < val:
                                    eng.wait_ge(t.sem.h, val)
                                    seen_vals[key] = val
                            inst = fn(eng, lambda mul, add, mod=None, vmin=None, shr=None, pmul=None, i=i: pyval(i, mul, add, mod, vmin, shr, pmul))
                            if fence:
                                inst = eng.drain()
                            if inc is not None:
                                assert inst is not None
                                inst.then_inc(inc.h, inc_by)
        for s in self.sems:
            assert s.n < 2 ** 30, f"sem {s.name} count {s.n} too high"

def tmax(*tokens):
    absd = {}
    rel = []
    for t in tokens:
        if t is None:
            continue
        ts = t if isinstance(t, list) else [t]
        for x in ts:
            if x is None:
                continue
            if x.kind == "abs":
                if x.sem in absd:
                    absd[x.sem] = max(absd[x.sem], x.val)
                else:
                    absd[x.sem] = x.val
            elif x.kind == "expr":
                rel.append(x)
            else:
                rel.append(x)
    out = [Tok(s, "abs", v) for s, v in absd.items()]
    rd = {}
    for x in rel:
        if x.kind == "expr":
            key = (x.sem, "expr", x.mul, x.add)
        elif x.kind == "exprfn":
            key = (x.sem, "exprfn", x.add)
        else:
            key = (x.sem, x.frame, x.lb)
        if key not in rd or (x.kind == "rel" and rd[key].off < x.off):
            rd[key] = x
    return out + list(rd.values())


def build_program(debug=(), p0_tiles=NT, phases=("p0", "p1"), n_exp=NEXP):
    nc = bass.Bass("TRN2", target_bir_lowering=False)
    dbg = set(debug)

    def di(name, shape, dt=F32):
        return nc.dram_tensor(name, list(shape), dt, kind="ExternalInput").ap()
    x0 = di("x", (T, D))
    wqT = di("wqT", (D, D)); wkT = di("wkT", (D, D)); wvT = di("wvT", (D, D)); woT = di("woT", (D, D))
    bq = di("bq", (D,)); bk = di("bk", (D,)); bv = di("bv", (D,)); bo = di("bo", (D,))
    wrT = di("wrT", (D, NEXP)); br = di("br", (NEXP,))
    w1T = di("w1T", (NEXP, D, DFF)); b1 = di("b1", (NEXP, DFF))
    w2T = di("w2T", (NEXP, DFF, D)); b2 = di("b2", (NEXP, D))
    awqT = di("awqT", (D, D)); awkT = di("awkT", (D, D)); awvT = di("awvT", (D, D)); awoT = di("awoT", (D, D))
    abq = di("abq", (D,)); abk = di("abk", (D,)); abv = di("abv", (D,)); abo = di("abo", (D,))
    iwqkv = di("iwqkv", (IDEPTH, D, I3)); iqs = di("iqs", (IDEPTH, DH)); iks = di("iks", (IDEPTH, DH))
    iwo = di("iwo", (IDEPTH, D, D)); ilng = di("ilng", (IDEPTH, D)); ilnb = di("ilnb", (IDEPTH, D))
    iw1 = di("iw1", (IDEPTH, D, DI2)); ib1 = di("ib1", (IDEPTH, DI2))
    iw2 = di("iw2", (IDEPTH, DI, D)); ib2 = di("ib2", (IDEPTH, D))
    masks0 = di("masks0", (4, 128, 512), BF16)
    ropeA = di("ropeA", (4, T, DH)); maskA = di("maskA", (128, 256))
    ropeI = di("ropeI", (4, T, DH)); maskI = di("maskI", (128, 384))

    def ds_(name, shape, dt=F32):
        return nc.dram_tensor(name, list(shape), dt, kind="Internal").ap()
    attn0_c = ds_("attn0_c", (T, D))
    moe_c = ds_("moe_c", (T, D))
    zA = ds_("zA", (T, D))
    zB = ds_("zB", (T, D))
    x1s = nc.dram_tensor("x1s", [T, D], F32, kind="ExternalOutput").ap()
    zs = nc.dram_tensor("zs", [T, D], F32, kind="ExternalOutput").ap()
    attn1_c = nc.dram_tensor("attn1_c", [T, D], F32, kind="ExternalOutput").ap()
    y = nc.dram_tensor("y", [T, D], F32, kind="ExternalOutput").ap()

    dbg_out = {}
    def dout(name, shape, dt=F32):
        ap = nc.dram_tensor(name, list(shape), dt, kind="ExternalOutput").ap()
        dbg_out[name] = ap
        return ap

    with ExitStack() as ctx:
        p = Prog(nc, ctx)

        ARB = ctx.enter_context(nc.sbuf_tensor("ARB", [128, 207424], U8))
        AR0 = ARB[:, 0:73728]
        AR1 = ARB[:, 73728:139264]
        AR2 = ARB[:, 139264:205824]

        def v0(off, n, dt=F32, arena=None):
            sz = {F32: 4, BF16: 2, I32: 4, U8: 1}[dt]
            a = arena if arena is not None else AR0
            return a[:, off:off + n * sz].bitcast(dt)
        def v1(off, n, dt=F32):
            return v0(off, n, dt, AR1)
        def v2(off, n, dt=F32):
            return v0(off, n, dt, AR2)

        ident = ctx.enter_context(nc.sbuf_tensor("ident", [128, 128], BF16))
        io32 = ctx.enter_context(nc.sbuf_tensor("io32", [128, 128], I32))
        MASKS = ctx.enter_context(nc.sbuf_tensor("MASKS", [128, NT * NEXP], F32))
        SMALL = ctx.enter_context(nc.sbuf_tensor("SMALL", [128, 2048], U8))
        def vs(off, n, dt=F32):
            sz = {F32: 4, BF16: 2, I32: 4}[dt]
            return SMALL[:, off:off + n * sz].bitcast(dt)

        PSB = [ctx.enter_context(nc.psum_tensor(f"psb{i}", [128, 512], F32))
               for i in range(8)]
        def pbf(i, off, n):
            return PSB[i][:, off:off + n // 2].bitcast(BF16)

        sQ = p.sem("sQ")
        sXT = [p.sem("sXT0"), p.sem("sXT1")]
        sG = p.sem("sG")
        sV = p.sem("sV")
        sS = p.sem("sS")
        sT = p.sem("sT")


        t_clr = []
        for off in range(0, 207424, 32768):
            hi = min(off + 32768, 207424)
            t_clr.append(p.emit("vector", lambda v, _i, off=off, hi=hi: v.memset(
                ARB[:, off:hi], 0), inc=sV))
        t_clr.append(p.emit("vector", lambda v, _i: v.memset(SMALL[:], 0), inc=sV))
        for b_ in range(8):
            t_clr.append(p.emit("vector", lambda v, _i, b_=b_: v.memset(PSB[b_][:], 0.0), inc=sV))
        clr_all = tmax(*t_clr)
        sGC = p.sem("sGC")
        t_io = p.emit("gpsimd", lambda g, _i: g.iota(io32[:], [[1, 128]], channel_multiplier=-1,
                                                 allow_small_or_imprecise_dtypes=True),
                      inc=sGC)
        t_id = p.emit("vector", lambda v, _i: v.tensor_scalar(out=ident[:], in0=io32[:], scalar1=0,
                                                          scalar2=None, op0=ALU.is_equal),
                      waits=tmax(t_io), inc=sV)

        EPS5 = vs(1024, 1)[:, 0:1]
        t_eps = p.emit("vector", lambda v, _i: v.memset(EPS5, 1.0e-5), waits=clr_all, inc=sV)

        WRB = vs(0, 64, BF16)
        t_wrb = p.emit("gpsimd", lambda g, _i: g.dma_start(
            WRB.rearrange("p (c e) -> p c e", c=DC),
            wrT.rearrange("(c p) e -> p c e", p=128)), waits=clr_all, inc=sG, inc_by=16)
        BRB = vs(128, 8, F32)
        t_brb = p.emit("sync", lambda s, _i: s.dma_start(
            BRB, br[None, :].broadcast_to([128, NEXP])), waits=clr_all, inc=sQ, inc_by=16)

        XNT = [v0(c * 9216 + 1024, T, BF16) for c in range(DC)]
        XT = [v1(0, 1024), v1(4096, 1024)]
        XNB = [v1(8192, 1024, BF16), v1(10240, 1024, BF16)]
        SQD = v1(12288, 1024)
        def stat(i, slot):
            return vs(256 + 64 * (i % 2) + 16 * slot, 4)[:, 0:1]

        tok_xnt = [[None] * NT for _ in range(DC)]
        tok_mask = [None] * NT
        tok_lg = [None] * NT
        prev_xt = [clr_all, clr_all]
        prev_xnb = [None, None]

        ZERO8 = vs(1088, 8)
        p.emit("vector", lambda v, _i: v.memset(ZERO8, 0.0), waits=clr_all, inc=sV)
        for t in range(p0_tiles):
            b = t % 2
            t_ld = p.emit("sync", lambda s, _i, t=t, b=b: s.dma_start(XT[b], x0[t * 128:(t + 1) * 128, :]),
                          waits=tmax(prev_xt[b]), inc=sXT[b], inc_by=16)
            meanraw = stat(b, 0); sqs = stat(b, 1); var0 = stat(b, 2); rstd = stat(b, 3)
            mean = vs(768 + 64 * b, 1)[:, 0:1]
            msq = vs(768 + 64 * b + 16, 1)[:, 0:1]
            t_m0 = p.emit("vector", lambda v, _i, b=b, meanraw=meanraw: v.reduce_sum(meanraw, XT[b], axis=AX),
                          waits=tmax(t_ld), inc=sV, fence=True)
            t_s0 = p.emit("scalar", lambda sc, _i, b=b, sqs=sqs: sc.activation(
                out=SQD, in_=XT[b], func=AF.Square, accum_out=sqs),
                waits=tmax(t_ld), inc=sS, fence=True)
            p.emit("vector", lambda v, _i, mean=mean, meanraw=meanraw: v.tensor_scalar(
                out=mean, in0=meanraw, scalar1=1.0 / D, scalar2=None, op0=ALU.mult), inc=sV)
            t_msq = p.emit("vector", lambda v, _i, msq=msq, meanraw=meanraw: v.tensor_scalar(
                out=msq, in0=meanraw, scalar1=meanraw, scalar2=None, op0=ALU.mult),
                inc=sV, fence=True)
            t_v0 = p.emit("vector", lambda v, _i, var0=var0, sqs=sqs, msq=msq: v.scalar_tensor_tensor(
                out=var0, in0=sqs, scalar=float(D), in1=msq, op0=ALU.mult, op1=ALU.subtract),
                waits=tmax(t_s0), inc=sV, fence=True)
            t_sq = p.emit("scalar", lambda sc, _i, var0=var0, rstd=rstd: sc.activation(
                out=rstd, in_=var0, func=AF.Sqrt, bias=EPS5, scale=1.0 / (D * D)),
                waits=tmax(t_v0, t_eps), inc=sS, fence=True)
            t_rc = p.emit("vector", lambda v, _i, rstd=rstd: v.reciprocal(rstd, rstd),
                          waits=tmax(t_sq), inc=sV, fence=True)
            t_xn = p.emit("vector", lambda v, _i, b=b, mean=mean, rstd=rstd: v.tensor_scalar(
                out=XNB[b], in0=XT[b], scalar1=mean, scalar2=rstd,
                op0=ALU.subtract, op1=ALU.mult), waits=tmax(t_rc, prev_xnb[b]), inc=sV)
            prev_xt[b] = tmax(t_xn, t_s0)
            bank = b
            tr_last = None
            for c in range(DC):
                tr_last = p.emit("tensor", lambda te, _i, b=b, c=c, bank=bank: te.transpose(
                    PSB[bank][:, c * 64:c * 64 + 64].bitcast(BF16),
                    XNB[b][:, c * 128:(c + 1) * 128], ident[:]),
                    waits=tmax(t_xn, t_id,
                               tok_xnt[DC - 1][t - 2] if t >= 2 else None), inc=sT)
            for c in range(DC):
                tok_xnt[c][t] = p.emit("vector", lambda v, _i, c=c, t=t, bank=bank: v.tensor_copy(
                    XNT[c][:, t * 128:(t + 1) * 128],
                    PSB[bank][:, c * 64:c * 64 + 64].bitcast(BF16)),
                    waits=tmax(tr_last), inc=sV)
            prev_xnb[b] = tok_xnt[DC - 1][t]
            lgbank = 2 + b
            t_mm = None
            for c in range(DC):
                t_mm = p.emit("tensor", lambda te, _i, c=c, t=t, lgbank=lgbank: te.matmul(
                    PSB[lgbank][:, 0:NEXP], XNT[c][:, t * 128:(t + 1) * 128],
                    WRB[:, c * 8:(c + 1) * 8],
                    start=(c == 0), stop=(c == DC - 1)),
                    waits=tmax(tok_xnt[c][t], t_wrb if t == 0 else None,
                               tok_lg[t - 2] if t >= 2 else None), inc=sT)
            LG = vs(512 + 64 * b, 8)
            t_lge = p.emit("vector", lambda v, _i, LG=LG, lgbank=lgbank: v.tensor_add(
                LG, PSB[lgbank][:, 0:NEXP], BRB), waits=tmax(t_mm, t_brb), inc=sV,
                fence=True)
            tok_lg[t] = t_lge
            mx = vs(768 + 32, 1)[:, 0:1]
            raw8 = vs(1120, 8)
            P8 = vs(1152, 8)
            p.emit("vector", lambda v, _i, LG=LG, mx=mx: v.reduce_max(mx, LG, axis=AX),
                   inc=sV, fence=True)
            p.emit("vector", lambda v, _i, LG=LG, mx=mx, raw8=raw8: v.tensor_scalar(
                out=raw8, in0=LG, scalar1=mx, scalar2=None, op0=ALU.is_equal),
                inc=sV, fence=True)
            p.emit("vector", lambda v, _i, raw8=raw8, P8=P8: v.tensor_tensor_scan(
                P8, raw8, ZERO8, 0.0, ALU.max, ALU.add), inc=sV, fence=True)
            p.emit("vector", lambda v, _i, t=t, P8=P8: v.tensor_copy(
                MASKS[:, t:t + 1], P8[:, 0:1]), inc=sV)
            tok_mask[t] = p.emit("vector", lambda v, _i, t=t, P8=P8: v.tensor_sub(
                MASKS.rearrange("p (e t2) -> p e t2", e=8)[:, 1:8, t:t + 1],
                P8[:, 1:8, None], P8[:, 0:7, None]),
                inc=sV, fence=True)

        tok_moe_done = None
        if "p1" in phases:
            sW1 = p.sem("sW1")
            sW2 = p.sem("sW2")
            sMCs = [p.sem(f"sMC{i_}") for i_ in range(4)]
            sP1 = p.sem("sP1")
            W1TB = [v2(dc * 8192, DFF, BF16) for dc in range(DC)]
            HB = [v1(fc * 1024, 512, BF16) for fc in range(32)]
            W2B4 = [v1(32768 + i_ * 2048, 1024, BF16) for i_ in range(4)]
            CTB = [v1(40960 + i_ * 4096, 1024) for i_ in range(4)]
            B2BC = v1(57344, 1024)
            B1C = vs(1216, 32)
            SCR = v1(61440, 1024)
            MC4 = vs(1408, 4)
            XG = [v0(c * 9216, 512, BF16) for c in range(DC)]
            KQv = v0(0, 8 * 4608, BF16)
            xnt_all = tmax(*[tok_xnt[c][t] for c in range(DC) for t in range(NT)])
            mask_all = tmax(*[tok_mask[t] for t in range(NT)])

            relu_toks = [None] * 32
            w2fc_tok = [None] * 32
            ct_toks = [None] * 4
            last_w1mm = None
            cp_toks = [None, None]
            with p.loop(64, ["tensor", "scalar", "vector"], managed=True) as fr:
                p.emit("vector", lambda v, iv: v.tensor_copy(
                    MC4[:, 0:4], MASKS[:, bass.ds(iv(4, 0), 4)]),
                    waits=tmax(mask_all, Tok(sV, "rel", frame=fr, off=12, lb=1)),
                    fence=True)
                for ch in range(4):
                    for j in range(8):
                        fc = ch * 8 + j
                        for dc in range(DC):
                            w = []
                            if j == 0 and dc == 0:
                                if ch == 0:
                                    w += [Tok(sS, "rel", frame=fr, off=32, lb=1),
                                          Tok(sV, "rel", frame=fr, off=12, lb=1),
                                          Tok(sW1, "ivw",
                                              mul=dict(mul=1, shr=3, pmul=160, add=160))]
                                    w.append(xnt_all)
                                else:
                                    w.append(Tok(sS, "rel", frame=fr, off=ch * 8))
                            last_w1mm = p.emit("tensor", lambda te, iv, fc=fc, dc=dc: te.matmul(
                                PSB[fc % 8][:, 0:512],
                                W1TB[dc][:, fc * 128:(fc + 1) * 128],
                                XNT[dc][:, bass.ds(iv(512, 0, mod=4096), 512)],
                                start=(dc == 0), stop=(dc == DC - 1)),
                                waits=tmax(*w), inc=sT)
                    for j in range(8):
                        fc = ch * 8 + j
                        w = [Tok(sT, "rel", frame=fr, off=(ch + 1) * 64)] if j == 0 else []
                        relu_toks[fc] = p.emit("scalar", lambda sc, iv, fc=fc: sc.activation(
                            out=HB[fc], in_=PSB[fc % 8][:, 0:512], func=AF.Relu,
                            bias=B1C[:, fc:fc + 1]),
                            waits=tmax(*w), inc=sS)
                for fc in range(32):
                    for tt in range(4):
                        for hf in range(2):
                            w = []
                            if tt == 0 and hf == 0:
                                if fc % 4 == 0:
                                    w.append(Tok(sW2, "expr", mul=32 * 16,
                                                 add=(fc + 4) * 16))
                                if fc == 0:
                                    w.append(Tok(sS, "rel", frame=fr, off=32))
                            tok = p.emit("tensor", lambda te, iv, fc=fc, tt=tt, hf=hf: te.matmul(
                                PSB[2 * tt + hf][:, 0:512],
                                HB[fc][:, tt * 128:(tt + 1) * 128],
                                W2B4[fc % 4][:, hf * 512:(hf + 1) * 512],
                                start=(fc == 0), stop=(fc == 31)),
                                waits=tmax(*w), inc=sT)
                    w2fc_tok[fc] = tok
                for tt in range(4):
                    tok_b2m = p.emit("vector", lambda v, iv, tt=tt: v.tensor_scalar(
                        out=SCR, in0=B2BC, scalar1=MC4[:, tt:tt + 1],
                        scalar2=None, op0=ALU.mult),
                        waits=tmax(Tok(sT, "rel", frame=fr, off=506 + 2 * tt),
                                   Tok(sMCs[tt], "expr", mul=16, add=0),
                                   (ct_toks[tt - 1] if tt > 0 else
                                    Tok(sV, "rel", frame=fr, off=12, lb=1))),
                        inc=sV)
                    last = None
                    for hf in range(2):
                        last = p.emit("vector", lambda v, iv, tt=tt, hf=hf: v.scalar_tensor_tensor(
                            out=CTB[tt][:, hf * 512:(hf + 1) * 512],
                            in0=PSB[2 * tt + hf][:, 0:512],
                            scalar=MC4[:, tt:tt + 1],
                            in1=SCR[:, hf * 512:(hf + 1) * 512],
                            op0=ALU.mult, op1=ALU.add),
                            waits=tmax(tok_b2m), inc=sV)
                    ct_toks[tt] = last

            for e in range(NEXP):
                wprev = clr_all
                if e >= 1:
                    wprev = tmax(tok_at(last_w1mm, e * 8 - 1),
                                 tok_at(relu_toks[31], e * 8 - 1),
                                 tok_at(ct_toks[3], e * 8 - 1))
                for dc in range(DC):
                    p.emit("gpsimd", lambda g, _i, e=e, dc=dc: g.dma_start(
                        W1TB[dc], w1T[e, dc * 128:(dc + 1) * 128, :]),
                        waits=wprev, inc=sW1, inc_by=16)
                p.emit("gpsimd", lambda g, _i, e=e: g.dma_start(
                    B1C.rearrange("p (a o) -> p a o", o=1),
                    b1[e].rearrange("(a q) -> q a", q=128)[:, :, None],
                    allow_slow_non_contiguous=True),
                    waits=wprev, inc=sW1, inc_by=16)
                p.emit("gpsimd", lambda g, _i, e=e: g.dma_start(
                    B2BC, b2[e][None, :].broadcast_to([128, D])),
                    waits=wprev, inc=sW1, inc_by=16)
                for g in range(8):
                    jg = e * 8 + g
                    for fc in range(32):
                        w = []
                        gi = jg * 32 + fc
                        if gi >= 4:
                            pj, pf = divmod(gi - 4, 32)
                            w.append(tok_at(w2fc_tok[pf], pj))
                        else:
                            w = [clr_all]
                        p.emit("gpsimd", lambda s, _i, e=e, fc=fc: s.dma_start(
                            W2B4[fc % 4], w2T[e, fc * 128:(fc + 1) * 128, :]),
                            waits=tmax(*w), inc=sW2, inc_by=16)
                    for tt in range(4):
                        t0 = (g * 4 + tt) * 128
                        p.emit("gpsimd", lambda gp, _i, e=e, t0=t0, tt=tt: gp.dma_start(
                            moe_c[t0:t0 + 128, :], CTB[tt],
                            accum_op=(ALU.add if e > 0 else ALU.bypass)),
                            waits=[tok_at(ct_toks[tt], jg)], inc=sMCs[tt], inc_by=16)
            tok_moe_done = tmax(*[Tok(s_, "abs", s_.n) for s_ in sMCs])

        tok_attn0_done = None
        if "p2" in phases:
            sWS = [p.sem("sWS0"), p.sem("sWS1")]
            sAC = [p.sem("sAC0"), p.sem("sAC1")]
            sP = p.sem("sP")
            KQ = v0(0, 8 * 4608, BF16)
            VA = v2(0, 32 * 1040, BF16)
            QN = v1(0, 1024, BF16)
            SQB = v1(2048, 1024, BF16)
            OUTF = v1(4096, 4096, BF16)
            PRBs = [v1(12288, 512, BF16), v1(13312, 512, BF16)]
            WSL = [v1(22592, 1024, BF16), v1(24640, 1024, BF16)]
            BIAS = v1(26688, 1024, BF16)
            BCS = v1(26688, 512, BF16)
            MBIG = v1(28736, 896, BF16)
            ONESB = v1(30528, 64, BF16)
            ZROB = v1(30656, 512, BF16)
            RECROW = AR1[64:65, 31680:31680 + 2048].bitcast(F32)
            RECBR = AR1[64:65, 33728:33728 + 1024].bitcast(BF16)
            RS16 = v1(34752, 16)
            CT0 = [v1(36864, 1024), v1(40960, 1024)]
            BOB = v1(45056, 1024)
            ESMALL = vs(1344, 1)[:, 0:1]
            SCL0 = 1.0 / (float(np.sqrt(128.0)) * 1.5)

            tok_pe_p1 = Tok(sT, "abs", sT.n)
            tok_sv_p1 = Tok(sV, "abs", sV.n)
            tok_ss_p1 = Tok(sS, "abs", sS.n)
            p1done = tmax(tok_pe_p1, tok_sv_p1, tok_ss_p1)

            t_vaones = p.emit("vector", lambda v, _i: v.memset(VA, 1.0),
                              waits=tmax(tok_moe_done) if tok_moe_done else [], inc=sV)
            t_zrob = p.emit("vector", lambda v, _i: v.memset(ZROB, 0.0), inc=sV)
            t_onesb = p.emit("vector", lambda v, _i: v.memset(ONESB, 1.0), inc=sV)
            t_eps8 = p.emit("vector", lambda v, _i: v.memset(ESMALL, 1.0e-8), inc=sV)
            MBI = AR1[:, 49152:49152 + 3584 * 4].bitcast(I32)
            t_mb1 = p.emit("gpsimd", lambda g, _i: g.iota(
                MBI[:, 0:896], [[1, 896]], base=-384, channel_multiplier=-1,
                allow_small_or_imprecise_dtypes=True),
                waits=tmax(tok_moe_done), inc=sGC)
            t_mb2 = p.emit("vector", lambda v, _i: v.tensor_scalar(
                out=MBIG, in0=MBI[:, 0:896], scalar1=0, scalar2=None,
                op0=ALU.is_ge), waits=tmax(t_mb1), inc=sV)
            t_bo0 = p.emit("sync", lambda s, _i: s.dma_start(
                BOB, bo[None, :].broadcast_to([128, D])),
                waits=tmax(tok_moe_done), inc=sQ, inc_by=16)
            t_bo1 = p.emit("scalar", lambda sc, _i: sc.mul(BOB, BOB, 0.5),
                           waits=tmax(t_bo0), inc=sS)
            setup_done = tmax(t_vaones, t_zrob, t_onesb, t_eps8, t_mb2, t_bo1, p1done)

            WTS = {"v": (wvT, bv), "k": (wkT, bk), "q": (wqT, bq)}
            tok_ws_free = [setup_done, setup_done]
            tok_bank_free = [setup_done] * 8
            tok_kq_write = [None] * NT
            tok_v_write = [None] * NT
            tok_ct_free = [setup_done, setup_done]
            prev_bias_read = setup_done
            ngroups = 1 if "g0" in dbg else 8
            tok_outf_free = setup_done
            prev_phaseB_banks = setup_done

            for g in range(ngroups):
                tiles = [4 * g + t for t in range(4)]
                for wname in ("v", "k", "q"):
                    wT, bvec = WTS[wname]
                    t_b = p.emit("gpsimd", lambda gp, _i, bvec=bvec: gp.dma_start(
                        BIAS, bvec[None, :].broadcast_to([128, D])),
                        waits=tmax(prev_phaseB_banks, prev_bias_read), inc=sG, inc_by=16)
                    mm_last = [None] * 4
                    bias_readers = []
                    wtoks = [None, None]
                    for dc in range(DC):
                        sl = dc % 2
                        wtoks[sl] = p.emit("gpsimd", lambda gp, _i, wT=wT, dc=dc, sl=sl: gp.dma_start(
                            WSL[sl], wT[dc * 128:(dc + 1) * 128, :]),
                            waits=tmax(tok_ws_free[sl]), inc=sWS[sl], inc_by=16)
                        for t in range(4):
                            for hf in range(2):
                                w = [wtoks[sl]]
                                if dc == 0:
                                    w.append(tok_bank_free[2 * t + hf])
                                mm_last[t] = p.emit("tensor", lambda te, _i, g=g, dc=dc, t=t, hf=hf, sl=sl: te.matmul(
                                    PSB[2 * t + hf][:, 0:512],
                                    XNT[dc][:, (4 * g + t) * 128:(4 * g + t + 1) * 128],
                                    WSL[sl][:, hf * 512:(hf + 1) * 512],
                                    start=(dc == 0), stop=(dc == DC - 1)),
                                    waits=tmax(*w), inc=sT)
                        tok_ws_free[sl] = mm_last[3]
                    for t in range(4):
                        tg = tiles[t]
                        if wname == "v":
                            last = None
                            for hf in range(2):
                                last = p.emit("vector", lambda v, _i, tg=tg, t=t, hf=hf: v.tensor_tensor(
                                    out=VA[:, tg * 1040:(tg + 1) * 1040].rearrange(
                                        "p (h c) -> p h c", c=65)[:, hf * 8:(hf + 1) * 8, 0:64],
                                    in0=PSB[2 * t + hf][:, 0:512].rearrange(
                                        "p (h c) -> p h c", c=64),
                                    in1=BIAS[:, hf * 512:(hf + 1) * 512].rearrange(
                                        "p (h c) -> p h c", c=64),
                                    op=ALU.add), waits=tmax(mm_last[t], t_b), inc=sV)
                            tok_v_write[tg] = last
                            tok_bank_free[2 * t] = last
                            tok_bank_free[2 * t + 1] = last
                            bias_readers.append(last)
                        else:
                            tqn = None
                            for hf in range(2):
                                tqn = p.emit("vector", lambda v, _i, t=t, hf=hf: v.tensor_tensor(
                                    out=QN[:, hf * 512:(hf + 1) * 512],
                                    in0=PSB[2 * t + hf][:, 0:512],
                                    in1=BIAS[:, hf * 512:(hf + 1) * 512],
                                    op=ALU.add), waits=tmax(mm_last[t], t_b), inc=sV)
                            bias_readers.append(tqn)
                            tsq = p.emit("scalar", lambda sc, _i: sc.activation(
                                out=SQB, in_=QN, func=AF.Square), waits=tmax(tqn), inc=sS)
                            trd = p.emit("vector", lambda v, _i: v.reduce_sum(
                                RS16, SQB.rearrange("p (h c) -> p h c", c=64), axis=AX),
                                waits=tmax(tsq), inc=sV, fence=True)
                            tsr = p.emit("scalar", lambda sc, _i: sc.activation(
                                out=RS16, in_=RS16, func=AF.Sqrt),
                                waits=tmax(trd), inc=sS, fence=True)
                            tmx_ = p.emit("vector", lambda v, _i: v.tensor_scalar(
                                out=RS16, in0=RS16, scalar1=1.0e-8, scalar2=None,
                                op0=ALU.max), waits=tmax(tsr), inc=sV, fence=True)
                            trc = p.emit("vector", lambda v, _i: v.reciprocal(RS16, RS16),
                                         waits=tmax(tmx_), inc=sV, fence=True)
                            tnm = p.emit("vector", lambda v, _i: v.tensor_tensor(
                                out=SQB.rearrange("p (h c) -> p h c", c=64),
                                in0=QN.rearrange("p (h c) -> p h c", c=64),
                                in1=RS16[:, :, None].broadcast_to([128, 16, 64]),
                                op=ALU.mult), waits=tmax(trc), inc=sV)
                            trt = None
                            for c in range(DC):
                                trt = p.emit("tensor", lambda te, _i, c=c, t=t: te.matmul(
                                    PSB[2 * t + (c % 2)][:, (c // 2) * 128:(c // 2) * 128 + 64].bitcast(BF16),
                                    SQB[:, c * 128:(c + 1) * 128], ident[:],
                                    is_transpose=True, start=True, stop=True),
                                    waits=tmax(tnm), inc=sT)
                            evl = None
                            for c in range(DC):
                                dst_off = c * 4608 + (tg * 128 if wname == "k" else 512 + tg * 128)
                                wv = [trt]
                                if wname == "q":
                                    wv.append(tok_pe_p1)
                                evl = p.emit("vector", lambda v, _i, c=c, t=t, dst_off=dst_off: v.tensor_copy(
                                    KQ[:, dst_off:dst_off + 128],
                                    PSB[2 * t + (c % 2)][:, (c // 2) * 128:(c // 2) * 128 + 64].bitcast(BF16)),
                                    waits=tmax(*wv), inc=sV)
                            tok_kq_write[tg] = evl
                            tok_bank_free[2 * t] = evl
                            tok_bank_free[2 * t + 1] = evl
                    prev_bias_read = tmax(*bias_readers)
                if "nophaseb" in dbg:
                    prev_phaseB_banks = tmax(*[tok_bank_free[i_] for i_ in range(8)])
                    continue
                kv_ready = tmax(*[tok_kq_write[j] for j in tiles],
                                *[tok_v_write[j] for j in tiles],
                                *[tok_kq_write[j] for j in range(4 * g)],
                                *[tok_v_write[j] for j in range(4 * g)])
                hsets = [list(range(0, 6)), list(range(6, 12)), list(range(12, 16))]
                evac_toks = []
                for hs in hsets:
                    pv_last = {}
                    sc_prev = {}
                    nkj = 4 * g + 4
                    for kj in range(nkj):
                        o = kj - 4 * g
                        for idx, h in enumerate(hs):
                            n = kj * len(hs) + idx
                            bk6 = 6 + (n % 2)
                            w = [kv_ready]
                            if n < 2:
                                w.append(prev_phaseB_banks)
                                w.append(setup_done)
                            if n >= 2:
                                w.append(sc_prev.get(n - 2))
                            tsc = p.emit("tensor", lambda te, _i, h=h, kj=kj, g=g, bk6=bk6: te.matmul(
                                PSB[bk6][:, 0:512],
                                KQ[(h % 2) * 64:(h % 2) * 64 + 64,
                                   (h // 2) * 4608 + kj * 128:(h // 2) * 4608 + kj * 128 + 128],
                                KQ[(h % 2) * 64:(h % 2) * 64 + 64,
                                   (h // 2) * 4608 + 512 + 4 * g * 128:
                                   (h // 2) * 4608 + 512 + 4 * g * 128 + 512],
                                start=True, stop=True),
                                waits=tmax(*w), inc=sT)
                            tex = p.emit("scalar", lambda sc, _i, n=n, bk6=bk6: sc.activation(
                                out=PRBs[n % 2], in_=PSB[bk6][:, 0:512],
                                func=AF.Exp, scale=SCL0),
                                waits=tmax(tsc, pv_last.get(n - 2)), inc=sS)
                            sc_prev[n] = tex
                            dep = tex
                            if o >= 0:
                                dep = p.emit("vector", lambda v, _i, n=n, o=o: v.tensor_tensor(
                                    out=PRBs[n % 2], in0=PRBs[n % 2],
                                    in1=MBIG[:, 384 - 128 * o:384 - 128 * o + 512],
                                    op=ALU.mult), waits=tmax(tex), inc=sV)
                            pv_last[n] = p.emit("tensor", lambda te, _i, idx=idx, h=h, kj=kj, n=n, nkj=nkj: te.matmul(
                                PSB[idx][0:65, 0:512],
                                VA[:, kj * 1040 + h * 65:kj * 1040 + h * 65 + 65],
                                PRBs[n % 2],
                                start=(kj == 0), stop=(kj == nkj - 1)),
                                waits=tmax(dep), inc=sT)
                    for idx, h in enumerate(hs):
                        trc0 = p.emit("vector", lambda v, _i, idx=idx: v.reciprocal(
                            RECROW[:, 0:512], PSB[idx][64:65, 0:512]),
                            waits=tmax(pv_last[(nkj - 1) * len(hs) + idx],
                                       tok_outf_free if idx == 0 else None),
                            inc=sV)
                        trcb = p.emit("vector", lambda v, _i: v.tensor_copy(
                            RECBR[:, 0:512], RECROW[:, 0:512]), waits=tmax(trc0), inc=sV)
                        tbc = p.emit("tensor", lambda te, _i: te.matmul(
                            PSB[6][0:64, 0:512], ONESB[64:65, 0:64], RECBR[:, 0:512],
                            start=True, stop=True), waits=tmax(trcb), inc=sT)
                        tbe = p.emit("vector", lambda v, _i: v.tensor_copy(
                            BCS[0:64, 0:512], PSB[6][0:64, 0:512]), waits=tmax(tbc), inc=sV)
                        evac_toks.append(p.emit("vector", lambda v, _i, idx=idx, h=h: v.tensor_tensor(
                            out=OUTF[(h % 2) * 64:(h % 2) * 64 + 64,
                                     (h // 2) * 512:(h // 2) * 512 + 512],
                            in0=PSB[idx][0:64, 0:512], in1=BCS[0:64, 0:512],
                            op=ALU.mult), waits=tmax(tbe), inc=sV))
                    prev_phaseB_banks = tmax(*evac_toks)
                wo_last = [None] * 4
                wtoks = [None, None]
                for dc in range(DC):
                    sl = dc % 2
                    wtoks[sl] = p.emit("gpsimd", lambda gp, _i, dc=dc, sl=sl: gp.dma_start(
                        WSL[sl], woT[dc * 128:(dc + 1) * 128, :]),
                        waits=tmax(tok_ws_free[sl]), inc=sWS[sl], inc_by=16)
                    for t in range(4):
                        for hf in range(2):
                            w = [wtoks[sl], tmax(*evac_toks)]
                            if dc == 0:
                                w.append(prev_phaseB_banks)
                            wo_last[t] = p.emit("tensor", lambda te, _i, dc=dc, t=t, hf=hf, sl=sl: te.matmul(
                                PSB[2 * t + hf][:, 0:512],
                                OUTF[:, dc * 512 + t * 128:dc * 512 + t * 128 + 128],
                                WSL[sl][:, hf * 512:(hf + 1) * 512],
                                start=(dc == 0), stop=(dc == DC - 1)),
                                waits=tmax(*w), inc=sT)
                    tok_ws_free[sl] = wo_last[3]
                ct_sts = []
                for t in range(4):
                    ctsl = t % 2
                    last = None
                    for hf in range(2):
                        last = p.emit("vector", lambda v, _i, t=t, hf=hf, ctsl=ctsl: v.scalar_tensor_tensor(
                            out=CT0[ctsl][:, hf * 512:(hf + 1) * 512],
                            in0=PSB[2 * t + hf][:, 0:512], scalar=0.5,
                            in1=BOB[:, hf * 512:(hf + 1) * 512],
                            op0=ALU.mult, op1=ALU.add),
                            waits=tmax(wo_last[t], t_bo1, tok_ct_free[ctsl]), inc=sV)
                    tok_bank_free[2 * t] = last
                    tok_bank_free[2 * t + 1] = last
                    ct_sts.append(last)
                    tok_ct_free[ctsl] = p.emit("sync", lambda s, _i, g=g, t=t, ctsl=ctsl: s.dma_start(
                        attn0_c[(4 * g + t) * 128:(4 * g + t + 1) * 128, :], CT0[ctsl]),
                        waits=tmax(last), inc=sAC[ctsl], inc_by=16)
                tok_outf_free = tmax(*wo_last)
                prev_phaseB_banks = tmax(*ct_sts)
            tok_attn0_done = tmax(tok_ct_free[0], tok_ct_free[1])

        if "p2d" in dbg:
            d_a0 = dout("d_a0", (T, D))
            wAlla = p.barrier_token()
            p.emit("sync", lambda s, _i, d_a0=d_a0: s.dma_start(d_a0, attn0_c),
                   waits=wAlla, inc=sQ, inc_by=16)
        if "p2q" in dbg:
            d_kq = dout("d_kq", (128, 36864))
            d_va = dout("d_va", (128, 33280))
            wAllq = p.barrier_token()
            p.emit("gpsimd", lambda s, _i, d_kq=d_kq: s.dma_start(
                d_kq, v0(0, 36864, BF16)), waits=wAllq, inc=sG, inc_by=16)
            p.emit("gpsimd", lambda s, _i, d_va=d_va: s.dma_start(
                d_va, v2(0, 33280, BF16)), waits=wAllq, inc=sG, inc_by=16)
        import dataclasses as _dc
        def shuf64(ap):
            return _dc.replace(ap, offset=ap.offset + 32,
                               ap=[[ap.ap[0][0], 128], [64, 16], [-32, 2], [1, 32]])
        tok_p4_done = None
        tok_ynt = [None] * NT
        YNT = [v0(c * 8192, T, BF16) for c in range(DC)]
        if "p4" in phases:
            sL4 = [p.sem("sL4a"), p.sem("sL4b")]
            sO4 = [p.sem("sO4a"), p.sem("sO4b")]
            p4pre = tmax(tok_attn0_done, tok_moe_done,
                         Tok(sT, "abs", sT.n), Tok(sV, "abs", sV.n),
                         Tok(sS, "abs", sS.n))
            X4 = [v1(50176, 1024), v1(54272, 1024)]
            A4s = v1(58368, 1024)
            YB4s = v1(62464, 1024, BF16)
            M4s = v0(65536, 1024)
            Y4s = v0(69632, 1024)
            prev_use = [p4pre, p4pre]
            prev_a4 = p4pre
            prev_m4 = p4pre
            prev_o2 = p4pre
            prev_trl = p4pre
            for t in range(NT):
                b = t % 2
                tl1 = p.emit("sync", lambda s, _i, t=t, b=b: s.dma_start(
                    X4[b], x0[t * 128:(t + 1) * 128, :]), waits=prev_use[b],
                    inc=sL4[b], inc_by=16)
                tl2 = p.emit("sync", lambda s, _i, t=t: s.dma_start(
                    A4s, attn0_c[t * 128:(t + 1) * 128, :]), waits=tmax(prev_a4),
                    inc=sL4[b], inc_by=16)
                tl3 = p.emit("sync", lambda s, _i, t=t: s.dma_start(
                    M4s, moe_c[t * 128:(t + 1) * 128, :]), waits=tmax(prev_m4),
                    inc=sL4[b], inc_by=16)
                tld = tmax(tl1, tl2, tl3)
                t_s1 = p.emit("vector", lambda v, _i, b=b: v.tensor_add(
                    X4[b], X4[b], A4s), waits=tmax(tld), inc=sV)
                t_s2 = p.emit("vector", lambda v, _i, b=b: v.tensor_add(
                    X4[b], X4[b], M4s), waits=tmax(t_s1), inc=sV)
                prev_m4 = t_s2
                t_o1 = p.emit("sync", lambda s, _i, t=t, b=b: s.dma_start(
                    x1s[t * 128:(t + 1) * 128, :], X4[b]), waits=tmax(t_s2),
                    inc=sO4[b], inc_by=16)
                meanraw = stat(b, 0); sqs = stat(b, 1); var0 = stat(b, 2); rstd = stat(b, 3)
                mean = vs(768 + 64 * b, 1)[:, 0:1]
                msq = vs(768 + 64 * b + 16, 1)[:, 0:1]
                t_m0 = p.emit("vector", lambda v, _i, b=b, meanraw=meanraw: v.reduce_sum(
                    meanraw, X4[b], axis=AX), waits=tmax(t_s2), inc=sV, fence=True)
                t_s0 = p.emit("scalar", lambda sc, _i, b=b, sqs=sqs: sc.activation(
                    out=A4s, in_=X4[b], func=AF.Square, accum_out=sqs),
                    waits=tmax(t_s2), inc=sS, fence=True)
                prev_a4 = t_s0
                p.emit("vector", lambda v, _i, mean=mean, meanraw=meanraw: v.tensor_scalar(
                    out=mean, in0=meanraw, scalar1=1.0 / D, scalar2=None, op0=ALU.mult), inc=sV)
                t_msq = p.emit("vector", lambda v, _i, msq=msq, meanraw=meanraw: v.tensor_scalar(
                    out=msq, in0=meanraw, scalar1=meanraw, scalar2=None, op0=ALU.mult),
                    inc=sV, fence=True)
                t_v0 = p.emit("vector", lambda v, _i, var0=var0, sqs=sqs, msq=msq: v.scalar_tensor_tensor(
                    out=var0, in0=sqs, scalar=float(D), in1=msq, op0=ALU.mult, op1=ALU.subtract),
                    waits=tmax(t_s0), inc=sV, fence=True)
                t_sq = p.emit("scalar", lambda sc, _i, var0=var0, rstd=rstd: sc.activation(
                    out=rstd, in_=var0, func=AF.Sqrt, bias=EPS5, scale=1.0 / (D * D)),
                    waits=tmax(t_v0), inc=sS, fence=True)
                t_rc = p.emit("vector", lambda v, _i, rstd=rstd: v.reciprocal(rstd, rstd),
                              waits=tmax(t_sq), inc=sV, fence=True)
                t_yn = p.emit("vector", lambda v, _i, b=b, mean=mean, rstd=rstd: v.tensor_scalar(
                    out=Y4s, in0=X4[b], scalar1=mean, scalar2=rstd,
                    op0=ALU.subtract, op1=ALU.mult),
                    waits=tmax(t_rc, prev_o2), inc=sV)
                t_o2 = p.emit("sync", lambda s, _i, t=t, b=b: s.dma_start(
                    zs[t * 128:(t + 1) * 128, :], Y4s), waits=tmax(t_yn),
                    inc=sO4[b], inc_by=16)
                prev_o2 = t_o2
                t_cb = p.emit("vector", lambda v, _i: v.tensor_copy(
                    YB4s, Y4s), waits=tmax(prev_trl, t_yn), inc=sV)
                trl = None
                for c in range(DC):
                    trl = p.emit("tensor", lambda te, _i, b=b, c=c: te.matmul(
                        PSB[b][:, c * 64:c * 64 + 64].bitcast(BF16),
                        YB4s[:, c * 128:(c + 1) * 128], ident[:],
                        is_transpose=True, start=True, stop=True),
                        waits=tmax(t_cb, p4pre,
                                   tok_ynt[t - 2] if t >= 2 else None), inc=sT)
                prev_trl = trl
                evl = None
                for c in range(DC):
                    evl = p.emit("vector", lambda v, _i, c=c, t=t, b=b: v.tensor_copy(
                        YNT[c][:, t * 128:(t + 1) * 128],
                        PSB[b][:, c * 64:c * 64 + 64].bitcast(BF16)),
                        waits=tmax(trl), inc=sV)
                tok_ynt[t] = evl
                prev_use[b] = tmax(evl, t_o1, t_o2, t_s0, t_yn)
            tok_p4_done = tmax(prev_use[0], prev_use[1])

        tok_attn1_done = None
        if "p5" in phases:
            sR5 = [p.sem("sR5a"), p.sem("sR5b")]
            sA5 = [p.sem("sA5a"), p.sem("sA5b")]
            sWL5 = p.sem("sWL5")
            WQ5 = lambda wi, dc: v2(wi * 16384 + dc * 2048, 1024, BF16)
            w5all = []
            p5pre = tmax(tok_p4_done, Tok(sT, "abs", sT.n), Tok(sV, "abs", sV.n))
            for wi, wt in enumerate([awqT, awkT, awvT, awoT]):
                for dc in range(DC):
                    w5all.append(p.emit("gpsimd", lambda gp, _i, wi=wi, dc=dc, wt=wt: gp.dma_start(
                        WQ5(wi, dc), wt[dc * 128:(dc + 1) * 128, :]),
                        waits=p5pre,
                        inc=sWL5, inc_by=16))
            tok_w5 = Tok(sWL5, "abs", sWL5.n)
            QT5 = v1(0, 1024)
            KT5 = v1(4096, 1024)
            TMP5 = v1(8192, 1024)
            RT5s = [[v1(12288 + (b_ * 4 + j) * 256, 64) for j in range(4)] for b_ in range(2)]
            QB5 = v1(14336, 1024, BF16)
            KB5 = v1(16384, 1024, BF16)
            QF5 = v1(18432, 1024, BF16)
            KR5 = [v1(20480, 1024, BF16), v1(22528, 1024, BF16)]
            VR5 = [v1(24576, 1024, BF16), v1(26624, 1024, BF16)]
            MA5 = v1(28672, 256)
            S5 = v1(29696, 256)
            PR5 = v1(30720, 256, BF16)
            PT5 = [v1(31232, 128, BF16), v1(31488, 128, BF16)]
            OB5 = v1(31744, 1024, BF16)
            OFM5 = v1(33792, 1024, BF16)
            BIA5 = v1(35840, 1024, BF16)
            BOB5 = v1(37888, 1024)
            CT5 = [v1(41984, 1024), v1(46080, 1024)]
            NM5 = vs(1472, 1)[:, 0:1]
            DN5 = vs(1536, 1)[:, 0:1]
            t_ma = p.emit("sync", lambda s, _i: s.dma_start(MA5, maskA),
                          waits=p5pre, inc=sQ, inc_by=16)
            t_bo5 = p.emit("sync", lambda s, _i: s.dma_start(
                BOB5, abo[None, :].broadcast_to([128, D])),
                waits=tmax(t_ma), inc=sQ, inc_by=16)
            B5 = [abq, abk, abv]
            prev_rt = [p5pre, p5pre]
            prev_bias5 = p5pre
            prev_ct5 = [p5pre, p5pre]
            prev_qkv_banks = p5pre
            prev_score_bank = [p5pre, p5pre]
            prev_kv = [p5pre, p5pre]
            prev_qf = p5pre
            prev_mask_h_by_bank = {}
            prev_t4 = {"q": p5pre, "k": p5pre}
            prev_tex = p5pre
            prev_trp = p5pre
            prev_pv_banks = p5pre
            prev_wo_banks = p5pre
            tok_kv = [None] * NT
            for t in range(NT):
                b = t % 2
                rts = RT5s[b]
                rtk = []
                for j in range(4):
                    rtk.append(p.emit("sync", lambda s, _i, j=j, t=t, rts=rts: s.dma_start(
                        rts[j], ropeA[j, t * 128:(t + 1) * 128, :]),
                        waits=tmax(prev_rt[b]), inc=sR5[b], inc_by=16))
                t_rt = tmax(*rtk)
                rope_toks = []
                for wi, wn in enumerate(("q", "k", "v")):
                    mm = None
                    for dc in range(DC):
                        for hf in range(2):
                            w = [tok_w5] if (t == 0 and wi == 0 and dc == 0) else []
                            w.append(tok_ynt[t])
                            if dc == 0:
                                w.append(prev_qkv_banks)
                            mm = p.emit("tensor", lambda te, _i, t=t, wi=wi, dc=dc, hf=hf: te.matmul(
                                PSB[4 + hf][:, 0:512],
                                YNT[dc][:, t * 128:(t + 1) * 128],
                                WQ5(wi, dc)[:, hf * 512:(hf + 1) * 512],
                                start=(dc == 0), stop=(dc == DC - 1)),
                                waits=tmax(*w), inc=sT)
                    tb5 = p.emit("gpsimd", lambda gp, _i, wi=wi: gp.dma_start(
                        BIA5, B5[wi][None, :].broadcast_to([128, D])),
                        waits=tmax(prev_bias5), inc=sG, inc_by=16)
                    if wn == "v":
                        lastv = None
                        for hf in range(2):
                            lastv = p.emit("vector", lambda v, _i, b=b, hf=hf: v.tensor_tensor(
                                out=VR5[b][:, hf * 512:(hf + 1) * 512],
                                in0=PSB[4 + hf][:, 0:512],
                                in1=BIA5[:, hf * 512:(hf + 1) * 512], op=ALU.add),
                                waits=tmax(mm, tb5, prev_kv[b]), inc=sV)
                        prev_bias5 = lastv
                        prev_qkv_banks = lastv
                        tok_v = lastv
                    else:
                        dst = QT5 if wn == "q" else KT5
                        dstb = QB5 if wn == "q" else KB5
                        j0 = 0 if wn == "q" else 2
                        t1 = None
                        for hf in range(2):
                            t1 = p.emit("vector", lambda v, _i, dst=dst, hf=hf: v.tensor_tensor(
                                out=dst[:, hf * 512:(hf + 1) * 512],
                                in0=PSB[4 + hf][:, 0:512],
                                in1=BIA5[:, hf * 512:(hf + 1) * 512], op=ALU.add),
                                waits=tmax(mm, tb5, prev_t4[wn]), inc=sV)
                        prev_bias5 = t1
                        t2 = p.emit("vector", lambda v, _i, dst=dst, rts=rts, j0=j0: v.tensor_tensor(
                            out=TMP5.rearrange("p (h c) -> p h c", c=64),
                            in0=shuf64(dst),
                            in1=rts[j0 + 1][:, None, :].broadcast_to([128, 16, 64]),
                            op=ALU.mult), waits=tmax(t1, t_rt), inc=sV)
                        t3 = p.emit("vector", lambda v, _i, dst=dst, rts=rts, j0=j0: v.tensor_tensor(
                            out=dst.rearrange("p (h c) -> p h c", c=64),
                            in0=dst.rearrange("p (h c) -> p h c", c=64),
                            in1=rts[j0][:, None, :].broadcast_to([128, 16, 64]),
                            op=ALU.mult), waits=tmax(t2), inc=sV)
                        t4 = p.emit("vector", lambda v, _i, dst=dst, dstb=dstb: v.tensor_add(
                            dstb, dst, TMP5), waits=tmax(t3), inc=sV)
                        prev_t4[wn] = t4
                        rope_toks.append(t4)
                        trl = None
                        pb = 6
                        for c in range(DC):
                            trl = p.emit("tensor", lambda te, _i, c=c, dstb=dstb, pb=pb: te.matmul(
                                PSB[pb][:, c * 64:c * 64 + 64].bitcast(BF16),
                                dstb[:, c * 128:(c + 1) * 128], ident[:],
                                is_transpose=True, start=True, stop=True),
                                waits=tmax(t4, prev_score_bank[0] if c == 0 else None),
                                inc=sT)
                        dfm = QF5 if wn == "q" else KR5[b]
                        evl = None
                        for c in range(DC):
                            wv = [trl]
                            if wn == "q" and c == 0:
                                wv.append(prev_qf)
                            if wn == "k" and c == 0:
                                wv.append(prev_kv[b])
                            evl = p.emit("vector", lambda v, _i, c=c, dfm=dfm: v.tensor_copy(
                                dfm[:, c * 128:(c + 1) * 128],
                                PSB[pb][:, c * 64:c * 64 + 64].bitcast(BF16)),
                                waits=tmax(*wv), inc=sV)
                        if wn == "q":
                            tok_q = evl
                        else:
                            tok_k = evl
                        prev_score_bank[0] = evl
                        prev_qkv_banks = evl
                tok_kv[t] = tmax(tok_k, tok_v)
                nk = 2 if t >= 1 else 1
                W_ = 128 * nk
                koff = 256 - W_
                pv_done = []
                prev_mask_h = None
                prev_pv_h = {}
                for h in range(NH):
                    sb = 6 + (h % 2)
                    smms = []
                    for ki in range(nk):
                        kt = t - (nk - 1) + ki
                        w = [tok_q, tok_kv[kt]] if ki == 0 else []
                        if ki == 0:
                            if h < 2:
                                w.append(prev_score_bank[1])
                            else:
                                w.append(prev_mask_h_by_bank.get(sb))
                        smms.append(p.emit("tensor", lambda te, _i, h=h, kt=kt, ki=ki, sb=sb: te.matmul(
                            PSB[sb][:, ki * 128:(ki + 1) * 128],
                            KR5[kt % 2][(h % 2) * 64:(h % 2) * 64 + 64,
                                        (h // 2) * 128:(h // 2) * 128 + 128],
                            QF5[(h % 2) * 64:(h % 2) * 64 + 64,
                                (h // 2) * 128:(h // 2) * 128 + 128],
                            start=True, stop=True, is_transpose=True) if False else te.matmul(
                            PSB[sb][:, ki * 128:(ki + 1) * 128],
                            QF5[(h % 2) * 64:(h % 2) * 64 + 64,
                                (h // 2) * 128:(h // 2) * 128 + 128],
                            KR5[kt % 2][(h % 2) * 64:(h % 2) * 64 + 64,
                                        (h // 2) * 128:(h // 2) * 128 + 128],
                            start=True, stop=True),
                            waits=tmax(*w), inc=sT))
                    tmsk = p.emit("vector", lambda v, _i, W_=W_, koff=koff, sb=sb: v.tensor_tensor(
                        out=S5[:, 0:W_], in0=PSB[sb][:, 0:W_],
                        in1=MA5[:, koff:koff + W_], op=ALU.add),
                        waits=tmax(*smms, t_ma, prev_tex), inc=sV)
                    prev_mask_h_by_bank[sb] = tmsk
                    tmx5 = p.emit("vector", lambda v, _i, W_=W_: v.reduce_max(
                        NM5, S5[:, 0:W_], axis=AX, negate=True),
                        waits=tmax(tmsk), inc=sV, fence=True)
                    tex5 = p.emit("scalar", lambda sc, _i, W_=W_: sc.activation(
                        out=PR5[:, 0:W_], in_=S5[:, 0:W_], func=AF.Exp,
                        bias=NM5, accum_out=DN5),
                        waits=tmax(tmx5, prev_trp), inc=sS, fence=True)
                    prev_tex = tex5
                    trc5 = p.emit("vector", lambda v, _i: v.reciprocal(DN5, DN5),
                                  waits=tmax(tex5), inc=sV, fence=True)
                    tnr5 = p.emit("vector", lambda v, _i, W_=W_: v.tensor_scalar(
                        out=PR5[:, 0:W_], in0=PR5[:, 0:W_], scalar1=DN5,
                        scalar2=None, op0=ALU.mult), waits=tmax(trc5), inc=sV)
                    trp = []
                    for ki in range(nk):
                        trp.append(p.emit("tensor", lambda te, _i, ki=ki: te.matmul(
                            PSB[5][:, 64 * ki:64 * ki + 64].bitcast(BF16),
                            PR5[:, ki * 128:(ki + 1) * 128], ident[:],
                            is_transpose=True, start=True, stop=True),
                            waits=tmax(tnr5,
                                       prev_pv_h.get(h - 2) if ki == 0 else None),
                            inc=sT))
                    prev_trp = trp[-1]
                    tpt = []
                    for ki in range(nk):
                        tpt.append(p.emit("vector", lambda v, _i, ki=ki: v.tensor_copy(
                            PT5[ki], PSB[5][:, 64 * ki:64 * ki + 64].bitcast(BF16)),
                            waits=tmax(trp[-1]), inc=sV))
                    pvb = h % 2
                    last = None
                    for ki in range(nk):
                        kt = t - (nk - 1) + ki
                        w = [tpt[ki]]
                        if h < 2 and ki == 0:
                            w.append(prev_pv_banks)
                        last = p.emit("tensor", lambda te, _i, h=h, ki=ki, kt=kt, nk=nk: te.matmul(
                            PSB[2 if h < 8 else 3][:, (h % 8) * 64:(h % 8) * 64 + 64],
                            PT5[ki],
                            VR5[kt % 2][:, h * 64:h * 64 + 64],
                            start=(ki == 0), stop=(ki == nk - 1)),
                            waits=tmax(*w), inc=sT)
                    prev_pv_h[h] = last
                    pv_done.append(last)
                tob = []
                for hb in range(2):
                    tob.append(p.emit("vector", lambda v, _i, hb=hb: v.tensor_copy(
                        OB5[:, hb * 512:(hb + 1) * 512], PSB[2 + hb][:, 0:512]),
                        waits=tmax(*pv_done), inc=sV))
                trw = None
                for c in range(DC):
                    trw = p.emit("tensor", lambda te, _i, c=c: te.matmul(
                        PSB[4 + (c % 2)][:, (c // 2) * 128:(c // 2) * 128 + 64].bitcast(BF16),
                        OB5[:, c * 128:(c + 1) * 128], ident[:],
                        is_transpose=True, start=True, stop=True),
                        waits=tmax(*tob), inc=sT)
                tofm = None
                for c in range(DC):
                    tofm = p.emit("vector", lambda v, _i, c=c: v.tensor_copy(
                        OFM5[:, c * 128:(c + 1) * 128],
                        PSB[4 + (c % 2)][:, (c // 2) * 128:(c // 2) * 128 + 64].bitcast(BF16)),
                        waits=tmax(trw), inc=sV)
                wo_last = None
                for dc in range(DC):
                    for hf in range(2):
                        w = [tofm]
                        if dc == 0 and hf == 0:
                            w.append(prev_wo_banks)
                        wo_last = p.emit("tensor", lambda te, _i, dc=dc, hf=hf: te.matmul(
                            PSB[2 + hf][:, 0:512],
                            OFM5[:, dc * 128:(dc + 1) * 128],
                            WQ5(3, dc)[:, hf * 512:(hf + 1) * 512],
                            start=(dc == 0), stop=(dc == DC - 1)),
                            waits=tmax(*w), inc=sT)
                tct5 = None
                for hf in range(2):
                    tct5 = p.emit("vector", lambda v, _i, b=b, hf=hf: v.tensor_tensor(
                        out=CT5[b][:, hf * 512:(hf + 1) * 512],
                        in0=PSB[2 + hf][:, 0:512],
                        in1=BOB5[:, hf * 512:(hf + 1) * 512], op=ALU.add),
                        waits=tmax(wo_last, t_bo5, prev_ct5[b]), inc=sV)
                prev_wo_banks = tct5
                prev_ct5[b] = p.emit("sync", lambda s, _i, t=t, b=b: s.dma_start(
                    attn1_c[t * 128:(t + 1) * 128, :], CT5[b]),
                    waits=tmax(tct5), inc=sA5[b], inc_by=16)
                prev_pv_banks = tct5
                prev_score_bank[1] = tmax(*[prev_mask_h_by_bank[s_] for s_ in (6, 7)
                                            if s_ in prev_mask_h_by_bank])
                prev_kv[0] = prev_kv[1] = tmax(*pv_done)
                prev_score_bank[0] = tmax(prev_score_bank[1])
                prev_qkv_banks = tofm
                prev_qf = tmax(*pv_done, prev_score_bank[1])
                prev_rt[b] = tmax(*rope_toks)
            tok_attn1_done = tmax(prev_ct5[0], prev_ct5[1])

        z6_src = zs
        if "p6" in phases:
            from types import SimpleNamespace
            E6 = SimpleNamespace()
            E6.ident = ident; E6.t_id = t_id; E6.PSB = PSB
            E6.SM6 = vs(1536, 64)
            E6.maskI = maskI
            cur6 = [0]
            def al6(nelem, dt):
                sz = {F32: 4, BF16: 2, I32: 4, U8: 1}[dt]
                off = (cur6[0] + 63) // 64 * 64
                cur6[0] = off + nelem * sz
                return ARB[:, off:off + nelem * sz].bitcast(dt)
            E6.MIB = al6(384, F32)
            E6.WQK = al6(8 * I3, BF16); E6.WOB = al6(8 * 1024, BF16)
            E6.QSB = al6(64, F32); E6.KSB = al6(64, F32)
            E6.RTB = al6(2 * 4 * 64, F32)
            E6.SH = al6(2 * 1024, F32); E6.SHB = al6(2 * 1024, BF16)
            E6.XR = al6(2 * 1024, F32); E6.XST = al6(1024, BF16)
            E6.QN = al6(1024, F32); E6.SQ6 = al6(1024, F32); E6.TMP6 = al6(1024, F32)
            E6.QB = al6(1024, BF16); E6.KB = al6(1024, BF16)
            E6.QT = al6(1024, BF16); E6.KT = al6(3 * 1024, BF16); E6.VR = al6(3 * 1024, BF16)
            E6.S5 = al6(2 * 384, F32); E6.PR = al6(2 * 384, BF16); E6.PT = al6(2 * 384, BF16)
            E6.OB = al6(1024, BF16); E6.OFM = al6(1024, BF16); E6.CT = al6(2 * 1024, F32)
            E6.DNB = al6(64, F32)
            a6_end = cur6[0]
            cur6[0] = 0
            E6.W1B = al6(8 * DI2, BF16); E6.W2B = al6(22 * 1024, BF16)
            E6.B1BC = al6(DI2, F32); E6.GBC = al6(D, F32); E6.BBC = al6(D, F32)
            E6.B2BC = al6(D, F32); E6.F2 = al6(2 * 1024, F32); E6.FR = al6(2 * 1024, F32)
            E6.XNS = al6(1024, F32); E6.XNB = al6(1024, BF16); E6.XNT6 = al6(1024, BF16)
            E6.ASB = al6(512, F32); E6.GEB = al6(512, BF16)
            E6.HB = al6(DI, BF16); E6.HTB = al6(22 * 128, BF16)
            assert max(a6_end, cur6[0]) <= 207424, (a6_end, cur6[0])
            E6.sW = p.sem("s6W"); E6.sM = p.sem("s6M")
            E6.sLD = [p.sem("s6LDa"), p.sem("s6LDb")]
            E6.sR6 = [p.sem("s6Ra"), p.sem("s6Rb")]
            E6.sFD = [p.sem("s6FDa"), p.sem("s6FDb")]
            E6.sZO = [p.sem("s6ZOa"), p.sem("s6ZOb")]
            E6.sV = sV; E6.sS = sS; E6.sT = sT
            cur_z = zs
            for i6 in range(IDEPTH):
                nxt6 = zA if cur_z is not zA else zB
                emit_p6a_sec(p, E6, cur_z, nxt6, iwqkv[i6], iwo[i6],
                             iqs[i6], iks[i6], ropeI)
                cur_z = nxt6
                nxt6 = zA if cur_z is not zA else zB
                emit_p6f_sec(p, E6, cur_z, nxt6, iw1[i6], ib1[i6], iw2[i6],
                             ib2[i6], ilng[i6], ilnb[i6])
                cur_z = nxt6
            z6_src = cur_z

        if "p7" in phases:
            sL7 = [p.sem("sL7a"), p.sem("sL7b")]
            sY7 = [p.sem("sY7a"), p.sem("sY7b")]
            p7pre = p.barrier_token()
            X7 = [v1(0, 1024), v1(4096, 1024)]
            A7 = [v1(8192, 1024), v1(12288, 1024)]
            Z7 = [v1(16384, 1024), v1(20480, 1024)]
            prev7 = [p7pre, p7pre]
            for t in range(NT):
                b = t % 2
                l1 = p.emit("sync", lambda s, _i, t=t, b=b: s.dma_start(
                    X7[b], x1s[t * 128:(t + 1) * 128, :]), waits=tmax(prev7[b]),
                    inc=sL7[b], inc_by=16)
                l2 = p.emit("sync", lambda s, _i, t=t, b=b: s.dma_start(
                    A7[b], attn1_c[t * 128:(t + 1) * 128, :]), waits=tmax(prev7[b]),
                    inc=sL7[b], inc_by=16)
                l3 = p.emit("sync", lambda s, _i, t=t, b=b: s.dma_start(
                    Z7[b], z6_src[t * 128:(t + 1) * 128, :]), waits=tmax(prev7[b]),
                    inc=sL7[b], inc_by=16)
                a1 = p.emit("vector", lambda v, _i, b=b: v.tensor_add(
                    X7[b], X7[b], A7[b]), waits=tmax(l1, l2, l3), inc=sV)
                a2 = p.emit("vector", lambda v, _i, b=b: v.tensor_add(
                    X7[b], X7[b], Z7[b]), waits=tmax(a1), inc=sV)
                prev7[b] = p.emit("sync", lambda s, _i, t=t, b=b: s.dma_start(
                    y[t * 128:(t + 1) * 128, :], X7[b]), waits=tmax(a2),
                    inc=sY7[b], inc_by=16)
            p.emit("sync", lambda s, _i: s.nop(),
                   waits=tmax(prev7[0], prev7[1]))
            p.emit("gpsimd", lambda g, _i: g.nop(),
                   waits=tmax(prev7[0], prev7[1]))

        if "p5d" in dbg:
            d_a1 = dout("d_a1", (T, D))
            wAll5 = p.barrier_token()
            p.emit("sync", lambda s, _i, d_a1=d_a1: s.dma_start(d_a1, attn1_c),
                   waits=wAll5, inc=sQ, inc_by=16)
        if "p0x" in dbg:
            d_xt = dout("d_xt", (2, 128, 1024))
            d_xnb = dout("d_xnb", (2, 128, 1024))
            d_small = dout("d_small", (128, 512))
            wAllx = p.barrier_token()
            for i in range(2):
                p.emit("sync", lambda s, _i, i=i, d_xt=d_xt: s.dma_start(d_xt[i], XT[i]),
                       waits=wAllx, inc=sQ, inc_by=16)
                p.emit("gpsimd", lambda g, _i, i=i, d_xnb=d_xnb: g.dma_start(d_xnb[i], XNB[i]),
                       waits=wAllx, inc=sG, inc_by=16)
            p.emit("sync", lambda s, _i, d_small=d_small: s.dma_start(
                d_small, SMALL[:, 0:2048].bitcast(F32)), waits=wAllx, inc=sQ, inc_by=16)

        if "p0" in dbg:
            d_xnt = dout("d_xnt", (DC, 128, T))
            d_masks = dout("d_masks", (128, NT * NEXP))
            wAll = p.barrier_token()
            for c in range(DC):
                p.emit("gpsimd", lambda g, _i, c=c, d_xnt=d_xnt: g.dma_start(
                    d_xnt[c], XNT[c]), waits=wAll, inc=sG, inc_by=16)
            p.emit("sync", lambda s, _i, d_masks=d_masks: s.dma_start(d_masks, MASKS[:]),
                   waits=wAll, inc=sQ, inc_by=16)

        wEnd = p.barrier_token()
        p.emit("sync", lambda s, _i: s.nop(), waits=wEnd)
        p.emit("gpsimd", lambda g, _i: g.nop(), waits=wEnd)

        p.finalize()
    return nc, dbg_out



import dataclasses as _dc

def shuf64(ap):
    return _dc.replace(ap, offset=ap.offset + 32,
                       ap=[[ap.ap[0][0], 128], [64, 16], [-32, 2], [1, 32]])


def emit_p6a_sec(p, E, zin, zout, wqkv, wo, qs, ks, ropeI):
    entry = p.barrier_token()
    sW = E.sW; sM = E.sM; sLD = E.sLD; sR6 = E.sR6; sZO = E.sZO
    sV = E.sV; sS = E.sS; sT = E.sT
    WQK = E.WQK; WOB = E.WOB; QSB = E.QSB; KSB = E.KSB; MIB = E.MIB
    RTB = E.RTB; SH = E.SH; SHB = E.SHB; XR = E.XR; XST = E.XST
    QN = E.QN; SQ6 = E.SQ6; TMP6 = E.TMP6; SM6 = E.SM6
    QB = E.QB; KB = E.KB; QT = E.QT; KT = E.KT; VR = E.VR
    S5 = E.S5; PR = E.PR; PT = E.PT; OB = E.OB; OFM = E.OFM; CT = E.CT
    DNB = E.DNB
    ident = E.ident; PSB = E.PSB
    t_id = E.t_id
    RS16a = SM6[:, 0:16]
    RS16b = SM6[:, 16:32]
    NM6 = SM6[:, 32:33]
    DN6 = SM6[:, 33:34]

    z0r = p.emit("vector", lambda v, _i: v.memset(SH[0:1, 512:1024], 0.0),
                 waits=entry, inc=sV)
    wl = []
    for dc in range(DC):
        wl.append(p.emit("gpsimd", lambda g, _i, dc=dc: g.dma_start(
            WQK[:, dc * I3:(dc + 1) * I3], wqkv[dc * 128:(dc + 1) * 128, :]),
            waits=entry, inc=sW, inc_by=16))
    for dc in range(DC):
        wl.append(p.emit("gpsimd", lambda g, _i, dc=dc: g.dma_start(
            WOB[:, dc * 1024:(dc + 1) * 1024], wo[dc * 128:(dc + 1) * 128, :]),
            waits=entry, inc=sW, inc_by=16))
    tok_w = wl[-1]
    m1_ = p.emit("sync", lambda s, _i: s.dma_start(
        QSB[:], qs[None, :].broadcast_to([128, DH])), waits=entry, inc=sM, inc_by=16)
    m2_ = p.emit("sync", lambda s, _i: s.dma_start(
        KSB[:], ks[None, :].broadcast_to([128, DH])), waits=entry, inc=sM, inc_by=16)
    tok_qk = tmax(m1_, m2_)
    tok_m = p.emit("sync", lambda s, _i: s.dma_start(MIB[:], E.maskI),
                   waits=entry, inc=sM, inc_by=16)

    ent = tmax(entry, z0r)
    wSH = [ent, ent]
    wSHB = [ent, ent]
    wXR = [ent, ent]
    wQNprev = ent
    wTMPprev = ent
    evac_prev = ent
    tok_b45_read = ent
    tok_b6_read = {6: ent, 7: ent}
    prev_tex_p = [ent, ent]
    prev_trp_p = [ent, ent]
    prev_tpt = ent
    prev_ct = ent
    prev_rt = [ent, ent]
    prev_store = [ent, ent]
    tok_evq = None
    tok_evk = [None] * NT
    tok_vcp = [None] * NT

    for t in range(NT):
        b = t % 2
        k3 = t % 3
        shb = SH[:, b * 1024:(b + 1) * 1024]
        shbb = SHB[:, b * 1024:(b + 1) * 1024]
        xrb = XR[:, b * 1024:(b + 1) * 1024]
        l1 = p.emit("sync", lambda s, _i, t=t, shb=shb: s.dma_start(
            shb[:, 0:512], zin[t * 128:(t + 1) * 128, 0:512]),
            waits=tmax(wSH[b]), inc=sLD[b], inc_by=16)
        if t == 0:
            l2 = p.emit("sync", lambda s, _i, shb=shb: s.dma_start(
                shb[1:128, 512:1024], zin[0:127, 512:1024]), inc=sLD[b], inc_by=16)
        else:
            l2 = p.emit("sync", lambda s, _i, t=t, shb=shb: s.dma_start(
                shb[:, 512:1024], zin[t * 128 - 1:(t + 1) * 128 - 1, 512:1024]),
                inc=sLD[b], inc_by=16)
        l3 = p.emit("sync", lambda s, _i, t=t, xrb=xrb: s.dma_start(
            xrb, zin[t * 128:(t + 1) * 128, :]),
            waits=tmax(wXR[b]), inc=sLD[b], inc_by=16)
        ld = tmax(l1, l2, l3)
        rts = [RTB[:, (b * 4 + j) * 64:(b * 4 + j + 1) * 64] for j in range(4)]
        rl = []
        for j in range(4):
            rl.append(p.emit("sync", lambda s, _i, t=t, j=j, rts=rts: s.dma_start(
                rts[j], ropeI[j, t * 128:(t + 1) * 128, :]),
                waits=tmax(prev_rt[b]), inc=sR6[b], inc_by=16))
        t_rt = tmax(*rl)
        c1 = p.emit("vector", lambda v, _i, shbb=shbb, shb=shb: v.tensor_copy(shbb, shb),
                    waits=tmax(ld, wSHB[b]), inc=sV)
        wSH[b] = c1
        trl = None
        for c in range(DC):
            trl = p.emit("tensor", lambda te, _i, c=c, shbb=shbb: te.matmul(
                PSB[0][:, c * 64:c * 64 + 64].bitcast(BF16),
                shbb[:, c * 128:(c + 1) * 128], ident[:],
                is_transpose=True, start=True, stop=True),
                waits=tmax(c1, t_id, evac_prev if c == 0 else None), inc=sT)
        wSHB[b] = trl
        evl = None
        for c in range(DC):
            evl = p.emit("vector", lambda v, _i, c=c: v.tensor_copy(
                XST[:, c * 128:(c + 1) * 128],
                PSB[0][:, c * 64:c * 64 + 64].bitcast(BF16)),
                waits=tmax(trl), inc=sV)
        evac_prev = evl
        for wi, wn in enumerate(("q", "k", "v")):
            mm = None
            for dc in range(DC):
                for hf in range(2):
                    w = [evl, tok_b45_read] if dc == 0 else []
                    if t == 0 and wi == 0 and dc == 0:
                        w.append(tok_w)
                    mm = p.emit("tensor", lambda te, _i, wi=wi, dc=dc, hf=hf: te.matmul(
                        PSB[4 + hf][:, 0:512],
                        XST[:, dc * 128:(dc + 1) * 128],
                        WQK[:, dc * I3 + wi * 1024 + hf * 512:
                            dc * I3 + wi * 1024 + (hf + 1) * 512],
                        start=(dc == 0), stop=(dc == DC - 1)),
                        waits=tmax(*w), inc=sT)
            if wn == "v":
                lastv = None
                for hf in range(2):
                    lastv = p.emit("vector", lambda v, _i, k3=k3, hf=hf: v.tensor_copy(
                        VR[:, k3 * 1024 + hf * 512:k3 * 1024 + (hf + 1) * 512],
                        PSB[4 + hf][:, 0:512]),
                        waits=tmax(mm), inc=sV)
                tok_vcp[t] = lastv
                tok_b45_read = lastv
            else:
                qn = None
                for hf in range(2):
                    qn = p.emit("vector", lambda v, _i, hf=hf: v.tensor_copy(
                        QN[:, hf * 512:(hf + 1) * 512], PSB[4 + hf][:, 0:512]),
                        waits=tmax(mm, wQNprev), inc=sV)
                tok_b45_read = qn
                t_sq = p.emit("scalar", lambda sc, _i: sc.activation(
                    out=SQ6[:], in_=QN[:], func=AF.Square), waits=tmax(qn), inc=sS)
                t_rs = p.emit("vector", lambda v, _i: v.tensor_reduce(
                    RS16a, SQ6.rearrange("p (h c) -> p h c", c=64), axis=AX,
                    op=ALU.add), waits=tmax(t_sq), inc=sV, fence=True)
                t_sr = p.emit("scalar", lambda sc, _i: sc.activation(
                    out=RS16b, in_=RS16a, func=AF.Sqrt), waits=tmax(t_rs), inc=sS,
                    fence=True)
                t_mx = p.emit("vector", lambda v, _i: v.tensor_scalar(
                    out=RS16b, in0=RS16b, scalar1=1.0e-12, scalar2=None,
                    op0=ALU.max), waits=tmax(t_sr), inc=sV, fence=True)
                t_rc = p.emit("vector", lambda v, _i: v.reciprocal(RS16b, RS16b),
                              waits=tmax(t_mx), inc=sV, fence=True)
                scb = QSB if wn == "q" else KSB
                m1 = p.emit("vector", lambda v, _i: v.tensor_tensor(
                    out=QN.rearrange("p (h c) -> p h c", c=64),
                    in0=QN.rearrange("p (h c) -> p h c", c=64),
                    in1=RS16b[:, :, None].broadcast_to([128, 16, 64]),
                    op=ALU.mult), waits=tmax(t_rc), inc=sV)
                m2 = p.emit("vector", lambda v, _i, scb=scb: v.tensor_tensor(
                    out=QN.rearrange("p (h c) -> p h c", c=64),
                    in0=QN.rearrange("p (h c) -> p h c", c=64),
                    in1=scb[:, None, :].broadcast_to([128, 16, 64]),
                    op=ALU.mult), waits=tmax(m1, tok_qk), inc=sV)
                j0 = 0 if wn == "q" else 2
                dstb = QB if wn == "q" else KB
                t2 = p.emit("vector", lambda v, _i, rts=rts, j0=j0: v.tensor_tensor(
                    out=TMP6.rearrange("p (h c) -> p h c", c=64),
                    in0=shuf64(QN[:]),
                    in1=rts[j0 + 1][:, None, :].broadcast_to([128, 16, 64]),
                    op=ALU.mult), waits=tmax(m2, t_rt, wTMPprev), inc=sV)
                t3 = p.emit("vector", lambda v, _i, rts=rts, j0=j0: v.tensor_tensor(
                    out=QN.rearrange("p (h c) -> p h c", c=64),
                    in0=QN.rearrange("p (h c) -> p h c", c=64),
                    in1=rts[j0][:, None, :].broadcast_to([128, 16, 64]),
                    op=ALU.mult), waits=tmax(t2), inc=sV)
                t4 = p.emit("vector", lambda v, _i, dstb=dstb: v.tensor_add(
                    dstb[:], QN[:], TMP6[:]), waits=tmax(t3), inc=sV)
                wQNprev = t4
                wTMPprev = t4
                dfm = QT[:] if wn == "q" else KT[:, k3 * 1024:(k3 + 1) * 1024]
                trq = None
                for c in range(DC):
                    trq = p.emit("tensor", lambda te, _i, c=c, dstb=dstb: te.matmul(
                        PSB[6][:, c * 64:c * 64 + 64].bitcast(BF16),
                        dstb[:, c * 128:(c + 1) * 128], ident[:],
                        is_transpose=True, start=True, stop=True),
                        waits=tmax(t4, tok_b6_read[6] if c == 0 else None), inc=sT)
                evq = None
                for c in range(DC):
                    evq = p.emit("vector", lambda v, _i, c=c, dfm=dfm: v.tensor_copy(
                        dfm[:, c * 128:(c + 1) * 128],
                        PSB[6][:, c * 64:c * 64 + 64].bitcast(BF16)),
                        waits=tmax(trq), inc=sV)
                tok_b6_read[6] = evq
                if wn == "q":
                    tok_evq = evq
                else:
                    tok_evk[t] = evq
        prev_rt[b] = wTMPprev
        nk = min(t + 1, 3)
        W_ = 128 * nk
        pv_done = []
        DN16 = DNB[:, 0:16]
        DNR = DNB[:, 16:32]
        NM2 = DNB[:, 32:34]
        tex_last = None
        for h in range(NH):
            sbk = 6 + (h % 2)
            hp = h % 2
            S5h = S5[:, hp * 384:hp * 384 + 384]
            PRh = PR[:, hp * 384:hp * 384 + 384]
            PTh = PT[:, hp * 384:hp * 384 + 384]
            NM6h = NM2[:, hp:hp + 1]
            smms = []
            for ki in range(nk):
                kt = t - (nk - 1) + ki
                ksl = kt % 3
                w = []
                if ki == 0:
                    w = [tok_evq, tok_evk[kt], tok_vcp[kt], tok_b6_read[sbk]]
                smms.append(p.emit("tensor", lambda te, _i, h=h, ki=ki, ksl=ksl, sbk=sbk: te.matmul(
                    PSB[sbk][:, ki * 128:(ki + 1) * 128],
                    QT[(h % 2) * 64:(h % 2) * 64 + 64,
                       (h // 2) * 128:(h // 2) * 128 + 128],
                    KT[(h % 2) * 64:(h % 2) * 64 + 64,
                       ksl * 1024 + (h // 2) * 128:ksl * 1024 + (h // 2) * 128 + 128],
                    start=True, stop=True),
                    waits=tmax(*w), inc=sT))
            tmsk = p.emit("vector", lambda v, _i, W_=W_, nk=nk, sbk=sbk, S5h=S5h: v.tensor_tensor(
                out=S5h[:, 0:W_], in0=PSB[sbk][:, 0:W_],
                in1=MIB[:, (3 - nk) * 128:384], op=ALU.add),
                waits=tmax(*smms, tok_m, prev_tex_p[hp]), inc=sV)
            tok_b6_read[sbk] = tmsk
            tmx = p.emit("vector", lambda v, _i, W_=W_, S5h=S5h, NM6h=NM6h: v.reduce_max(
                NM6h, S5h[:, 0:W_], axis=AX, negate=True),
                waits=tmax(tmsk), inc=sV, fence=True)
            tex = p.emit("scalar", lambda sc, _i, W_=W_, h=h, S5h=S5h, PRh=PRh, NM6h=NM6h: sc.activation(
                out=PRh[:, 0:W_], in_=S5h[:, 0:W_], func=AF.Exp,
                bias=NM6h, accum_out=DN16[:, h:h + 1]),
                waits=tmax(tmx, prev_trp_p[hp]), inc=sS, fence=True)
            prev_tex_p[hp] = tex
            tex_last = tex
            trp = []
            for ki in range(nk):
                w = [tex]
                if ki == 0:
                    w.append(prev_tpt)
                trp.append(p.emit("tensor", lambda te, _i, ki=ki, PRh=PRh: te.matmul(
                    PSB[5][:, ki * 64:ki * 64 + 64].bitcast(BF16),
                    PRh[:, ki * 128:(ki + 1) * 128], ident[:],
                    is_transpose=True, start=True, stop=True),
                    waits=tmax(*w), inc=sT))
            prev_trp_p[hp] = trp[-1]
            tpt = []
            for ki in range(nk):
                tpt.append(p.emit("vector", lambda v, _i, ki=ki, PTh=PTh: v.tensor_copy(
                    PTh[:, ki * 128:(ki + 1) * 128],
                    PSB[5][:, ki * 64:ki * 64 + 64].bitcast(BF16)),
                    waits=tmax(trp[ki]), inc=sV))
            prev_tpt = tpt[-1]
            last = None
            for ki in range(nk):
                kt = t - (nk - 1) + ki
                ksl = kt % 3
                w = [tpt[ki]]
                if h < 2 and ki == 0:
                    w.append(prev_ct)
                last = p.emit("tensor", lambda te, _i, h=h, ki=ki, ksl=ksl, nk=nk, PTh=PTh: te.matmul(
                    PSB[2 if h < 8 else 3][:, (h % 8) * 64:(h % 8) * 64 + 64],
                    PTh[:, ki * 128:(ki + 1) * 128],
                    VR[:, ksl * 1024 + h * 64:ksl * 1024 + h * 64 + 64],
                    start=(ki == 0), stop=(ki == nk - 1)),
                    waits=tmax(*w), inc=sT)
            pv_done.append(last)
        t_dnr = p.emit("vector", lambda v, _i: v.reciprocal(DNR, DN16),
                       waits=tmax(tex_last), inc=sV, fence=True)
        tob = []
        for hb in range(2):
            tob.append(p.emit("vector", lambda v, _i, hb=hb: v.tensor_tensor(
                out=OB[:, hb * 512:(hb + 1) * 512].rearrange("p (h c) -> p h c", c=64),
                in0=PSB[2 + hb][:, 0:512].rearrange("p (h c) -> p h c", c=64),
                in1=DNR[:, hb * 8:(hb + 1) * 8, None].broadcast_to([128, 8, 64]),
                op=ALU.mult),
                waits=tmax(*pv_done, t_dnr), inc=sV))
        trw = None
        for c in range(DC):
            trw = p.emit("tensor", lambda te, _i, c=c: te.matmul(
                PSB[4 + (c % 2)][:, (c // 2) * 128:(c // 2) * 128 + 64].bitcast(BF16),
                OB[:, c * 128:(c + 1) * 128], ident[:],
                is_transpose=True, start=True, stop=True),
                waits=tmax(*tob), inc=sT)
        tofm = None
        for c in range(DC):
            tofm = p.emit("vector", lambda v, _i, c=c: v.tensor_copy(
                OFM[:, c * 128:(c + 1) * 128],
                PSB[4 + (c % 2)][:, (c // 2) * 128:(c // 2) * 128 + 64].bitcast(BF16)),
                waits=tmax(trw), inc=sV)
        tok_b45_read = tofm
        wo_last = None
        for dc in range(DC):
            for hf in range(2):
                w = [tofm]
                if dc == 0 and hf == 0:
                    w.append(prev_ct)
                wo_last = p.emit("tensor", lambda te, _i, dc=dc, hf=hf: te.matmul(
                    PSB[2 + hf][:, 0:512],
                    OFM[:, dc * 128:(dc + 1) * 128],
                    WOB[:, dc * 1024 + hf * 512:dc * 1024 + (hf + 1) * 512],
                    start=(dc == 0), stop=(dc == DC - 1)),
                    waits=tmax(*w), inc=sT)
        ctb = CT[:, b * 1024:(b + 1) * 1024]
        tct = None
        for hf in range(2):
            tct = p.emit("vector", lambda v, _i, ctb=ctb, hf=hf, xrb=xrb: v.tensor_tensor(
                out=ctb[:, hf * 512:(hf + 1) * 512],
                in0=PSB[2 + hf][:, 0:512],
                in1=xrb[:, hf * 512:(hf + 1) * 512], op=ALU.add),
                waits=tmax(wo_last, ld, prev_store[b]), inc=sV)
        prev_ct = tct
        wXR[b] = tct
        prev_store[b] = p.emit("sync", lambda s, _i, t=t, ctb=ctb: s.dma_start(
            zout[t * 128:(t + 1) * 128, :], ctb),
            waits=tmax(tct), inc=sZO[b], inc_by=16)
    return tmax(prev_store[0], prev_store[1])


def emit_p6f_sec(p, E, zin, zout, w1, b1, w2, b2, lng, lnb):
    NFC = 22
    entry = p.barrier_token()
    sW = E.sW; sM = E.sM; sFD = E.sFD; sZO = E.sZO
    sV = E.sV; sS = E.sS; sT = E.sT
    W1B = E.W1B; W2B = E.W2B; B1BC = E.B1BC; GBC = E.GBC; BBC = E.BBC
    B2BC = E.B2BC; F2 = E.F2; FR = E.FR; XNS = E.XNS; XNB = E.XNB
    XNT6 = E.XNT6; ASB = E.ASB; GEB = E.GEB; HB = E.HB
    HTB = E.HTB; SM6 = E.SM6
    GSB = XNS[:, 0:512]
    ident = E.ident; PSB = E.PSB
    t_id = E.t_id
    EPSB = SM6[:, 40:41]

    z0r = p.emit("vector", lambda v, _i: v.memset(F2[0:1, 512:1024], 0.0),
                 waits=entry, inc=sV)
    t_eps = p.emit("vector", lambda v, _i: v.memset(EPSB, 1.0e-5),
                   waits=entry, inc=sV)
    wl = []
    for dc in range(DC):
        wl.append(p.emit("gpsimd", lambda g, _i, dc=dc: g.dma_start(
            W1B[:, dc * DI2:(dc + 1) * DI2], w1[dc * 128:(dc + 1) * 128, :]),
            waits=entry, inc=sW, inc_by=16))
    for fc in range(NFC):
        cw = min(128, DI - fc * 128)
        wl.append(p.emit("gpsimd", lambda g, _i, fc=fc, cw=cw: g.dma_start(
            W2B[0:cw, fc * 1024:(fc + 1) * 1024], w2[fc * 128:fc * 128 + cw, :]),
            waits=entry, inc=sW, inc_by=16))
    tok_w = wl[-1]
    ml = []
    ml.append(p.emit("sync", lambda s, _i: s.dma_start(
        B1BC[:], b1[None, :].broadcast_to([128, DI2])), waits=entry, inc=sM, inc_by=16))
    ml.append(p.emit("sync", lambda s, _i: s.dma_start(
        GBC[:], lng[None, :].broadcast_to([128, D])), waits=entry, inc=sM, inc_by=16))
    ml.append(p.emit("sync", lambda s, _i: s.dma_start(
        BBC[:], lnb[None, :].broadcast_to([128, D])), waits=entry, inc=sM, inc_by=16))
    ml.append(p.emit("sync", lambda s, _i: s.dma_start(
        B2BC[:], b2[None, :].broadcast_to([128, D])), waits=entry, inc=sM, inc_by=16))
    tok_m = ml[-1]

    ent = tmax(entry, z0r, t_eps)
    wF2 = [ent, ent]
    wFR = [ent, ent]
    wXNS = ent
    wXNB = ent
    evac_prev = ent
    tok_bank_read = [ent] * 3
    prev_th = ent
    prev_ht_evac = [ent, ent]
    prev_w2_read = ent
    prev_store = [ent, ent]
    prev_out_read = ent

    for t in range(NT):
        b = t % 2
        f2b = F2[:, b * 1024:(b + 1) * 1024]
        frb = FR[:, b * 1024:(b + 1) * 1024]
        outb = f2b
        f1 = p.emit("sync", lambda s, _i, t=t, f2b=f2b: s.dma_start(
            f2b[:, 0:512], zin[t * 128:(t + 1) * 128, 0:512]),
            waits=tmax(wF2[b]), inc=sFD[b], inc_by=16)
        if t == 0:
            f2_ = p.emit("sync", lambda s, _i, f2b=f2b: s.dma_start(
                f2b[1:128, 512:1024], zin[0:127, 512:1024]), inc=sFD[b], inc_by=16)
        else:
            f2_ = p.emit("sync", lambda s, _i, t=t, f2b=f2b: s.dma_start(
                f2b[:, 512:1024], zin[t * 128 - 1:(t + 1) * 128 - 1, 512:1024]),
                inc=sFD[b], inc_by=16)
        f3 = p.emit("sync", lambda s, _i, t=t, frb=frb: s.dma_start(
            frb, zin[t * 128:(t + 1) * 128, :]),
            waits=tmax(wFR[b]), inc=sFD[b], inc_by=16)
        ld = tmax(f1, f2_, f3)
        meanraw = SM6[:, 0:1]; sqs = SM6[:, 1:2]; var0 = SM6[:, 2:3]
        rstd = SM6[:, 3:4]; mean = SM6[:, 4:5]; msq = SM6[:, 5:6]
        t_m0 = p.emit("vector", lambda v, _i, f2b=f2b, meanraw=meanraw: v.reduce_sum(
            meanraw, f2b, axis=AX), waits=tmax(ld), inc=sV, fence=True)
        t_s0 = p.emit("scalar", lambda sc, _i, f2b=f2b, sqs=sqs: sc.activation(
            out=XNS[:], in_=f2b, func=AF.Square, accum_out=sqs),
            waits=tmax(ld, wXNS), inc=sS, fence=True)
        p.emit("vector", lambda v, _i, mean=mean, meanraw=meanraw: v.tensor_scalar(
            out=mean, in0=meanraw, scalar1=1.0 / D, scalar2=None, op0=ALU.mult),
            waits=tmax(t_m0), inc=sV)
        t_msq = p.emit("vector", lambda v, _i, msq=msq, meanraw=meanraw: v.tensor_scalar(
            out=msq, in0=meanraw, scalar1=meanraw, scalar2=None, op0=ALU.mult),
            inc=sV, fence=True)
        t_v0 = p.emit("vector", lambda v, _i, var0=var0, sqs=sqs, msq=msq: v.scalar_tensor_tensor(
            out=var0, in0=sqs, scalar=float(D), in1=msq, op0=ALU.mult, op1=ALU.subtract),
            waits=tmax(t_s0, t_msq), inc=sV, fence=True)
        t_sq = p.emit("scalar", lambda sc, _i, var0=var0, rstd=rstd: sc.activation(
            out=rstd, in_=var0, func=AF.Sqrt, bias=EPSB, scale=1.0 / (D * D)),
            waits=tmax(t_v0), inc=sS, fence=True)
        t_rc = p.emit("vector", lambda v, _i, rstd=rstd: v.reciprocal(rstd, rstd),
                      waits=tmax(t_sq), inc=sV, fence=True)
        t_xn = p.emit("vector", lambda v, _i, f2b=f2b, mean=mean, rstd=rstd: v.tensor_scalar(
            out=XNS[:], in0=f2b, scalar1=mean, scalar2=rstd,
            op0=ALU.subtract, op1=ALU.mult), waits=tmax(t_rc, t_s0), inc=sV)
        wF2[b] = tmax(t_xn, t_s0)
        t_xg = p.emit("vector", lambda v, _i: v.tensor_tensor(
            out=XNS[:], in0=XNS[:], in1=GBC[:], op=ALU.mult),
            waits=tmax(t_xn, tok_m), inc=sV)
        t_xb = p.emit("vector", lambda v, _i: v.tensor_add(
            XNB[:], XNS[:], BBC[:]), waits=tmax(t_xg, wXNB), inc=sV)
        wXNS = t_xb
        trl = None
        for c in range(DC):
            trl = p.emit("tensor", lambda te, _i, c=c: te.matmul(
                PSB[0][:, c * 64:c * 64 + 64].bitcast(BF16),
                XNB[:, c * 128:(c + 1) * 128], ident[:],
                is_transpose=True, start=True, stop=True),
                waits=tmax(t_xb, t_id, evac_prev if c == 0 else None), inc=sT)
        wXNB = trl
        evl = None
        for c in range(DC):
            evl = p.emit("vector", lambda v, _i, c=c: v.tensor_copy(
                XNT6[:, c * 128:(c + 1) * 128],
                PSB[0][:, c * 64:c * 64 + 64].bitcast(BF16)),
                waits=tmax(trl), inc=sV)
        evac_prev = evl
        th_tok = [None] * 6
        for j in range(6):
            aw = 512 if j < 5 else 170
            bkA = 2 + (j % 3) * 2
            bkB = 3 + (j % 3) * 2
            mmA = None; mmB = None
            for dc in range(DC):
                w = []
                if dc == 0:
                    w = [evl, tok_bank_read[j % 3]]
                    if t == 0 and j == 0:
                        w.append(tok_w)
                mmA = p.emit("tensor", lambda te, _i, j=j, dc=dc, aw=aw, bkA=bkA: te.matmul(
                    PSB[bkA][:, 0:aw],
                    XNT6[:, dc * 128:(dc + 1) * 128],
                    W1B[:, dc * DI2 + j * 512:dc * DI2 + j * 512 + aw],
                    start=(dc == 0), stop=(dc == DC - 1)),
                    waits=tmax(*w), inc=sT)
            for dc in range(DC):
                mmB = p.emit("tensor", lambda te, _i, j=j, dc=dc, aw=aw, bkB=bkB: te.matmul(
                    PSB[bkB][:, 0:aw],
                    XNT6[:, dc * 128:(dc + 1) * 128],
                    W1B[:, dc * DI2 + DI + j * 512:dc * DI2 + DI + j * 512 + aw],
                    start=(dc == 0), stop=(dc == DC - 1)),
                    inc=sT)
            ta = p.emit("vector", lambda v, _i, j=j, aw=aw, bkA=bkA: v.tensor_tensor(
                out=ASB[:, 0:aw], in0=PSB[bkA][:, 0:aw],
                in1=B1BC[:, j * 512:j * 512 + aw], op=ALU.add),
                waits=tmax(mmA, mmB, tok_m, prev_th), inc=sV)
            tg = p.emit("vector", lambda v, _i, j=j, aw=aw, bkB=bkB: v.tensor_tensor(
                out=GSB[:, 0:aw], in0=PSB[bkB][:, 0:aw],
                in1=B1BC[:, DI + j * 512:DI + j * 512 + aw], op=ALU.add),
                waits=tmax(ta), inc=sV)
            tok_bank_read[j % 3] = tg
            ge = p.emit("scalar", lambda sc, _i, aw=aw: sc.activation(
                out=GEB[:, 0:aw], in_=GSB[:, 0:aw], func=AF.Gelu),
                waits=tmax(tg, t_id if (t == 0 and j == 0) else None), inc=sS)
            th = p.emit("vector", lambda v, _i, j=j, aw=aw: v.tensor_tensor(
                out=HB[:, j * 512:j * 512 + aw], in0=ASB[:, 0:aw],
                in1=GEB[:, 0:aw], op=ALU.mult),
                waits=tmax(ge), inc=sV)
            th_tok[j] = th
            prev_th = th
        wXNS = th_tok[5]
        ht_ev = None
        for fc in range(NFC):
            cw = min(128, DI - fc * 128)
            bk = fc % 2
            trh = p.emit("tensor", lambda te, _i, fc=fc, cw=cw, bk=bk: te.matmul(
                PSB[bk][:, 0:64].bitcast(BF16)[0:cw, :],
                HB[:, fc * 128:fc * 128 + cw], ident[:],
                is_transpose=True, start=True, stop=True),
                waits=tmax(th_tok[min(fc // 4, 5)], prev_ht_evac[bk],
                           th_tok[5] if fc == 0 else None), inc=sT)
            ht_ev = p.emit("vector", lambda v, _i, fc=fc, cw=cw, bk=bk: v.tensor_copy(
                HTB[0:cw, fc * 128:(fc + 1) * 128],
                PSB[bk][:, 0:64].bitcast(BF16)[0:cw, :]),
                waits=tmax(trh), inc=sV)
            prev_ht_evac[bk] = ht_ev
        w2_last = None
        for fc in range(NFC):
            cw = min(128, DI - fc * 128)
            for hf in range(2):
                w = []
                if fc == 0:
                    w = [ht_ev, prev_w2_read]
                w2_last = p.emit("tensor", lambda te, _i, fc=fc, cw=cw, hf=hf: te.matmul(
                    PSB[6 + hf][:, 0:512],
                    HTB[0:cw, fc * 128:(fc + 1) * 128],
                    W2B[0:cw, fc * 1024 + hf * 512:fc * 1024 + (hf + 1) * 512],
                    start=(fc == 0), stop=(fc == NFC - 1)),
                    waits=tmax(*w), inc=sT)
        to2 = None
        for hf in range(2):
            to1 = p.emit("vector", lambda v, _i, outb=outb, hf=hf: v.tensor_tensor(
                out=outb[:, hf * 512:(hf + 1) * 512],
                in0=PSB[6 + hf][:, 0:512],
                in1=B2BC[:, hf * 512:(hf + 1) * 512], op=ALU.add),
                waits=tmax(w2_last, tok_m, prev_out_read, t_xn, t_s0), inc=sV)
            to2 = p.emit("vector", lambda v, _i, outb=outb, frb=frb, hf=hf: v.tensor_add(
                outb[:, hf * 512:(hf + 1) * 512],
                outb[:, hf * 512:(hf + 1) * 512],
                frb[:, hf * 512:(hf + 1) * 512]),
                waits=tmax(to1, ld), inc=sV)
        prev_w2_read = to2
        tok_bank_read[2] = to2
        wFR[b] = to2
        prev_out_read = p.emit("sync", lambda s, _i, t=t, outb=outb: s.dma_start(
            zout[t * 128:(t + 1) * 128, :], outb),
            waits=tmax(to2), inc=sZO[b], inc_by=16)
        prev_store[b] = prev_out_read
        wF2[b] = prev_out_read
    return tmax(prev_store[0], prev_store[1])


def build_p6chain():
    from types import SimpleNamespace
    nc = bass.Bass("TRN2", target_bir_lowering=False)
    zin = nc.dram_tensor("zin", [T, D], F32, kind="ExternalInput").ap()
    iwqkv = nc.dram_tensor("iwqkv", [IDEPTH, D, I3], F32, kind="ExternalInput").ap()
    iqs = nc.dram_tensor("iqs", [IDEPTH, DH], F32, kind="ExternalInput").ap()
    iks = nc.dram_tensor("iks", [IDEPTH, DH], F32, kind="ExternalInput").ap()
    iwo = nc.dram_tensor("iwo", [IDEPTH, D, D], F32, kind="ExternalInput").ap()
    ilng = nc.dram_tensor("ilng", [IDEPTH, D], F32, kind="ExternalInput").ap()
    ilnb = nc.dram_tensor("ilnb", [IDEPTH, D], F32, kind="ExternalInput").ap()
    iw1 = nc.dram_tensor("iw1", [IDEPTH, D, DI2], F32, kind="ExternalInput").ap()
    ib1 = nc.dram_tensor("ib1", [IDEPTH, DI2], F32, kind="ExternalInput").ap()
    iw2 = nc.dram_tensor("iw2", [IDEPTH, DI, D], F32, kind="ExternalInput").ap()
    ib2 = nc.dram_tensor("ib2", [IDEPTH, D], F32, kind="ExternalInput").ap()
    ropeI = nc.dram_tensor("ropeI", [4, T, DH], F32, kind="ExternalInput").ap()
    maskI = nc.dram_tensor("maskI", [128, 384], F32, kind="ExternalInput").ap()
    zA = nc.dram_tensor("zA", [T, D], F32, kind="Internal").ap()
    zB = nc.dram_tensor("zB", [T, D], F32, kind="Internal").ap()
    zout = nc.dram_tensor("zout", [T, D], F32, kind="ExternalOutput").ap()

    with ExitStack() as ctx:
        p = Prog(nc, ctx)
        E = SimpleNamespace()
        E.ident = ctx.enter_context(nc.sbuf_tensor("ident", [128, 128], BF16))
        io6 = ctx.enter_context(nc.sbuf_tensor("io6", [128, 128], I32))
        E.SM6 = ctx.enter_context(nc.sbuf_tensor("SM6", [128, 64], F32))
        AR = ctx.enter_context(nc.sbuf_tensor("AR6", [128, 211712], U8))
        cur = [0]
        def al(nelem, dt):
            sz = {F32: 4, BF16: 2, I32: 4, U8: 1}[dt]
            off = (cur[0] + 63) // 64 * 64
            cur[0] = off + nelem * sz
            return AR[:, off:off + nelem * sz].bitcast(dt)
        E.MIB = al(384, F32)
        E.WQK = al(8 * I3, BF16); E.WOB = al(8 * 1024, BF16)
        E.QSB = al(64, F32); E.KSB = al(64, F32)
        E.RTB = al(2 * 4 * 64, F32)
        E.SH = al(2 * 1024, F32); E.SHB = al(2 * 1024, BF16)
        E.XR = al(2 * 1024, F32); E.XST = al(1024, BF16)
        E.QN = al(1024, F32); E.SQ6 = al(1024, F32); E.TMP6 = al(1024, F32)
        E.QB = al(1024, BF16); E.KB = al(1024, BF16)
        E.QT = al(1024, BF16); E.KT = al(3 * 1024, BF16); E.VR = al(3 * 1024, BF16)
        E.S5 = al(2 * 384, F32); E.PR = al(2 * 384, BF16); E.PT = al(2 * 384, BF16)
        E.OB = al(1024, BF16); E.OFM = al(1024, BF16); E.CT = al(2 * 1024, F32)
        E.DNB = al(64, F32)
        attn_end = cur[0]
        cur[0] = 0
        E.W1B = al(8 * DI2, BF16); E.W2B = al(22 * 1024, BF16)
        E.B1BC = al(DI2, F32); E.GBC = al(D, F32); E.BBC = al(D, F32)
        E.B2BC = al(D, F32); E.F2 = al(2 * 1024, F32); E.FR = al(2 * 1024, F32)
        E.XNS = al(1024, F32); E.XNB = al(1024, BF16); E.XNT6 = al(1024, BF16)
        E.ASB = al(512, F32); E.GEB = al(512, BF16)
        E.HB = al(DI, BF16); E.HTB = al(22 * 128, BF16)
        assert max(attn_end, cur[0]) <= 211712, (attn_end, cur[0])
        E.PSB = [ctx.enter_context(nc.psum_tensor(f"psb{i}", [128, 512], F32))
                 for i in range(8)]

        E.sW = p.sem("sW"); E.sM = p.sem("sM")
        E.sLD = [p.sem("sLDa"), p.sem("sLDb")]
        E.sR6 = [p.sem("sR6a"), p.sem("sR6b")]
        E.sFD = [p.sem("sFDa"), p.sem("sFDb")]
        E.sZO = [p.sem("sZOa"), p.sem("sZOb")]
        E.sV = p.sem("sV"); E.sS = p.sem("sS"); E.sT = p.sem("sT")
        sGC = p.sem("sGC")

        t_clr = [p.emit("vector", lambda v, _i, b_=b_: v.memset(E.PSB[b_][:], 0.0),
                        inc=E.sV) for b_ in range(8)]
        clr = tmax(*t_clr)
        t_io = p.emit("gpsimd", lambda g, _i: g.iota(
            io6[:], [[1, 128]], channel_multiplier=-1,
            allow_small_or_imprecise_dtypes=True), inc=sGC)
        E.t_id = p.emit("vector", lambda v, _i: v.tensor_scalar(
            out=E.ident[:], in0=io6[:], scalar1=0, scalar2=None, op0=ALU.is_equal),
            waits=tmax(t_io, clr), inc=E.sV)
        E.maskI = maskI

        cur_z = zin
        other = {id(zA): zB, id(zB): zA}
        last = None
        for i in range(IDEPTH):
            nxt = zA if cur_z is not zA else zB
            emit_p6a_sec(p, E, cur_z, nxt, iwqkv[i], iwo[i], iqs[i], iks[i], ropeI)
            cur_z = nxt
            nxt = zout if i == IDEPTH - 1 else (zA if cur_z is not zA else zB)
            last = emit_p6f_sec(p, E, cur_z, nxt, iw1[i], ib1[i], iw2[i], ib2[i],
                                ilng[i], ilnb[i])
            cur_z = nxt
        p.emit("sync", lambda s, _i: s.nop(), waits=last)
        p.emit("gpsimd", lambda g, _i: g.nop(), waits=last)
        p.finalize()
    return nc


def build_p6a():
    nc = bass.Bass("TRN2", target_bir_lowering=False)
    zin = nc.dram_tensor("zin", [T, D], F32, kind="ExternalInput").ap()
    wqkv = nc.dram_tensor("wqkv", [D, I3], F32, kind="ExternalInput").ap()
    wo = nc.dram_tensor("wo", [D, D], F32, kind="ExternalInput").ap()
    qs = nc.dram_tensor("qs", [DH], F32, kind="ExternalInput").ap()
    ks = nc.dram_tensor("ks", [DH], F32, kind="ExternalInput").ap()
    ropeI = nc.dram_tensor("ropeI", [4, T, DH], F32, kind="ExternalInput").ap()
    maskI = nc.dram_tensor("maskI", [128, 384], F32, kind="ExternalInput").ap()
    zout = nc.dram_tensor("zout", [T, D], F32, kind="ExternalOutput").ap()

    with ExitStack() as ctx:
        p = Prog(nc, ctx)
        sb = lambda n, sh, dt: ctx.enter_context(nc.sbuf_tensor(n, sh, dt))
        WQK = sb("WQK", [128, 8 * I3], BF16)
        WOB = sb("WOB", [128, 8 * 1024], BF16)
        QSB = sb("QSB", [128, 64], F32)
        KSB = sb("KSB", [128, 64], F32)
        MIB = sb("MIB", [128, 384], F32)
        RTB = sb("RTB", [128, 2 * 4 * 64], F32)
        SH = sb("SH", [128, 2 * 1024], F32)
        SHB = sb("SHB", [128, 2 * 1024], BF16)
        XR = sb("XR", [128, 2 * 1024], F32)
        XST = sb("XST", [128, 1024], BF16)
        QN = sb("QN", [128, 1024], F32)
        SQ6 = sb("SQ6", [128, 1024], F32)
        TMP6 = sb("TMP6", [128, 1024], F32)
        SM6 = sb("SM6", [128, 64], F32)
        QB = sb("QB", [128, 1024], BF16)
        KB = sb("KB", [128, 1024], BF16)
        QT = sb("QT", [128, 1024], BF16)
        KT = sb("KT", [128, 3 * 1024], BF16)
        VR = sb("VR", [128, 3 * 1024], BF16)
        S5 = sb("S5", [128, 384], F32)
        PR = sb("PR", [128, 384], BF16)
        PT = sb("PT", [128, 384], BF16)
        OB = sb("OB", [128, 1024], BF16)
        OFM = sb("OFM", [128, 1024], BF16)
        CT = sb("CT", [128, 2 * 1024], F32)
        ident = sb("ident", [128, 128], BF16)
        io6 = sb("io6", [128, 128], I32)
        PSB = [ctx.enter_context(nc.psum_tensor(f"psb{i}", [128, 512], F32))
               for i in range(8)]
        RS16a = SM6[:, 0:16]
        RS16b = SM6[:, 16:32]
        NM6 = SM6[:, 32:33]
        DN6 = SM6[:, 33:34]

        import dataclasses as _dc

        def shuf64(ap):
            return _dc.replace(ap, offset=ap.offset + 32,
                               ap=[[ap.ap[0][0], 128], [64, 16], [-32, 2], [1, 32]])

        sW = p.sem("sW")
        sM = p.sem("sM")
        sLD = [p.sem("sLDa"), p.sem("sLDb")]
        sR6 = [p.sem("sR6a"), p.sem("sR6b")]
        sZO = [p.sem("sZOa"), p.sem("sZOb")]
        sV = p.sem("sV"); sS = p.sem("sS"); sT = p.sem("sT"); sGC = p.sem("sGC")

        t_clr = [p.emit("vector", lambda v, _i, b_=b_: v.memset(PSB[b_][:], 0.0), inc=sV)
                 for b_ in range(8)]
        t_clr.append(p.emit("vector", lambda v, _i: v.memset(SH[:], 0.0), inc=sV))
        clr = tmax(*t_clr)
        t_io = p.emit("gpsimd", lambda g, _i: g.iota(io6[:], [[1, 128]], channel_multiplier=-1,
                                                     allow_small_or_imprecise_dtypes=True), inc=sGC)
        t_id = p.emit("vector", lambda v, _i: v.tensor_scalar(
            out=ident[:], in0=io6[:], scalar1=0, scalar2=None, op0=ALU.is_equal),
            waits=tmax(t_io), inc=sV)
        wl = []
        for dc in range(DC):
            wl.append(p.emit("gpsimd", lambda g, _i, dc=dc: g.dma_start(
                WQK[:, dc * I3:(dc + 1) * I3], wqkv[dc * 128:(dc + 1) * 128, :]),
                inc=sW, inc_by=16))
        for dc in range(DC):
            wl.append(p.emit("gpsimd", lambda g, _i, dc=dc: g.dma_start(
                WOB[:, dc * 1024:(dc + 1) * 1024], wo[dc * 128:(dc + 1) * 128, :]),
                inc=sW, inc_by=16))
        tok_w = wl[-1]
        m1_ = p.emit("sync", lambda s, _i: s.dma_start(
            QSB[:], qs[None, :].broadcast_to([128, DH])), inc=sM, inc_by=16)
        m2_ = p.emit("sync", lambda s, _i: s.dma_start(
            KSB[:], ks[None, :].broadcast_to([128, DH])), inc=sM, inc_by=16)
        m3_ = p.emit("sync", lambda s, _i: s.dma_start(MIB[:], maskI), inc=sM, inc_by=16)
        tok_m = tmax(m3_)

        wSH = [clr, clr]
        wSHB = [tmax(t_id), tmax(t_id)]
        wXR = [clr, clr]
        wQNprev = tmax(t_id)
        wTMPprev = tmax(t_id)
        evac_prev = tmax(t_id)
        tok_b45_read = tmax(t_id)
        tok_b6_read = {6: tmax(t_id), 7: tmax(t_id)}
        prev_tex = tmax(t_id)
        prev_trp = tmax(t_id)
        prev_ct = tmax(t_id)
        prev_rt = [clr, clr]
        prev_store = [clr, clr]
        tok_evq = None
        tok_evk = [None] * NT
        tok_vcp = [None] * NT

        for t in range(NT):
            b = t % 2
            k3 = t % 3
            shb = SH[:, b * 1024:(b + 1) * 1024]
            shbb = SHB[:, b * 1024:(b + 1) * 1024]
            xrb = XR[:, b * 1024:(b + 1) * 1024]
            l1 = p.emit("sync", lambda s, _i, t=t, shb=shb: s.dma_start(
                shb[:, 0:512], zin[t * 128:(t + 1) * 128, 0:512]),
                waits=tmax(wSH[b]), inc=sLD[b], inc_by=16)
            if t == 0:
                l2 = p.emit("sync", lambda s, _i, shb=shb: s.dma_start(
                    shb[1:128, 512:1024], zin[0:127, 512:1024]), inc=sLD[b], inc_by=16)
            else:
                l2 = p.emit("sync", lambda s, _i, t=t, shb=shb: s.dma_start(
                    shb[:, 512:1024], zin[t * 128 - 1:(t + 1) * 128 - 1, 512:1024]),
                    inc=sLD[b], inc_by=16)
            l3 = p.emit("sync", lambda s, _i, t=t, xrb=xrb: s.dma_start(
                xrb, zin[t * 128:(t + 1) * 128, :]),
                waits=tmax(wXR[b]), inc=sLD[b], inc_by=16)
            ld = tmax(l1, l2, l3)
            rts = [RTB[:, (b * 4 + j) * 64:(b * 4 + j + 1) * 64] for j in range(4)]
            rl = []
            for j in range(4):
                rl.append(p.emit("sync", lambda s, _i, t=t, j=j, rts=rts: s.dma_start(
                    rts[j], ropeI[j, t * 128:(t + 1) * 128, :]),
                    waits=tmax(prev_rt[b]), inc=sR6[b], inc_by=16))
            t_rt = tmax(*rl)
            c1 = p.emit("vector", lambda v, _i, shbb=shbb, shb=shb: v.tensor_copy(shbb, shb),
                        waits=tmax(ld, wSHB[b], clr), inc=sV)
            wSH[b] = c1
            trl = None
            for c in range(DC):
                trl = p.emit("tensor", lambda te, _i, c=c, shbb=shbb: te.matmul(
                    PSB[0][:, c * 64:c * 64 + 64].bitcast(BF16),
                    shbb[:, c * 128:(c + 1) * 128], ident[:],
                    is_transpose=True, start=True, stop=True),
                    waits=tmax(c1, t_id, evac_prev if c == 0 else None), inc=sT)
            wSHB[b] = trl
            evl = None
            for c in range(DC):
                evl = p.emit("vector", lambda v, _i, c=c: v.tensor_copy(
                    XST[:, c * 128:(c + 1) * 128],
                    PSB[0][:, c * 64:c * 64 + 64].bitcast(BF16)),
                    waits=tmax(trl), inc=sV)
            evac_prev = evl
            for wi, wn in enumerate(("q", "k", "v")):
                mm = None
                for dc in range(DC):
                    for hf in range(2):
                        w = [evl, tok_b45_read] if dc == 0 else []
                        if t == 0 and wi == 0 and dc == 0:
                            w.append(tok_w)
                        mm = p.emit("tensor", lambda te, _i, wi=wi, dc=dc, hf=hf: te.matmul(
                            PSB[4 + hf][:, 0:512],
                            XST[:, dc * 128:(dc + 1) * 128],
                            WQK[:, dc * I3 + wi * 1024 + hf * 512:
                                dc * I3 + wi * 1024 + (hf + 1) * 512],
                            start=(dc == 0), stop=(dc == DC - 1)),
                            waits=tmax(*w), inc=sT)
                if wn == "v":
                    lastv = None
                    for hf in range(2):
                        lastv = p.emit("vector", lambda v, _i, k3=k3, hf=hf: v.tensor_copy(
                            VR[:, k3 * 1024 + hf * 512:k3 * 1024 + (hf + 1) * 512],
                            PSB[4 + hf][:, 0:512]),
                            waits=tmax(mm), inc=sV)
                    tok_vcp[t] = lastv
                    tok_b45_read = lastv
                else:
                    qn = None
                    for hf in range(2):
                        qn = p.emit("vector", lambda v, _i, hf=hf: v.tensor_copy(
                            QN[:, hf * 512:(hf + 1) * 512], PSB[4 + hf][:, 0:512]),
                            waits=tmax(mm, wQNprev), inc=sV)
                    tok_b45_read = qn
                    t_sq = p.emit("scalar", lambda sc, _i: sc.activation(
                        out=SQ6[:], in_=QN[:], func=AF.Square), waits=tmax(qn), inc=sS)
                    t_rs = p.emit("vector", lambda v, _i: v.tensor_reduce(
                        RS16a, SQ6.rearrange("p (h c) -> p h c", c=64), axis=AX,
                        op=ALU.add), waits=tmax(t_sq), inc=sV, fence=True)
                    t_sr = p.emit("scalar", lambda sc, _i: sc.activation(
                        out=RS16b, in_=RS16a, func=AF.Sqrt), waits=tmax(t_rs), inc=sS,
                        fence=True)
                    t_mx = p.emit("vector", lambda v, _i: v.tensor_scalar(
                        out=RS16b, in0=RS16b, scalar1=1.0e-12, scalar2=None,
                        op0=ALU.max), waits=tmax(t_sr), inc=sV, fence=True)
                    t_rc = p.emit("vector", lambda v, _i: v.reciprocal(RS16b, RS16b),
                                  waits=tmax(t_mx), inc=sV, fence=True)
                    scb = QSB if wn == "q" else KSB
                    m1 = p.emit("vector", lambda v, _i: v.tensor_tensor(
                        out=QN.rearrange("p (h c) -> p h c", c=64),
                        in0=QN.rearrange("p (h c) -> p h c", c=64),
                        in1=RS16b[:, :, None].broadcast_to([128, 16, 64]),
                        op=ALU.mult), waits=tmax(t_rc), inc=sV)
                    m2 = p.emit("vector", lambda v, _i, scb=scb: v.tensor_tensor(
                        out=QN.rearrange("p (h c) -> p h c", c=64),
                        in0=QN.rearrange("p (h c) -> p h c", c=64),
                        in1=scb[:, None, :].broadcast_to([128, 16, 64]),
                        op=ALU.mult), waits=tmax(m1, tok_m), inc=sV)
                    j0 = 0 if wn == "q" else 2
                    dstb = QB if wn == "q" else KB
                    t2 = p.emit("vector", lambda v, _i, rts=rts, j0=j0: v.tensor_tensor(
                        out=TMP6.rearrange("p (h c) -> p h c", c=64),
                        in0=shuf64(QN[:]),
                        in1=rts[j0 + 1][:, None, :].broadcast_to([128, 16, 64]),
                        op=ALU.mult), waits=tmax(m2, t_rt, wTMPprev), inc=sV)
                    t3 = p.emit("vector", lambda v, _i, rts=rts, j0=j0: v.tensor_tensor(
                        out=QN.rearrange("p (h c) -> p h c", c=64),
                        in0=QN.rearrange("p (h c) -> p h c", c=64),
                        in1=rts[j0][:, None, :].broadcast_to([128, 16, 64]),
                        op=ALU.mult), waits=tmax(t2), inc=sV)
                    t4 = p.emit("vector", lambda v, _i, dstb=dstb: v.tensor_add(
                        dstb[:], QN[:], TMP6[:]), waits=tmax(t3), inc=sV)
                    wQNprev = t4
                    wTMPprev = t4
                    dfm = QT[:] if wn == "q" else KT[:, k3 * 1024:(k3 + 1) * 1024]
                    trq = None
                    for c in range(DC):
                        trq = p.emit("tensor", lambda te, _i, c=c, dstb=dstb: te.matmul(
                            PSB[6][:, c * 64:c * 64 + 64].bitcast(BF16),
                            dstb[:, c * 128:(c + 1) * 128], ident[:],
                            is_transpose=True, start=True, stop=True),
                            waits=tmax(t4, tok_b6_read[6] if c == 0 else None), inc=sT)
                    evq = None
                    for c in range(DC):
                        evq = p.emit("vector", lambda v, _i, c=c, dfm=dfm: v.tensor_copy(
                            dfm[:, c * 128:(c + 1) * 128],
                            PSB[6][:, c * 64:c * 64 + 64].bitcast(BF16)),
                            waits=tmax(trq), inc=sV)
                    tok_b6_read[6] = evq
                    if wn == "q":
                        tok_evq = evq
                    else:
                        tok_evk[t] = evq
            prev_rt[b] = wTMPprev
            nk = min(t + 1, 3)
            W_ = 128 * nk
            pv_done = []
            for h in range(NH):
                sbk = 6 + (h % 2)
                smms = []
                for ki in range(nk):
                    kt = t - (nk - 1) + ki
                    ksl = kt % 3
                    w = []
                    if ki == 0:
                        w = [tok_evq, tok_evk[kt], tok_vcp[kt], tok_b6_read[sbk]]
                    smms.append(p.emit("tensor", lambda te, _i, h=h, ki=ki, ksl=ksl, sbk=sbk: te.matmul(
                        PSB[sbk][:, ki * 128:(ki + 1) * 128],
                        QT[(h % 2) * 64:(h % 2) * 64 + 64,
                           (h // 2) * 128:(h // 2) * 128 + 128],
                        KT[(h % 2) * 64:(h % 2) * 64 + 64,
                           ksl * 1024 + (h // 2) * 128:ksl * 1024 + (h // 2) * 128 + 128],
                        start=True, stop=True),
                        waits=tmax(*w), inc=sT))
                tmsk = p.emit("vector", lambda v, _i, W_=W_, nk=nk, sbk=sbk: v.tensor_tensor(
                    out=S5[:, 0:W_], in0=PSB[sbk][:, 0:W_],
                    in1=MIB[:, (3 - nk) * 128:384], op=ALU.add),
                    waits=tmax(*smms, tok_m, prev_tex), inc=sV)
                tok_b6_read[sbk] = tmsk
                tmx = p.emit("vector", lambda v, _i, W_=W_: v.reduce_max(
                    NM6, S5[:, 0:W_], axis=AX, negate=True),
                    waits=tmax(tmsk), inc=sV, fence=True)
                tex = p.emit("scalar", lambda sc, _i, W_=W_: sc.activation(
                    out=PR[:, 0:W_], in_=S5[:, 0:W_], func=AF.Exp,
                    bias=NM6, accum_out=DN6),
                    waits=tmax(tmx, prev_trp), inc=sS, fence=True)
                prev_tex = tex
                trc = p.emit("vector", lambda v, _i: v.reciprocal(DN6, DN6),
                             waits=tmax(tex), inc=sV, fence=True)
                tnr = p.emit("vector", lambda v, _i, W_=W_: v.tensor_scalar(
                    out=PR[:, 0:W_], in0=PR[:, 0:W_], scalar1=DN6,
                    scalar2=None, op0=ALU.mult), waits=tmax(trc), inc=sV)
                trp = []
                for ki in range(nk):
                    trp.append(p.emit("tensor", lambda te, _i, ki=ki: te.matmul(
                        PSB[5][:, ki * 64:ki * 64 + 64].bitcast(BF16),
                        PR[:, ki * 128:(ki + 1) * 128], ident[:],
                        is_transpose=True, start=True, stop=True),
                        waits=tmax(tnr), inc=sT))
                prev_trp = trp[-1]
                tpt = []
                for ki in range(nk):
                    tpt.append(p.emit("vector", lambda v, _i, ki=ki: v.tensor_copy(
                        PT[:, ki * 128:(ki + 1) * 128],
                        PSB[5][:, ki * 64:ki * 64 + 64].bitcast(BF16)),
                        waits=tmax(trp[ki]), inc=sV))
                last = None
                for ki in range(nk):
                    kt = t - (nk - 1) + ki
                    ksl = kt % 3
                    w = [tpt[ki]]
                    if h < 2 and ki == 0:
                        w.append(prev_ct)
                    last = p.emit("tensor", lambda te, _i, h=h, ki=ki, ksl=ksl, nk=nk: te.matmul(
                        PSB[2 if h < 8 else 3][:, (h % 8) * 64:(h % 8) * 64 + 64],
                        PT[:, ki * 128:(ki + 1) * 128],
                        VR[:, ksl * 1024 + h * 64:ksl * 1024 + h * 64 + 64],
                        start=(ki == 0), stop=(ki == nk - 1)),
                        waits=tmax(*w), inc=sT)
                pv_done.append(last)
            tob = []
            for hb in range(2):
                tob.append(p.emit("vector", lambda v, _i, hb=hb: v.tensor_copy(
                    OB[:, hb * 512:(hb + 1) * 512], PSB[2 + hb][:, 0:512]),
                    waits=tmax(*pv_done), inc=sV))
            trw = None
            for c in range(DC):
                trw = p.emit("tensor", lambda te, _i, c=c: te.matmul(
                    PSB[4 + (c % 2)][:, (c // 2) * 128:(c // 2) * 128 + 64].bitcast(BF16),
                    OB[:, c * 128:(c + 1) * 128], ident[:],
                    is_transpose=True, start=True, stop=True),
                    waits=tmax(*tob), inc=sT)
            tofm = None
            for c in range(DC):
                tofm = p.emit("vector", lambda v, _i, c=c: v.tensor_copy(
                    OFM[:, c * 128:(c + 1) * 128],
                    PSB[4 + (c % 2)][:, (c // 2) * 128:(c // 2) * 128 + 64].bitcast(BF16)),
                    waits=tmax(trw), inc=sV)
            tok_b45_read = tofm
            wo_last = None
            for dc in range(DC):
                for hf in range(2):
                    w = [tofm]
                    if dc == 0 and hf == 0:
                        w.append(prev_ct)
                    wo_last = p.emit("tensor", lambda te, _i, dc=dc, hf=hf: te.matmul(
                        PSB[2 + hf][:, 0:512],
                        OFM[:, dc * 128:(dc + 1) * 128],
                        WOB[:, dc * 1024 + hf * 512:dc * 1024 + (hf + 1) * 512],
                        start=(dc == 0), stop=(dc == DC - 1)),
                        waits=tmax(*w), inc=sT)
            ctb = CT[:, b * 1024:(b + 1) * 1024]
            tct = None
            for hf in range(2):
                tct = p.emit("vector", lambda v, _i, ctb=ctb, hf=hf, xrb=xrb: v.tensor_tensor(
                    out=ctb[:, hf * 512:(hf + 1) * 512],
                    in0=PSB[2 + hf][:, 0:512],
                    in1=xrb[:, hf * 512:(hf + 1) * 512], op=ALU.add),
                    waits=tmax(wo_last, ld, prev_store[b]), inc=sV)
            prev_ct = tct
            wXR[b] = tct
            prev_store[b] = p.emit("sync", lambda s, _i, t=t, ctb=ctb: s.dma_start(
                zout[t * 128:(t + 1) * 128, :], ctb),
                waits=tmax(tct), inc=sZO[b], inc_by=16)
        p.emit("sync", lambda s, _i: s.nop(), waits=tmax(prev_store[0], prev_store[1]))
        p.emit("gpsimd", lambda g, _i: g.nop(), waits=tmax(prev_store[0], prev_store[1]))
        p.finalize()
    return nc



def build_p6f():
    nc = bass.Bass("TRN2", target_bir_lowering=False)
    zin = nc.dram_tensor("zin", [T, D], F32, kind="ExternalInput").ap()
    w1 = nc.dram_tensor("w1", [D, DI2], F32, kind="ExternalInput").ap()
    b1 = nc.dram_tensor("b1", [DI2], F32, kind="ExternalInput").ap()
    w2 = nc.dram_tensor("w2", [DI, D], F32, kind="ExternalInput").ap()
    b2 = nc.dram_tensor("b2", [D], F32, kind="ExternalInput").ap()
    lng = nc.dram_tensor("lng", [D], F32, kind="ExternalInput").ap()
    lnb = nc.dram_tensor("lnb", [D], F32, kind="ExternalInput").ap()
    zout = nc.dram_tensor("zout", [T, D], F32, kind="ExternalOutput").ap()

    NFC = 22
    with ExitStack() as ctx:
        p = Prog(nc, ctx)
        sb = lambda n, sh, dt: ctx.enter_context(nc.sbuf_tensor(n, sh, dt))
        W1B = sb("W1B", [128, 8 * DI2], BF16)
        W2B = sb("W2B", [128, NFC * 1024], BF16)
        B1BC = sb("B1BC", [128, DI2], F32)
        GBC = sb("GBC", [128, D], F32)
        BBC = sb("BBC", [128, D], F32)
        B2BC = sb("B2BC", [128, D], F32)
        F2 = sb("F2", [128, 2 * 1024], F32)
        FR = sb("FR", [128, 2 * 1024], F32)
        XNS = sb("XNS", [128, 1024], F32)
        XNB = sb("XNB", [128, 1024], BF16)
        XNT6 = sb("XNT6", [128, 1024], BF16)
        ASB = sb("ASB", [128, 512], F32)
        GSB = sb("GSB", [128, 512], F32)
        GEB = sb("GEB", [128, 512], BF16)
        HB = sb("HB", [128, DI], BF16)
        HTB = sb("HTB", [128, NFC * 128], BF16)
        OUTB = sb("OUTB", [128, 1024], F32)
        SM6 = sb("SM6", [128, 64], F32)
        ident = sb("ident", [128, 128], BF16)
        io6 = GEB[:, 0:256].bitcast(I32)
        PSB = [ctx.enter_context(nc.psum_tensor(f"psb{i}", [128, 512], F32))
               for i in range(8)]

        sW = p.sem("sW")
        sM = p.sem("sM")
        sFD = [p.sem("sFDa"), p.sem("sFDb")]
        sZO = [p.sem("sZOa"), p.sem("sZOb")]
        sV = p.sem("sV"); sS = p.sem("sS"); sT = p.sem("sT"); sGC = p.sem("sGC")

        t_clr = [p.emit("vector", lambda v, _i, b_=b_: v.memset(PSB[b_][:], 0.0), inc=sV)
                 for b_ in range(8)]
        t_clr.append(p.emit("vector", lambda v, _i: v.memset(F2[:], 0.0), inc=sV))
        clr = tmax(*t_clr)
        t_io = p.emit("gpsimd", lambda g, _i: g.iota(io6[:], [[1, 128]], channel_multiplier=-1,
                                                     allow_small_or_imprecise_dtypes=True), inc=sGC)
        t_id = p.emit("vector", lambda v, _i: v.tensor_scalar(
            out=ident[:], in0=io6[:], scalar1=0, scalar2=None, op0=ALU.is_equal),
            waits=tmax(t_io), inc=sV)
        wl = []
        for dc in range(DC):
            wl.append(p.emit("gpsimd", lambda g, _i, dc=dc: g.dma_start(
                W1B[:, dc * DI2:(dc + 1) * DI2], w1[dc * 128:(dc + 1) * 128, :]),
                inc=sW, inc_by=16))
        for fc in range(NFC):
            cw = min(128, DI - fc * 128)
            wl.append(p.emit("gpsimd", lambda g, _i, fc=fc, cw=cw: g.dma_start(
                W2B[0:cw, fc * 1024:(fc + 1) * 1024], w2[fc * 128:fc * 128 + cw, :]),
                inc=sW, inc_by=16))
        tok_w = wl[-1]
        ml = []
        ml.append(p.emit("sync", lambda s, _i: s.dma_start(
            B1BC[:], b1[None, :].broadcast_to([128, DI2])), inc=sM, inc_by=16))
        ml.append(p.emit("sync", lambda s, _i: s.dma_start(
            GBC[:], lng[None, :].broadcast_to([128, D])), inc=sM, inc_by=16))
        ml.append(p.emit("sync", lambda s, _i: s.dma_start(
            BBC[:], lnb[None, :].broadcast_to([128, D])), inc=sM, inc_by=16))
        ml.append(p.emit("sync", lambda s, _i: s.dma_start(
            B2BC[:], b2[None, :].broadcast_to([128, D])), inc=sM, inc_by=16))
        tok_m = ml[-1]

        EPSB = SM6[:, 40:41]
        t_eps = p.emit("vector", lambda v, _i: v.memset(EPSB, 1.0e-5), inc=sV)

        wF2 = [clr, clr]
        wFR = [clr, clr]
        wXNS = tmax(t_id)
        wXNB = tmax(t_id)
        evac_prev = tmax(t_id)
        tok_bank_read = [tmax(t_id)] * 3
        prev_th = tmax(t_id)
        prev_ht_evac = [tmax(t_id), tmax(t_id)]
        prev_w2_read = tmax(t_id)
        prev_store = [clr, clr]
        prev_out_read = clr

        for t in range(NT):
            b = t % 2
            f2b = F2[:, b * 1024:(b + 1) * 1024]
            frb = FR[:, b * 1024:(b + 1) * 1024]
            outb = OUTB[:]
            f1 = p.emit("sync", lambda s, _i, t=t, f2b=f2b: s.dma_start(
                f2b[:, 0:512], zin[t * 128:(t + 1) * 128, 0:512]),
                waits=tmax(wF2[b]), inc=sFD[b], inc_by=16)
            if t == 0:
                f2_ = p.emit("sync", lambda s, _i, f2b=f2b: s.dma_start(
                    f2b[1:128, 512:1024], zin[0:127, 512:1024]), inc=sFD[b], inc_by=16)
            else:
                f2_ = p.emit("sync", lambda s, _i, t=t, f2b=f2b: s.dma_start(
                    f2b[:, 512:1024], zin[t * 128 - 1:(t + 1) * 128 - 1, 512:1024]),
                    inc=sFD[b], inc_by=16)
            f3 = p.emit("sync", lambda s, _i, t=t, frb=frb: s.dma_start(
                frb, zin[t * 128:(t + 1) * 128, :]),
                waits=tmax(wFR[b]), inc=sFD[b], inc_by=16)
            ld = tmax(f1, f2_, f3)
            meanraw = SM6[:, 0:1]; sqs = SM6[:, 1:2]; var0 = SM6[:, 2:3]
            rstd = SM6[:, 3:4]; mean = SM6[:, 4:5]; msq = SM6[:, 5:6]
            t_m0 = p.emit("vector", lambda v, _i, f2b=f2b, meanraw=meanraw: v.reduce_sum(
                meanraw, f2b, axis=AX), waits=tmax(ld), inc=sV, fence=True)
            t_s0 = p.emit("scalar", lambda sc, _i, f2b=f2b, sqs=sqs: sc.activation(
                out=XNS[:], in_=f2b, func=AF.Square, accum_out=sqs),
                waits=tmax(ld, wXNS), inc=sS, fence=True)
            p.emit("vector", lambda v, _i, mean=mean, meanraw=meanraw: v.tensor_scalar(
                out=mean, in0=meanraw, scalar1=1.0 / D, scalar2=None, op0=ALU.mult),
                waits=tmax(t_m0), inc=sV)
            t_msq = p.emit("vector", lambda v, _i, msq=msq, meanraw=meanraw: v.tensor_scalar(
                out=msq, in0=meanraw, scalar1=meanraw, scalar2=None, op0=ALU.mult),
                inc=sV, fence=True)
            t_v0 = p.emit("vector", lambda v, _i, var0=var0, sqs=sqs, msq=msq: v.scalar_tensor_tensor(
                out=var0, in0=sqs, scalar=float(D), in1=msq, op0=ALU.mult, op1=ALU.subtract),
                waits=tmax(t_s0, t_msq), inc=sV, fence=True)
            t_sq = p.emit("scalar", lambda sc, _i, var0=var0, rstd=rstd: sc.activation(
                out=rstd, in_=var0, func=AF.Sqrt, bias=EPSB, scale=1.0 / (D * D)),
                waits=tmax(t_v0, t_eps), inc=sS, fence=True)
            t_rc = p.emit("vector", lambda v, _i, rstd=rstd: v.reciprocal(rstd, rstd),
                          waits=tmax(t_sq), inc=sV, fence=True)
            t_xn = p.emit("vector", lambda v, _i, f2b=f2b, mean=mean, rstd=rstd: v.tensor_scalar(
                out=XNS[:], in0=f2b, scalar1=mean, scalar2=rstd,
                op0=ALU.subtract, op1=ALU.mult), waits=tmax(t_rc, t_s0), inc=sV)
            wF2[b] = tmax(t_xn, t_s0)
            t_xg = p.emit("vector", lambda v, _i: v.tensor_tensor(
                out=XNS[:], in0=XNS[:], in1=GBC[:], op=ALU.mult),
                waits=tmax(t_xn, tok_m), inc=sV)
            t_xb = p.emit("vector", lambda v, _i: v.tensor_add(
                XNB[:], XNS[:], BBC[:]), waits=tmax(t_xg, wXNB), inc=sV)
            wXNS = t_xb
            trl = None
            for c in range(DC):
                trl = p.emit("tensor", lambda te, _i, c=c: te.matmul(
                    PSB[0][:, c * 64:c * 64 + 64].bitcast(BF16),
                    XNB[:, c * 128:(c + 1) * 128], ident[:],
                    is_transpose=True, start=True, stop=True),
                    waits=tmax(t_xb, t_id, evac_prev if c == 0 else None), inc=sT)
            wXNB = trl
            evl = None
            for c in range(DC):
                evl = p.emit("vector", lambda v, _i, c=c: v.tensor_copy(
                    XNT6[:, c * 128:(c + 1) * 128],
                    PSB[0][:, c * 64:c * 64 + 64].bitcast(BF16)),
                    waits=tmax(trl), inc=sV)
            evac_prev = evl
            th_tok = [None] * 6
            for j in range(6):
                aw = 512 if j < 5 else 170
                bkA = 2 + (j % 3) * 2
                bkB = 3 + (j % 3) * 2
                mmA = None; mmB = None
                for dc in range(DC):
                    w = []
                    if dc == 0:
                        w = [evl, tok_bank_read[j % 3]]
                        if t == 0 and j == 0:
                            w.append(tok_w)
                    mmA = p.emit("tensor", lambda te, _i, j=j, dc=dc, aw=aw, bkA=bkA: te.matmul(
                        PSB[bkA][:, 0:aw],
                        XNT6[:, dc * 128:(dc + 1) * 128],
                        W1B[:, dc * DI2 + j * 512:dc * DI2 + j * 512 + aw],
                        start=(dc == 0), stop=(dc == DC - 1)),
                        waits=tmax(*w), inc=sT)
                for dc in range(DC):
                    mmB = p.emit("tensor", lambda te, _i, j=j, dc=dc, aw=aw, bkB=bkB: te.matmul(
                        PSB[bkB][:, 0:aw],
                        XNT6[:, dc * 128:(dc + 1) * 128],
                        W1B[:, dc * DI2 + DI + j * 512:dc * DI2 + DI + j * 512 + aw],
                        start=(dc == 0), stop=(dc == DC - 1)),
                        inc=sT)
                ta = p.emit("vector", lambda v, _i, j=j, aw=aw, bkA=bkA: v.tensor_tensor(
                    out=ASB[:, 0:aw], in0=PSB[bkA][:, 0:aw],
                    in1=B1BC[:, j * 512:j * 512 + aw], op=ALU.add),
                    waits=tmax(mmA, mmB, tok_m, prev_th), inc=sV)
                tg = p.emit("vector", lambda v, _i, j=j, aw=aw, bkB=bkB: v.tensor_tensor(
                    out=GSB[:, 0:aw], in0=PSB[bkB][:, 0:aw],
                    in1=B1BC[:, DI + j * 512:DI + j * 512 + aw], op=ALU.add),
                    waits=tmax(ta), inc=sV)
                tok_bank_read[j % 3] = tg
                ge = p.emit("scalar", lambda sc, _i, aw=aw: sc.activation(
                    out=GEB[:, 0:aw], in_=GSB[:, 0:aw], func=AF.Gelu),
                    waits=tmax(tg, t_id if (t == 0 and j == 0) else None), inc=sS)
                th = p.emit("vector", lambda v, _i, j=j, aw=aw: v.tensor_tensor(
                    out=HB[:, j * 512:j * 512 + aw], in0=ASB[:, 0:aw],
                    in1=GEB[:, 0:aw], op=ALU.mult),
                    waits=tmax(ge), inc=sV)
                th_tok[j] = th
                prev_th = th
            ht_ev = None
            for fc in range(NFC):
                cw = min(128, DI - fc * 128)
                bk = fc % 2
                trh = p.emit("tensor", lambda te, _i, fc=fc, cw=cw, bk=bk: te.matmul(
                    PSB[bk][:, 0:64].bitcast(BF16)[0:cw, :],
                    HB[:, fc * 128:fc * 128 + cw], ident[:],
                    is_transpose=True, start=True, stop=True),
                    waits=tmax(th_tok[min(fc // 4, 5)], prev_ht_evac[bk],
                               th_tok[5] if fc == 0 else None), inc=sT)
                ht_ev = p.emit("vector", lambda v, _i, fc=fc, cw=cw, bk=bk: v.tensor_copy(
                    HTB[0:cw, fc * 128:(fc + 1) * 128],
                    PSB[bk][:, 0:64].bitcast(BF16)[0:cw, :]),
                    waits=tmax(trh), inc=sV)
                prev_ht_evac[bk] = ht_ev
            w2_last = None
            for fc in range(NFC):
                cw = min(128, DI - fc * 128)
                for hf in range(2):
                    w = []
                    if fc == 0:
                        w = [ht_ev, prev_w2_read]
                    w2_last = p.emit("tensor", lambda te, _i, fc=fc, cw=cw, hf=hf: te.matmul(
                        PSB[6 + hf][:, 0:512],
                        HTB[0:cw, fc * 128:(fc + 1) * 128],
                        W2B[0:cw, fc * 1024 + hf * 512:fc * 1024 + (hf + 1) * 512],
                        start=(fc == 0), stop=(fc == NFC - 1)),
                        waits=tmax(*w), inc=sT)
            to2 = None
            for hf in range(2):
                to1 = p.emit("vector", lambda v, _i, outb=outb, hf=hf: v.tensor_tensor(
                    out=outb[:, hf * 512:(hf + 1) * 512],
                    in0=PSB[6 + hf][:, 0:512],
                    in1=B2BC[:, hf * 512:(hf + 1) * 512], op=ALU.add),
                    waits=tmax(w2_last, tok_m, prev_out_read), inc=sV)
                to2 = p.emit("vector", lambda v, _i, outb=outb, frb=frb, hf=hf: v.tensor_add(
                    outb[:, hf * 512:(hf + 1) * 512],
                    outb[:, hf * 512:(hf + 1) * 512],
                    frb[:, hf * 512:(hf + 1) * 512]),
                    waits=tmax(to1, ld), inc=sV)
            prev_w2_read = to2
            tok_bank_read[2] = to2
            wFR[b] = to2
            prev_out_read = p.emit("sync", lambda s, _i, t=t, outb=outb: s.dma_start(
                zout[t * 128:(t + 1) * 128, :], outb),
                waits=tmax(to2), inc=sZO[b], inc_by=16)
            prev_store[b] = prev_out_read
        p.emit("sync", lambda s, _i: s.nop(), waits=tmax(prev_store[0], prev_store[1]))
        p.emit("gpsimd", lambda g, _i: g.nop(), waits=tmax(prev_store[0], prev_store[1]))
        p.finalize()
    return nc



def build_p7s():
    nc = bass.Bass("TRN2", target_bir_lowering=False)
    x1 = nc.dram_tensor("x1", [T, D], F32, kind="ExternalInput").ap()
    a1 = nc.dram_tensor("a1", [T, D], F32, kind="ExternalInput").ap()
    z6 = nc.dram_tensor("z6", [T, D], F32, kind="ExternalInput").ap()
    y = nc.dram_tensor("y", [T, D], F32, kind="ExternalOutput").ap()
    with ExitStack() as ctx:
        p = Prog(nc, ctx)
        X7 = ctx.enter_context(nc.sbuf_tensor("X7", [128, 2 * 3 * 1024], F32))
        sL7 = [p.sem("sL7a"), p.sem("sL7b")]
        sY7 = [p.sem("sY7a"), p.sem("sY7b")]
        sV = p.sem("sV")
        prev7 = [None, None]
        for t in range(NT):
            b = t % 2
            xb = X7[:, b * 3072:b * 3072 + 1024]
            ab = X7[:, b * 3072 + 1024:b * 3072 + 2048]
            zb = X7[:, b * 3072 + 2048:b * 3072 + 3072]
            l1 = p.emit("sync", lambda s, _i, t=t, xb=xb: s.dma_start(
                xb, x1[t * 128:(t + 1) * 128, :]), waits=tmax(prev7[b]),
                inc=sL7[b], inc_by=16)
            l2 = p.emit("sync", lambda s, _i, t=t, ab=ab: s.dma_start(
                ab, a1[t * 128:(t + 1) * 128, :]), waits=tmax(prev7[b]),
                inc=sL7[b], inc_by=16)
            l3 = p.emit("sync", lambda s, _i, t=t, zb=zb: s.dma_start(
                zb, z6[t * 128:(t + 1) * 128, :]), waits=tmax(prev7[b]),
                inc=sL7[b], inc_by=16)
            a_1 = p.emit("vector", lambda v, _i, xb=xb, ab=ab: v.tensor_add(
                xb, xb, ab), waits=tmax(l1, l2, l3), inc=sV)
            a_2 = p.emit("vector", lambda v, _i, xb=xb, zb=zb: v.tensor_add(
                xb, xb, zb), waits=tmax(a_1), inc=sV)
            prev7[b] = p.emit("sync", lambda s, _i, t=t, xb=xb: s.dma_start(
                y[t * 128:(t + 1) * 128, :], xb), waits=tmax(a_2),
                inc=sY7[b], inc_by=16)
        p.emit("sync", lambda s, _i: s.nop(), waits=tmax(prev7[0], prev7[1]))
        p.emit("gpsimd", lambda g, _i: g.nop(), waits=tmax(prev7[0], prev7[1]))
        p.finalize()
    return nc


def prep_core_inputs(inputs, core):
    g = inputs
    cm = {
        "x": np.ascontiguousarray(g["L0_attn_x"][core]),
        "wqT": np.ascontiguousarray(g["L0_attn_Wq"].T),
        "wkT": np.ascontiguousarray(g["L0_attn_Wk"].T),
        "wvT": np.ascontiguousarray(g["L0_attn_Wv"].T),
        "woT": np.ascontiguousarray(g["L0_attn_Wo"].T),
        "bq": g["L0_attn_bq"], "bk": g["L0_attn_bk"],
        "bv": g["L0_attn_bv"], "bo": g["L0_attn_bo"],
        "wrT": np.ascontiguousarray(g["L0_ffn_Wr"].T),
        "br": g["L0_ffn_br"],
        "w1T": np.ascontiguousarray(np.swapaxes(g["L0_ffn_W1"], 1, 2)),
        "b1": g["L0_ffn_b1"],
        "w2T": np.ascontiguousarray(np.swapaxes(g["L0_ffn_W2"], 1, 2)),
        "b2": g["L0_ffn_b2"],
        "awqT": np.ascontiguousarray(g["L1_attn_Wq"].T),
        "awkT": np.ascontiguousarray(g["L1_attn_Wk"].T),
        "awvT": np.ascontiguousarray(g["L1_attn_Wv"].T),
        "awoT": np.ascontiguousarray(g["L1_attn_Wo"].T),
        "abq": g["L1_attn_bq"], "abk": g["L1_attn_bk"],
        "abv": g["L1_attn_bv"], "abo": g["L1_attn_bo"],
        "iwqkv": g["L1_ffn_Wqkv"], "iqs": g["L1_ffn_q_scale"], "iks": g["L1_ffn_k_scale"],
        "iwo": g["L1_ffn_Wo"], "ilng": g["L1_ffn_ln_g"], "ilnb": g["L1_ffn_ln_b"],
        "iw1": g["L1_ffn_W1"], "ib1": g["L1_ffn_b1"],
        "iw2": g["L1_ffn_W2"], "ib2": g["L1_ffn_b2"],
    }
    cm.update(host_constants())
    return {k: np.ascontiguousarray(v) for k, v in cm.items()}


def kernel(**inputs):
    cores = list(range(8))
    nc1, _ = build_program(phases=("p0", "p1", "p2", "p4", "p5", "p6", "p7"))
    in_maps = [prep_core_inputs(inputs, core) for core in cores]
    r1 = run_bass_kernel_spmd(nc1, in_maps, core_ids=cores).results
    return np.stack([r["y"] for r in r1], axis=0)


def kernel_3launch(**inputs):
    cores = list(range(8))
    g = inputs
    nc1, _ = build_program(phases=("p0", "p1", "p2", "p4", "p5"))
    in_maps = [prep_core_inputs(inputs, core) for core in cores]
    r1 = run_bass_kernel_spmd(nc1, in_maps, core_ids=cores).results
    x1s = [r["x1s"] for r in r1]
    a1s = [r["attn1_c"] for r in r1]
    z = [r["zs"] for r in r1]
    hc = host_constants()
    nc6 = build_p6chain()
    wk = {"iwqkv": np.ascontiguousarray(g["L1_ffn_Wqkv"]),
          "iqs": np.ascontiguousarray(g["L1_ffn_q_scale"]),
          "iks": np.ascontiguousarray(g["L1_ffn_k_scale"]),
          "iwo": np.ascontiguousarray(g["L1_ffn_Wo"]),
          "ilng": np.ascontiguousarray(g["L1_ffn_ln_g"]),
          "ilnb": np.ascontiguousarray(g["L1_ffn_ln_b"]),
          "iw1": np.ascontiguousarray(g["L1_ffn_W1"]),
          "ib1": np.ascontiguousarray(g["L1_ffn_b1"]),
          "iw2": np.ascontiguousarray(g["L1_ffn_W2"]),
          "ib2": np.ascontiguousarray(g["L1_ffn_b2"]),
          "ropeI": hc["ropeI"], "maskI": hc["maskI"]}
    m6 = [dict(zin=z[c], **wk) for c in cores]
    r6 = run_bass_kernel_spmd(nc6, m6, core_ids=cores).results
    z = [r["zout"] for r in r6]
    nc7 = build_p7s()
    m7 = [{"x1": x1s[c], "a1": a1s[c], "z6": z[c]} for c in cores]
    r7 = run_bass_kernel_spmd(nc7, m7, core_ids=cores).results
    return np.stack([r["y"] for r in r7], axis=0)
